# version 10
# baseline (speedup 1.0000x reference)
import contextlib
import numpy as np
import ml_dtypes
import concourse.bass as bass
import concourse.mybir as mybir
from concourse.bass_utils import run_bass_kernel_spmd

F32 = mybir.dt.float32
BF16 = mybir.dt.bfloat16
AF = mybir.ActivationFunctionType
ALU = mybir.AluOpType
AX = mybir.AxisListType

NC = 8
T = 2048
D = 1024
I = 2816
H = 16
DH = 64
SC = 128
NSC = T // SC
NPT = D // 128
NIT = I // 128
NTT = T // 128
MAGIC = float(np.float32(12582912.0))
MAGIC64 = 6755399441055744.0
BIG4 = float(np.float32(4.0 * (MAGIC64 - MAGIC)))
QMAX = 127.0
EPS = 1e-5

BF = ml_dtypes.bfloat16


class Prog:

    ENGINES = ("sync", "vector", "scalar", "tensor", "gpsimd")
    CHAINED = ("vector", "scalar", "gpsimd")

    def __init__(self, nc):
        self.nc = nc
        self.es = contextlib.ExitStack()
        self.ops = []
        self.cnt = {}
        self.chain = {}

    def init_chain_sems(self):
        for en in self.CHAINED:
            self.chain[en] = self.sem(f"chain_{en}")

    def sem(self, name):
        s = self.es.enter_context(self.nc.semaphore(name))
        self.cnt[id(s)] = 0
        return s

    def emit(self, eng, fn, waits=(), inc=None, dma=False):
        incs = []
        waits = [(s, v) for (s, v) in waits if v > 0]
        if eng in self.CHAINED and not dma:
            assert inc is None, "compute ops on chained engines use the chain sem"
            cs = self.chain[eng]
            prior = self.cnt[id(cs)]
            if prior > 0:
                waits = waits + [(cs, prior)]
            self.cnt[id(cs)] += 1
            incs = [(cs, 1)]
        elif inc is not None:
            sem, amt = inc
            self.cnt[id(sem)] += amt
            incs = [(sem, amt)]
        self.ops.append((eng, fn, waits, incs))

    def tok(self, sem):
        return self.cnt[id(sem)]

    def barrier(self, sems):
        toks = [(s, self.tok(s)) for s in sems]
        for en in self.ENGINES:
            self.emit(en, lambda e: e.engine_nop() if hasattr(e, "engine_nop")
                      else e.nop(), waits=toks)

    def build(self, block):
        for en in self.ENGINES:
            ops = [(fn, waits, incs) for (e, fn, waits, incs) in self.ops
                   if e == en]
            if not ops:
                continue

            def body(eng, _ops=ops):
                seen = {}
                for (fn, waits, incs) in _ops:
                    for (sem, v) in waits:
                        if seen.get(id(sem), 0) >= v:
                            continue
                        eng.wait_ge(sem, v)
                        seen[id(sem)] = v
                    inst = fn(eng)
                    for (sem, amt) in incs:
                        inst = inst.then_inc(sem, amt)

            getattr(block, en)(body)


class Stream:

    MAXQ = 1

    def __init__(self, p, name, nslots=1):
        self.p = p
        self.sems = [p.sem(f"{name}{i}") for i in range(nslots)]
        self.counts = [0] * nslots
        self.waited = [0] * nslots
        self.k = 0

    def issue(self, eng, fn, waits=(), slot=None):
        if slot is None:
            slot = self.k % len(self.sems)
            self.k += 1
        s = self.sems[slot]
        waits = list(waits)
        if (self.counts[slot] - self.waited[slot]) // 16 >= self.MAXQ:
            self.waited[slot] = self.counts[slot]
            waits.append((s, self.waited[slot]))
        self.counts[slot] += 16
        self.p.emit(eng, fn, waits=waits, inc=(s, 16), dma=True)
        return (s, self.counts[slot])

    def total_tokens(self):
        return [(s, c) for s, c in zip(self.sems, self.counts) if c > 0]


def dram_in(nc, name, shape, dtype=F32):
    return nc.dram_tensor(name, shape, dtype, kind="ExternalInput").ap()


def dram_out(nc, name, shape, dtype=F32):
    return nc.dram_tensor(name, shape, dtype, kind="ExternalOutput").ap()


ACT_CONSTS = [MAGIC, EPS, -0.5, 128.25, -128.25, 255.5, 0.5, 2.0, 0.25]


def setup_consts(p, nc, st_const):
    cvals = dram_in(nc, "cvals", [128, 16])
    const_sb = p.es.enter_context(nc.sbuf_tensor("const_sb", [128, 16], F32))
    for i, v in enumerate(ACT_CONSTS):
        nc.const_aps.register(F32, float(v), const_sb[:, i:i + 1])
    st_const.issue("sync", lambda e: e.dma_start(out=const_sb[:, :], in_=cvals))
    return const_sb


def emit_adaptive_magic(p, const_sb, s):
    p.emit("vector", lambda e: e.tensor_scalar(
        const_sb[:, 9:10], const_sb[:, 8:9], MAGIC, -MAGIC,
        ALU.add, ALU.add), waits=list(s.get("cgate", [])))
    p.emit("vector", lambda e: e.tensor_scalar(
        const_sb[:, 9:10], const_sb[:, 9:10], BIG4, MAGIC,
        ALU.mult, ALU.add))
    p.emit("vector", lambda e: e.tensor_scalar(
        const_sb[:, 10:11], const_sb[:, 9:10], -1.0, None, ALU.mult))
    return p.tok(s["dve"])


def fm(x_sb, pt, c0=0, n=T):
    return x_sb[:, pt * T + c0: pt * T + c0 + n]


def emit_ln(p, x_sb, xln_dst_fn, scr, onecol_s, onec, psS, psB, rows, s):
    s_pe, s_dve, s_act = s["pe"], s["dve"], s["act"]
    srow, sqrow, rstdrow = rows["s"], rows["sq"], rows["rstd"]
    murow = srow
    for which in range(2):
        for pt in range(NPT):
            buf = pt % 2
            war = [(s_pe, p.tok(s_pe))] if pt >= 2 or which == 1 else []
            if which == 0:
                p.emit("scalar", lambda e, pt=pt, buf=buf: e.copy(
                    scr[:, buf * T:(buf + 1) * T], fm(x_sb, pt)),
                    waits=war)
            else:
                p.emit("scalar", lambda e, pt=pt, buf=buf: e.activation(
                    scr[:, buf * T:(buf + 1) * T], fm(x_sb, pt), AF.Square),
                    waits=war)
            t_act = p.tok(s_act)
            for tk in range(T // 512):
                p.emit("tensor", lambda e, pt=pt, buf=buf, tk=tk: e.matmul(
                    psS[0:1, tk * 512:(tk + 1) * 512], onecol_s[:, 0:1],
                    scr[:, buf * T + tk * 512: buf * T + (tk + 1) * 512],
                    start=(pt == 0), stop=(pt == NPT - 1)),
                    waits=[(s_act, t_act)],
                    inc=(s_pe, 1) if tk == T // 512 - 1 else None)
        t_pe = p.tok(s_pe)
        dstrow = srow if which == 0 else sqrow
        p.emit("vector", lambda e, dstrow=dstrow: e.tensor_copy(
            dstrow[:, :], psS[0:1, :]), waits=[(s_pe, t_pe)])
        if which == 0:
            t_dve = p.tok(s_dve)
            p.emit("tensor", lambda e: e.nop(), waits=[(s_dve, t_dve)])
    mu2 = rows["mu2"]
    p.emit("vector", lambda e: e.tensor_scalar(
        murow[:, :], srow[:, :], 1.0 / D, None, ALU.mult))
    p.emit("vector", lambda e: e.tensor_tensor(
        mu2, murow[:, :], murow[:, :], ALU.mult))
    p.emit("vector", lambda e: e.scalar_tensor_tensor(
        sqrow[:, :], sqrow[:, :], 1.0 / D, mu2,
        ALU.mult, ALU.subtract))
    t = p.tok(s_dve)
    p.emit("scalar", lambda e: e.activation(
        sqrow[:, :], sqrow[:, :], AF.Ln, bias=EPS, scale=1.0),
        waits=[(s_dve, t)] + list(s.get("cgate", [])))
    p.emit("scalar", lambda e: e.activation(
        rstdrow[:, :], sqrow[:, :], AF.Exp, bias=0.0, scale=-0.5))
    t_rows = p.tok(s_act)
    for tk in range(T // 512):
        mu_c = psB[:, (tk % 2) * 512:(tk % 2) * 512 + 512]
        rs_c = psB[:, 1024 + (tk % 2) * 512:1024 + (tk % 2) * 512 + 512]
        war = [(s_dve, p.tok(s_dve))]
        p.emit("tensor", lambda e, tk=tk, mu_c=mu_c: e.matmul(
            mu_c, onec[:, :], murow[:, tk * 512:(tk + 1) * 512],
            start=True, stop=True), waits=[(s_act, t_rows)] + war)
        p.emit("tensor", lambda e, tk=tk, rs_c=rs_c: e.matmul(
            rs_c, onec[:, :], rstdrow[:, tk * 512:(tk + 1) * 512],
            start=True, stop=True), inc=(s_pe, 1))
        t_bc = p.tok(s_pe)
        for pt in range(NPT):
            dst = xln_dst_fn(pt)
            p.emit("vector", lambda e, pt=pt, tk=tk, mu_c=mu_c, dst=dst:
                   e.tensor_tensor(
                tmpcol(dst, tk), fm(x_sb, pt, tk * 512, 512),
                mu_c, ALU.subtract), waits=[(s_pe, t_bc)])
            p.emit("vector", lambda e, pt=pt, tk=tk, rs_c=rs_c, dst=dst:
                   e.tensor_tensor(tmpcol(dst, tk), tmpcol(dst, tk),
                                   rs_c, ALU.mult))
    return p.tok(s_dve)


def tmpcol(dst, tk):
    return dst[:, tk * 512:(tk + 1) * 512]


def emit_absmax_fm(p, src_fn, npt, mx, mxt, idf, psum_m, mrow, mval, s,
                   extra_w=()):
    s_pe, s_dve = s["pe"], s["dve"]
    for pt in range(npt):
        dst = mx if pt == 0 else mxt
        p.emit("vector", lambda e, pt=pt, dst=dst: e.reduce_max(
            dst[:, :], src_fn(pt), AX.X, apply_absolute_value=True),
            waits=list(extra_w))
        if pt > 0:
            p.emit("vector", lambda e: e.tensor_tensor(
                mx[:, :], mx[:, :], mxt[:, :], ALU.max))
    t = p.tok(s_dve)
    p.emit("tensor", lambda e: e.transpose(psum_m[0:1, :], mx[:, :],
                                           idf[:, :]),
           waits=[(s_dve, t)], inc=(s_pe, 1))
    t = p.tok(s_pe)
    p.emit("vector", lambda e: e.tensor_copy(mrow[:, :], psum_m[0:1, :]),
           waits=[(s_pe, t)])
    p.emit("vector", lambda e: e.reduce_max(mval[:, :], mrow[:, :], AX.X),
          )
    return p.tok(s_dve)


def build_L1():
    nc = bass.Bass("TRN2", target_bir_lowering=False)
    x_in = dram_in(nc, "x", [T, D])
    wg_in = dram_in(nc, "w_gate", [3 * 128, D])
    wu_in = dram_in(nc, "w_up", [3 * 128, D])
    wd_in = dram_in(nc, "w_down", [128, I])
    ones1 = dram_in(nc, "ones1", [1, 128])
    onescol = dram_in(nc, "onescol", [128, 1], BF16)
    iden_f = dram_in(nc, "iden_f", [128, 128])

    xfm_o = dram_out(nc, "xfm", [D, T])
    xln_o = dram_out(nc, "xln", [D, T])
    m1_o = dram_out(nc, "m1", [1, 1])
    qg_o = dram_out(nc, "qg", [3 * 128, D], BF16)
    qu_o = dram_out(nc, "qu", [3 * 128, D], BF16)
    qd_o = dram_out(nc, "qd", [128, I], BF16)
    sg_o = dram_out(nc, "sg", [3 * 128, 1])
    su_o = dram_out(nc, "su", [3 * 128, 1])
    sdn_o = dram_out(nc, "sdn", [128, 1])

    p = Prog(nc)
    with p.es:
        sb = p.es.enter_context
        x_sb = sb(nc.sbuf_tensor("x_sb", [128, NPT * T], F32))
        xln_sb = sb(nc.sbuf_tensor("xln_sb", [128, NPT * T], F32))
        xtm = sb(nc.sbuf_tensor("xtm", [128, 2 * D], F32))
        scr = sb(nc.sbuf_tensor("scr", [128, 2 * T], BF16))
        onec = sb(nc.sbuf_tensor("onec", [1, 128], F32))
        onecol_s = sb(nc.sbuf_tensor("onecol_s", [128, 1], BF16))
        idf = sb(nc.sbuf_tensor("idf", [128, 128], F32))
        srow = sb(nc.sbuf_tensor("srow", [1, T], F32))
        sqrow = sb(nc.sbuf_tensor("sqrow", [1, T], F32))
        mu2row = sb(nc.sbuf_tensor("mu2row", [1, T], F32))
        rstdrow = sqrow
        mx = sb(nc.sbuf_tensor("mx", [128, 1], F32))
        mxt = sb(nc.sbuf_tensor("mxt", [128, 1], F32))
        mrow = sb(nc.sbuf_tensor("mrow", [1, 128], F32))
        mval = sb(nc.sbuf_tensor("mval", [1, 1], F32))
        wtile = sb(nc.sbuf_tensor("wtile", [128, 2 * I], F32))
        qtile = sb(nc.sbuf_tensor("qtile", [128, 2 * I], BF16))
        wmax = sb(nc.sbuf_tensor("wmax", [128, 2], F32))
        winv = sb(nc.sbuf_tensor("winv", [128, 2], F32))
        wsc = sb(nc.sbuf_tensor("wsc", [128, 2], F32))
        psA = sb(nc.psum_tensor("psA", [128, 512], F32))
        psB2 = sb(nc.psum_tensor("psB2", [128, 512], F32))
        psS = sb(nc.psum_tensor("psS", [128, T], F32))
        psM = sb(nc.psum_tensor("psM", [128, 128], F32))

        p.init_chain_sems()
        s_pe = p.sem("s_pe")
        s_dve = p.chain["vector"]
        s_act = p.chain["scalar"]
        st_c = Stream(p, "stC", 4)
        st_w = Stream(p, "stW", 2)
        st_x = Stream(p, "stX", 2)
        st_o = Stream(p, "stO", 1)
        st_q = Stream(p, "stQ", 2)
        s = dict(pe=s_pe, dve=s_dve, act=s_act)

        const_sb = setup_consts(p, nc, st_c)
        st_c.issue("sync", lambda e: e.dma_start(out=onec[:, :], in_=ones1))
        st_c.issue("sync", lambda e: e.dma_start(out=onecol_s[:, :],
                                                 in_=onescol))
        st_c.issue("sync", lambda e: e.dma_start(out=idf[:, :],
                                                 in_=iden_f))
        cgate = st_c.total_tokens()
        s["cgate"] = cgate
        for en in ("tensor", "vector", "scalar"):
            p.emit(en, lambda e: e.nop(), waits=cgate,
                   dma=(en not in p.CHAINED))
        emit_adaptive_magic(p, const_sb, s)

        qw_last_dve = [0, 0]
        qw_st = [None, None]

        def quant_weight(w_in, q_out, s_out, ntiles, width):
            for it in range(ntiles):
                buf = it % 2
                tok_ld = st_w.issue("sync", lambda e, it=it, buf=buf:
                                    e.dma_start(
                    out=wtile[:, buf * I:buf * I + width],
                    in_=w_in[it * 128:(it + 1) * 128, :]),
                    waits=[(s_dve, qw_last_dve[buf])], slot=buf)
                war_q = [qw_st[buf]] if qw_st[buf] else []
                p.emit("vector", lambda e, buf=buf, width=width: e.reduce_max(
                    wmax[:, buf:buf + 1], wtile[:, buf * I:buf * I + width],
                    AX.X, apply_absolute_value=True),
                    waits=[tok_ld] + war_q)
                p.emit("vector", lambda e, buf=buf: e.reciprocal(
                    winv[:, buf:buf + 1], wmax[:, buf:buf + 1]))
                p.emit("vector", lambda e, buf=buf: e.tensor_scalar(
                    winv[:, buf:buf + 1], winv[:, buf:buf + 1], QMAX, None,
                    ALU.mult))
                p.emit("vector", lambda e, buf=buf: e.tensor_scalar(
                    wsc[:, buf:buf + 1], wmax[:, buf:buf + 1], 1.0 / QMAX,
                    None, ALU.mult))
                t_dve = p.tok(s_dve)
                p.emit("scalar", lambda e, buf=buf, width=width: e.activation(
                    wtile[:, buf * I:buf * I + width],
                    wtile[:, buf * I:buf * I + width],
                    AF.Identity, scale=winv[:, buf:buf + 1]),
                    waits=[(s_dve, t_dve)] + cgate)
                t_act = p.tok(s_act)
                p.emit("vector", lambda e, buf=buf, width=width:
                       e.tensor_scalar(
                    qtile[:, buf * I:buf * I + width],
                    wtile[:, buf * I:buf * I + width],
                    const_sb[:, 9:10], const_sb[:, 10:11],
                    ALU.add, ALU.add),
                    waits=[(s_act, t_act)])
                qw_last_dve[buf] = p.tok(s_dve)
                t_dve = p.tok(s_dve)
                st_q.issue("gpsimd", lambda e, it=it, buf=buf, width=width,
                           q_out=q_out: e.dma_start(
                    out=q_out[it * 128:(it + 1) * 128, :],
                    in_=qtile[:, buf * I:buf * I + width]),
                    waits=[(s_dve, t_dve)], slot=buf)
                qw_st[buf] = st_q.issue("gpsimd", lambda e, it=it, buf=buf,
                                        s_out=s_out: e.dma_start(
                    out=s_out[it * 128:(it + 1) * 128, :],
                    in_=wsc[:, buf:buf + 1]),
                    waits=[(s_dve, t_dve)], slot=buf)

        quant_weight(wg_in, qg_o, sg_o, 3, D)
        quant_weight(wu_in, qu_o, su_o, 3, D)
        quant_weight(wd_in, qd_o, sdn_o, 1, I)

        xv = x_sb[:, :].rearrange("p (pt t) -> p pt t", pt=NPT)
        evac_tok = {}
        for tt in range(NTT):
            buf = tt % 2
            w = [(s_pe, evac_tok.get(tt - 2, 0))]
            tok_ld = st_x.issue("sync", lambda e, tt=tt, buf=buf: e.dma_start(
                out=xtm[:, buf * D:(buf + 1) * D],
                in_=x_in[tt * 128:(tt + 1) * 128, :]),
                waits=w, slot=buf)
            for g in range(2):
                ps = psA if g == 0 else psB2
                wv = [(s_dve, p.tok(s_dve))]
                for j in range(4):
                    pt = g * 4 + j
                    p.emit("tensor",
                           lambda e, pt=pt, j=j, buf=buf, ps=ps: e.transpose(
                               ps[:, j * 128:(j + 1) * 128],
                               xtm[:, buf * D + pt * 128:
                                   buf * D + (pt + 1) * 128],
                               idf[:, :]),
                           waits=[tok_ld] + cgate + wv,
                           inc=(s_pe, 1) if j == 3 else None)
                t_pe = p.tok(s_pe)
                dstv = xv[:, g * 4:(g + 1) * 4, tt * 128:(tt + 1) * 128]
                psv = ps[:, :].rearrange("p (j t) -> p j t", j=4)
                p.emit("vector", lambda e, dstv=dstv, psv=psv: e.tensor_copy(
                    dstv, psv), waits=[(s_pe, t_pe)])
            evac_tok[tt] = p.tok(s_pe)
        t_transposed = p.tok(s_dve)

        for pt in range(NPT):
            st_o.issue("gpsimd", lambda e, pt=pt: e.dma_start(
                out=xfm_o[pt * 128:(pt + 1) * 128, :], in_=fm(x_sb, pt)),
                waits=[(s_dve, t_transposed)])

        p.emit("scalar", lambda e: e.nop(), waits=[(s_dve, t_transposed)])
        rows = dict(s=srow, sq=sqrow, rstd=rstdrow, mu2=mu2row[:, :])
        t_xln = emit_ln(p, x_sb, lambda pt: fm(xln_sb, pt), scr, onecol_s,
                        onec, psS, psS, rows, s)
        for pt in range(NPT):
            st_o.issue("gpsimd", lambda e, pt=pt: e.dma_start(
                out=xln_o[pt * 128:(pt + 1) * 128, :], in_=fm(xln_sb, pt)),
                waits=[(s_dve, t_xln)])
        t_m = emit_absmax_fm(p, lambda pt: fm(xln_sb, pt), NPT, mx, mxt, idf,
                             psM, mrow, mval, s, extra_w=[(s_dve, t_xln)])
        st_o.issue("gpsimd", lambda e: e.dma_start(out=m1_o, in_=mval[:, :]),
                   waits=[(s_dve, t_m)])

        fin = st_o.total_tokens() + st_q.total_tokens()
        p.emit("sync", lambda e: e.nop(), waits=fin)

        block = p.es.enter_context(nc.Block())
        p.build(block)
    return nc


def emit_global_scale(p, nc, m_in, clip_in, mrow8, clipv, gv, sxv, invv,
                      sx_b, inv_b, onec, ps, s, st_c):
    s_pe, s_dve, s_act = s["pe"], s["dve"], s["act"]
    t1 = st_c.issue("sync", lambda e: e.dma_start(
        out=mrow8[:, :], in_=m_in.rearrange("a b -> b a")))
    t2 = st_c.issue("sync", lambda e: e.dma_start(out=clipv[:, :], in_=clip_in))
    p.emit("vector", lambda e: e.reduce_max(gv[:, :], mrow8[:, :], AX.X),
           waits=[t1])
    p.emit("scalar", lambda e: e.activation(clipv[:, :], clipv[:, :],
                                            AF.Sigmoid), waits=[t2])
    t_act = p.tok(s_act)
    p.emit("vector", lambda e: e.tensor_tensor(gv[:, :], gv[:, :],
                                               clipv[:, :], ALU.mult),
           waits=[(s_act, t_act)])
    p.emit("vector", lambda e: e.tensor_scalar(sxv[:, :], gv[:, :],
                                               1.0 / QMAX, None, ALU.mult))
    p.emit("vector", lambda e: e.reciprocal(invv[:, :], sxv[:, :]))
    t_dve = p.tok(s_dve)
    p.emit("tensor", lambda e: e.matmul(ps[:, 0:1], onec[:, :], sxv[:, :],
                                        start=True, stop=True),
           waits=[(s_dve, t_dve)])
    p.emit("tensor", lambda e: e.matmul(ps[:, 1:2], onec[:, :], invv[:, :],
                                        start=True, stop=True),
           inc=(s_pe, 1))
    t_pe = p.tok(s_pe)
    p.emit("vector", lambda e: e.tensor_copy(sx_b[:, :], ps[:, 0:1]),
           waits=[(s_pe, t_pe)])
    p.emit("vector", lambda e: e.tensor_copy(inv_b[:, :], ps[:, 1:2]))
    return p.tok(s_dve)


def emit_quant_tile(p, dst_bf, src_ap, inv_b, tmp, s, mcol, extra_w=()):
    s_act, s_dve = s["act"], s["dve"]
    p.emit("scalar", lambda e: e.activation(
        tmp, src_ap, AF.Relu, bias=128.25, scale=inv_b[:, 0:1]),
        waits=list(extra_w) + list(s.get("cgate", [])))
    t_act = p.tok(s_act)
    p.emit("vector", lambda e: e.tensor_scalar(
        tmp, tmp, 255.5, -128.25, ALU.min, ALU.add), waits=[(s_act, t_act)])
    p.emit("vector", lambda e: e.tensor_scalar(
        dst_bf, tmp, mcol[:, 9:10], mcol[:, 10:11], ALU.add, ALU.add))
    return p.tok(s_dve)


def build_L2():
    nc = bass.Bass("TRN2", target_bir_lowering=False)
    xln_in = dram_in(nc, "xln", [D, T])
    m_in = dram_in(nc, "m_all", [NC, 1])
    clip_in = dram_in(nc, "clip", [1, 1])
    qg_in = dram_in(nc, "qg", [I, D], BF16)
    qu_in = dram_in(nc, "qu", [I, D], BF16)
    sg_in = dram_in(nc, "sg", [I, 1])
    su_in = dram_in(nc, "su", [I, 1])
    ones1 = dram_in(nc, "ones1", [1, 128])
    iden_f = dram_in(nc, "iden_f", [128, 128])
    a_o = dram_out(nc, "a", [I, T])
    m2_o = dram_out(nc, "m2", [1, 1])

    p = Prog(nc)
    with p.es:
        sb = p.es.enter_context
        xq = sb(nc.sbuf_tensor("xq", [128, NPT * T], BF16))
        qgT = sb(nc.sbuf_tensor("qgT", [128, NPT * I], BF16))
        quT = sb(nc.sbuf_tensor("quT", [128, NPT * I], BF16))
        xln_t = sb(nc.sbuf_tensor("xln_t", [128, 2 * T], F32))
        tmp = sb(nc.sbuf_tensor("tmp", [128, T], F32))
        silu_sb = sb(nc.sbuf_tensor("silu_sb", [128, T], F32))
        a_sb = sb(nc.sbuf_tensor("a_sb", [128, 2 * T], F32))
        onec = sb(nc.sbuf_tensor("onec", [1, 128], F32))
        idf = sb(nc.sbuf_tensor("idf", [128, 128], F32))
        mrow8 = sb(nc.sbuf_tensor("mrow8", [1, NC], F32))
        clipv = sb(nc.sbuf_tensor("clipv", [1, 1], F32))
        gv = sb(nc.sbuf_tensor("gv", [1, 1], F32))
        sxv = sb(nc.sbuf_tensor("sxv", [1, 1], F32))
        invv = sb(nc.sbuf_tensor("invv", [1, 1], F32))
        sx_b = sb(nc.sbuf_tensor("sx_b", [128, 1], F32))
        inv_b = sb(nc.sbuf_tensor("inv_b", [128, 1], F32))
        sg_all = sb(nc.sbuf_tensor("sg_all", [128, NIT], F32))
        su_all = sb(nc.sbuf_tensor("su_all", [128, NIT], F32))
        sxsg = sb(nc.sbuf_tensor("sxsg", [128, NIT], F32))
        sxsu = sb(nc.sbuf_tensor("sxsu", [128, NIT], F32))
        mx = sb(nc.sbuf_tensor("mx", [128, 1], F32))
        mxt = sb(nc.sbuf_tensor("mxt", [128, 1], F32))
        mrow = sb(nc.sbuf_tensor("mrow", [1, 128], F32))
        mval = sb(nc.sbuf_tensor("mval", [1, 1], F32))
        psG = sb(nc.psum_tensor("psG", [128, T], F32))
        psU = sb(nc.psum_tensor("psU", [128, T], F32))

        p.init_chain_sems()
        s_pe = p.sem("s_pe")
        s_dve = p.chain["vector"]
        s_act = p.chain["scalar"]
        st_c = Stream(p, "stC", 4)
        st_xl = Stream(p, "stXL", 2)
        st_w = Stream(p, "stW", 4)
        st_a = Stream(p, "stA", 2)
        st_o = Stream(p, "stO", 1)
        s = dict(pe=s_pe, dve=s_dve, act=s_act)

        const_sb = setup_consts(p, nc, st_c)
        st_c.issue("sync", lambda e: e.dma_start(out=onec[:, :], in_=ones1))
        st_c.issue("sync", lambda e: e.dma_start(out=idf[:, :], in_=iden_f))
        for it in range(NIT):
            st_c.issue("sync", lambda e, it=it: e.dma_start(
                out=sg_all[:, it:it + 1],
                in_=sg_in[it * 128:(it + 1) * 128, :]))
            st_c.issue("sync", lambda e, it=it: e.dma_start(
                out=su_all[:, it:it + 1],
                in_=su_in[it * 128:(it + 1) * 128, :]))
        s["cgate"] = st_c.total_tokens()
        tok_sg = tok_su = s["cgate"][0]
        for en in ("tensor", "vector", "scalar"):
            p.emit(en, lambda e: e.nop(), waits=s["cgate"],
                   dma=(en not in p.CHAINED))

        emit_adaptive_magic(p, const_sb, s)
        t_sc = emit_global_scale(p, nc, m_in, clip_in, mrow8, clipv, gv, sxv,
                                 invv, sx_b, inv_b, onec, psG, s, st_c)
        p.emit("vector", lambda e: e.tensor_scalar(
            sxsg[:, :], sg_all[:, :], sx_b[:, 0:1], None, ALU.mult),
            waits=[(s_dve, t_sc)])
        p.emit("vector", lambda e: e.tensor_scalar(
            sxsu[:, :], su_all[:, :], sx_b[:, 0:1], None, ALU.mult))

        t_xq = [0] * NPT
        q_done = [0, 0]
        for pt in range(NPT):
            buf = pt % 2
            tok = st_xl.issue("sync", lambda e, pt=pt, buf=buf: e.dma_start(
                out=xln_t[:, buf * T:(buf + 1) * T],
                in_=xln_in[pt * 128:(pt + 1) * 128, :]),
                waits=[(s_dve, q_done[buf])], slot=buf)
            emit_quant_tile(p, fm(xq, pt), xln_t[:, buf * T:(buf + 1) * T],
                            inv_b, tmp[:, :], s, const_sb,
                            extra_w=[tok, (s_dve, p.tok(s_dve))])
            q_done[buf] = p.tok(s_dve)
            t_xq[pt] = p.tok(s_dve)

        tokg = [None] * NPT
        toku = [None] * NPT
        for m in range(NPT):
            tokg[m] = st_w.issue("scalar", lambda e, m=m:
                                 e.dma_start_transpose(
                out=qgT[:, m * I:(m + 1) * I],
                in_=qg_in[:, m * 128:(m + 1) * 128]))
            toku[m] = st_w.issue("scalar", lambda e, m=m:
                                 e.dma_start_transpose(
                out=quT[:, m * I:(m + 1) * I],
                in_=qu_in[:, m * 128:(m + 1) * 128]))

        t_silu = 0
        t_astt = 0
        ast_tok = [None, None]
        for ct in range(NIT):
            buf = ct % 2
            for m in range(NPT):
                for tk in range(T // 512):
                    p.emit("tensor", lambda e, m=m, tk=tk, ct=ct: e.matmul(
                        psG[:, tk * 512:(tk + 1) * 512],
                        qgT[:, m * I + ct * 128: m * I + (ct + 1) * 128],
                        xq[:, m * T + tk * 512: m * T + (tk + 1) * 512],
                        start=(m == 0), stop=(m == NPT - 1)),
                        waits=[tokg[m], (s_dve, t_xq[m]),
                               (s_act, t_silu)],
                        inc=(s_pe, 1) if (m, tk) == (NPT - 1, 3) else None)
            t_gate = p.tok(s_pe)
            for m in range(NPT):
                for tk in range(T // 512):
                    p.emit("tensor", lambda e, m=m, tk=tk, ct=ct: e.matmul(
                        psU[:, tk * 512:(tk + 1) * 512],
                        quT[:, m * I + ct * 128: m * I + (ct + 1) * 128],
                        xq[:, m * T + tk * 512: m * T + (tk + 1) * 512],
                        start=(m == 0), stop=(m == NPT - 1)),
                        waits=[toku[m], (s_dve, t_astt)],
                        inc=(s_pe, 1) if (m, tk) == (NPT - 1, 3) else None)
            t_up = p.tok(s_pe)
            p.emit("scalar", lambda e, ct=ct: e.activation(
                silu_sb[:, :], psG[:, :], AF.Silu,
                scale=sxsg[:, ct:ct + 1]),
                waits=[(s_pe, t_gate), (s_dve, t_astt)])
            t_silu = p.tok(s_act)
            war_st = [ast_tok[buf]] if ast_tok[buf] else []
            p.emit("vector", lambda e, ct=ct, buf=buf: e.scalar_tensor_tensor(
                a_sb[:, buf * T:(buf + 1) * T], psU[:, :],
                sxsu[:, ct:ct + 1], silu_sb[:, :], ALU.mult, ALU.mult),
                waits=[(s_pe, t_up), (s_act, t_silu)] + war_st)
            t_astt = p.tok(s_dve)
            if ct == 0:
                p.emit("vector", lambda e, buf=buf: e.reduce_max(
                    mx[:, :], a_sb[:, buf * T:(buf + 1) * T], AX.X,
                    apply_absolute_value=True))
            else:
                p.emit("vector", lambda e, buf=buf: e.reduce_max(
                    mxt[:, :], a_sb[:, buf * T:(buf + 1) * T], AX.X,
                    apply_absolute_value=True))
                p.emit("vector", lambda e: e.tensor_tensor(
                    mx[:, :], mx[:, :], mxt[:, :], ALU.max))
            t_dve = p.tok(s_dve)
            ast_tok[buf] = st_a.issue("gpsimd", lambda e, ct=ct, buf=buf:
                                      e.dma_start(
                out=a_o[ct * 128:(ct + 1) * 128, :],
                in_=a_sb[:, buf * T:(buf + 1) * T]),
                waits=[(s_dve, t_dve)], slot=buf)

        t_dve = p.tok(s_dve)
        p.emit("tensor", lambda e: e.transpose(psG[0:1, 0:128], mx[:, :],
                                               idf[:, :]),
               waits=[(s_dve, t_dve), (s_act, t_silu)], inc=(s_pe, 1))
        t_pe = p.tok(s_pe)
        p.emit("vector", lambda e: e.tensor_copy(mrow[:, :], psG[0:1, 0:128]),
               waits=[(s_pe, t_pe)])
        p.emit("vector", lambda e: e.reduce_max(mval[:, :], mrow[:, :], AX.X))
        t_dve = p.tok(s_dve)
        st_o.issue("gpsimd", lambda e: e.dma_start(out=m2_o, in_=mval[:, :]),
                   waits=[(s_dve, t_dve)])

        fin = st_o.total_tokens() + st_a.total_tokens()
        p.emit("sync", lambda e: e.nop(), waits=fin)

        block = p.es.enter_context(nc.Block())
        p.build(block)
    return nc


def build_L3(final=False):
    nc = bass.Bass("TRN2", target_bir_lowering=False)
    a_in = dram_in(nc, "a", [I, T])
    x_in = dram_in(nc, "xfm", [D, T])
    m_in = dram_in(nc, "m_all", [NC, 1])
    clip_in = dram_in(nc, "clip", [1, 1])
    qd_in = dram_in(nc, "qd", [D, I], BF16)
    sdn_in = dram_in(nc, "sdn", [D, 1])
    ones1 = dram_in(nc, "ones1", [1, 128])
    onescol = dram_in(nc, "onescol", [128, 1], BF16)
    iden_f = dram_in(nc, "iden_f", [128, 128])
    if final:
        out_o = dram_out(nc, "out", [T, D])
    else:
        x1_o = dram_out(nc, "x1", [D, T])
        xln2_o = dram_out(nc, "xln2", [D, T], BF16)
        last_o = dram_out(nc, "lastrow", [1, D], BF16)

    p = Prog(nc)
    with p.es:
        sb = p.es.enter_context
        h_sb = sb(nc.sbuf_tensor("h_sb", [128, NIT * T], BF16))
        qdT = sb(nc.sbuf_tensor("qdT", [128, NIT * D], BF16))
        io_t = sb(nc.sbuf_tensor("io_t", [128, 2 * T], F32))
        x1_t = sb(nc.sbuf_tensor("x1_t", [128, 2 * T], F32))
        tmp = sb(nc.sbuf_tensor("tmp", [128, T], F32))
        onec = sb(nc.sbuf_tensor("onec", [1, 128], F32))
        onecol_s = sb(nc.sbuf_tensor("onecol_s", [128, 1], BF16))
        idf = sb(nc.sbuf_tensor("idf", [128, 128], F32))
        mrow8 = sb(nc.sbuf_tensor("mrow8", [1, NC], F32))
        clipv = sb(nc.sbuf_tensor("clipv", [1, 1], F32))
        gv = sb(nc.sbuf_tensor("gv", [1, 1], F32))
        sdv = sb(nc.sbuf_tensor("sdv", [1, 1], F32))
        invv = sb(nc.sbuf_tensor("invv", [1, 1], F32))
        sd_b = sb(nc.sbuf_tensor("sd_b", [128, 1], F32))
        inv_b = sb(nc.sbuf_tensor("inv_b", [128, 1], F32))
        sdn_all = sb(nc.sbuf_tensor("sdn_all", [128, NPT], F32))
        halfsc = sb(nc.sbuf_tensor("halfsc", [128, NPT], F32))
        if final:
            stage = sb(nc.sbuf_tensor("stage", [128, 1024], F32))
        psD = sb(nc.psum_tensor("psD", [128, T], F32))
        psS = sb(nc.psum_tensor("psS", [128, T], F32))
        if not final:
            scr = sb(nc.sbuf_tensor("scr", [128, 2 * T], BF16))
            srow = sb(nc.sbuf_tensor("srow", [1, T], F32))
            sqrow = sb(nc.sbuf_tensor("sqrow", [1, T], F32))
            rstdrow = sqrow
            xl_t = sb(nc.sbuf_tensor("xl_t", [128, 2 * T], BF16))

        p.init_chain_sems()
        s_pe = p.sem("s_pe")
        s_dve = p.chain["vector"]
        s_act = p.chain["scalar"]
        st_c = Stream(p, "stC", 4)
        st_a = Stream(p, "stA", 2)
        st_x = Stream(p, "stX", 2)
        st_w = Stream(p, "stW", 4)
        st_o = Stream(p, "stO", 1)
        st_x1 = Stream(p, "stX1", 2)
        st_l = Stream(p, "stL", 2)
        s = dict(pe=s_pe, dve=s_dve, act=s_act)

        const_sb = setup_consts(p, nc, st_c)
        st_c.issue("sync", lambda e: e.dma_start(out=onec[:, :], in_=ones1))
        st_c.issue("sync", lambda e: e.dma_start(out=onecol_s[:, :],
                                                 in_=onescol))
        st_c.issue("sync", lambda e: e.dma_start(out=idf[:, :], in_=iden_f))
        for pt in range(NPT):
            st_c.issue("sync", lambda e, pt=pt: e.dma_start(
                out=sdn_all[:, pt:pt + 1],
                in_=sdn_in[pt * 128:(pt + 1) * 128, :]))
        s["cgate"] = st_c.total_tokens()
        for en in ("tensor", "vector", "scalar"):
            p.emit(en, lambda e: e.nop(), waits=s["cgate"],
                   dma=(en not in p.CHAINED))
        emit_adaptive_magic(p, const_sb, s)

        t_sc = emit_global_scale(p, nc, m_in, clip_in, mrow8, clipv, gv, sdv,
                                 invv, sd_b, inv_b, onec, psD, s, st_c)
        p.emit("vector", lambda e: e.tensor_scalar(
            halfsc[:, :], sdn_all[:, :], sd_b[:, 0:1], 0.5,
            ALU.mult, ALU.mult), waits=[(s_dve, t_sc)])

        tok_wd = [None] * NIT
        for it in range(NIT):
            tok_wd[it] = st_w.issue("scalar", lambda e, it=it:
                                    e.dma_start_transpose(
                out=qdT[:, it * D:(it + 1) * D],
                in_=qd_in[:, it * 128:(it + 1) * 128]))

        t_h = [0] * NIT
        for it in range(NIT):
            buf = it % 2
            tok = st_a.issue("sync", lambda e, it=it, buf=buf: e.dma_start(
                out=io_t[:, buf * T:(buf + 1) * T],
                in_=a_in[it * 128:(it + 1) * 128, :]),
                waits=[(s_dve, p.tok(s_dve))], slot=buf)
            emit_quant_tile(p, h_sb[:, it * T:(it + 1) * T],
                            io_t[:, buf * T:(buf + 1) * T],
                            inv_b, tmp[:, :], s, const_sb,
                            extra_w=[tok, (s_dve, p.tok(s_dve))])
            t_h[it] = p.tok(s_dve)
        t_hall = p.tok(s_dve)

        x1_st = [None, None]
        x1t_war = [0, 0]
        io_war = [0, 0]
        t_evac = 0
        for dt in range(NPT):
            buf = dt % 2
            tok_x = st_x.issue("sync", lambda e, dt=dt, buf=buf: e.dma_start(
                out=io_t[:, buf * T:(buf + 1) * T],
                in_=x_in[dt * 128:(dt + 1) * 128, :]),
                waits=[(s_dve, io_war[buf]), (s_act, p.tok(s_act))],
                slot=buf)
            for it in range(NIT):
                for tk in range(T // 512):
                    p.emit("tensor", lambda e, it=it, tk=tk, dt=dt: e.matmul(
                        psD[:, tk * 512:(tk + 1) * 512],
                        qdT[:, it * D + dt * 128: it * D + (dt + 1) * 128],
                        h_sb[:, it * T + tk * 512: it * T + (tk + 1) * 512],
                        start=(it == 0), stop=(it == NIT - 1)),
                        waits=[tok_wd[it], (s_dve, t_h[it]),
                               (s_dve, t_evac)],
                        inc=(s_pe, 1) if (it, tk) == (NIT - 1, 3) else None)
            t_mm = p.tok(s_pe)
            if final:
                war = [(s_pe, x1t_war[buf])] if x1t_war[buf] else []
            else:
                war = [x1_st[buf]] if x1_st[buf] else []
            p.emit("vector", lambda e, dt=dt, buf=buf: e.scalar_tensor_tensor(
                x1_t[:, buf * T:(buf + 1) * T], psD[:, :],
                halfsc[:, dt:dt + 1], io_t[:, buf * T:(buf + 1) * T],
                ALU.mult, ALU.add),
                waits=[(s_pe, t_mm), tok_x] + war)
            t_evac = p.tok(s_dve)
            io_war[buf] = t_evac
            if final:
                for g in range(4):
                    gb = g % 2
                    wv = [(s_dve, p.tok(s_dve))]
                    for j in range(4):
                        tt = g * 4 + j
                        p.emit("tensor", lambda e, tt=tt, j=j, buf=buf, gb=gb:
                               e.transpose(
                            psS[:, gb * 512 + j * 128:
                                gb * 512 + (j + 1) * 128],
                            x1_t[:, buf * T + tt * 128:
                                 buf * T + (tt + 1) * 128], idf[:, :]),
                            waits=[(s_dve, t_evac)] + wv,
                            inc=(s_pe, 1) if j == 3 else None)
                    t_tr = p.tok(s_pe)
                    war = [x1_st[gb]] if x1_st[gb] else []
                    p.emit("vector", lambda e, gb=gb: e.tensor_copy(
                        stage[:, gb * 512:(gb + 1) * 512],
                        psS[:, gb * 512:(gb + 1) * 512]),
                        waits=[(s_pe, t_tr)] + war)
                    t_cp = p.tok(s_dve)
                    dst = out_o[g * 512:(g + 1) * 512,
                                dt * 128:(dt + 1) * 128].rearrange(
                        "(j q) c -> q j c", q=128)
                    srcv = stage[:, gb * 512:(gb + 1) * 512].rearrange(
                        "q (j c) -> q j c", j=4)
                    x1_st[gb] = st_x1.issue(
                        "gpsimd", lambda e, dst=dst, srcv=srcv: e.dma_start(
                            out=dst, in_=srcv),
                        waits=[(s_dve, t_cp)], slot=gb)
                x1t_war[buf] = p.tok(s_pe)
            else:
                x1_st[buf] = st_x1.issue(
                    "gpsimd", lambda e, dt=dt, buf=buf: e.dma_start(
                        out=x1_o[dt * 128:(dt + 1) * 128, :],
                        in_=x1_t[:, buf * T:(buf + 1) * T]),
                    waits=[(s_dve, t_evac)], slot=buf)

        if not final:
            rows = dict(s=srow, sq=sqrow, rstd=rstdrow,
                        mu2=tmp[0:1, :])
            emit_ln_streamed(p, x1_o, st_x, io_t, tmp, scr, onecol_s,
                             onec, psS, psD, rows, xl_t, xln2_o, last_o,
                             st_l, x1_st, s)

        fin = (st_o.total_tokens() + st_x1.total_tokens()
               + st_l.total_tokens())
        p.emit("sync", lambda e: e.nop(), waits=fin)

        block = p.es.enter_context(nc.Block())
        p.build(block)
    return nc


def emit_ln_streamed(p, x1_o, st_x, io_t, tmp, scr, onecol_s, onec, psS,
                     psB, rows, xl_t, xln2_o, last_o, st_l, x1_st, s):
    s_pe, s_dve, s_act = s["pe"], s["dve"], s["act"]
    srow, sqrow, rstdrow = rows["s"], rows["sq"], rows["rstd"]
    x1_ready = [t for t in x1_st if t]
    ld_tok = [0, 0]
    for which in range(2):
        for pt in range(NPT):
            buf = pt % 2
            war = [(s_act, p.tok(s_act))]
            tok = st_x.issue("sync", lambda e, pt=pt, buf=buf: e.dma_start(
                out=io_t[:, buf * T:(buf + 1) * T],
                in_=x1_o[pt * 128:(pt + 1) * 128, :]),
                waits=x1_ready + war, slot=buf)
            war2 = [(s_pe, p.tok(s_pe))]
            if which == 0:
                p.emit("scalar", lambda e, buf=buf: e.copy(
                    scr[:, buf * T:(buf + 1) * T],
                    io_t[:, buf * T:(buf + 1) * T]), waits=[tok] + war2)
            else:
                p.emit("scalar", lambda e, buf=buf: e.activation(
                    scr[:, buf * T:(buf + 1) * T],
                    io_t[:, buf * T:(buf + 1) * T], AF.Square),
                    waits=[tok] + war2)
            t_act = p.tok(s_act)
            for tk in range(T // 512):
                p.emit("tensor", lambda e, buf=buf, tk=tk, pt=pt: e.matmul(
                    psS[0:1, tk * 512:(tk + 1) * 512], onecol_s[:, 0:1],
                    scr[:, buf * T + tk * 512: buf * T + (tk + 1) * 512],
                    start=(pt == 0), stop=(pt == NPT - 1)),
                    waits=[(s_act, t_act)],
                    inc=(s_pe, 1) if tk == 3 else None)
        t_pe = p.tok(s_pe)
        dstrow = srow if which == 0 else sqrow
        p.emit("vector", lambda e, dstrow=dstrow: e.tensor_copy(
            dstrow[:, :], psS[0:1, :]), waits=[(s_pe, t_pe)])
        if which == 0:
            t_dve = p.tok(s_dve)
            p.emit("tensor", lambda e: e.nop(), waits=[(s_dve, t_dve)])
    mu2 = rows["mu2"]
    p.emit("vector", lambda e: e.tensor_scalar(
        srow[:, :], srow[:, :], 1.0 / D, None, ALU.mult))
    p.emit("vector", lambda e: e.tensor_tensor(
        mu2, srow[:, :], srow[:, :], ALU.mult))
    p.emit("vector", lambda e: e.scalar_tensor_tensor(
        sqrow[:, :], sqrow[:, :], 1.0 / D, mu2,
        ALU.mult, ALU.subtract))
    t = p.tok(s_dve)
    p.emit("scalar", lambda e: e.activation(
        sqrow[:, :], sqrow[:, :], AF.Ln, bias=EPS, scale=1.0),
        waits=[(s_dve, t)] + list(s.get("cgate", [])))
    p.emit("scalar", lambda e: e.activation(
        rstdrow[:, :], sqrow[:, :], AF.Exp, bias=0.0, scale=-0.5))
    t_rows = p.tok(s_act)
    wv = [(s_dve, p.tok(s_dve))]
    for tk in range(T // 512):
        p.emit("tensor", lambda e, tk=tk: e.matmul(
            psS[:, tk * 512:(tk + 1) * 512], onec[:, :],
            srow[:, tk * 512:(tk + 1) * 512], start=True, stop=True),
            waits=[(s_act, t_rows)] + wv)
        p.emit("tensor", lambda e, tk=tk: e.matmul(
            psB[:, tk * 512:(tk + 1) * 512], onec[:, :],
            rstdrow[:, tk * 512:(tk + 1) * 512], start=True, stop=True),
            inc=(s_pe, 1) if tk == 3 else None)
    t_bc = p.tok(s_pe)
    prev_q = 0
    st_tok = [None, None]
    for pt in range(NPT):
        buf = pt % 2
        tok = st_x.issue("sync", lambda e, pt=pt, buf=buf: e.dma_start(
            out=io_t[:, buf * T:(buf + 1) * T],
            in_=x1_o[pt * 128:(pt + 1) * 128, :]),
            waits=[(s_dve, prev_q)], slot=buf)
        p.emit("vector", lambda e, buf=buf: e.tensor_tensor(
            tmp[:, :], io_t[:, buf * T:(buf + 1) * T], psS[:, :],
            ALU.subtract), waits=[tok, (s_pe, t_bc)])
        war = [st_tok[buf]] if st_tok[buf] else []
        p.emit("vector", lambda e, buf=buf: e.tensor_tensor(
            xl_t[:, buf * T:(buf + 1) * T], tmp[:, :], psB[:, :], ALU.mult),
            waits=war)
        prev_q = p.tok(s_dve)
        t_dve = p.tok(s_dve)
        st_l.issue("gpsimd", lambda e, pt=pt, buf=buf: e.dma_start(
            out=xln2_o[pt * 128:(pt + 1) * 128, :],
            in_=xl_t[:, buf * T:(buf + 1) * T]),
            waits=[(s_dve, t_dve)], slot=buf)
        st_tok[buf] = st_l.issue("gpsimd", lambda e, pt=pt, buf=buf:
                                 e.dma_start(
            out=last_o[0:1, pt * 128:(pt + 1) * 128],
            in_=xl_t[:, buf * T + T - 1: buf * T + T]),
            waits=[(s_dve, t_dve)], slot=buf)


def build_L4(nsc=NSC, cut=99):
    nc = bass.Bass("TRN2", target_bir_lowering=False)
    xln2_in = dram_in(nc, "xln2", [D, T], BF16)
    last_in = dram_in(nc, "lastprev", [1, D], BF16)
    w_r = dram_in(nc, "w_r", [D, D])
    w_k = dram_in(nc, "w_k", [D, D])
    w_v = dram_in(nc, "w_v", [D, D])
    w_g = dram_in(nc, "w_g", [D, D])
    tmix = dram_in(nc, "tmix", [4 * D, 1])
    tdec = dram_in(nc, "tdec", [1, D])
    ones1 = dram_in(nc, "ones1", [1, 128])
    iden_b = dram_in(nc, "iden_b", [128, 128], BF16)
    maskc_in = dram_in(nc, "maskc", [128, 16 * 128], BF16)
    parity_in = dram_in(nc, "parity", [128, D], BF16)
    iotas = dram_in(nc, "iotas", [1, 6 * 128])
    core0 = dram_in(nc, "core0", [1, 1])

    attn_o = dram_out(nc, "attnout", [T, D], BF16)
    g_o = dram_out(nc, "g_out", [T, D], BF16)
    r_o = dram_out(nc, "r_out", [D, T], BF16)
    st_o_d = dram_out(nc, "st_out", [DH, D])

    wscr = nc.dram_tensor("wscr", [D, D], BF16, kind="Internal").ap()

    p = Prog(nc)
    with p.es:
        sb = p.es.enter_context
        mix = sb(nc.sbuf_tensor("mix", [128, NPT * T], BF16))
        xln_t = sb(nc.sbuf_tensor("xln_t", [128, 2 * T], BF16))
        wT = sb(nc.sbuf_tensor("wT", [128, NPT * D], BF16))
        r_fm = sb(nc.sbuf_tensor("r_fm", [128, NPT * T], BF16))
        k_tm = sb(nc.sbuf_tensor("k_tm", [128, NTT * D], BF16))
        v_tm = sb(nc.sbuf_tensor("v_tm", [128, NTT * D], BF16))
        g_t = sb(nc.sbuf_tensor("g_t", [128, 2 * D], BF16))
        sc_col = sb(nc.sbuf_tensor("sc_col", [128, 4], F32))
        attn_t = sb(nc.sbuf_tensor("attn_t", [128, 2 * D], BF16))
        prevcol = sb(nc.sbuf_tensor("prevcol", [128, NPT], BF16))
        tmix_s = sb(nc.sbuf_tensor("tmix_s", [128, 4 * NPT], F32))
        onec = sb(nc.sbuf_tensor("onec", [1, 128], F32))
        idb = sb(nc.sbuf_tensor("idb", [128, 128], BF16))
        maskc = sb(nc.sbuf_tensor("maskc_sb", [128, 16 * 128], BF16))
        iot = sb(nc.sbuf_tensor("iot", [1, 6 * 128], F32))
        lnwrow = sb(nc.sbuf_tensor("lnwrow", [1, D], F32))
        c0v = sb(nc.sbuf_tensor("c0v", [1, 1], F32))
        c0b = sb(nc.sbuf_tensor("c0b", [128, 1], F32))
        P2 = sb(nc.sbuf_tensor("P2", [128, D], BF16))
        QT1 = sb(nc.sbuf_tensor("QT1", [128, D], BF16))
        QA = sb(nc.sbuf_tensor("QA", [128, D], BF16))
        QKH = sb(nc.sbuf_tensor("QKH", [128, D], BF16))
        W1f = sb(nc.sbuf_tensor("W1f", [128, D], BF16))
        S2 = sb(nc.sbuf_tensor("S2", [128, D], F32))
        S2t = sb(nc.sbuf_tensor("S2t", [128, D], BF16))
        T1k = sb(nc.sbuf_tensor("T1k", [128, D], BF16))
        T2k = sb(nc.sbuf_tensor("T2k", [128, D], BF16))
        KHk = sb(nc.sbuf_tensor("KHk", [128, D], BF16))
        Amk = sb(nc.sbuf_tensor("Amk", [128, 2 * D], BF16))
        Amk2 = sb(nc.sbuf_tensor("Amk2", [128, D], BF16))
        vfm2 = sb(nc.sbuf_tensor("vfm2", [128, 2 * D], BF16))
        parity = sb(nc.sbuf_tensor("parity_sb", [128, D], BF16))
        psA_ = sb(nc.psum_tensor("psA_", [128, D], F32))
        psB_ = sb(nc.psum_tensor("psB_", [128, D], F32))
        psC_ = sb(nc.psum_tensor("psC_", [128, D], F32))
        psT_ = sb(nc.psum_tensor("psT_", [128, D], BF16))

        p.init_chain_sems()
        s_pe = p.sem("s_pe")
        s_dve = p.chain["vector"]
        s_act = p.chain["scalar"]
        st_c = Stream(p, "stC", 4)
        st_x = Stream(p, "stX", 2)
        st_wc = Stream(p, "stWC", 1)
        st_w = Stream(p, "stW", 4)
        st_ao = Stream(p, "stAO", 2)
        st_go = Stream(p, "stGO", 2)
        st_ro = Stream(p, "stRO", 2)
        st_o = Stream(p, "stO", 1)
        s = dict(pe=s_pe, dve=s_dve, act=s_act)

        setup_consts(p, nc, st_c)
        st_c.issue("sync", lambda e: e.dma_start(out=onec[:, :], in_=ones1))
        st_c.issue("sync", lambda e: e.dma_start(out=idb[:, :], in_=iden_b))
        st_c.issue("sync", lambda e: e.dma_start(out=maskc[:, :],
                                                 in_=maskc_in))
        st_c.issue("sync", lambda e: e.dma_start(out=parity[:, :],
                                                 in_=parity_in))
        st_c.issue("sync", lambda e: e.dma_start(out=c0v[:, :], in_=core0))
        for i in range(4 * NPT):
            st_c.issue("sync", lambda e, i=i: e.dma_start(
                out=tmix_s[:, i:i + 1], in_=tmix[i * 128:(i + 1) * 128, :]))
        for pt in range(NPT):
            st_c.issue("sync", lambda e, pt=pt: e.dma_start(
                out=prevcol[:, pt:pt + 1],
                in_=last_in[0:1, pt * 128:(pt + 1) * 128]))
        cgate = st_c.total_tokens()
        s["cgate"] = cgate
        for en in ("tensor", "vector", "scalar"):
            p.emit(en, lambda e: e.nop(), waits=cgate,
                   dma=(en not in p.CHAINED))

        p.emit("scalar", lambda e: e.activation(lnwrow[:, :], lnwrow[:, :],
                                                AF.Exp))
        t_exp = p.tok(s_act)
        p.emit("vector", lambda e: e.tensor_scalar(lnwrow[:, :], lnwrow[:, :],
                                                   -1.0, None, ALU.mult),
               waits=[(s_act, t_exp)])
        t_lnw = p.tok(s_dve)
        pats = [(P2, 0), (QT1, 1), (QA, 2), (QKH, 3), (W1f, 4)]
        for (dst, idx) in pats:
            wv = [(s_act, p.tok(s_act))]
            for half in range(2):
                p.emit("tensor", lambda e, idx=idx, half=half: e.matmul(
                    psA_[:, half * 512:(half + 1) * 512],
                    iot[:, idx * 128:(idx + 1) * 128],
                    lnwrow[:, half * 512:(half + 1) * 512],
                    start=True, stop=True),
                    waits=[(s_dve, t_lnw)] + wv,
                    inc=(s_pe, 1) if half == 1 else None)
            t_pe = p.tok(s_pe)
            p.emit("scalar", lambda e, dst=dst: e.activation(
                dst[:, :], psA_[:, :], AF.Exp), waits=[(s_pe, t_pe)])
        t_pats = p.tok(s_act)
        p.emit("vector", lambda e: e.tensor_tensor(
            W1f[:, :], W1f[:, :], parity[:, :], ALU.mult),
            waits=[(s_act, t_pats)])
        p.emit("vector", lambda e: e.memset(vfm2[:, :], 0.0))
        p.emit("vector", lambda e: e.memset(Amk2[:, :], 0.0))
        t_pats_v = p.tok(s_dve)
        p.emit("tensor", lambda e: e.matmul(
            psB_[:, 0:1], onec[:, :], c0v[:, :], start=True, stop=True),
            inc=(s_pe, 1))
        t_pe = p.tok(s_pe)
        p.emit("vector", lambda e: e.tensor_copy(c0b[:, :], psB_[:, 0:1]),
               waits=[(s_pe, t_pe)])
        t_c0 = p.tok(s_dve)

        prev_wtoks = []

        def prep_weight(w_in, widx):
            nonlocal_toks = prev_wtoks[:]
            tok_cast = st_wc.issue("gpsimd", lambda e, w_in=w_in: e.dma_start(
                out=wscr, in_=w_in), waits=nonlocal_toks)
            toks = []
            for m in range(NPT):
                toks.append(st_w.issue("scalar", lambda e, m=m:
                                       e.dma_start_transpose(
                    out=wT[:, m * D:(m + 1) * D],
                    in_=wscr[:, m * 128:(m + 1) * 128]),
                    waits=[tok_cast, (s_pe, p.tok(s_pe))]))
            prev_wtoks[:] = toks
            return toks

        def build_mix(widx, t_gate_mix):
            t_pe_war = p.tok(s_pe)
            for pt in range(NPT):
                buf = pt % 2
                tok = st_x.issue("sync", lambda e, pt=pt, buf=buf:
                                 e.dma_start(
                    out=xln_t[:, buf * T:(buf + 1) * T],
                    in_=xln2_in[pt * 128:(pt + 1) * 128, :]),
                    waits=[(s_dve, t_gate_mix[buf])], slot=buf)
                p.emit("vector", lambda e, pt=pt, buf=buf: e.tensor_tensor(
                    mix[:, pt * T + 1: pt * T + T],
                    xln_t[:, buf * T: buf * T + T - 1],
                    xln_t[:, buf * T + 1: buf * T + T], ALU.subtract),
                    waits=[tok, (s_pe, t_pe_war)])
                p.emit("vector", lambda e, pt=pt, buf=buf: e.tensor_tensor(
                    mix[:, pt * T: pt * T + 1], prevcol[:, pt:pt + 1],
                    xln_t[:, buf * T: buf * T + 1], ALU.subtract))
                p.emit("vector", lambda e, pt=pt, buf=buf: e.tensor_tensor(
                    sc_col[:, 0:1], xln_t[:, buf * T + 1: buf * T + 2],
                    prevcol[:, pt:pt + 1], ALU.subtract),
                    waits=[(s_dve, t_c0)])
                p.emit("vector", lambda e, pt=pt: e.scalar_tensor_tensor(
                    mix[:, pt * T: pt * T + 1], sc_col[:, 0:1], c0b[:, 0:1],
                    mix[:, pt * T: pt * T + 1], ALU.mult, ALU.add))
                p.emit("vector", lambda e, pt=pt, buf=buf, widx=widx:
                       e.scalar_tensor_tensor(
                    fm(mix, pt), fm(mix, pt),
                    tmix_s[:, widx * NPT + pt: widx * NPT + pt + 1],
                    xln_t[:, buf * T:(buf + 1) * T], ALU.mult, ALU.add))
                t_gate_mix[buf] = p.tok(s_dve)
            return p.tok(s_dve)

        t_gm = [0, 0]
        tok_w = prep_weight(w_g, 3)
        t_mix = build_mix(3, t_gm)
        g_st = [None, None]
        t_evK = 0
        for tt in range(NTT):
            buf = tt % 2
            for m in range(NPT):
                for half in range(2):
                    p.emit("tensor", lambda e, m=m, half=half, tt=tt:
                           e.matmul(
                        psC_[:, half * 512:(half + 1) * 512],
                        fm(mix, m)[:, tt * 128:(tt + 1) * 128],
                        wT[:, m * D + half * 512: m * D + (half + 1) * 512],
                        start=(m == 0), stop=(m == NPT - 1)),
                        waits=tok_w + [(s_dve, t_mix), (s_act, t_evK)],
                        inc=(s_pe, 1) if (m, half) == (NPT - 1, 1) else None)
            t_mm = p.tok(s_pe)
            war = [g_st[buf]] if g_st[buf] else []
            p.emit("scalar", lambda e, buf=buf: e.activation(
                g_t[:, buf * D:(buf + 1) * D], psC_[:, :], AF.Sigmoid),
                waits=[(s_pe, t_mm)] + war)
            t_evK = p.tok(s_act)
            g_st[buf] = st_go.issue("gpsimd", lambda e, tt=tt, buf=buf:
                                    e.dma_start(
                out=g_o[tt * 128:(tt + 1) * 128, :],
                in_=g_t[:, buf * D:(buf + 1) * D]),
                waits=[(s_act, t_evK)], slot=buf)

        tok_w = prep_weight(w_r, 0)
        t_mix = build_mix(0, t_gm)
        r_st = [None, None]
        t_evR = 0
        for ct in range(NPT):
            buf = ct % 2
            for m in range(NPT):
                for tk in range(T // 512):
                    ps = psA_ if tk < 2 else psB_
                    p.emit("tensor", lambda e, m=m, tk=tk, ct=ct, ps=ps:
                           e.matmul(
                        ps[:, (tk % 2) * 512:((tk % 2) + 1) * 512],
                        wT[:, m * D + ct * 128: m * D + (ct + 1) * 128],
                        fm(mix, m)[:, tk * 512:(tk + 1) * 512],
                        start=(m == 0), stop=(m == NPT - 1)),
                        waits=tok_w + [(s_dve, t_mix), (s_dve, t_evR)],
                        inc=(s_pe, 1) if (m, tk) == (NPT - 1, 3) else None)
            t_mm = p.tok(s_pe)
            p.emit("vector", lambda e, ct=ct: e.tensor_copy(
                fm(r_fm, ct)[:, 0:1024], psA_[:, :]), waits=[(s_pe, t_mm)])
            p.emit("vector", lambda e, ct=ct: e.tensor_copy(
                fm(r_fm, ct)[:, 1024:2048], psB_[:, :]))
            t_evR = p.tok(s_dve)
            r_st[buf] = st_ro.issue("gpsimd", lambda e, ct=ct: e.dma_start(
                out=r_o[ct * 128:(ct + 1) * 128, :], in_=fm(r_fm, ct)),
                waits=[(s_dve, t_evR)], slot=buf)

        def proj_tm(w_in, widx, dst):
            tok_w = prep_weight(w_in, widx)
            t_mix = build_mix(widx, t_gm)
            t_ev = 0
            for tt in range(NTT):
                for m in range(NPT):
                    for half in range(2):
                        p.emit("tensor", lambda e, m=m, half=half, tt=tt:
                               e.matmul(
                            psC_[:, half * 512:(half + 1) * 512],
                            fm(mix, m)[:, tt * 128:(tt + 1) * 128],
                            wT[:, m * D + half * 512:
                               m * D + (half + 1) * 512],
                            start=(m == 0), stop=(m == NPT - 1)),
                            waits=tok_w + [(s_dve, t_mix), (s_dve, t_ev)],
                            inc=(s_pe, 1) if (m, half) == (NPT - 1, 1)
                            else None)
                t_mm = p.tok(s_pe)
                p.emit("vector", lambda e, tt=tt, dst=dst: e.tensor_copy(
                    dst[:, tt * D:(tt + 1) * D], psC_[:, :]),
                    waits=[(s_pe, t_mm)])
                t_ev = p.tok(s_dve)
            return t_ev

        t_k = proj_tm(w_k, 1, k_tm)
        t_v = proj_tm(w_v, 2, v_tm)

        p.emit("vector", lambda e: e.memset(S2[:, :], 0.0))
        p.emit("vector", lambda e: e.memset(S2t[:, :], 0.0))
        t_scan_pre = p.tok(s_dve)
        ao_st = [None, None]
        dbg_st = []
        for sc in range(nsc):
            buf = sc % 2
            Ksc = k_tm[:, sc * D:(sc + 1) * D]
            Vsc = v_tm[:, sc * D:(sc + 1) * D]
            p.emit("vector", lambda e, Ksc=Ksc: e.tensor_tensor(
                T1k[:, :], Ksc, QT1[:, :], ALU.mult),
                waits=[(s_dve, t_k), (s_dve, t_v), (s_act, t_pats),
                       (s_pe, p.tok(s_pe))] + dbg_st)
            p.emit("vector", lambda e, Ksc=Ksc: e.tensor_tensor(
                T2k[:, :], Ksc, QA[:, :], ALU.mult))
            p.emit("vector", lambda e, Ksc=Ksc: e.tensor_tensor(
                KHk[:, :], Ksc, QKH[:, :], ALU.mult))
            t_kvar = p.tok(s_dve)
            if cut < 2:
                continue
            wv = [(s_dve, p.tok(s_dve))]
            for grp in range(NPT):
                p.emit("tensor", lambda e, grp=grp, Vsc=Vsc: e.transpose(
                    psT_[:, grp * 128:(grp + 1) * 128],
                    Vsc[:, grp * 128:(grp + 1) * 128], idb[:, :]),
                    waits=wv + [(s_dve, t_v)],
                    inc=(s_pe, 1) if grp == NPT - 1 else None)
            t_tr = p.tok(s_pe)
            ve = (vfm2[0:64, :].rearrange("p (g two s) -> p g two s",
                                          two=2, s=128)[:, :, 0, :])
            vo = (vfm2[64:128, :].rearrange("p (g two s) -> p g two s",
                                            two=2, s=128)[:, :, 1, :])
            pe_ = psT_[0:64, :].rearrange("p (g s) -> p g s", s=128)
            po_ = psT_[64:128, :].rearrange("p (g s) -> p g s", s=128)
            p.emit("vector", lambda e, ve=ve, pe_=pe_: e.tensor_copy(ve, pe_),
                   waits=[(s_pe, t_tr)])
            p.emit("vector", lambda e, vo=vo, po_=po_: e.tensor_copy(vo, po_))
            t_vfm = p.tok(s_dve)
            if cut < 3:
                continue
            for gidx in range(2):
                war_ev = [(s_dve, p.tok(s_dve))]
                for hh in range(8):
                    h = gidx * 8 + hh
                    p.emit("tensor", lambda e, h=h, hh=hh, sc=sc: e.matmul(
                        psA_[:, hh * 128:(hh + 1) * 128],
                        vfm2[:, h * 128:(h + 1) * 128],
                        r_fm[:, (h // 2) * T + sc * 128:
                             (h // 2) * T + (sc + 1) * 128],
                        start=True, stop=True),
                        waits=[(s_dve, t_vfm)] + war_ev,
                        inc=(s_pe, 1) if hh == 7 else None)
                t_pe2 = p.tok(s_pe)
                p.emit("vector", lambda e, gidx=gidx: e.tensor_tensor(
                    Amk[:, gidx * D:(gidx + 1) * D], psA_[:, :],
                    maskc[:, gidx * D:(gidx + 1) * D], ALU.mult),
                    waits=[(s_pe, t_pe2)])
                a2d = (Amk2[0:64, gidx * 512:(gidx + 1) * 512]
                       .rearrange("p (g s) -> p g s", s=64))
                a2s = (psA_[0:64, :].rearrange("p (g two s) -> p g two s",
                                               two=2, s=64)[:, :, 1, :])
                p.emit("vector", lambda e, a2d=a2d, a2s=a2s:
                       e.tensor_copy(a2d, a2s))
            t_am = p.tok(s_dve)
            if cut < 4:
                continue
            for h in range(H):
                hc = h * 128
                hi = h * DH
                p.emit("tensor", lambda e, h=h, hi=hi, sc=sc: e.matmul(
                    psA_[0:64, hi:hi + DH],
                    r_fm[:, (h // 2) * T + sc * 128:
                         (h // 2) * T + sc * 128 + 64],
                    S2t[:, hi:hi + DH],
                    start=True, stop=True, skip_group_check=True),
                    waits=[(s_dve, t_am)])
                p.emit("tensor", lambda e, h=h, hi=hi: e.matmul(
                    psA_[64:128, hi:hi + DH],
                    Amk2[:, h * 64:(h + 1) * 64], T2k[:, hi:hi + DH],
                    start=True, stop=True,
                    tile_position=(0, 64), skip_group_check=True))
                p.emit("tensor", lambda e, h=h, hc=hc, hi=hi: e.matmul(
                    psB_[0:64, hi:hi + DH],
                    Amk[:, hc:hc + 64], T1k[:, hi:hi + DH],
                    start=True, stop=True, skip_group_check=True))
                p.emit("tensor", lambda e, h=h, hc=hc, hi=hi: e.matmul(
                    psB_[64:128, hi:hi + DH],
                    Amk[:, hc + 64:hc + 128], T1k[:, hi:hi + DH],
                    start=True, stop=True, tile_position=(0, 64),
                    skip_group_check=True),
                    inc=(s_pe, 1) if h == H - 1 else None)
            t_o1 = p.tok(s_pe)
            dbg_toks = []
            if cut < 5:
                continue
            war = [ao_st[buf]] if ao_st[buf] else []
            p.emit("vector", lambda e, buf=buf: e.tensor_copy(
                attn_t[:, buf * D:(buf + 1) * D], psA_[:, :]),
                waits=[(s_pe, t_o1)] + war)
            p.emit("vector", lambda e, buf=buf: e.tensor_tensor(
                attn_t[:, buf * D:(buf + 1) * D], psB_[:, :],
                attn_t[:, buf * D:(buf + 1) * D], ALU.add))
            p.emit("vector", lambda e, buf=buf: e.tensor_tensor(
                attn_t[:, buf * D:(buf + 1) * D],
                attn_t[:, buf * D:(buf + 1) * D], P2[:, :], ALU.mult))
            t_ao = p.tok(s_dve)
            ao_st[buf] = st_ao.issue("gpsimd", lambda e, sc=sc, buf=buf:
                                     e.dma_start(
                out=attn_o[sc * 128:(sc + 1) * 128, :],
                in_=attn_t[:, buf * D:(buf + 1) * D]),
                waits=[(s_dve, t_ao)], slot=buf)
            if cut < 6:
                continue
            for h in range(H):
                hi = h * DH
                p.emit("tensor", lambda e, h=h, hi=hi, Vsc=Vsc: e.matmul(
                    psC_[0:64, hi:hi + DH], Vsc[:, hi:hi + DH],
                    KHk[:, hi:hi + DH], start=True, stop=True),
                    waits=[(s_dve, t_kvar)],
                    inc=(s_pe, 1) if h == H - 1 else None)
            t_up = p.tok(s_pe)
            if cut < 7:
                continue
            p.emit("vector", lambda e: e.tensor_copy(S2[0:64, :],
                                                     psC_[0:64, :]),
                   waits=[(s_pe, t_up)] + dbg_toks)
            p.emit("vector", lambda e: e.tensor_copy(S2[64:128, :],
                                                     psC_[0:64, :]))
            p.emit("vector", lambda e: e.tensor_tensor(
                S2t[:, :], S2[:, :], W1f[:, :], ALU.mult))
        t_fin = p.tok(s_dve)
        if nsc > 0:
            st_o.issue("gpsimd", lambda e: e.dma_start(
                out=st_o_d, in_=S2[0:64, :]), waits=[(s_dve, t_fin)])

        fin = (st_o.total_tokens() + st_ao.total_tokens()
               + st_go.total_tokens() + st_ro.total_tokens())
        p.emit("sync", lambda e: e.nop(), waits=fin)

        block = p.es.enter_context(nc.Block())
        p.build(block)
    return nc




TH = 17 * 128
NSC5 = 17
THX = TH + 1


def fmh(x_sb, pt, c0=0, n=TH):
    return x_sb[:, pt * TH + c0: pt * TH + c0 + n]


def build_L45():
    nc = bass.Bass("TRN2", target_bir_lowering=False)
    xh_in = dram_in(nc, "xln2h", [D, THX], BF16)
    x1_in = dram_in(nc, "x1", [D, T])
    w_r = dram_in(nc, "w_r", [D, D])
    w_k = dram_in(nc, "w_k", [D, D])
    w_v = dram_in(nc, "w_v", [D, D])
    w_g = dram_in(nc, "w_g", [D, D])
    w_o = dram_in(nc, "w_o", [D, D])
    tmix = dram_in(nc, "tmix", [4 * D, 1])
    tdec = dram_in(nc, "tdec", [1, D])
    lng_in = dram_in(nc, "ln_g", [D, 1])
    lnb_in = dram_in(nc, "ln_b", [D, 1])
    ones1 = dram_in(nc, "ones1", [1, 128])
    onescol = dram_in(nc, "onescol", [128, 1], BF16)
    iden_b = dram_in(nc, "iden_b", [128, 128], BF16)
    iden_f = dram_in(nc, "iden_f", [128, 128])
    maskc_in = dram_in(nc, "maskc", [128, 16 * 128], BF16)
    parity_in = dram_in(nc, "parity", [128, D], BF16)
    iotas = dram_in(nc, "iotas", [1, 6 * 128])
    core0 = dram_in(nc, "core0", [1, 1])

    x2_o = dram_out(nc, "x2", [D, T])
    xln4_o = dram_out(nc, "xln4", [D, T])
    m3_o = dram_out(nc, "m3", [1, 1])

    wscr = nc.dram_tensor("wscr45", [D, D], BF16, kind="Internal").ap()
    wscr2 = nc.dram_tensor("wscr45b", [D, D], BF16, kind="Internal").ap()
    g_int = nc.dram_tensor("g_int", [T, D], BF16, kind="Internal").ap()

    p = Prog(nc)
    with p.es:
        sb = p.es.enter_context
        attn_res = sb(nc.sbuf_tensor("attn_res", [128, NTT * D], BF16))
        tmix_s = sb(nc.sbuf_tensor("tmix_s", [128, 4 * NPT], F32))
        sc_col = sb(nc.sbuf_tensor("sc_col", [128, 4], F32))
        onec = sb(nc.sbuf_tensor("onec", [1, 128], F32))
        onecol_s = sb(nc.sbuf_tensor("onecol_s", [128, 1], BF16))
        idb = sb(nc.sbuf_tensor("idb", [128, 128], BF16))
        lng = sb(nc.sbuf_tensor("lng", [128, NPT], F32))
        lnb = sb(nc.sbuf_tensor("lnb", [128, NPT], F32))
        c0v = sb(nc.sbuf_tensor("c0v", [1, 1], F32))
        c0b = sb(nc.sbuf_tensor("c0b", [128, 1], F32))
        psA_ = sb(nc.psum_tensor("psA_", [128, D], F32))
        psT_ = sb(nc.psum_tensor("psT_", [128, D], BF16))

        p.init_chain_sems()
        s_pe = p.sem("s_pe")
        s_dve = p.chain["vector"]
        s_act = p.chain["scalar"]
        st_c = Stream(p, "stC", 4)
        st_x = Stream(p, "stX", 2)
        st_wc = Stream(p, "stWC", 1)
        st_w = Stream(p, "stW", 4)
        st_go = Stream(p, "stGO", 2)
        st_gl = Stream(p, "stGL", 2)
        st_x1 = Stream(p, "stX1", 2)
        st_o = Stream(p, "stO", 2)
        st_m = Stream(p, "stM", 1)
        s_gps = p.chain["gpsimd"]
        s = dict(pe=s_pe, dve=s_dve, act=s_act)

        const_sb = setup_consts(p, nc, st_c)
        st_c.issue("sync", lambda e: e.dma_start(out=onec[:, :], in_=ones1))
        st_c.issue("sync", lambda e: e.dma_start(out=onecol_s[:, :],
                                                 in_=onescol))
        st_c.issue("sync", lambda e: e.dma_start(out=idb[:, :], in_=iden_b))
        st_c.issue("sync", lambda e: e.dma_start(out=c0v[:, :], in_=core0))
        for i in range(4 * NPT):
            st_c.issue("sync", lambda e, i=i: e.dma_start(
                out=tmix_s[:, i:i + 1], in_=tmix[i * 128:(i + 1) * 128, :]))
        for pt in range(NPT):
            st_c.issue("sync", lambda e, pt=pt: e.dma_start(
                out=lng[:, pt:pt + 1], in_=lng_in[pt * 128:(pt + 1) * 128, :]))
            st_c.issue("sync", lambda e, pt=pt: e.dma_start(
                out=lnb[:, pt:pt + 1], in_=lnb_in[pt * 128:(pt + 1) * 128, :]))
        cgate = st_c.total_tokens()
        s["cgate"] = cgate
        for en in ("tensor", "vector", "scalar", "gpsimd"):
            p.emit(en, lambda e: e.nop(), waits=cgate,
                   dma=(en not in p.CHAINED))

        p.emit("tensor", lambda e: e.matmul(
            psA_[:, 0:1], onec[:, :], c0v[:, :], start=True, stop=True),
            inc=(s_pe, 1))
        t_pe = p.tok(s_pe)
        p.emit("vector", lambda e: e.tensor_copy(c0b[:, :], psA_[:, 0:1]),
               waits=[(s_pe, t_pe)])
        t_c0 = p.tok(s_dve)

        esB = contextlib.ExitStack()
        sbB = esB.enter_context
        r_fm = sbB(nc.sbuf_tensor("r_fm", [128, NPT * T], BF16))
        k_tm = sbB(nc.sbuf_tensor("k_tm", [128, NSC5 * D], BF16))
        mix = sbB(nc.sbuf_tensor("mix", [128, NPT * TH], BF16))
        g_t = sbB(nc.sbuf_tensor("g_t", [128, 2 * D], BF16))
        psB_ = sbB(nc.psum_tensor("psB_", [128, D], F32))
        psC_ = sbB(nc.psum_tensor("psC_", [128, D], F32))

        esB1 = contextlib.ExitStack()
        sb1 = esB1.enter_context
        vmix = sb1(nc.sbuf_tensor("vmix", [128, NPT * TH], BF16))
        xln_t = sb1(nc.sbuf_tensor("xln_t", [128, THX], BF16))
        wT = sb1(nc.sbuf_tensor("wT", [128, 2 * NPT * D], BF16))

        wscrs = [wscr, wscr2]
        widx_ctr = [0]
        proj_done = [0, 0]

        def prep_weight(w_in):
            wb = widx_ctr[0] % 2
            widx_ctr[0] += 1
            tok_cast = st_wc.issue("gpsimd", lambda e, w_in=w_in, wb=wb:
                                   e.dma_start(
                out=wscrs[wb], in_=w_in))
            toks = []
            for m in range(NPT):
                toks.append(st_w.issue("scalar", lambda e, m=m, wb=wb:
                                       e.dma_start_transpose(
                    out=wT[:, wb * NPT * D + m * D: wb * NPT * D + (m + 1) * D],
                    in_=wscrs[wb][:, m * 128:(m + 1) * 128]),
                    waits=[tok_cast, (s_pe, proj_done[wb])]))
            return toks, wb

        t_mix_pt = {}

        def build_mix(widx, t_gate_mix, mb, t_pe_war):
            for pt in range(NPT):
                tok = st_x.issue("sync", lambda e, pt=pt: e.dma_start(
                    out=xln_t[:, :],
                    in_=xh_in[pt * 128:(pt + 1) * 128, :]),
                    waits=[(s_dve, t_gate_mix[0]), (s_gps, t_gm_g[0])],
                    slot=0)
                p.emit("gpsimd", lambda e, pt=pt, mb=mb: e.tensor_tensor(
                    fmh(mb, pt),
                    xln_t[:, 0:TH],
                    xln_t[:, 1:1 + TH],
                    ALU.subtract),
                    waits=[tok, (s_pe, t_pe_war),
                           (s_dve, t_mix_pt.get((id(mb), pt), 0))])
                t_sub = p.tok(s_gps)
                p.emit("vector", lambda e, pt=pt: e.tensor_tensor(
                    sc_col[:, 0:1], xln_t[:, 130:131],
                    xln_t[:, 128:129], ALU.subtract),
                    waits=[(s_dve, t_c0), tok])
                p.emit("vector", lambda e, pt=pt, mb=mb:
                       e.scalar_tensor_tensor(
                    fmh(mb, pt, 128, 1), sc_col[:, 0:1], c0b[:, 0:1],
                    fmh(mb, pt, 128, 1), ALU.mult, ALU.add),
                    waits=[(s_gps, t_sub)])
                p.emit("vector", lambda e, pt=pt, widx=widx, mb=mb:
                       e.scalar_tensor_tensor(
                    fmh(mb, pt), fmh(mb, pt),
                    tmix_s[:, widx * NPT + pt: widx * NPT + pt + 1],
                    xln_t[:, 1:1 + TH],
                    ALU.mult, ALU.add))
                t_gate_mix[0] = p.tok(s_dve)
                t_gm_g[0] = p.tok(s_gps)
                t_mix_pt[(id(mb), pt)] = p.tok(s_dve)
            return p.tok(s_dve)

        t_gm = [0, 0]
        t_gm_g = [0, 0]
        tok_w, wb = prep_weight(w_g)
        t_mix = build_mix(3, t_gm, mix, 0)
        gmix = mix
        g_st = [None, None]
        t_evK = [0, 0]
        for tt in range(1, NSC5):
            buf = tt % 2
            ps = psC_ if buf == 0 else psB_
            for m in range(NPT):
                for half in range(2):
                    p.emit("tensor", lambda e, m=m, half=half, tt=tt, ps=ps,
                           wb=wb: e.matmul(
                        ps[:, half * 512:(half + 1) * 512],
                        fmh(gmix, m)[:, tt * 128:(tt + 1) * 128],
                        wT[:, wb * NPT * D + m * D + half * 512:
                           wb * NPT * D + m * D + (half + 1) * 512],
                        start=(m == 0), stop=(m == NPT - 1)),
                        waits=[tok_w[m], (s_dve, t_mix),
                               (s_act, t_evK[buf])],
                        inc=(s_pe, 1) if (m, half) == (NPT - 1, 1) else None)
            t_mm = p.tok(s_pe)
            war = [g_st[buf]] if g_st[buf] else []
            p.emit("scalar", lambda e, buf=buf, ps=ps: e.activation(
                g_t[:, buf * D:(buf + 1) * D], ps[:, :], AF.Sigmoid),
                waits=[(s_pe, t_mm)] + war)
            t_evK[buf] = p.tok(s_act)
            g_st[buf] = st_go.issue("gpsimd", lambda e, tt=tt, buf=buf:
                                    e.dma_start(
                out=g_int[(tt - 1) * 128:tt * 128, :],
                in_=g_t[:, buf * D:(buf + 1) * D]),
                waits=[(s_act, t_evK[buf])], slot=buf)
        proj_done[wb] = p.tok(s_pe)
        t_projg = p.tok(s_pe)

        tok_w, wb = prep_weight(w_r)
        t_mix = build_mix(0, t_gm, vmix, 0)
        t_evR = 0
        for ct in range(NPT):
            for m in range(NPT):
                for tk in range(4):
                    ps = psA_ if tk < 2 else psB_
                    p.emit("tensor", lambda e, m=m, tk=tk, ct=ct, ps=ps,
                           wb=wb: e.matmul(
                        ps[:, (tk % 2) * 512:((tk % 2) + 1) * 512],
                        wT[:, wb * NPT * D + m * D + ct * 128:
                           wb * NPT * D + m * D + (ct + 1) * 128],
                        fmh(vmix, m)[:, 128 + tk * 512:128 + (tk + 1) * 512],
                        start=(m == 0), stop=(m == NPT - 1)),
                        waits=[tok_w[m], (s_dve, t_mix), (s_dve, t_evR)],
                        inc=(s_pe, 1) if (m, tk) == (NPT - 1, 3)
                        else None)
            t_mm = p.tok(s_pe)
            p.emit("vector", lambda e, ct=ct: e.tensor_copy(
                fm(r_fm, ct)[:, 0:1024], psA_[:, :]), waits=[(s_pe, t_mm)])
            p.emit("vector", lambda e, ct=ct: e.tensor_copy(
                fm(r_fm, ct)[:, 1024:2048], psB_[:, :]))
            t_evR = p.tok(s_dve)
        proj_done[wb] = p.tok(s_pe)

        def proj_tm(w_in, widx, dst, mb, t_pe_war):
            tok_w, wb = prep_weight(w_in)
            t_mix = build_mix(widx, t_gm, mb, t_pe_war)
            t_ev = [0, 0]
            for tt in range(NSC5):
                buf = tt % 2
                ps = psC_ if buf == 0 else psB_
                for m in range(NPT):
                    for half in range(2):
                        p.emit("tensor", lambda e, m=m, half=half, tt=tt,
                               ps=ps, wb=wb, mb=mb: e.matmul(
                            ps[:, half * 512:(half + 1) * 512],
                            fmh(mb, m)[:, tt * 128:(tt + 1) * 128],
                            wT[:, wb * NPT * D + m * D + half * 512:
                               wb * NPT * D + m * D + (half + 1) * 512],
                            start=(m == 0), stop=(m == NPT - 1)),
                            waits=[tok_w[m], (s_dve, t_mix),
                                   (s_dve, t_ev[buf])],
                            inc=(s_pe, 1) if (m, half) == (NPT - 1, 1)
                            else None)
                t_mm = p.tok(s_pe)
                p.emit("vector", lambda e, tt=tt, dst=dst, ps=ps:
                       e.tensor_copy(
                    dst[:, tt * D:(tt + 1) * D], ps[:, :]),
                    waits=[(s_pe, t_mm)])
                t_ev[buf] = p.tok(s_dve)
            proj_done[wb] = p.tok(s_pe)
            return p.tok(s_dve)

        t_projr = p.tok(s_pe)
        t_k = proj_tm(w_k, 1, k_tm, mix, t_projg)
        v_tm = mix
        t_v = proj_tm(w_v, 2, v_tm, vmix, t_projr)

        bar = ([(s_pe, p.tok(s_pe)), (s_dve, p.tok(s_dve)),
                (s_act, p.tok(s_act)), (s_gps, p.tok(s_gps))]
               + st_x.total_tokens()
               + st_w.total_tokens() + st_wc.total_tokens())
        for en in ("tensor", "vector", "scalar", "gpsimd"):
            p.emit(en, lambda e: e.nop(), waits=bar,
                   dma=(en not in p.CHAINED))
        esB1.close()

        esB2 = contextlib.ExitStack()
        sb2 = esB2.enter_context
        maskc = sb2(nc.sbuf_tensor("maskc_sb", [128, 16 * 128], BF16))
        parity = sb2(nc.sbuf_tensor("parity_sb", [128, D], BF16))
        iot = sb2(nc.sbuf_tensor("iot", [1, 6 * 128], F32))
        lnwrow = sb2(nc.sbuf_tensor("lnwrow", [1, D], F32))
        P2 = sb2(nc.sbuf_tensor("P2", [128, D], BF16))
        QT1 = sb2(nc.sbuf_tensor("QT1", [128, D], BF16))
        QA = sb2(nc.sbuf_tensor("QA", [128, D], BF16))
        QKH = sb2(nc.sbuf_tensor("QKH", [128, D], BF16))
        W1f = sb2(nc.sbuf_tensor("W1f", [128, D], BF16))
        S2 = sb2(nc.sbuf_tensor("S2", [128, D], F32))
        S2t = sb2(nc.sbuf_tensor("S2t", [128, D], BF16))
        T1k = sb2(nc.sbuf_tensor("T1k", [128, 2 * D], BF16))
        T2k = sb2(nc.sbuf_tensor("T2k", [128, 2 * D], BF16))
        KHk = sb2(nc.sbuf_tensor("KHk", [128, 2 * D], BF16))
        Amk = sb2(nc.sbuf_tensor("Amk", [128, 4 * D], BF16))
        Amk2 = sb2(nc.sbuf_tensor("Amk2", [128, 2 * D], BF16))
        vfm2 = sb2(nc.sbuf_tensor("vfm2", [128, 2 * D], BF16))

        st_c.issue("sync", lambda e: e.dma_start(out=maskc[:, :],
                                                 in_=maskc_in), waits=bar)
        st_c.issue("sync", lambda e: e.dma_start(out=parity[:, :],
                                                 in_=parity_in), waits=bar)
        st_c.issue("sync", lambda e: e.dma_start(out=iot[:, :], in_=iotas),
                   waits=bar)
        st_c.issue("sync", lambda e: e.dma_start(out=lnwrow[:, :], in_=tdec),
                   waits=bar)
        cgate2 = st_c.total_tokens()
        p.emit("vector", lambda e: e.nop(), waits=cgate2)
        p.emit("scalar", lambda e: e.nop(), waits=cgate2)
        p.emit("scalar", lambda e: e.activation(lnwrow[:, :], lnwrow[:, :],
                                                AF.Exp))
        t_exp = p.tok(s_act)
        p.emit("vector", lambda e: e.tensor_scalar(lnwrow[:, :], lnwrow[:, :],
                                                   -1.0, None, ALU.mult),
               waits=[(s_act, t_exp)])
        t_lnw = p.tok(s_dve)

        pats = [(P2, 0), (QT1, 1), (QA, 2), (QKH, 3), (W1f, 4)]
        for (dst, idx) in pats:
            wv = [(s_act, p.tok(s_act))]
            for half in range(2):
                p.emit("tensor", lambda e, idx=idx, half=half: e.matmul(
                    psA_[:, half * 512:(half + 1) * 512],
                    iot[:, idx * 128:(idx + 1) * 128],
                    lnwrow[:, half * 512:(half + 1) * 512],
                    start=True, stop=True),
                    waits=[(s_dve, t_lnw)] + wv,
                    inc=(s_pe, 1) if half == 1 else None)
            t_pe = p.tok(s_pe)
            p.emit("scalar", lambda e, dst=dst: e.activation(
                dst[:, :], psA_[:, :], AF.Exp), waits=[(s_pe, t_pe)])
        t_pats = p.tok(s_act)
        p.emit("vector", lambda e: e.tensor_tensor(
            W1f[:, :], W1f[:, :], parity[:, :], ALU.mult),
            waits=[(s_act, t_pats)])
        p.emit("vector", lambda e: e.memset(vfm2[:, :], 0.0))
        p.emit("vector", lambda e: e.memset(Amk2[:, :], 0.0))
        p.emit("vector", lambda e: e.memset(S2[:, :], 0.0))
        p.emit("vector", lambda e: e.memset(S2t[:, :], 0.0))
        t_scan_pre = p.tok(s_dve)

        t_am_s = {}
        t_kv_s = {}
        t_s2t_s = {}
        t_s2e_s = {0: 0}
        t_attn_s = {0: 0}

        def prep(sc):
            b = sc % 2
            bD = b * D
            Ksc = k_tm[:, sc * D:(sc + 1) * D]
            Vsc = v_tm[:, sc * D:(sc + 1) * D]
            war_pe = [(s_pe, p.tok(s_pe))]
            p.emit("gpsimd", lambda e, Ksc=Ksc, bD=bD: e.tensor_tensor(
                T1k[:, bD:bD + D], Ksc, QT1[:, :], ALU.mult),
                waits=[(s_dve, t_k), (s_dve, t_v), (s_act, t_pats)] + war_pe)
            p.emit("gpsimd", lambda e, Ksc=Ksc, bD=bD: e.tensor_tensor(
                T2k[:, bD:bD + D], Ksc, QA[:, :], ALU.mult))
            p.emit("gpsimd", lambda e, Ksc=Ksc, bD=bD: e.tensor_tensor(
                KHk[:, bD:bD + D], Ksc, QKH[:, :], ALU.mult))
            t_kv_s[sc] = p.tok(s_gps)
            wv = [(s_dve, p.tok(s_dve))]
            for grp in range(NPT):
                p.emit("tensor", lambda e, grp=grp, Vsc=Vsc: e.transpose(
                    psT_[:, grp * 128:(grp + 1) * 128],
                    Vsc[:, grp * 128:(grp + 1) * 128], idb[:, :]),
                    waits=wv + [(s_dve, t_v)],
                    inc=(s_pe, 1) if grp == NPT - 1 else None)
            t_tr = p.tok(s_pe)
            ve = (vfm2[0:64, :].rearrange("p (g two s) -> p g two s",
                                          two=2, s=128)[:, :, 0, :])
            vo = (vfm2[64:128, :].rearrange("p (g two s) -> p g two s",
                                            two=2, s=128)[:, :, 1, :])
            pe_ = psT_[0:64, :].rearrange("p (g s) -> p g s", s=128)
            po_ = psT_[64:128, :].rearrange("p (g s) -> p g s", s=128)
            p.emit("vector", lambda e, ve=ve, pe_=pe_: e.tensor_copy(ve, pe_),
                   waits=[(s_pe, t_tr)])
            p.emit("vector", lambda e, vo=vo, po_=po_: e.tensor_copy(vo, po_))
            t_vfm = p.tok(s_dve)
            for gidx in range(2):
                war_ev = [(s_dve, p.tok(s_dve))]
                for hh in range(8):
                    h = gidx * 8 + hh
                    p.emit("tensor", lambda e, h=h, hh=hh, sc=sc: e.matmul(
                        psA_[:, hh * 128:(hh + 1) * 128],
                        vfm2[:, h * 128:(h + 1) * 128],
                        r_fm[:, (h // 2) * T + (sc - 1) * 128:
                             (h // 2) * T + sc * 128],
                        start=True, stop=True),
                        waits=[(s_dve, t_vfm)] + war_ev,
                        inc=(s_pe, 1) if hh == 7 else None)
                t_pe2 = p.tok(s_pe)
                p.emit("vector", lambda e, gidx=gidx, b=b: e.tensor_tensor(
                    Amk[:, b * 2 * D + gidx * D: b * 2 * D + (gidx + 1) * D],
                    psA_[:, :], maskc[:, gidx * D:(gidx + 1) * D], ALU.mult),
                    waits=[(s_pe, t_pe2)])
                a2d = (Amk2[0:64, bD + gidx * 512: bD + (gidx + 1) * 512]
                       .rearrange("p (g s) -> p g s", s=64))
                a2s = (psA_[0:64, :].rearrange("p (g two s) -> p g two s",
                                               two=2, s=64)[:, :, 1, :])
                p.emit("vector", lambda e, a2d=a2d, a2s=a2s:
                       e.tensor_copy(a2d, a2s))
            t_am_s[sc] = p.tok(s_dve)

        def out(sc):
            b = sc % 2
            bD = b * D
            Vsc = v_tm[:, sc * D:(sc + 1) * D]
            for h in range(H):
                hc = b * 2 * D + h * 128
                hi = h * DH
                p.emit("tensor", lambda e, h=h, hi=hi, sc=sc: e.matmul(
                    psB_[0:64, hi:hi + DH],
                    r_fm[:, (h // 2) * T + (sc - 1) * 128:
                         (h // 2) * T + (sc - 1) * 128 + 64],
                    S2t[:, hi:hi + DH],
                    start=True, stop=False, skip_group_check=True),
                    waits=[(s_dve, t_am_s[sc]), (s_gps, t_s2t_s[sc - 1]),
                           (s_gps, t_kv_s[sc]),
                           (s_dve, t_attn_s[sc - 1])])
                p.emit("tensor", lambda e, h=h, hi=hi, bD=bD: e.matmul(
                    psB_[64:128, hi:hi + DH],
                    Amk2[:, bD + h * 64: bD + (h + 1) * 64],
                    T2k[:, bD + hi: bD + hi + DH],
                    start=True, stop=False,
                    tile_position=(0, 64), skip_group_check=True))
                p.emit("tensor", lambda e, h=h, hc=hc, hi=hi, bD=bD: e.matmul(
                    psB_[0:64, hi:hi + DH],
                    Amk[:, hc:hc + 64], T1k[:, bD + hi: bD + hi + DH],
                    start=False, stop=True, skip_group_check=True))
                p.emit("tensor", lambda e, h=h, hc=hc, hi=hi, bD=bD: e.matmul(
                    psB_[64:128, hi:hi + DH],
                    Amk[:, hc + 64:hc + 128], T1k[:, bD + hi: bD + hi + DH],
                    start=False, stop=True, tile_position=(0, 64),
                    skip_group_check=True),
                    inc=(s_pe, 1) if h == H - 1 else None)
            t_o1 = p.tok(s_pe)
            dst = attn_res[:, (sc - 1) * D: sc * D]
            p.emit("vector", lambda e, dst=dst: e.tensor_tensor(
                dst, psB_[:, :], P2[:, :], ALU.mult), waits=[(s_pe, t_o1)])
            t_attn_s[sc] = p.tok(s_dve)
            if sc < NSC5 - 1:
                for h in range(H):
                    hi = h * DH
                    p.emit("tensor", lambda e, h=h, hi=hi, Vsc=Vsc, bD=bD:
                           e.matmul(
                        psC_[0:64, hi:hi + DH], Vsc[:, hi:hi + DH],
                        KHk[:, bD + hi: bD + hi + DH],
                        start=True, stop=True),
                        waits=[(s_gps, t_kv_s[sc]),
                               (s_dve, t_s2e_s[sc - 1])],
                        inc=(s_pe, 1) if h == H - 1 else None)
                t_up = p.tok(s_pe)
                p.emit("vector", lambda e: e.tensor_copy(S2[0:64, :],
                                                         psC_[0:64, :]),
                       waits=[(s_pe, t_up)])
                p.emit("vector", lambda e: e.tensor_copy(S2[64:128, :],
                                                         psC_[0:64, :]))
                t_s2e_s[sc] = p.tok(s_dve)
                p.emit("gpsimd", lambda e: e.tensor_tensor(
                    S2t[:, :], S2[:, :], W1f[:, :], ALU.mult),
                    waits=[(s_dve, t_s2e_s[sc])])
                t_s2t_s[sc] = p.tok(s_gps)

        Ksc0 = k_tm[:, 0:D]
        Vsc0 = v_tm[:, 0:D]
        p.emit("gpsimd", lambda e: e.tensor_tensor(
            KHk[:, 0:D], Ksc0, QKH[:, :], ALU.mult),
            waits=[(s_dve, t_k), (s_dve, t_v), (s_act, t_pats)])
        t_kh0 = p.tok(s_gps)
        for h in range(H):
            hi = h * DH
            p.emit("tensor", lambda e, h=h, hi=hi: e.matmul(
                psC_[0:64, hi:hi + DH], Vsc0[:, hi:hi + DH],
                KHk[:, hi:hi + DH], start=True, stop=True),
                waits=[(s_gps, t_kh0)],
                inc=(s_pe, 1) if h == H - 1 else None)
        t_up0 = p.tok(s_pe)
        p.emit("vector", lambda e: e.tensor_copy(S2[0:64, :], psC_[0:64, :]),
               waits=[(s_pe, t_up0)])
        p.emit("vector", lambda e: e.tensor_copy(S2[64:128, :],
                                                 psC_[0:64, :]))
        t_s2e_s[0] = p.tok(s_dve)
        p.emit("gpsimd", lambda e: e.tensor_tensor(
            S2t[:, :], S2[:, :], W1f[:, :], ALU.mult),
            waits=[(s_dve, t_s2e_s[0])])
        t_s2t_s[0] = p.tok(s_gps)

        prep(1)
        for sc in range(1, NSC5):
            if sc < NSC5 - 1:
                prep(sc + 1)
            out(sc)

        bar2 = ([(s_pe, p.tok(s_pe)), (s_dve, p.tok(s_dve)),
                 (s_act, p.tok(s_act)), (s_gps, p.tok(s_gps))]
                + st_go.total_tokens() + st_c.total_tokens())
        for en in ("tensor", "vector", "scalar", "gpsimd"):
            p.emit(en, lambda e: e.nop(), waits=bar2,
                   dma=(en not in p.CHAINED))
        esB2.close()
        esB.close()

        esC = contextlib.ExitStack()
        sbC = esC.enter_context
        y_sb = sbC(nc.sbuf_tensor("y_sb", [128, NPT * T], F32))
        idf = sbC(nc.sbuf_tensor("idf", [128, 128], F32))
        psS = sbC(nc.psum_tensor("psS", [128, T], F32))
        st_c.issue("sync", lambda e: e.dma_start(out=idf[:, :], in_=iden_f),
                   waits=bar2)
        p.emit("tensor", lambda e: e.nop(), waits=st_c.total_tokens(),
               dma=True)
        esC1 = contextlib.ExitStack()
        sbC1 = esC1.enter_context
        agfm = sbC1(nc.sbuf_tensor("agfm", [128, NPT * T], BF16))
        woT = sbC1(nc.sbuf_tensor("woT", [128, NPT * D], BF16))
        g_t2 = sbC1(nc.sbuf_tensor("g_t2", [128, 2 * D], BF16))

        tok_cast = st_wc.issue("gpsimd", lambda e: e.dma_start(
            out=wscr, in_=w_o), waits=bar2)
        tok_w = []
        for m in range(NPT):
            tok_w.append(st_w.issue("scalar", lambda e, m=m:
                                    e.dma_start_transpose(
                out=woT[:, m * D:(m + 1) * D],
                in_=wscr[:, m * 128:(m + 1) * 128]),
                waits=[tok_cast]))

        agv = agfm[:, :].rearrange("p (pt t) -> p pt t", pt=NPT)
        t_evT = 0
        tr_tok = [0, 0]
        for tt in range(NTT):
            buf = tt % 2
            tok_g = st_gl.issue("sync", lambda e, tt=tt, buf=buf: e.dma_start(
                out=g_t2[:, buf * D:(buf + 1) * D],
                in_=g_int[tt * 128:(tt + 1) * 128, :]),
                waits=[(s_dve, p.tok(s_dve))], slot=buf)
            p.emit("gpsimd", lambda e, tt=tt, buf=buf: e.tensor_tensor(
                attn_res[:, tt * D:(tt + 1) * D],
                attn_res[:, tt * D:(tt + 1) * D],
                g_t2[:, buf * D:(buf + 1) * D], ALU.mult),
                waits=[tok_g, (s_pe, tr_tok[buf])])
            t_ag = p.tok(s_gps)
            wv = [(s_dve, p.tok(s_dve))]
            for pt2 in range(NPT):
                p.emit("tensor", lambda e, pt2=pt2, tt=tt: e.transpose(
                    psT_[:, pt2 * 128:(pt2 + 1) * 128],
                    attn_res[:, tt * D + pt2 * 128: tt * D + (pt2 + 1) * 128],
                    idb[:, :]),
                    waits=[(s_gps, t_ag)] + wv,
                    inc=(s_pe, 1) if pt2 == NPT - 1 else None)
            t_tr = p.tok(s_pe)
            tr_tok[buf] = t_tr
            psv = psT_[:, :].rearrange("p (j t) -> p j t", j=NPT)
            dstv = agv[:, :, tt * 128:(tt + 1) * 128]
            p.emit("vector", lambda e, psv=psv, dstv=dstv: e.tensor_copy(
                dstv, psv), waits=[(s_pe, t_tr)])
            t_evT = p.tok(s_dve)

        t_evY = 0
        for ct in range(NPT):
            for m in range(NPT):
                for tk in range(T // 512):
                    ps = psA_ if tk < 2 else psS
                    p.emit("tensor", lambda e, m=m, tk=tk, ct=ct, ps=ps:
                           e.matmul(
                        ps[:, (tk % 2) * 512:((tk % 2) + 1) * 512],
                        woT[:, m * D + ct * 128: m * D + (ct + 1) * 128],
                        fm(agfm, m)[:, tk * 512:(tk + 1) * 512],
                        start=(m == 0), stop=(m == NPT - 1)),
                        waits=tok_w + [(s_dve, t_evT), (s_dve, t_evY)],
                        inc=(s_pe, 1) if (m, tk) == (NPT - 1, 3) else None)
            t_mm = p.tok(s_pe)
            p.emit("vector", lambda e, ct=ct: e.tensor_copy(
                fm(y_sb, ct)[:, 0:1024], psA_[:, :]), waits=[(s_pe, t_mm)])
            p.emit("vector", lambda e, ct=ct: e.tensor_copy(
                fm(y_sb, ct)[:, 1024:2048], psS[:, 0:1024]))
            t_evY = p.tok(s_dve)

        bar3 = ([(s_pe, p.tok(s_pe)), (s_dve, p.tok(s_dve)),
                 (s_act, p.tok(s_act)), (s_gps, p.tok(s_gps))]
                + st_gl.total_tokens()
                + st_w.total_tokens() + st_wc.total_tokens())
        for en in ("tensor", "vector", "scalar", "gpsimd"):
            p.emit(en, lambda e: e.nop(), waits=bar3,
                   dma=(en not in p.CHAINED))
        esC1.close()
        io_t = sbC(nc.sbuf_tensor("io_t", [128, 2 * T], F32))
        xl_t = sbC(nc.sbuf_tensor("xl_t", [128, 2 * T], F32))
        tmp = sbC(nc.sbuf_tensor("tmp", [128, T], F32))
        yb_t = tmp
        scr = sbC(nc.sbuf_tensor("scr", [128, 2 * T], BF16))
        srow = sbC(nc.sbuf_tensor("srow", [1, T], F32))
        sqrow = sbC(nc.sbuf_tensor("sqrow", [1, T], F32))
        mu2row = tmp
        rstdrow = sqrow
        mx = sbC(nc.sbuf_tensor("mx", [128, 1], F32))
        mxt = sbC(nc.sbuf_tensor("mxt", [128, 1], F32))
        mrow = sbC(nc.sbuf_tensor("mrow", [1, 128], F32))
        mval = sbC(nc.sbuf_tensor("mval", [1, 1], F32))

        rows = dict(s=srow, sq=sqrow, rstd=rstdrow, mu2=mu2row[0:1, :])
        t_ln3 = emit_ln(p, y_sb, lambda pt: fm(y_sb, pt), scr, onecol_s,
                        onec, psS, psS, rows, s)
        x2st = [None, None]
        for pt in range(NPT):
            buf = pt % 2
            war_st = [x2st[buf]] if x2st[buf] else []
            tok_x = st_x1.issue("sync", lambda e, pt=pt, buf=buf: e.dma_start(
                out=io_t[:, buf * T:(buf + 1) * T],
                in_=x1_in[pt * 128:(pt + 1) * 128, :]),
                waits=[(s_dve, p.tok(s_dve))] + war_st, slot=buf)
            p.emit("scalar", lambda e, pt=pt: e.activation(
                yb_t[:, :], fm(y_sb, pt), AF.Identity,
                bias=lnb[:, pt:pt + 1], scale=lng[:, pt:pt + 1]),
                waits=[(s_dve, t_ln3), (s_dve, p.tok(s_dve))])
            t_yb = p.tok(s_act)
            war = [x2st[buf]] if x2st[buf] else []
            p.emit("vector", lambda e, buf=buf: e.tensor_tensor(
                io_t[:, buf * T:(buf + 1) * T],
                io_t[:, buf * T:(buf + 1) * T], yb_t[:, :], ALU.add),
                waits=[tok_x, (s_act, t_yb)] + war)
            t_x2 = p.tok(s_dve)
            x2st[buf] = st_o.issue("gpsimd", lambda e, pt=pt, buf=buf:
                                   e.dma_start(
                out=x2_o[pt * 128:(pt + 1) * 128, :],
                in_=io_t[:, buf * T:(buf + 1) * T]),
                waits=[(s_dve, t_x2)], slot=buf)

        rows4 = dict(s=srow, sq=sqrow, rstd=rstdrow, mu2=mu2row[0:1, :])
        emit_ln_streamed_max(p, x2_o, st_x1, io_t, tmp, scr, onecol_s, onec,
                             psS, psA_, psT_, rows4, xl_t, xln4_o, st_o,
                             x2st, mx, mxt, mrow, mval, idf, m3_o, st_m, s)

        fin = (st_o.total_tokens() + st_m.total_tokens())
        p.emit("sync", lambda e: e.nop(), waits=fin)
        esC.close()

        block = p.es.enter_context(nc.Block())
        p.build(block)
    return nc


def build_L5():
    nc = bass.Bass("TRN2", target_bir_lowering=False)
    x1_in = dram_in(nc, "x1", [D, T])
    attn_in = dram_in(nc, "attnout", [T, D], BF16)
    g_in = dram_in(nc, "g_out", [T, D], BF16)
    r_in = dram_in(nc, "r_out", [D, T], BF16)
    st_in = dram_in(nc, "st_prev", [DH, D])
    w_o = dram_in(nc, "w_o", [D, D])
    lng_in = dram_in(nc, "ln_g", [D, 1])
    lnb_in = dram_in(nc, "ln_b", [D, 1])
    tdec = dram_in(nc, "tdec", [1, D])
    ones1 = dram_in(nc, "ones1", [1, 128])
    onescol = dram_in(nc, "onescol", [128, 1], BF16)
    iden_b = dram_in(nc, "iden_b", [128, 128], BF16)
    iden_f = dram_in(nc, "iden_f", [128, 128])
    iotas = dram_in(nc, "iotas", [1, 6 * 128])
    parity_in = dram_in(nc, "parity", [128, D], BF16)
    x2_o = dram_out(nc, "x2", [D, T])
    xln4_o = dram_out(nc, "xln4", [D, T])
    m3_o = dram_out(nc, "m3", [1, 1])

    wscr = nc.dram_tensor("wscr5", [D, D], BF16, kind="Internal").ap()

    p = Prog(nc)
    with p.es:
        sb = p.es.enter_context
        y_sb = sb(nc.sbuf_tensor("y_sb", [128, NPT * T], F32))
        agfm = sb(nc.sbuf_tensor("agfm", [128, NPT * T], BF16))
        woT = sb(nc.sbuf_tensor("woT", [128, NPT * D], BF16))
        attn_t = sb(nc.sbuf_tensor("attn_t", [128, 2 * D], BF16))
        g_t = sb(nc.sbuf_tensor("g_t", [128, 2 * D], BF16))
        io_t = sb(nc.sbuf_tensor("io_t", [128, 2 * T], F32))
        xl_t = sb(nc.sbuf_tensor("xl_t", [128, 2 * T], F32))
        tmp = sb(nc.sbuf_tensor("tmp", [128, T], F32))
        yb_t = tmp
        scr = sb(nc.sbuf_tensor("scr", [128, 2 * T], BF16))
        r0_sb = sb(nc.sbuf_tensor("r0_sb", [128, NPT * 128], BF16))
        s0f = sb(nc.sbuf_tensor("s0f", [128, D], F32))
        s0b = sb(nc.sbuf_tensor("s0b", [128, D], BF16))
        par5 = sb(nc.sbuf_tensor("par5", [128, D], BF16))
        Pinit = sb(nc.sbuf_tensor("Pinit", [128, D], BF16))
        lnwrow = sb(nc.sbuf_tensor("lnwrow", [1, D], F32))
        onec = sb(nc.sbuf_tensor("onec", [1, 128], F32))
        onecol_s = sb(nc.sbuf_tensor("onecol_s", [128, 1], BF16))
        idb = sb(nc.sbuf_tensor("idb", [128, 128], BF16))
        idf = sb(nc.sbuf_tensor("idf", [128, 128], F32))
        iot = sb(nc.sbuf_tensor("iot", [1, 6 * 128], F32))
        lng = sb(nc.sbuf_tensor("lng", [128, NPT], F32))
        lnb = sb(nc.sbuf_tensor("lnb", [128, NPT], F32))
        srow = sb(nc.sbuf_tensor("srow", [1, T], F32))
        sqrow = sb(nc.sbuf_tensor("sqrow", [1, T], F32))
        mu2row = tmp
        rstdrow = sqrow
        mx = sb(nc.sbuf_tensor("mx", [128, 1], F32))
        mxt = sb(nc.sbuf_tensor("mxt", [128, 1], F32))
        mrow = sb(nc.sbuf_tensor("mrow", [1, 128], F32))
        mval = sb(nc.sbuf_tensor("mval", [1, 1], F32))
        psA_ = sb(nc.psum_tensor("psA_", [128, D], F32))
        psS = sb(nc.psum_tensor("psS", [128, T], F32))
        psT_ = sb(nc.psum_tensor("psT_", [128, D], BF16))

        p.init_chain_sems()
        s_pe = p.sem("s_pe")
        s_dve = p.chain["vector"]
        s_act = p.chain["scalar"]
        st_c = Stream(p, "stC", 4)
        st_wc = Stream(p, "stWC", 1)
        st_w = Stream(p, "stW", 4)
        st_i = Stream(p, "stI", 2)
        st_x = Stream(p, "stX", 2)
        st_o = Stream(p, "stO", 2)
        st_m = Stream(p, "stM", 1)
        s = dict(pe=s_pe, dve=s_dve, act=s_act)

        setup_consts(p, nc, st_c)
        st_c.issue("sync", lambda e: e.dma_start(out=onec[:, :], in_=ones1))
        st_c.issue("sync", lambda e: e.dma_start(out=onecol_s[:, :],
                                                 in_=onescol))
        st_c.issue("sync", lambda e: e.dma_start(out=idb[:, :], in_=iden_b))
        st_c.issue("sync", lambda e: e.dma_start(out=idf[:, :], in_=iden_f))
        st_c.issue("sync", lambda e: e.dma_start(out=iot[:, :], in_=iotas))
        st_c.issue("sync", lambda e: e.dma_start(out=lnwrow[:, :], in_=tdec))
        st_c.issue("sync", lambda e: e.dma_start(out=s0f[0:64, :],
                                                 in_=st_in))
        st_c.issue("sync", lambda e: e.dma_start(out=s0f[64:128, :],
                                                 in_=st_in))
        st_c.issue("sync", lambda e: e.dma_start(out=par5[:, :],
                                                 in_=parity_in))
        for pt in range(NPT):
            st_c.issue("sync", lambda e, pt=pt: e.dma_start(
                out=lng[:, pt:pt + 1], in_=lng_in[pt * 128:(pt + 1) * 128, :]))
            st_c.issue("sync", lambda e, pt=pt: e.dma_start(
                out=lnb[:, pt:pt + 1], in_=lnb_in[pt * 128:(pt + 1) * 128, :]))
            st_c.issue("sync", lambda e, pt=pt: e.dma_start(
                out=r0_sb[:, pt * 128:(pt + 1) * 128],
                in_=r_in[pt * 128:(pt + 1) * 128, 0:128]))
        cgate = st_c.total_tokens()
        s["cgate"] = cgate
        for en in ("tensor", "vector", "scalar"):
            p.emit(en, lambda e: e.nop(), waits=cgate,
                   dma=(en not in p.CHAINED))

        p.emit("scalar", lambda e: e.activation(lnwrow[:, :], lnwrow[:, :],
                                                AF.Exp))
        t_exp = p.tok(s_act)
        p.emit("vector", lambda e: e.tensor_scalar(
            lnwrow[:, :], lnwrow[:, :], -1.0, None, ALU.mult),
            waits=[(s_act, t_exp)])
        t_lnw = p.tok(s_dve)
        for half in range(2):
            p.emit("tensor", lambda e, half=half: e.matmul(
                psA_[:, half * 512:(half + 1) * 512],
                iot[:, 5 * 128:6 * 128],
                lnwrow[:, half * 512:(half + 1) * 512],
                start=True, stop=True),
                waits=[(s_dve, t_lnw)],
                inc=(s_pe, 1) if half == 1 else None)
        t_pe = p.tok(s_pe)
        p.emit("scalar", lambda e: e.activation(Pinit[:, :], psA_[:, :],
                                                AF.Exp),
               waits=[(s_pe, t_pe)])
        t_pinit = p.tok(s_act)
        p.emit("vector", lambda e: e.tensor_copy(s0b[:, :], s0f[:, :]))
        p.emit("vector", lambda e: e.tensor_tensor(
            s0b[:, :], s0b[:, :], par5[:, :], ALU.mult))
        t_s0 = p.tok(s_dve)

        for h in range(H):
            hi = h * DH
            p.emit("tensor", lambda e, h=h, hi=hi: e.matmul(
                psA_[:, hi:hi + DH],
                r0_sb[:, (h // 2) * 128:(h // 2) * 128 + 128],
                s0b[:, hi:hi + DH],
                start=True, stop=True),
                waits=[(s_dve, t_s0), (s_act, t_pinit)],
                inc=(s_pe, 1) if h == H - 1 else None)
        t_init = p.tok(s_pe)

        tok_cast = st_wc.issue("gpsimd", lambda e: e.dma_start(
            out=wscr, in_=w_o))
        tok_w = []
        for m in range(NPT):
            tok_w.append(st_w.issue("scalar", lambda e, m=m:
                                    e.dma_start_transpose(
                out=woT[:, m * D:(m + 1) * D],
                in_=wscr[:, m * 128:(m + 1) * 128]),
                waits=[tok_cast]))

        agv = agfm[:, :].rearrange("p (pt t) -> p pt t", pt=NPT)
        t_evT = 0
        tr_tok = [0, 0]
        for tt in range(NTT):
            buf = tt % 2
            war_i = [(s_dve, p.tok(s_dve)), (s_pe, tr_tok[buf])]
            tok_a = st_i.issue("sync", lambda e, tt=tt, buf=buf: e.dma_start(
                out=attn_t[:, buf * D:(buf + 1) * D],
                in_=attn_in[tt * 128:(tt + 1) * 128, :]),
                waits=war_i, slot=buf)
            tok_g = st_i.issue("sync", lambda e, tt=tt, buf=buf: e.dma_start(
                out=g_t[:, buf * D:(buf + 1) * D],
                in_=g_in[tt * 128:(tt + 1) * 128, :]),
                waits=war_i, slot=buf)
            if tt == 0:
                p.emit("vector", lambda e: e.tensor_tensor(
                    tmp[:, 0:D], psA_[:, :], Pinit[:, :], ALU.mult),
                    waits=[(s_pe, t_init), tok_a])
                p.emit("vector", lambda e, buf=buf: e.tensor_tensor(
                    attn_t[:, buf * D:(buf + 1) * D],
                    attn_t[:, buf * D:(buf + 1) * D], tmp[:, 0:D], ALU.add))
            p.emit("vector", lambda e, buf=buf: e.tensor_tensor(
                attn_t[:, buf * D:(buf + 1) * D],
                attn_t[:, buf * D:(buf + 1) * D],
                g_t[:, buf * D:(buf + 1) * D], ALU.mult),
                waits=[tok_a, tok_g])
            t_ag = p.tok(s_dve)
            wv = [(s_dve, p.tok(s_dve))]
            for pt2 in range(NPT):
                p.emit("tensor", lambda e, pt2=pt2, buf=buf: e.transpose(
                    psT_[:, pt2 * 128:(pt2 + 1) * 128],
                    attn_t[:, buf * D + pt2 * 128: buf * D + (pt2 + 1) * 128],
                    idb[:, :]),
                    waits=[(s_dve, t_ag)] + wv + [(s_act, t_pinit)],
                    inc=(s_pe, 1) if pt2 == NPT - 1 else None)
            t_tr = p.tok(s_pe)
            tr_tok[buf] = t_tr
            psv = psT_[:, :].rearrange("p (j t) -> p j t", j=NPT)
            dstv = agv[:, :, tt * 128:(tt + 1) * 128]
            p.emit("vector", lambda e, psv=psv, dstv=dstv: e.tensor_copy(
                dstv, psv), waits=[(s_pe, t_tr)])
            t_evT = p.tok(s_dve)

        t_evY = 0
        for ct in range(NPT):
            for m in range(NPT):
                for tk in range(T // 512):
                    ps = psA_ if tk < 2 else psS
                    p.emit("tensor", lambda e, m=m, tk=tk, ct=ct, ps=ps:
                           e.matmul(
                        ps[:, (tk % 2) * 512:((tk % 2) + 1) * 512],
                        woT[:, m * D + ct * 128: m * D + (ct + 1) * 128],
                        fm(agfm, m)[:, tk * 512:(tk + 1) * 512],
                        start=(m == 0), stop=(m == NPT - 1)),
                        waits=tok_w + [(s_dve, t_evT), (s_dve, t_evY)],
                        inc=(s_pe, 1) if (m, tk) == (NPT - 1, 3) else None)
            t_mm = p.tok(s_pe)
            p.emit("vector", lambda e, ct=ct: e.tensor_copy(
                fm(y_sb, ct)[:, 0:1024], psA_[:, :]), waits=[(s_pe, t_mm)])
            p.emit("vector", lambda e, ct=ct: e.tensor_copy(
                fm(y_sb, ct)[:, 1024:2048], psS[:, 0:1024]))
            t_evY = p.tok(s_dve)

        p.emit("tensor", lambda e: e.nop(),
               waits=[(s_dve, p.tok(s_dve))], dma=True)
        p.emit("scalar", lambda e: e.nop(), waits=[(s_dve, p.tok(s_dve))])
        rows = dict(s=srow, sq=sqrow, rstd=rstdrow, mu2=mu2row[0:1, :])
        t_ln3 = emit_ln(p, y_sb, lambda pt: fm(y_sb, pt), scr, onecol_s,
                        onec, psS, psS, rows, s)
        x2st = [None, None]
        for pt in range(NPT):
            buf = pt % 2
            war_st = [x2st[buf]] if x2st[buf] else []
            tok_x = st_x.issue("sync", lambda e, pt=pt, buf=buf: e.dma_start(
                out=io_t[:, buf * T:(buf + 1) * T],
                in_=x1_in[pt * 128:(pt + 1) * 128, :]),
                waits=[(s_dve, p.tok(s_dve))] + war_st, slot=buf)
            p.emit("scalar", lambda e, pt=pt: e.activation(
                yb_t[:, :], fm(y_sb, pt), AF.Identity,
                bias=lnb[:, pt:pt + 1], scale=lng[:, pt:pt + 1]),
                waits=[(s_dve, t_ln3), (s_dve, p.tok(s_dve))])
            t_yb = p.tok(s_act)
            war = [x2st[buf]] if x2st[buf] else []
            p.emit("vector", lambda e, buf=buf: e.tensor_tensor(
                io_t[:, buf * T:(buf + 1) * T],
                io_t[:, buf * T:(buf + 1) * T], yb_t[:, :], ALU.add),
                waits=[tok_x, (s_act, t_yb)] + war)
            t_x2 = p.tok(s_dve)
            x2st[buf] = st_o.issue("gpsimd", lambda e, pt=pt, buf=buf:
                                   e.dma_start(
                out=x2_o[pt * 128:(pt + 1) * 128, :],
                in_=io_t[:, buf * T:(buf + 1) * T]),
                waits=[(s_dve, t_x2)], slot=buf)

        rows4 = dict(s=srow, sq=sqrow, rstd=rstdrow, mu2=mu2row[0:1, :])
        emit_ln_streamed_max(p, x2_o, st_x, io_t, tmp, scr, onecol_s, onec,
                             psS, psA_, psT_, rows4, xl_t, xln4_o, st_o,
                             x2st, mx, mxt, mrow, mval, idf, m3_o, st_m, s)

        fin = (st_o.total_tokens() + st_m.total_tokens())
        p.emit("sync", lambda e: e.nop(), waits=fin)

        block = p.es.enter_context(nc.Block())
        p.build(block)
    return nc


def emit_ln_streamed_max(p, x_o, st_x, io_t, tmp, scr, onecol_s, onec, psS,
                         psB1, psB2, rows, xl_t, xln_o, st_o, x_ready_st,
                         mx, mxt, mrow, mval, idf, m_o, st_m, s):
    s_pe, s_dve, s_act = s["pe"], s["dve"], s["act"]
    srow, sqrow, rstdrow = rows["s"], rows["sq"], rows["rstd"]
    x1_ready = [t for t in x_ready_st if t]
    for which in range(2):
        for pt in range(NPT):
            buf = pt % 2
            war = [(s_act, p.tok(s_act))]
            tok = st_x.issue("sync", lambda e, pt=pt, buf=buf: e.dma_start(
                out=io_t[:, buf * T:(buf + 1) * T],
                in_=x_o[pt * 128:(pt + 1) * 128, :]),
                waits=x1_ready + war, slot=buf)
            war2 = [(s_pe, p.tok(s_pe))]
            if which == 0:
                p.emit("scalar", lambda e, buf=buf: e.copy(
                    scr[:, buf * T:(buf + 1) * T],
                    io_t[:, buf * T:(buf + 1) * T]), waits=[tok] + war2)
            else:
                p.emit("scalar", lambda e, buf=buf: e.activation(
                    scr[:, buf * T:(buf + 1) * T],
                    io_t[:, buf * T:(buf + 1) * T], AF.Square),
                    waits=[tok] + war2)
            t_act = p.tok(s_act)
            for tk in range(T // 512):
                p.emit("tensor", lambda e, buf=buf, tk=tk, pt=pt: e.matmul(
                    psS[0:1, tk * 512:(tk + 1) * 512], onecol_s[:, 0:1],
                    scr[:, buf * T + tk * 512: buf * T + (tk + 1) * 512],
                    start=(pt == 0), stop=(pt == NPT - 1)),
                    waits=[(s_act, t_act)],
                    inc=(s_pe, 1) if tk == 3 else None)
        t_pe = p.tok(s_pe)
        dstrow = srow if which == 0 else sqrow
        p.emit("vector", lambda e, dstrow=dstrow: e.tensor_copy(
            dstrow[:, :], psS[0:1, :]), waits=[(s_pe, t_pe)])
        if which == 0:
            t_dve = p.tok(s_dve)
            p.emit("tensor", lambda e: e.nop(), waits=[(s_dve, t_dve)])
    mu2 = rows["mu2"]
    p.emit("vector", lambda e: e.tensor_scalar(
        srow[:, :], srow[:, :], 1.0 / D, None, ALU.mult))
    p.emit("vector", lambda e: e.tensor_tensor(
        mu2, srow[:, :], srow[:, :], ALU.mult))
    p.emit("vector", lambda e: e.scalar_tensor_tensor(
        sqrow[:, :], sqrow[:, :], 1.0 / D, mu2, ALU.mult, ALU.subtract))
    t = p.tok(s_dve)
    p.emit("scalar", lambda e: e.activation(
        sqrow[:, :], sqrow[:, :], AF.Ln, bias=EPS, scale=1.0),
        waits=[(s_dve, t)] + list(s.get("cgate", [])))
    p.emit("scalar", lambda e: e.activation(
        rstdrow[:, :], sqrow[:, :], AF.Exp, bias=0.0, scale=-0.5))
    t_rows = p.tok(s_act)
    wv = [(s_dve, p.tok(s_dve))]
    for tk in range(T // 512):
        p.emit("tensor", lambda e, tk=tk: e.matmul(
            psS[:, tk * 512:(tk + 1) * 512], onec[:, :],
            srow[:, tk * 512:(tk + 1) * 512], start=True, stop=True),
            waits=[(s_act, t_rows)] + wv,
            inc=(s_pe, 1) if tk == 3 else None)
    t_bc = p.tok(s_pe)
    prev_q = 0
    st_tok = [None, None]
    for pt in range(NPT):
        buf = pt % 2
        tok = st_x.issue("sync", lambda e, pt=pt, buf=buf: e.dma_start(
            out=io_t[:, buf * T:(buf + 1) * T],
            in_=x_o[pt * 128:(pt + 1) * 128, :]),
            waits=[(s_dve, prev_q)], slot=buf)
        p.emit("vector", lambda e, buf=buf: e.tensor_tensor(
            tmp[:, :], io_t[:, buf * T:(buf + 1) * T], psS[:, :],
            ALU.subtract), waits=[tok, (s_pe, t_bc)])
        for tk in range(T // 512):
            half = tk % 2
            wv2 = [(s_dve, p.tok(s_dve))]
            p.emit("tensor", lambda e, tk=tk, half=half: e.matmul(
                psB1[:, half * 512:(half + 1) * 512], onec[:, :],
                rstdrow[:, tk * 512:(tk + 1) * 512], start=True, stop=True),
                waits=wv2 + [(s_act, t_rows)],
                inc=(s_pe, 1))
            t_b2 = p.tok(s_pe)
            war = [st_tok[buf]] if st_tok[buf] and tk == 0 else []
            p.emit("vector", lambda e, buf=buf, tk=tk, half=half:
                   e.tensor_tensor(
                xl_t[:, buf * T + tk * 512: buf * T + (tk + 1) * 512],
                tmp[:, tk * 512:(tk + 1) * 512],
                psB1[:, half * 512:(half + 1) * 512], ALU.mult),
                waits=[(s_pe, t_b2)] + war)
        dst = mx if pt == 0 else mxt
        p.emit("vector", lambda e, buf=buf, dst=dst: e.reduce_max(
            dst[:, :], xl_t[:, buf * T:(buf + 1) * T], AX.X,
            apply_absolute_value=True))
        if pt > 0:
            p.emit("vector", lambda e: e.tensor_tensor(
                mx[:, :], mx[:, :], mxt[:, :], ALU.max))
        prev_q = p.tok(s_dve)
        t_dve = p.tok(s_dve)
        st_tok[buf] = st_o.issue("gpsimd", lambda e, pt=pt, buf=buf:
                                 e.dma_start(
            out=xln_o[pt * 128:(pt + 1) * 128, :],
            in_=xl_t[:, buf * T:(buf + 1) * T]),
            waits=[(s_dve, t_dve)], slot=buf)
    t_dve = p.tok(s_dve)
    p.emit("tensor", lambda e: e.transpose(psS[0:1, 0:128], mx[:, :],
                                           idf[:, :]),
           waits=[(s_dve, t_dve)], inc=(s_pe, 1))
    t_pe = p.tok(s_pe)
    p.emit("vector", lambda e: e.tensor_copy(mrow[:, :], psS[0:1, 0:128]),
           waits=[(s_pe, t_pe)])
    p.emit("vector", lambda e: e.reduce_max(mval[:, :], mrow[:, :], AX.X))
    t_dve = p.tok(s_dve)
    st_m.issue("gpsimd", lambda e: e.dma_start(out=m_o, in_=mval[:, :]),
               waits=[(s_dve, t_dve)])


def host_consts():
    c = {}
    c["ones1"] = np.ones((1, 128), np.float32)
    c["onescol"] = np.ones((128, 1), BF)
    c["iden_f"] = np.eye(128, dtype=np.float32)
    cv = np.zeros((128, 16), np.float32)
    cv[:, :len(ACT_CONSTS)] = np.asarray(ACT_CONSTS, np.float32)[None, :]
    c["cvals"] = cv
    c["iden_b"] = np.eye(128, dtype=BF)
    s = np.arange(128)
    blk = ((s[:, None] <= s[None, :])
           & ((s[:, None] < 64) == (s[None, :] < 64))).astype(BF)
    c["maskc"] = np.tile(blk, (1, 16))
    par = np.zeros((128, 1024), np.float32)
    for h in range(16):
        par[(64 * (h % 2)):(64 * (h % 2) + 64), h * 64:(h + 1) * 64] = 1.0
    c["parity"] = par.astype(BF)
    iot = np.zeros((1, 6 * 128), np.float32)
    iot[0, 0:128] = s % 64
    iot[0, 128:256] = -(s % 64)
    iot[0, 256:384] = np.where(s < 64, 64 - s, 200)
    iot[0, 384:512] = 127 - s
    iot[0, 512:640] = 1
    iot[0, 640:768] = s + 1
    c["iotas"] = iot
    return c


_NCS = {}


def _programs():
    if not _NCS:
        _NCS["L1"] = build_L1()
        _NCS["L2"] = build_L2()
        _NCS["L3"] = build_L3(final=False)
        _NCS["L45"] = build_L45()
        _NCS["L7"] = build_L3(final=True)
    return _NCS


def _run(nc, in_maps):
    res = run_bass_kernel_spmd(nc, in_maps, core_ids=list(range(NC)))
    return res.results


def kernel(L0_attn_x, L0_attn_time_decay, L0_attn_time_mix_k,
           L0_attn_time_mix_v, L0_attn_time_mix_r, L0_attn_time_mix_g,
           L0_attn_W_r, L0_attn_W_k, L0_attn_W_v, L0_attn_W_g, L0_attn_W_o,
           L0_attn_ln_gamma, L0_attn_ln_beta, L0_ffn_clip_x,
           L0_ffn_clip_down, L0_ffn_w_gate, L0_ffn_w_up, L0_ffn_w_down):
    f32 = np.float32
    x = np.ascontiguousarray(np.asarray(L0_attn_x, f32).reshape(16384, D))
    wg = np.ascontiguousarray(np.asarray(L0_ffn_w_gate, f32))
    wu = np.ascontiguousarray(np.asarray(L0_ffn_w_up, f32))
    wd = np.ascontiguousarray(np.asarray(L0_ffn_w_down, f32))
    w_r = np.ascontiguousarray(np.asarray(L0_attn_W_r, f32))
    w_k = np.ascontiguousarray(np.asarray(L0_attn_W_k, f32))
    w_v = np.ascontiguousarray(np.asarray(L0_attn_W_v, f32))
    w_g = np.ascontiguousarray(np.asarray(L0_attn_W_g, f32))
    w_o = np.ascontiguousarray(np.asarray(L0_attn_W_o, f32))
    tmix = np.ascontiguousarray(np.concatenate([
        np.asarray(L0_attn_time_mix_r, f32),
        np.asarray(L0_attn_time_mix_k, f32),
        np.asarray(L0_attn_time_mix_v, f32),
        np.asarray(L0_attn_time_mix_g, f32)]).reshape(4 * D, 1))
    tdec = np.ascontiguousarray(np.asarray(L0_attn_time_decay,
                                           f32).reshape(1, D))
    lng = np.ascontiguousarray(np.asarray(L0_attn_ln_gamma, f32).reshape(D, 1))
    lnb = np.ascontiguousarray(np.asarray(L0_attn_ln_beta, f32).reshape(D, 1))
    clip_x = np.asarray(L0_ffn_clip_x, f32).reshape(1, 1)
    clip_d = np.asarray(L0_ffn_clip_down, f32).reshape(1, 1)
    consts = host_consts()
    ncs = _programs()
    zrow = np.zeros((1, D), BF)
    zst = np.zeros((DH, D), f32)
    ones_c0 = np.ones((1, 1), f32)
    zero_c0 = np.zeros((1, 1), f32)

    def cmap(keys):
        return {k: consts[k] for k in keys}

    k1 = cmap(["ones1", "onescol", "iden_f", "cvals"])
    pad = np.ones((3 * 128 * NC - I, D), np.float32)
    wgp = np.concatenate([np.asarray(wg, np.float32), pad], axis=0)
    wup = np.concatenate([np.asarray(wu, np.float32), pad], axis=0)
    r1 = _run(ncs["L1"], [
        dict(k1, x=x[c * T:(c + 1) * T],
             w_gate=wgp[c * 384:(c + 1) * 384],
             w_up=wup[c * 384:(c + 1) * 384],
             w_down=wd[c * 128:(c + 1) * 128])
        for c in range(NC)])
    m1 = np.concatenate([r1[c]["m1"] for c in range(NC)], axis=0)
    qg_f = np.concatenate([r1[c]["qg"] for c in range(NC)], axis=0)[:I]
    qu_f = np.concatenate([r1[c]["qu"] for c in range(NC)], axis=0)[:I]
    sg_f = np.concatenate([r1[c]["sg"] for c in range(NC)], axis=0)[:I]
    su_f = np.concatenate([r1[c]["su"] for c in range(NC)], axis=0)[:I]
    qd_f = np.concatenate([r1[c]["qd"] for c in range(NC)], axis=0)
    sdn_f = np.concatenate([r1[c]["sdn"] for c in range(NC)], axis=0)

    k2 = cmap(["ones1", "iden_f", "cvals"])
    r2 = _run(ncs["L2"], [
        dict(k2, xln=r1[c]["xln"], m_all=m1, clip=clip_x, qg=qg_f,
             qu=qu_f, sg=sg_f, su=su_f)
        for c in range(NC)])
    m2 = np.concatenate([r2[c]["m2"] for c in range(NC)], axis=0)

    k3 = cmap(["ones1", "onescol", "iden_f", "cvals"])
    r3 = _run(ncs["L3"], [
        dict(k3, a=r2[c]["a"], xfm=r1[c]["xfm"], m_all=m2, clip=clip_d,
             qd=qd_f, sdn=sdn_f)
        for c in range(NC)])

    k45 = cmap(["ones1", "onescol", "iden_b", "iden_f", "maskc", "parity",
                "iotas", "cvals"])
    zhalo = np.zeros((D, 129), BF)
    xh = [np.ascontiguousarray(np.concatenate(
        [(np.asarray(r3[c - 1]["xln2"])[:, T - 129:T] if c > 0 else zhalo),
         np.asarray(r3[c]["xln2"])], axis=1)) for c in range(NC)]
    r5 = _run(ncs["L45"], [
        dict(k45, xln2h=xh[c], x1=r3[c]["x1"],
             w_r=w_r, w_k=w_k, w_v=w_v, w_g=w_g, w_o=w_o,
             tmix=tmix, tdec=tdec, ln_g=lng, ln_b=lnb,
             core0=(ones_c0 if c == 0 else zero_c0))
        for c in range(NC)])
    m3 = np.concatenate([r5[c]["m3"] for c in range(NC)], axis=0)

    r6 = _run(ncs["L2"], [
        dict(k2, xln=r5[c]["xln4"], m_all=m3, clip=clip_x, qg=qg_f,
             qu=qu_f, sg=sg_f, su=su_f)
        for c in range(NC)])
    m4 = np.concatenate([r6[c]["m2"] for c in range(NC)], axis=0)

    r7 = _run(ncs["L7"], [
        dict(k3, a=r6[c]["a"], xfm=r5[c]["x2"], m_all=m4, clip=clip_d,
             qd=qd_f, sdn=sdn_f)
        for c in range(NC)])

    out = np.concatenate([r7[c]["out"] for c in range(NC)], axis=0)
    return out.reshape(1, 16384, D)


# version 11
# speedup vs baseline: 1.0300x; 1.0300x over previous
import contextlib
import numpy as np
import ml_dtypes
import concourse.bass as bass
import concourse.mybir as mybir
from concourse.bass_utils import run_bass_kernel_spmd

F32 = mybir.dt.float32
BF16 = mybir.dt.bfloat16
AF = mybir.ActivationFunctionType
ALU = mybir.AluOpType
AX = mybir.AxisListType

NC = 8
T = 2048
D = 1024
I = 2816
H = 16
DH = 64
SC = 128
NSC = T // SC
NPT = D // 128
NIT = I // 128
NTT = T // 128
MAGIC = float(np.float32(12582912.0))
MAGIC64 = 6755399441055744.0
BIG4 = float(np.float32(4.0 * (MAGIC64 - MAGIC)))
QMAX = 127.0
EPS = 1e-5

BF = ml_dtypes.bfloat16


class Prog:

    ENGINES = ("sync", "vector", "scalar", "tensor", "gpsimd")
    CHAINED = ("vector", "scalar", "gpsimd")

    def __init__(self, nc):
        self.nc = nc
        self.es = contextlib.ExitStack()
        self.ops = []
        self.cnt = {}
        self.chain = {}

    def init_chain_sems(self):
        for en in self.CHAINED:
            self.chain[en] = self.sem(f"chain_{en}")

    def sem(self, name):
        s = self.es.enter_context(self.nc.semaphore(name))
        self.cnt[id(s)] = 0
        return s

    def emit(self, eng, fn, waits=(), inc=None, dma=False):
        incs = []
        waits = [(s, v) for (s, v) in waits if v > 0]
        if eng in self.CHAINED and not dma:
            assert inc is None, "compute ops on chained engines use the chain sem"
            cs = self.chain[eng]
            prior = self.cnt[id(cs)]
            if prior > 0:
                waits = waits + [(cs, prior)]
            self.cnt[id(cs)] += 1
            incs = [(cs, 1)]
        elif inc is not None:
            sem, amt = inc
            self.cnt[id(sem)] += amt
            incs = [(sem, amt)]
        self.ops.append((eng, fn, waits, incs))

    def tok(self, sem):
        return self.cnt[id(sem)]

    def barrier(self, sems):
        toks = [(s, self.tok(s)) for s in sems]
        for en in self.ENGINES:
            self.emit(en, lambda e: e.engine_nop() if hasattr(e, "engine_nop")
                      else e.nop(), waits=toks)

    def build(self, block):
        for en in self.ENGINES:
            ops = [(fn, waits, incs) for (e, fn, waits, incs) in self.ops
                   if e == en]
            if not ops:
                continue

            def body(eng, _ops=ops):
                seen = {}
                for (fn, waits, incs) in _ops:
                    for (sem, v) in waits:
                        if seen.get(id(sem), 0) >= v:
                            continue
                        eng.wait_ge(sem, v)
                        seen[id(sem)] = v
                    inst = fn(eng)
                    for (sem, amt) in incs:
                        inst = inst.then_inc(sem, amt)

            getattr(block, en)(body)


class Stream:

    MAXQ = 1

    def __init__(self, p, name, nslots=1):
        self.p = p
        self.sems = [p.sem(f"{name}{i}") for i in range(nslots)]
        self.counts = [0] * nslots
        self.waited = [0] * nslots
        self.k = 0

    def issue(self, eng, fn, waits=(), slot=None):
        if slot is None:
            slot = self.k % len(self.sems)
            self.k += 1
        s = self.sems[slot]
        waits = list(waits)
        if (self.counts[slot] - self.waited[slot]) // 16 >= self.MAXQ:
            self.waited[slot] = self.counts[slot]
            waits.append((s, self.waited[slot]))
        self.counts[slot] += 16
        self.p.emit(eng, fn, waits=waits, inc=(s, 16), dma=True)
        return (s, self.counts[slot])

    def total_tokens(self):
        return [(s, c) for s, c in zip(self.sems, self.counts) if c > 0]


def dram_in(nc, name, shape, dtype=F32):
    return nc.dram_tensor(name, shape, dtype, kind="ExternalInput").ap()


def dram_out(nc, name, shape, dtype=F32):
    return nc.dram_tensor(name, shape, dtype, kind="ExternalOutput").ap()


ACT_CONSTS = [MAGIC, EPS, -0.5, 128.25, -128.25, 255.5, 0.5, 2.0, 0.25]


def setup_consts(p, nc, st_const):
    cvals = dram_in(nc, "cvals", [128, 16])
    const_sb = p.es.enter_context(nc.sbuf_tensor("const_sb", [128, 16], F32))
    for i, v in enumerate(ACT_CONSTS):
        nc.const_aps.register(F32, float(v), const_sb[:, i:i + 1])
    st_const.issue("sync", lambda e: e.dma_start(out=const_sb[:, :], in_=cvals))
    return const_sb


def emit_adaptive_magic(p, const_sb, s):
    p.emit("vector", lambda e: e.tensor_scalar(
        const_sb[:, 9:10], const_sb[:, 8:9], MAGIC, -MAGIC,
        ALU.add, ALU.add), waits=list(s.get("cgate", [])))
    p.emit("vector", lambda e: e.tensor_scalar(
        const_sb[:, 9:10], const_sb[:, 9:10], BIG4, MAGIC,
        ALU.mult, ALU.add))
    p.emit("vector", lambda e: e.tensor_scalar(
        const_sb[:, 10:11], const_sb[:, 9:10], -1.0, None, ALU.mult))
    return p.tok(s["dve"])


def fm(x_sb, pt, c0=0, n=T):
    return x_sb[:, pt * T + c0: pt * T + c0 + n]


def emit_ln(p, x_sb, xln_dst_fn, scr, onecol_s, onec, psS, psB, rows, s):
    s_pe, s_dve, s_act = s["pe"], s["dve"], s["act"]
    srow, sqrow, rstdrow = rows["s"], rows["sq"], rows["rstd"]
    murow = srow
    for which in range(2):
        for pt in range(NPT):
            buf = pt % 2
            war = [(s_pe, p.tok(s_pe))] if pt >= 2 or which == 1 else []
            if which == 0:
                p.emit("scalar", lambda e, pt=pt, buf=buf: e.copy(
                    scr[:, buf * T:(buf + 1) * T], fm(x_sb, pt)),
                    waits=war)
            else:
                p.emit("scalar", lambda e, pt=pt, buf=buf: e.activation(
                    scr[:, buf * T:(buf + 1) * T], fm(x_sb, pt), AF.Square),
                    waits=war)
            t_act = p.tok(s_act)
            for tk in range(T // 512):
                p.emit("tensor", lambda e, pt=pt, buf=buf, tk=tk: e.matmul(
                    psS[0:1, tk * 512:(tk + 1) * 512], onecol_s[:, 0:1],
                    scr[:, buf * T + tk * 512: buf * T + (tk + 1) * 512],
                    start=(pt == 0), stop=(pt == NPT - 1)),
                    waits=[(s_act, t_act)],
                    inc=(s_pe, 1) if tk == T // 512 - 1 else None)
        t_pe = p.tok(s_pe)
        dstrow = srow if which == 0 else sqrow
        p.emit("vector", lambda e, dstrow=dstrow: e.tensor_copy(
            dstrow[:, :], psS[0:1, :]), waits=[(s_pe, t_pe)])
        if which == 0:
            t_dve = p.tok(s_dve)
            p.emit("tensor", lambda e: e.nop(), waits=[(s_dve, t_dve)])
    mu2 = rows["mu2"]
    p.emit("vector", lambda e: e.tensor_scalar(
        murow[:, :], srow[:, :], 1.0 / D, None, ALU.mult))
    p.emit("vector", lambda e: e.tensor_tensor(
        mu2, murow[:, :], murow[:, :], ALU.mult))
    p.emit("vector", lambda e: e.scalar_tensor_tensor(
        sqrow[:, :], sqrow[:, :], 1.0 / D, mu2,
        ALU.mult, ALU.subtract))
    t = p.tok(s_dve)
    p.emit("scalar", lambda e: e.activation(
        sqrow[:, :], sqrow[:, :], AF.Ln, bias=EPS, scale=1.0),
        waits=[(s_dve, t)] + list(s.get("cgate", [])))
    p.emit("scalar", lambda e: e.activation(
        rstdrow[:, :], sqrow[:, :], AF.Exp, bias=0.0, scale=-0.5))
    t_rows = p.tok(s_act)
    for tk in range(T // 512):
        mu_c = psB[:, (tk % 2) * 512:(tk % 2) * 512 + 512]
        rs_c = psB[:, 1024 + (tk % 2) * 512:1024 + (tk % 2) * 512 + 512]
        war = [(s_dve, p.tok(s_dve))]
        p.emit("tensor", lambda e, tk=tk, mu_c=mu_c: e.matmul(
            mu_c, onec[:, :], murow[:, tk * 512:(tk + 1) * 512],
            start=True, stop=True), waits=[(s_act, t_rows)] + war)
        p.emit("tensor", lambda e, tk=tk, rs_c=rs_c: e.matmul(
            rs_c, onec[:, :], rstdrow[:, tk * 512:(tk + 1) * 512],
            start=True, stop=True), inc=(s_pe, 1))
        t_bc = p.tok(s_pe)
        for pt in range(NPT):
            dst = xln_dst_fn(pt)
            p.emit("vector", lambda e, pt=pt, tk=tk, mu_c=mu_c, dst=dst:
                   e.tensor_tensor(
                tmpcol(dst, tk), fm(x_sb, pt, tk * 512, 512),
                mu_c, ALU.subtract), waits=[(s_pe, t_bc)])
            p.emit("vector", lambda e, pt=pt, tk=tk, rs_c=rs_c, dst=dst:
                   e.tensor_tensor(tmpcol(dst, tk), tmpcol(dst, tk),
                                   rs_c, ALU.mult))
    return p.tok(s_dve)


def tmpcol(dst, tk):
    return dst[:, tk * 512:(tk + 1) * 512]


def emit_absmax_fm(p, src_fn, npt, mx, mxt, idf, psum_m, mrow, mval, s,
                   extra_w=()):
    s_pe, s_dve = s["pe"], s["dve"]
    for pt in range(npt):
        dst = mx if pt == 0 else mxt
        p.emit("vector", lambda e, pt=pt, dst=dst: e.reduce_max(
            dst[:, :], src_fn(pt), AX.X, apply_absolute_value=True),
            waits=list(extra_w))
        if pt > 0:
            p.emit("vector", lambda e: e.tensor_tensor(
                mx[:, :], mx[:, :], mxt[:, :], ALU.max))
    t = p.tok(s_dve)
    p.emit("tensor", lambda e: e.transpose(psum_m[0:1, :], mx[:, :],
                                           idf[:, :]),
           waits=[(s_dve, t)], inc=(s_pe, 1))
    t = p.tok(s_pe)
    p.emit("vector", lambda e: e.tensor_copy(mrow[:, :], psum_m[0:1, :]),
           waits=[(s_pe, t)])
    p.emit("vector", lambda e: e.reduce_max(mval[:, :], mrow[:, :], AX.X),
          )
    return p.tok(s_dve)


def build_L1():
    nc = bass.Bass("TRN2", target_bir_lowering=False)
    x_in = dram_in(nc, "x", [T, D])
    wg_in = dram_in(nc, "w_gate", [3 * 128, D])
    wu_in = dram_in(nc, "w_up", [3 * 128, D])
    wd_in = dram_in(nc, "w_down", [128, I])
    ones1 = dram_in(nc, "ones1", [1, 128])
    onescol = dram_in(nc, "onescol", [128, 1], BF16)
    iden_f = dram_in(nc, "iden_f", [128, 128])

    xfm_o = dram_out(nc, "xfm", [D, T])
    xln_o = dram_out(nc, "xln", [D, T])
    m1_o = dram_out(nc, "m1", [1, 1])
    qg_o = dram_out(nc, "qg", [3 * 128, D], BF16)
    qu_o = dram_out(nc, "qu", [3 * 128, D], BF16)
    qd_o = dram_out(nc, "qd", [128, I], BF16)
    sg_o = dram_out(nc, "sg", [3 * 128, 1])
    su_o = dram_out(nc, "su", [3 * 128, 1])
    sdn_o = dram_out(nc, "sdn", [128, 1])

    p = Prog(nc)
    with p.es:
        sb = p.es.enter_context
        x_sb = sb(nc.sbuf_tensor("x_sb", [128, NPT * T], F32))
        xln_sb = sb(nc.sbuf_tensor("xln_sb", [128, NPT * T], F32))
        xtm = sb(nc.sbuf_tensor("xtm", [128, 2 * D], F32))
        scr = sb(nc.sbuf_tensor("scr", [128, 2 * T], BF16))
        onec = sb(nc.sbuf_tensor("onec", [1, 128], F32))
        onecol_s = sb(nc.sbuf_tensor("onecol_s", [128, 1], BF16))
        idf = sb(nc.sbuf_tensor("idf", [128, 128], F32))
        srow = sb(nc.sbuf_tensor("srow", [1, T], F32))
        sqrow = sb(nc.sbuf_tensor("sqrow", [1, T], F32))
        mu2row = sb(nc.sbuf_tensor("mu2row", [1, T], F32))
        rstdrow = sqrow
        mx = sb(nc.sbuf_tensor("mx", [128, 1], F32))
        mxt = sb(nc.sbuf_tensor("mxt", [128, 1], F32))
        mrow = sb(nc.sbuf_tensor("mrow", [1, 128], F32))
        mval = sb(nc.sbuf_tensor("mval", [1, 1], F32))
        wtile = sb(nc.sbuf_tensor("wtile", [128, 2 * I], F32))
        qtile = sb(nc.sbuf_tensor("qtile", [128, 2 * I], BF16))
        wmax = sb(nc.sbuf_tensor("wmax", [128, 2], F32))
        winv = sb(nc.sbuf_tensor("winv", [128, 2], F32))
        wsc = sb(nc.sbuf_tensor("wsc", [128, 2], F32))
        psA = sb(nc.psum_tensor("psA", [128, 512], F32))
        psB2 = sb(nc.psum_tensor("psB2", [128, 512], F32))
        psS = sb(nc.psum_tensor("psS", [128, T], F32))
        psM = sb(nc.psum_tensor("psM", [128, 128], F32))

        p.init_chain_sems()
        s_pe = p.sem("s_pe")
        s_dve = p.chain["vector"]
        s_act = p.chain["scalar"]
        st_c = Stream(p, "stC", 4)
        st_w = Stream(p, "stW", 2)
        st_x = Stream(p, "stX", 2)
        st_o = Stream(p, "stO", 1)
        st_q = Stream(p, "stQ", 2)
        s = dict(pe=s_pe, dve=s_dve, act=s_act)

        const_sb = setup_consts(p, nc, st_c)
        st_c.issue("sync", lambda e: e.dma_start(out=onec[:, :], in_=ones1))
        st_c.issue("sync", lambda e: e.dma_start(out=onecol_s[:, :],
                                                 in_=onescol))
        st_c.issue("sync", lambda e: e.dma_start(out=idf[:, :],
                                                 in_=iden_f))
        cgate = st_c.total_tokens()
        s["cgate"] = cgate
        for en in ("tensor", "vector", "scalar"):
            p.emit(en, lambda e: e.nop(), waits=cgate,
                   dma=(en not in p.CHAINED))
        emit_adaptive_magic(p, const_sb, s)

        qw_last_dve = [0, 0]
        qw_st = [None, None]

        def quant_weight(w_in, q_out, s_out, ntiles, width):
            for it in range(ntiles):
                buf = it % 2
                tok_ld = st_w.issue("sync", lambda e, it=it, buf=buf:
                                    e.dma_start(
                    out=wtile[:, buf * I:buf * I + width],
                    in_=w_in[it * 128:(it + 1) * 128, :]),
                    waits=[(s_dve, qw_last_dve[buf])], slot=buf)
                war_q = [qw_st[buf]] if qw_st[buf] else []
                p.emit("vector", lambda e, buf=buf, width=width: e.reduce_max(
                    wmax[:, buf:buf + 1], wtile[:, buf * I:buf * I + width],
                    AX.X, apply_absolute_value=True),
                    waits=[tok_ld] + war_q)
                p.emit("vector", lambda e, buf=buf: e.reciprocal(
                    winv[:, buf:buf + 1], wmax[:, buf:buf + 1]))
                p.emit("vector", lambda e, buf=buf: e.tensor_scalar(
                    winv[:, buf:buf + 1], winv[:, buf:buf + 1], QMAX, None,
                    ALU.mult))
                p.emit("vector", lambda e, buf=buf: e.tensor_scalar(
                    wsc[:, buf:buf + 1], wmax[:, buf:buf + 1], 1.0 / QMAX,
                    None, ALU.mult))
                t_dve = p.tok(s_dve)
                p.emit("scalar", lambda e, buf=buf, width=width: e.activation(
                    wtile[:, buf * I:buf * I + width],
                    wtile[:, buf * I:buf * I + width],
                    AF.Identity, scale=winv[:, buf:buf + 1]),
                    waits=[(s_dve, t_dve)] + cgate)
                t_act = p.tok(s_act)
                p.emit("vector", lambda e, buf=buf, width=width:
                       e.tensor_scalar(
                    qtile[:, buf * I:buf * I + width],
                    wtile[:, buf * I:buf * I + width],
                    const_sb[:, 9:10], const_sb[:, 10:11],
                    ALU.add, ALU.add),
                    waits=[(s_act, t_act)])
                qw_last_dve[buf] = p.tok(s_dve)
                t_dve = p.tok(s_dve)
                st_q.issue("gpsimd", lambda e, it=it, buf=buf, width=width,
                           q_out=q_out: e.dma_start(
                    out=q_out[it * 128:(it + 1) * 128, :],
                    in_=qtile[:, buf * I:buf * I + width]),
                    waits=[(s_dve, t_dve)], slot=buf)
                qw_st[buf] = st_q.issue("gpsimd", lambda e, it=it, buf=buf,
                                        s_out=s_out: e.dma_start(
                    out=s_out[it * 128:(it + 1) * 128, :],
                    in_=wsc[:, buf:buf + 1]),
                    waits=[(s_dve, t_dve)], slot=buf)

        quant_weight(wg_in, qg_o, sg_o, 3, D)
        quant_weight(wu_in, qu_o, su_o, 3, D)
        quant_weight(wd_in, qd_o, sdn_o, 1, I)

        xv = x_sb[:, :].rearrange("p (pt t) -> p pt t", pt=NPT)
        evac_tok = {}
        for tt in range(NTT):
            buf = tt % 2
            w = [(s_pe, evac_tok.get(tt - 2, 0))]
            tok_ld = st_x.issue("sync", lambda e, tt=tt, buf=buf: e.dma_start(
                out=xtm[:, buf * D:(buf + 1) * D],
                in_=x_in[tt * 128:(tt + 1) * 128, :]),
                waits=w, slot=buf)
            for g in range(2):
                ps = psA if g == 0 else psB2
                wv = [(s_dve, p.tok(s_dve))]
                for j in range(4):
                    pt = g * 4 + j
                    p.emit("tensor",
                           lambda e, pt=pt, j=j, buf=buf, ps=ps: e.transpose(
                               ps[:, j * 128:(j + 1) * 128],
                               xtm[:, buf * D + pt * 128:
                                   buf * D + (pt + 1) * 128],
                               idf[:, :]),
                           waits=[tok_ld] + cgate + wv,
                           inc=(s_pe, 1) if j == 3 else None)
                t_pe = p.tok(s_pe)
                dstv = xv[:, g * 4:(g + 1) * 4, tt * 128:(tt + 1) * 128]
                psv = ps[:, :].rearrange("p (j t) -> p j t", j=4)
                p.emit("vector", lambda e, dstv=dstv, psv=psv: e.tensor_copy(
                    dstv, psv), waits=[(s_pe, t_pe)])
            evac_tok[tt] = p.tok(s_pe)
        t_transposed = p.tok(s_dve)

        for pt in range(NPT):
            st_o.issue("gpsimd", lambda e, pt=pt: e.dma_start(
                out=xfm_o[pt * 128:(pt + 1) * 128, :], in_=fm(x_sb, pt)),
                waits=[(s_dve, t_transposed)])

        p.emit("scalar", lambda e: e.nop(), waits=[(s_dve, t_transposed)])
        rows = dict(s=srow, sq=sqrow, rstd=rstdrow, mu2=mu2row[:, :])
        t_xln = emit_ln(p, x_sb, lambda pt: fm(xln_sb, pt), scr, onecol_s,
                        onec, psS, psS, rows, s)
        for pt in range(NPT):
            st_o.issue("gpsimd", lambda e, pt=pt: e.dma_start(
                out=xln_o[pt * 128:(pt + 1) * 128, :], in_=fm(xln_sb, pt)),
                waits=[(s_dve, t_xln)])
        t_m = emit_absmax_fm(p, lambda pt: fm(xln_sb, pt), NPT, mx, mxt, idf,
                             psM, mrow, mval, s, extra_w=[(s_dve, t_xln)])
        st_o.issue("gpsimd", lambda e: e.dma_start(out=m1_o, in_=mval[:, :]),
                   waits=[(s_dve, t_m)])

        fin = st_o.total_tokens() + st_q.total_tokens()
        p.emit("sync", lambda e: e.nop(), waits=fin)

        block = p.es.enter_context(nc.Block())
        p.build(block)
    return nc


def emit_global_scale(p, nc, m_in, clip_in, mrow8, clipv, gv, sxv, invv,
                      sx_b, inv_b, onec, ps, s, st_c):
    s_pe, s_dve, s_act = s["pe"], s["dve"], s["act"]
    t1 = st_c.issue("sync", lambda e: e.dma_start(
        out=mrow8[:, :], in_=m_in.rearrange("a b -> b a")))
    t2 = st_c.issue("sync", lambda e: e.dma_start(out=clipv[:, :], in_=clip_in))
    p.emit("vector", lambda e: e.reduce_max(gv[:, :], mrow8[:, :], AX.X),
           waits=[t1])
    p.emit("scalar", lambda e: e.activation(clipv[:, :], clipv[:, :],
                                            AF.Sigmoid), waits=[t2])
    t_act = p.tok(s_act)
    p.emit("vector", lambda e: e.tensor_tensor(gv[:, :], gv[:, :],
                                               clipv[:, :], ALU.mult),
           waits=[(s_act, t_act)])
    p.emit("vector", lambda e: e.tensor_scalar(sxv[:, :], gv[:, :],
                                               1.0 / QMAX, None, ALU.mult))
    p.emit("vector", lambda e: e.reciprocal(invv[:, :], sxv[:, :]))
    t_dve = p.tok(s_dve)
    p.emit("tensor", lambda e: e.matmul(ps[:, 0:1], onec[:, :], sxv[:, :],
                                        start=True, stop=True),
           waits=[(s_dve, t_dve)])
    p.emit("tensor", lambda e: e.matmul(ps[:, 1:2], onec[:, :], invv[:, :],
                                        start=True, stop=True),
           inc=(s_pe, 1))
    t_pe = p.tok(s_pe)
    p.emit("vector", lambda e: e.tensor_copy(sx_b[:, :], ps[:, 0:1]),
           waits=[(s_pe, t_pe)])
    p.emit("vector", lambda e: e.tensor_copy(inv_b[:, :], ps[:, 1:2]))
    return p.tok(s_dve)


def emit_quant_tile(p, dst_bf, src_ap, inv_b, tmp, s, mcol, extra_w=()):
    s_act, s_dve = s["act"], s["dve"]
    p.emit("scalar", lambda e: e.activation(
        tmp, src_ap, AF.Relu, bias=128.25, scale=inv_b[:, 0:1]),
        waits=list(extra_w) + list(s.get("cgate", [])))
    t_act = p.tok(s_act)
    p.emit("vector", lambda e: e.tensor_scalar(
        tmp, tmp, 255.5, -128.25, ALU.min, ALU.add), waits=[(s_act, t_act)])
    p.emit("vector", lambda e: e.tensor_scalar(
        dst_bf, tmp, mcol[:, 9:10], mcol[:, 10:11], ALU.add, ALU.add))
    return p.tok(s_dve)


def build_L2():
    nc = bass.Bass("TRN2", target_bir_lowering=False)
    xln_in = dram_in(nc, "xln", [D, T])
    m_in = dram_in(nc, "m_all", [NC, 1])
    clip_in = dram_in(nc, "clip", [1, 1])
    qg_in = dram_in(nc, "qg", [I, D], BF16)
    qu_in = dram_in(nc, "qu", [I, D], BF16)
    sg_in = dram_in(nc, "sg", [I, 1])
    su_in = dram_in(nc, "su", [I, 1])
    ones1 = dram_in(nc, "ones1", [1, 128])
    iden_f = dram_in(nc, "iden_f", [128, 128])
    a_o = dram_out(nc, "a", [I, T])
    m2_o = dram_out(nc, "m2", [1, 1])

    p = Prog(nc)
    with p.es:
        sb = p.es.enter_context
        xq = sb(nc.sbuf_tensor("xq", [128, NPT * T], BF16))
        qgT = sb(nc.sbuf_tensor("qgT", [128, NPT * I], BF16))
        quT = sb(nc.sbuf_tensor("quT", [128, NPT * I], BF16))
        xln_t = sb(nc.sbuf_tensor("xln_t", [128, 2 * T], F32))
        tmp = sb(nc.sbuf_tensor("tmp", [128, T], F32))
        silu_sb = sb(nc.sbuf_tensor("silu_sb", [128, T], F32))
        a_sb = sb(nc.sbuf_tensor("a_sb", [128, 2 * T], F32))
        onec = sb(nc.sbuf_tensor("onec", [1, 128], F32))
        idf = sb(nc.sbuf_tensor("idf", [128, 128], F32))
        mrow8 = sb(nc.sbuf_tensor("mrow8", [1, NC], F32))
        clipv = sb(nc.sbuf_tensor("clipv", [1, 1], F32))
        gv = sb(nc.sbuf_tensor("gv", [1, 1], F32))
        sxv = sb(nc.sbuf_tensor("sxv", [1, 1], F32))
        invv = sb(nc.sbuf_tensor("invv", [1, 1], F32))
        sx_b = sb(nc.sbuf_tensor("sx_b", [128, 1], F32))
        inv_b = sb(nc.sbuf_tensor("inv_b", [128, 1], F32))
        sg_all = sb(nc.sbuf_tensor("sg_all", [128, NIT], F32))
        su_all = sb(nc.sbuf_tensor("su_all", [128, NIT], F32))
        sxsg = sb(nc.sbuf_tensor("sxsg", [128, NIT], F32))
        sxsu = sb(nc.sbuf_tensor("sxsu", [128, NIT], F32))
        mx = sb(nc.sbuf_tensor("mx", [128, 1], F32))
        mxt = sb(nc.sbuf_tensor("mxt", [128, 1], F32))
        mrow = sb(nc.sbuf_tensor("mrow", [1, 128], F32))
        mval = sb(nc.sbuf_tensor("mval", [1, 1], F32))
        psG = sb(nc.psum_tensor("psG", [128, T], F32))
        psU = sb(nc.psum_tensor("psU", [128, T], F32))

        p.init_chain_sems()
        s_pe = p.sem("s_pe")
        s_dve = p.chain["vector"]
        s_act = p.chain["scalar"]
        st_c = Stream(p, "stC", 4)
        st_xl = Stream(p, "stXL", 2)
        st_w = Stream(p, "stW", 4)
        st_a = Stream(p, "stA", 2)
        st_o = Stream(p, "stO", 1)
        s = dict(pe=s_pe, dve=s_dve, act=s_act)

        const_sb = setup_consts(p, nc, st_c)
        st_c.issue("sync", lambda e: e.dma_start(out=onec[:, :], in_=ones1))
        st_c.issue("sync", lambda e: e.dma_start(out=idf[:, :], in_=iden_f))
        for it in range(NIT):
            st_c.issue("sync", lambda e, it=it: e.dma_start(
                out=sg_all[:, it:it + 1],
                in_=sg_in[it * 128:(it + 1) * 128, :]))
            st_c.issue("sync", lambda e, it=it: e.dma_start(
                out=su_all[:, it:it + 1],
                in_=su_in[it * 128:(it + 1) * 128, :]))
        s["cgate"] = st_c.total_tokens()
        tok_sg = tok_su = s["cgate"][0]
        for en in ("tensor", "vector", "scalar"):
            p.emit(en, lambda e: e.nop(), waits=s["cgate"],
                   dma=(en not in p.CHAINED))

        emit_adaptive_magic(p, const_sb, s)
        t_sc = emit_global_scale(p, nc, m_in, clip_in, mrow8, clipv, gv, sxv,
                                 invv, sx_b, inv_b, onec, psG, s, st_c)
        p.emit("vector", lambda e: e.tensor_scalar(
            sxsg[:, :], sg_all[:, :], sx_b[:, 0:1], None, ALU.mult),
            waits=[(s_dve, t_sc)])
        p.emit("vector", lambda e: e.tensor_scalar(
            sxsu[:, :], su_all[:, :], sx_b[:, 0:1], None, ALU.mult))

        t_xq = [0] * NPT
        q_done = [0, 0]
        for pt in range(NPT):
            buf = pt % 2
            tok = st_xl.issue("sync", lambda e, pt=pt, buf=buf: e.dma_start(
                out=xln_t[:, buf * T:(buf + 1) * T],
                in_=xln_in[pt * 128:(pt + 1) * 128, :]),
                waits=[(s_dve, q_done[buf])], slot=buf)
            emit_quant_tile(p, fm(xq, pt), xln_t[:, buf * T:(buf + 1) * T],
                            inv_b, tmp[:, :], s, const_sb,
                            extra_w=[tok, (s_dve, p.tok(s_dve))])
            q_done[buf] = p.tok(s_dve)
            t_xq[pt] = p.tok(s_dve)

        tokg = [None] * NPT
        toku = [None] * NPT
        for m in range(NPT):
            tokg[m] = st_w.issue("scalar", lambda e, m=m:
                                 e.dma_start_transpose(
                out=qgT[:, m * I:(m + 1) * I],
                in_=qg_in[:, m * 128:(m + 1) * 128]))
            toku[m] = st_w.issue("scalar", lambda e, m=m:
                                 e.dma_start_transpose(
                out=quT[:, m * I:(m + 1) * I],
                in_=qu_in[:, m * 128:(m + 1) * 128]))

        t_silu = 0
        t_astt = 0
        ast_tok = [None, None]
        for ct in range(NIT):
            buf = ct % 2
            for m in range(NPT):
                for tk in range(T // 512):
                    p.emit("tensor", lambda e, m=m, tk=tk, ct=ct: e.matmul(
                        psG[:, tk * 512:(tk + 1) * 512],
                        qgT[:, m * I + ct * 128: m * I + (ct + 1) * 128],
                        xq[:, m * T + tk * 512: m * T + (tk + 1) * 512],
                        start=(m == 0), stop=(m == NPT - 1)),
                        waits=[tokg[m], (s_dve, t_xq[m]),
                               (s_act, t_silu)],
                        inc=(s_pe, 1) if (m, tk) == (NPT - 1, 3) else None)
            t_gate = p.tok(s_pe)
            for m in range(NPT):
                for tk in range(T // 512):
                    p.emit("tensor", lambda e, m=m, tk=tk, ct=ct: e.matmul(
                        psU[:, tk * 512:(tk + 1) * 512],
                        quT[:, m * I + ct * 128: m * I + (ct + 1) * 128],
                        xq[:, m * T + tk * 512: m * T + (tk + 1) * 512],
                        start=(m == 0), stop=(m == NPT - 1)),
                        waits=[toku[m], (s_dve, t_astt)],
                        inc=(s_pe, 1) if (m, tk) == (NPT - 1, 3) else None)
            t_up = p.tok(s_pe)
            p.emit("scalar", lambda e, ct=ct: e.activation(
                silu_sb[:, :], psG[:, :], AF.Silu,
                scale=sxsg[:, ct:ct + 1]),
                waits=[(s_pe, t_gate), (s_dve, t_astt)])
            t_silu = p.tok(s_act)
            war_st = [ast_tok[buf]] if ast_tok[buf] else []
            p.emit("vector", lambda e, ct=ct, buf=buf: e.scalar_tensor_tensor(
                a_sb[:, buf * T:(buf + 1) * T], psU[:, :],
                sxsu[:, ct:ct + 1], silu_sb[:, :], ALU.mult, ALU.mult),
                waits=[(s_pe, t_up), (s_act, t_silu)] + war_st)
            t_astt = p.tok(s_dve)
            if ct == 0:
                p.emit("vector", lambda e, buf=buf: e.reduce_max(
                    mx[:, :], a_sb[:, buf * T:(buf + 1) * T], AX.X,
                    apply_absolute_value=True))
            else:
                p.emit("vector", lambda e, buf=buf: e.reduce_max(
                    mxt[:, :], a_sb[:, buf * T:(buf + 1) * T], AX.X,
                    apply_absolute_value=True))
                p.emit("vector", lambda e: e.tensor_tensor(
                    mx[:, :], mx[:, :], mxt[:, :], ALU.max))
            t_dve = p.tok(s_dve)
            ast_tok[buf] = st_a.issue("gpsimd", lambda e, ct=ct, buf=buf:
                                      e.dma_start(
                out=a_o[ct * 128:(ct + 1) * 128, :],
                in_=a_sb[:, buf * T:(buf + 1) * T]),
                waits=[(s_dve, t_dve)], slot=buf)

        t_dve = p.tok(s_dve)
        p.emit("tensor", lambda e: e.transpose(psG[0:1, 0:128], mx[:, :],
                                               idf[:, :]),
               waits=[(s_dve, t_dve), (s_act, t_silu)], inc=(s_pe, 1))
        t_pe = p.tok(s_pe)
        p.emit("vector", lambda e: e.tensor_copy(mrow[:, :], psG[0:1, 0:128]),
               waits=[(s_pe, t_pe)])
        p.emit("vector", lambda e: e.reduce_max(mval[:, :], mrow[:, :], AX.X))
        t_dve = p.tok(s_dve)
        st_o.issue("gpsimd", lambda e: e.dma_start(out=m2_o, in_=mval[:, :]),
                   waits=[(s_dve, t_dve)])

        fin = st_o.total_tokens() + st_a.total_tokens()
        p.emit("sync", lambda e: e.nop(), waits=fin)

        block = p.es.enter_context(nc.Block())
        p.build(block)
    return nc


def build_L3(final=False):
    nc = bass.Bass("TRN2", target_bir_lowering=False)
    a_in = dram_in(nc, "a", [I, T])
    x_in = dram_in(nc, "xfm", [D, T])
    m_in = dram_in(nc, "m_all", [NC, 1])
    clip_in = dram_in(nc, "clip", [1, 1])
    qd_in = dram_in(nc, "qd", [D, I], BF16)
    sdn_in = dram_in(nc, "sdn", [D, 1])
    ones1 = dram_in(nc, "ones1", [1, 128])
    onescol = dram_in(nc, "onescol", [128, 1], BF16)
    iden_f = dram_in(nc, "iden_f", [128, 128])
    if final:
        out_o = dram_out(nc, "out", [T, D])
    else:
        x1_o = dram_out(nc, "x1", [D, T])
        xln2_o = dram_out(nc, "xln2", [D, T], BF16)
        last_o = dram_out(nc, "lastrow", [1, D], BF16)

    p = Prog(nc)
    with p.es:
        sb = p.es.enter_context
        h_sb = sb(nc.sbuf_tensor("h_sb", [128, NIT * T], BF16))
        qdT = sb(nc.sbuf_tensor("qdT", [128, NIT * D], BF16))
        io_t = sb(nc.sbuf_tensor("io_t", [128, 2 * T], F32))
        x1_t = sb(nc.sbuf_tensor("x1_t", [128, 2 * T], F32))
        tmp = sb(nc.sbuf_tensor("tmp", [128, T], F32))
        onec = sb(nc.sbuf_tensor("onec", [1, 128], F32))
        onecol_s = sb(nc.sbuf_tensor("onecol_s", [128, 1], BF16))
        idf = sb(nc.sbuf_tensor("idf", [128, 128], F32))
        mrow8 = sb(nc.sbuf_tensor("mrow8", [1, NC], F32))
        clipv = sb(nc.sbuf_tensor("clipv", [1, 1], F32))
        gv = sb(nc.sbuf_tensor("gv", [1, 1], F32))
        sdv = sb(nc.sbuf_tensor("sdv", [1, 1], F32))
        invv = sb(nc.sbuf_tensor("invv", [1, 1], F32))
        sd_b = sb(nc.sbuf_tensor("sd_b", [128, 1], F32))
        inv_b = sb(nc.sbuf_tensor("inv_b", [128, 1], F32))
        sdn_all = sb(nc.sbuf_tensor("sdn_all", [128, NPT], F32))
        halfsc = sb(nc.sbuf_tensor("halfsc", [128, NPT], F32))
        if final:
            stage = sb(nc.sbuf_tensor("stage", [128, 1024], F32))
        psD = sb(nc.psum_tensor("psD", [128, T], F32))
        psS = sb(nc.psum_tensor("psS", [128, T], F32))
        if not final:
            scr = sb(nc.sbuf_tensor("scr", [128, 2 * T], BF16))
            srow = sb(nc.sbuf_tensor("srow", [1, T], F32))
            sqrow = sb(nc.sbuf_tensor("sqrow", [1, T], F32))
            rstdrow = sqrow
            xl_t = sb(nc.sbuf_tensor("xl_t", [128, 2 * T], BF16))

        p.init_chain_sems()
        s_pe = p.sem("s_pe")
        s_dve = p.chain["vector"]
        s_act = p.chain["scalar"]
        st_c = Stream(p, "stC", 4)
        st_a = Stream(p, "stA", 2)
        st_x = Stream(p, "stX", 2)
        st_w = Stream(p, "stW", 4)
        st_o = Stream(p, "stO", 1)
        st_x1 = Stream(p, "stX1", 2)
        st_l = Stream(p, "stL", 2)
        s = dict(pe=s_pe, dve=s_dve, act=s_act)

        const_sb = setup_consts(p, nc, st_c)
        st_c.issue("sync", lambda e: e.dma_start(out=onec[:, :], in_=ones1))
        st_c.issue("sync", lambda e: e.dma_start(out=onecol_s[:, :],
                                                 in_=onescol))
        st_c.issue("sync", lambda e: e.dma_start(out=idf[:, :], in_=iden_f))
        for pt in range(NPT):
            st_c.issue("sync", lambda e, pt=pt: e.dma_start(
                out=sdn_all[:, pt:pt + 1],
                in_=sdn_in[pt * 128:(pt + 1) * 128, :]))
        s["cgate"] = st_c.total_tokens()
        for en in ("tensor", "vector", "scalar"):
            p.emit(en, lambda e: e.nop(), waits=s["cgate"],
                   dma=(en not in p.CHAINED))
        emit_adaptive_magic(p, const_sb, s)

        t_sc = emit_global_scale(p, nc, m_in, clip_in, mrow8, clipv, gv, sdv,
                                 invv, sd_b, inv_b, onec, psD, s, st_c)
        p.emit("vector", lambda e: e.tensor_scalar(
            halfsc[:, :], sdn_all[:, :], sd_b[:, 0:1], 0.5,
            ALU.mult, ALU.mult), waits=[(s_dve, t_sc)])

        tok_wd = [None] * NIT
        for it in range(NIT):
            tok_wd[it] = st_w.issue("scalar", lambda e, it=it:
                                    e.dma_start_transpose(
                out=qdT[:, it * D:(it + 1) * D],
                in_=qd_in[:, it * 128:(it + 1) * 128]))

        t_h = [0] * NIT
        for it in range(NIT):
            buf = it % 2
            tok = st_a.issue("sync", lambda e, it=it, buf=buf: e.dma_start(
                out=io_t[:, buf * T:(buf + 1) * T],
                in_=a_in[it * 128:(it + 1) * 128, :]),
                waits=[(s_dve, p.tok(s_dve))], slot=buf)
            emit_quant_tile(p, h_sb[:, it * T:(it + 1) * T],
                            io_t[:, buf * T:(buf + 1) * T],
                            inv_b, tmp[:, :], s, const_sb,
                            extra_w=[tok, (s_dve, p.tok(s_dve))])
            t_h[it] = p.tok(s_dve)
        t_hall = p.tok(s_dve)

        x1_st = [None, None]
        x1t_war = [0, 0]
        io_war = [0, 0]
        t_evac = 0
        for dt in range(NPT):
            buf = dt % 2
            tok_x = st_x.issue("sync", lambda e, dt=dt, buf=buf: e.dma_start(
                out=io_t[:, buf * T:(buf + 1) * T],
                in_=x_in[dt * 128:(dt + 1) * 128, :]),
                waits=[(s_dve, io_war[buf]), (s_act, p.tok(s_act))],
                slot=buf)
            for it in range(NIT):
                for tk in range(T // 512):
                    p.emit("tensor", lambda e, it=it, tk=tk, dt=dt: e.matmul(
                        psD[:, tk * 512:(tk + 1) * 512],
                        qdT[:, it * D + dt * 128: it * D + (dt + 1) * 128],
                        h_sb[:, it * T + tk * 512: it * T + (tk + 1) * 512],
                        start=(it == 0), stop=(it == NIT - 1)),
                        waits=[tok_wd[it], (s_dve, t_h[it]),
                               (s_dve, t_evac)],
                        inc=(s_pe, 1) if (it, tk) == (NIT - 1, 3) else None)
            t_mm = p.tok(s_pe)
            if final:
                war = [(s_pe, x1t_war[buf])] if x1t_war[buf] else []
            else:
                war = [x1_st[buf]] if x1_st[buf] else []
            p.emit("vector", lambda e, dt=dt, buf=buf: e.scalar_tensor_tensor(
                x1_t[:, buf * T:(buf + 1) * T], psD[:, :],
                halfsc[:, dt:dt + 1], io_t[:, buf * T:(buf + 1) * T],
                ALU.mult, ALU.add),
                waits=[(s_pe, t_mm), tok_x] + war)
            t_evac = p.tok(s_dve)
            io_war[buf] = t_evac
            if not final:
                p.emit("scalar", lambda e, buf=buf: e.copy(
                    scr[:, buf * T:(buf + 1) * T],
                    x1_t[:, buf * T:(buf + 1) * T]),
                    waits=[(s_dve, t_evac), (s_pe, p.tok(s_pe))])
                t_scp = p.tok(s_act)
                for tk in range(T // 512):
                    p.emit("tensor", lambda e, buf=buf, tk=tk, dt=dt:
                           e.matmul(
                        psS[0:1, tk * 512:(tk + 1) * 512], onecol_s[:, 0:1],
                        scr[:, buf * T + tk * 512: buf * T + (tk + 1) * 512],
                        start=(dt == 0), stop=(dt == NPT - 1)),
                        waits=[(s_act, t_scp)],
                        inc=(s_pe, 1) if tk == 3 else None)
            if final:
                for g in range(4):
                    gb = g % 2
                    wv = [(s_dve, p.tok(s_dve))]
                    for j in range(4):
                        tt = g * 4 + j
                        p.emit("tensor", lambda e, tt=tt, j=j, buf=buf, gb=gb:
                               e.transpose(
                            psS[:, gb * 512 + j * 128:
                                gb * 512 + (j + 1) * 128],
                            x1_t[:, buf * T + tt * 128:
                                 buf * T + (tt + 1) * 128], idf[:, :]),
                            waits=[(s_dve, t_evac)] + wv,
                            inc=(s_pe, 1) if j == 3 else None)
                    t_tr = p.tok(s_pe)
                    war = [x1_st[gb]] if x1_st[gb] else []
                    p.emit("vector", lambda e, gb=gb: e.tensor_copy(
                        stage[:, gb * 512:(gb + 1) * 512],
                        psS[:, gb * 512:(gb + 1) * 512]),
                        waits=[(s_pe, t_tr)] + war)
                    t_cp = p.tok(s_dve)
                    dst = out_o[g * 512:(g + 1) * 512,
                                dt * 128:(dt + 1) * 128].rearrange(
                        "(j q) c -> q j c", q=128)
                    srcv = stage[:, gb * 512:(gb + 1) * 512].rearrange(
                        "q (j c) -> q j c", j=4)
                    x1_st[gb] = st_x1.issue(
                        "gpsimd", lambda e, dst=dst, srcv=srcv: e.dma_start(
                            out=dst, in_=srcv),
                        waits=[(s_dve, t_cp)], slot=gb)
                x1t_war[buf] = p.tok(s_pe)
            else:
                x1_st[buf] = st_x1.issue(
                    "gpsimd", lambda e, dt=dt, buf=buf: e.dma_start(
                        out=x1_o[dt * 128:(dt + 1) * 128, :],
                        in_=x1_t[:, buf * T:(buf + 1) * T]),
                    waits=[(s_dve, t_evac)], slot=buf)

        if not final:
            rows = dict(s=srow, sq=sqrow, rstd=rstdrow,
                        mu2=tmp[0:1, :])
            emit_ln_streamed(p, x1_o, st_x, io_t, tmp, scr, onecol_s,
                             onec, psS, psD, rows, xl_t, xln2_o, last_o,
                             st_l, x1_st, s, pre_sum=True)

        fin = (st_o.total_tokens() + st_x1.total_tokens()
               + st_l.total_tokens())
        p.emit("sync", lambda e: e.nop(), waits=fin)

        block = p.es.enter_context(nc.Block())
        p.build(block)
    return nc


def emit_ln_streamed(p, x1_o, st_x, io_t, tmp, scr, onecol_s, onec, psS,
                     psB, rows, xl_t, xln2_o, last_o, st_l, x1_st, s,
                     pre_sum=False):
    s_pe, s_dve, s_act = s["pe"], s["dve"], s["act"]
    srow, sqrow, rstdrow = rows["s"], rows["sq"], rows["rstd"]
    x1_ready = [t for t in x1_st if t]
    if pre_sum:
        t_pe0 = p.tok(s_pe)
        p.emit("vector", lambda e: e.tensor_copy(srow[:, :], psS[0:1, :]),
               waits=[(s_pe, t_pe0)])
        t_dve = p.tok(s_dve)
        p.emit("tensor", lambda e: e.nop(), waits=[(s_dve, t_dve)])
    for which in range((1 if pre_sum else 0), 2):
        for pt in range(NPT):
            buf = pt % 2
            war = [(s_act, p.tok(s_act))]
            tok = st_x.issue("sync", lambda e, pt=pt, buf=buf: e.dma_start(
                out=io_t[:, buf * T:(buf + 1) * T],
                in_=x1_o[pt * 128:(pt + 1) * 128, :]),
                waits=x1_ready + war, slot=buf)
            war2 = [(s_pe, p.tok(s_pe))]
            if which == 0:
                p.emit("scalar", lambda e, buf=buf: e.copy(
                    scr[:, buf * T:(buf + 1) * T],
                    io_t[:, buf * T:(buf + 1) * T]), waits=[tok] + war2)
            else:
                p.emit("scalar", lambda e, buf=buf: e.activation(
                    scr[:, buf * T:(buf + 1) * T],
                    io_t[:, buf * T:(buf + 1) * T], AF.Square),
                    waits=[tok] + war2)
            t_act = p.tok(s_act)
            for tk in range(T // 512):
                p.emit("tensor", lambda e, buf=buf, tk=tk, pt=pt: e.matmul(
                    psS[0:1, tk * 512:(tk + 1) * 512], onecol_s[:, 0:1],
                    scr[:, buf * T + tk * 512: buf * T + (tk + 1) * 512],
                    start=(pt == 0), stop=(pt == NPT - 1)),
                    waits=[(s_act, t_act)],
                    inc=(s_pe, 1) if tk == 3 else None)
        t_pe = p.tok(s_pe)
        dstrow = srow if which == 0 else sqrow
        p.emit("vector", lambda e, dstrow=dstrow: e.tensor_copy(
            dstrow[:, :], psS[0:1, :]), waits=[(s_pe, t_pe)])
        if which == 0:
            t_dve = p.tok(s_dve)
            p.emit("tensor", lambda e: e.nop(), waits=[(s_dve, t_dve)])
    mu2 = rows["mu2"]
    p.emit("vector", lambda e: e.tensor_scalar(
        srow[:, :], srow[:, :], 1.0 / D, None, ALU.mult))
    p.emit("vector", lambda e: e.tensor_tensor(
        mu2, srow[:, :], srow[:, :], ALU.mult))
    p.emit("vector", lambda e: e.scalar_tensor_tensor(
        sqrow[:, :], sqrow[:, :], 1.0 / D, mu2,
        ALU.mult, ALU.subtract))
    t = p.tok(s_dve)
    p.emit("scalar", lambda e: e.activation(
        sqrow[:, :], sqrow[:, :], AF.Ln, bias=EPS, scale=1.0),
        waits=[(s_dve, t)] + list(s.get("cgate", [])))
    p.emit("scalar", lambda e: e.activation(
        rstdrow[:, :], sqrow[:, :], AF.Exp, bias=0.0, scale=-0.5))
    t_rows = p.tok(s_act)
    wv = [(s_dve, p.tok(s_dve))]
    for tk in range(T // 512):
        p.emit("tensor", lambda e, tk=tk: e.matmul(
            psS[:, tk * 512:(tk + 1) * 512], onec[:, :],
            srow[:, tk * 512:(tk + 1) * 512], start=True, stop=True),
            waits=[(s_act, t_rows)] + wv)
        p.emit("tensor", lambda e, tk=tk: e.matmul(
            psB[:, tk * 512:(tk + 1) * 512], onec[:, :],
            rstdrow[:, tk * 512:(tk + 1) * 512], start=True, stop=True),
            inc=(s_pe, 1) if tk == 3 else None)
    t_bc = p.tok(s_pe)
    prev_q = 0
    st_tok = [None, None]
    for pt in range(NPT):
        buf = pt % 2
        tok = st_x.issue("sync", lambda e, pt=pt, buf=buf: e.dma_start(
            out=io_t[:, buf * T:(buf + 1) * T],
            in_=x1_o[pt * 128:(pt + 1) * 128, :]),
            waits=[(s_dve, prev_q)], slot=buf)
        p.emit("vector", lambda e, buf=buf: e.tensor_tensor(
            tmp[:, :], io_t[:, buf * T:(buf + 1) * T], psS[:, :],
            ALU.subtract), waits=[tok, (s_pe, t_bc)])
        war = [st_tok[buf]] if st_tok[buf] else []
        p.emit("vector", lambda e, buf=buf: e.tensor_tensor(
            xl_t[:, buf * T:(buf + 1) * T], tmp[:, :], psB[:, :], ALU.mult),
            waits=war)
        prev_q = p.tok(s_dve)
        t_dve = p.tok(s_dve)
        st_l.issue("gpsimd", lambda e, pt=pt, buf=buf: e.dma_start(
            out=xln2_o[pt * 128:(pt + 1) * 128, :],
            in_=xl_t[:, buf * T:(buf + 1) * T]),
            waits=[(s_dve, t_dve)], slot=buf)
        st_tok[buf] = st_l.issue("gpsimd", lambda e, pt=pt, buf=buf:
                                 e.dma_start(
            out=last_o[0:1, pt * 128:(pt + 1) * 128],
            in_=xl_t[:, buf * T + T - 1: buf * T + T]),
            waits=[(s_dve, t_dve)], slot=buf)


def build_L4(nsc=NSC, cut=99):
    nc = bass.Bass("TRN2", target_bir_lowering=False)
    xln2_in = dram_in(nc, "xln2", [D, T], BF16)
    last_in = dram_in(nc, "lastprev", [1, D], BF16)
    w_r = dram_in(nc, "w_r", [D, D])
    w_k = dram_in(nc, "w_k", [D, D])
    w_v = dram_in(nc, "w_v", [D, D])
    w_g = dram_in(nc, "w_g", [D, D])
    tmix = dram_in(nc, "tmix", [4 * D, 1])
    tdec = dram_in(nc, "tdec", [1, D])
    ones1 = dram_in(nc, "ones1", [1, 128])
    iden_b = dram_in(nc, "iden_b", [128, 128], BF16)
    maskc_in = dram_in(nc, "maskc", [128, 16 * 128], BF16)
    parity_in = dram_in(nc, "parity", [128, D], BF16)
    iotas = dram_in(nc, "iotas", [1, 6 * 128])
    core0 = dram_in(nc, "core0", [1, 1])

    attn_o = dram_out(nc, "attnout", [T, D], BF16)
    g_o = dram_out(nc, "g_out", [T, D], BF16)
    r_o = dram_out(nc, "r_out", [D, T], BF16)
    st_o_d = dram_out(nc, "st_out", [DH, D])

    wscr = nc.dram_tensor("wscr", [D, D], BF16, kind="Internal").ap()

    p = Prog(nc)
    with p.es:
        sb = p.es.enter_context
        mix = sb(nc.sbuf_tensor("mix", [128, NPT * T], BF16))
        xln_t = sb(nc.sbuf_tensor("xln_t", [128, 2 * T], BF16))
        wT = sb(nc.sbuf_tensor("wT", [128, NPT * D], BF16))
        r_fm = sb(nc.sbuf_tensor("r_fm", [128, NPT * T], BF16))
        k_tm = sb(nc.sbuf_tensor("k_tm", [128, NTT * D], BF16))
        v_tm = sb(nc.sbuf_tensor("v_tm", [128, NTT * D], BF16))
        g_t = sb(nc.sbuf_tensor("g_t", [128, 2 * D], BF16))
        sc_col = sb(nc.sbuf_tensor("sc_col", [128, 4], F32))
        attn_t = sb(nc.sbuf_tensor("attn_t", [128, 2 * D], BF16))
        prevcol = sb(nc.sbuf_tensor("prevcol", [128, NPT], BF16))
        tmix_s = sb(nc.sbuf_tensor("tmix_s", [128, 4 * NPT], F32))
        onec = sb(nc.sbuf_tensor("onec", [1, 128], F32))
        idb = sb(nc.sbuf_tensor("idb", [128, 128], BF16))
        maskc = sb(nc.sbuf_tensor("maskc_sb", [128, 16 * 128], BF16))
        iot = sb(nc.sbuf_tensor("iot", [1, 6 * 128], F32))
        lnwrow = sb(nc.sbuf_tensor("lnwrow", [1, D], F32))
        c0v = sb(nc.sbuf_tensor("c0v", [1, 1], F32))
        c0b = sb(nc.sbuf_tensor("c0b", [128, 1], F32))
        P2 = sb(nc.sbuf_tensor("P2", [128, D], BF16))
        QT1 = sb(nc.sbuf_tensor("QT1", [128, D], BF16))
        QA = sb(nc.sbuf_tensor("QA", [128, D], BF16))
        QKH = sb(nc.sbuf_tensor("QKH", [128, D], BF16))
        W1f = sb(nc.sbuf_tensor("W1f", [128, D], BF16))
        S2 = sb(nc.sbuf_tensor("S2", [128, D], F32))
        S2t = sb(nc.sbuf_tensor("S2t", [128, D], BF16))
        T1k = sb(nc.sbuf_tensor("T1k", [128, D], BF16))
        T2k = sb(nc.sbuf_tensor("T2k", [128, D], BF16))
        KHk = sb(nc.sbuf_tensor("KHk", [128, D], BF16))
        Amk = sb(nc.sbuf_tensor("Amk", [128, 2 * D], BF16))
        Amk2 = sb(nc.sbuf_tensor("Amk2", [128, D], BF16))
        vfm2 = sb(nc.sbuf_tensor("vfm2", [128, 2 * D], BF16))
        parity = sb(nc.sbuf_tensor("parity_sb", [128, D], BF16))
        psA_ = sb(nc.psum_tensor("psA_", [128, D], F32))
        psB_ = sb(nc.psum_tensor("psB_", [128, D], F32))
        psC_ = sb(nc.psum_tensor("psC_", [128, D], F32))
        psT_ = sb(nc.psum_tensor("psT_", [128, D], BF16))

        p.init_chain_sems()
        s_pe = p.sem("s_pe")
        s_dve = p.chain["vector"]
        s_act = p.chain["scalar"]
        st_c = Stream(p, "stC", 4)
        st_x = Stream(p, "stX", 2)
        st_wc = Stream(p, "stWC", 1)
        st_w = Stream(p, "stW", 4)
        st_ao = Stream(p, "stAO", 2)
        st_go = Stream(p, "stGO", 2)
        st_ro = Stream(p, "stRO", 2)
        st_o = Stream(p, "stO", 1)
        s = dict(pe=s_pe, dve=s_dve, act=s_act)

        setup_consts(p, nc, st_c)
        st_c.issue("sync", lambda e: e.dma_start(out=onec[:, :], in_=ones1))
        st_c.issue("sync", lambda e: e.dma_start(out=idb[:, :], in_=iden_b))
        st_c.issue("sync", lambda e: e.dma_start(out=maskc[:, :],
                                                 in_=maskc_in))
        st_c.issue("sync", lambda e: e.dma_start(out=parity[:, :],
                                                 in_=parity_in))
        st_c.issue("sync", lambda e: e.dma_start(out=c0v[:, :], in_=core0))
        for i in range(4 * NPT):
            st_c.issue("sync", lambda e, i=i: e.dma_start(
                out=tmix_s[:, i:i + 1], in_=tmix[i * 128:(i + 1) * 128, :]))
        for pt in range(NPT):
            st_c.issue("sync", lambda e, pt=pt: e.dma_start(
                out=prevcol[:, pt:pt + 1],
                in_=last_in[0:1, pt * 128:(pt + 1) * 128]))
        cgate = st_c.total_tokens()
        s["cgate"] = cgate
        for en in ("tensor", "vector", "scalar"):
            p.emit(en, lambda e: e.nop(), waits=cgate,
                   dma=(en not in p.CHAINED))

        p.emit("scalar", lambda e: e.activation(lnwrow[:, :], lnwrow[:, :],
                                                AF.Exp))
        t_exp = p.tok(s_act)
        p.emit("vector", lambda e: e.tensor_scalar(lnwrow[:, :], lnwrow[:, :],
                                                   -1.0, None, ALU.mult),
               waits=[(s_act, t_exp)])
        t_lnw = p.tok(s_dve)
        pats = [(P2, 0), (QT1, 1), (QA, 2), (QKH, 3), (W1f, 4)]
        for (dst, idx) in pats:
            wv = [(s_act, p.tok(s_act))]
            for half in range(2):
                p.emit("tensor", lambda e, idx=idx, half=half: e.matmul(
                    psA_[:, half * 512:(half + 1) * 512],
                    iot[:, idx * 128:(idx + 1) * 128],
                    lnwrow[:, half * 512:(half + 1) * 512],
                    start=True, stop=True),
                    waits=[(s_dve, t_lnw)] + wv,
                    inc=(s_pe, 1) if half == 1 else None)
            t_pe = p.tok(s_pe)
            p.emit("scalar", lambda e, dst=dst: e.activation(
                dst[:, :], psA_[:, :], AF.Exp), waits=[(s_pe, t_pe)])
        t_pats = p.tok(s_act)
        p.emit("vector", lambda e: e.tensor_tensor(
            W1f[:, :], W1f[:, :], parity[:, :], ALU.mult),
            waits=[(s_act, t_pats)])
        p.emit("vector", lambda e: e.memset(vfm2[:, :], 0.0))
        p.emit("vector", lambda e: e.memset(Amk2[:, :], 0.0))
        t_pats_v = p.tok(s_dve)
        p.emit("tensor", lambda e: e.matmul(
            psB_[:, 0:1], onec[:, :], c0v[:, :], start=True, stop=True),
            inc=(s_pe, 1))
        t_pe = p.tok(s_pe)
        p.emit("vector", lambda e: e.tensor_copy(c0b[:, :], psB_[:, 0:1]),
               waits=[(s_pe, t_pe)])
        t_c0 = p.tok(s_dve)

        prev_wtoks = []

        def prep_weight(w_in, widx):
            nonlocal_toks = prev_wtoks[:]
            tok_cast = st_wc.issue("gpsimd", lambda e, w_in=w_in: e.dma_start(
                out=wscr, in_=w_in), waits=nonlocal_toks)
            toks = []
            for m in range(NPT):
                toks.append(st_w.issue("scalar", lambda e, m=m:
                                       e.dma_start_transpose(
                    out=wT[:, m * D:(m + 1) * D],
                    in_=wscr[:, m * 128:(m + 1) * 128]),
                    waits=[tok_cast, (s_pe, p.tok(s_pe))]))
            prev_wtoks[:] = toks
            return toks

        def build_mix(widx, t_gate_mix):
            t_pe_war = p.tok(s_pe)
            for pt in range(NPT):
                buf = pt % 2
                tok = st_x.issue("sync", lambda e, pt=pt, buf=buf:
                                 e.dma_start(
                    out=xln_t[:, buf * T:(buf + 1) * T],
                    in_=xln2_in[pt * 128:(pt + 1) * 128, :]),
                    waits=[(s_dve, t_gate_mix[buf])], slot=buf)
                p.emit("vector", lambda e, pt=pt, buf=buf: e.tensor_tensor(
                    mix[:, pt * T + 1: pt * T + T],
                    xln_t[:, buf * T: buf * T + T - 1],
                    xln_t[:, buf * T + 1: buf * T + T], ALU.subtract),
                    waits=[tok, (s_pe, t_pe_war)])
                p.emit("vector", lambda e, pt=pt, buf=buf: e.tensor_tensor(
                    mix[:, pt * T: pt * T + 1], prevcol[:, pt:pt + 1],
                    xln_t[:, buf * T: buf * T + 1], ALU.subtract))
                p.emit("vector", lambda e, pt=pt, buf=buf: e.tensor_tensor(
                    sc_col[:, 0:1], xln_t[:, buf * T + 1: buf * T + 2],
                    prevcol[:, pt:pt + 1], ALU.subtract),
                    waits=[(s_dve, t_c0)])
                p.emit("vector", lambda e, pt=pt: e.scalar_tensor_tensor(
                    mix[:, pt * T: pt * T + 1], sc_col[:, 0:1], c0b[:, 0:1],
                    mix[:, pt * T: pt * T + 1], ALU.mult, ALU.add))
                p.emit("vector", lambda e, pt=pt, buf=buf, widx=widx:
                       e.scalar_tensor_tensor(
                    fm(mix, pt), fm(mix, pt),
                    tmix_s[:, widx * NPT + pt: widx * NPT + pt + 1],
                    xln_t[:, buf * T:(buf + 1) * T], ALU.mult, ALU.add))
                t_gate_mix[buf] = p.tok(s_dve)
            return p.tok(s_dve)

        t_gm = [0, 0]
        tok_w = prep_weight(w_g, 3)
        t_mix = build_mix(3, t_gm)
        g_st = [None, None]
        t_evK = 0
        for tt in range(NTT):
            buf = tt % 2
            for m in range(NPT):
                for half in range(2):
                    p.emit("tensor", lambda e, m=m, half=half, tt=tt:
                           e.matmul(
                        psC_[:, half * 512:(half + 1) * 512],
                        fm(mix, m)[:, tt * 128:(tt + 1) * 128],
                        wT[:, m * D + half * 512: m * D + (half + 1) * 512],
                        start=(m == 0), stop=(m == NPT - 1)),
                        waits=tok_w + [(s_dve, t_mix), (s_act, t_evK)],
                        inc=(s_pe, 1) if (m, half) == (NPT - 1, 1) else None)
            t_mm = p.tok(s_pe)
            war = [g_st[buf]] if g_st[buf] else []
            p.emit("scalar", lambda e, buf=buf: e.activation(
                g_t[:, buf * D:(buf + 1) * D], psC_[:, :], AF.Sigmoid),
                waits=[(s_pe, t_mm)] + war)
            t_evK = p.tok(s_act)
            g_st[buf] = st_go.issue("gpsimd", lambda e, tt=tt, buf=buf:
                                    e.dma_start(
                out=g_o[tt * 128:(tt + 1) * 128, :],
                in_=g_t[:, buf * D:(buf + 1) * D]),
                waits=[(s_act, t_evK)], slot=buf)

        tok_w = prep_weight(w_r, 0)
        t_mix = build_mix(0, t_gm)
        r_st = [None, None]
        t_evR = 0
        for ct in range(NPT):
            buf = ct % 2
            for m in range(NPT):
                for tk in range(T // 512):
                    ps = psA_ if tk < 2 else psB_
                    p.emit("tensor", lambda e, m=m, tk=tk, ct=ct, ps=ps:
                           e.matmul(
                        ps[:, (tk % 2) * 512:((tk % 2) + 1) * 512],
                        wT[:, m * D + ct * 128: m * D + (ct + 1) * 128],
                        fm(mix, m)[:, tk * 512:(tk + 1) * 512],
                        start=(m == 0), stop=(m == NPT - 1)),
                        waits=tok_w + [(s_dve, t_mix), (s_dve, t_evR)],
                        inc=(s_pe, 1) if (m, tk) == (NPT - 1, 3) else None)
            t_mm = p.tok(s_pe)
            p.emit("vector", lambda e, ct=ct: e.tensor_copy(
                fm(r_fm, ct)[:, 0:1024], psA_[:, :]), waits=[(s_pe, t_mm)])
            p.emit("vector", lambda e, ct=ct: e.tensor_copy(
                fm(r_fm, ct)[:, 1024:2048], psB_[:, :]))
            t_evR = p.tok(s_dve)
            r_st[buf] = st_ro.issue("gpsimd", lambda e, ct=ct: e.dma_start(
                out=r_o[ct * 128:(ct + 1) * 128, :], in_=fm(r_fm, ct)),
                waits=[(s_dve, t_evR)], slot=buf)

        def proj_tm(w_in, widx, dst):
            tok_w = prep_weight(w_in, widx)
            t_mix = build_mix(widx, t_gm)
            t_ev = 0
            for tt in range(NTT):
                for m in range(NPT):
                    for half in range(2):
                        p.emit("tensor", lambda e, m=m, half=half, tt=tt:
                               e.matmul(
                            psC_[:, half * 512:(half + 1) * 512],
                            fm(mix, m)[:, tt * 128:(tt + 1) * 128],
                            wT[:, m * D + half * 512:
                               m * D + (half + 1) * 512],
                            start=(m == 0), stop=(m == NPT - 1)),
                            waits=tok_w + [(s_dve, t_mix), (s_dve, t_ev)],
                            inc=(s_pe, 1) if (m, half) == (NPT - 1, 1)
                            else None)
                t_mm = p.tok(s_pe)
                p.emit("vector", lambda e, tt=tt, dst=dst: e.tensor_copy(
                    dst[:, tt * D:(tt + 1) * D], psC_[:, :]),
                    waits=[(s_pe, t_mm)])
                t_ev = p.tok(s_dve)
            return t_ev

        t_k = proj_tm(w_k, 1, k_tm)
        t_v = proj_tm(w_v, 2, v_tm)

        p.emit("vector", lambda e: e.memset(S2[:, :], 0.0))
        p.emit("vector", lambda e: e.memset(S2t[:, :], 0.0))
        t_scan_pre = p.tok(s_dve)
        ao_st = [None, None]
        dbg_st = []
        for sc in range(nsc):
            buf = sc % 2
            Ksc = k_tm[:, sc * D:(sc + 1) * D]
            Vsc = v_tm[:, sc * D:(sc + 1) * D]
            p.emit("vector", lambda e, Ksc=Ksc: e.tensor_tensor(
                T1k[:, :], Ksc, QT1[:, :], ALU.mult),
                waits=[(s_dve, t_k), (s_dve, t_v), (s_act, t_pats),
                       (s_pe, p.tok(s_pe))] + dbg_st)
            p.emit("vector", lambda e, Ksc=Ksc: e.tensor_tensor(
                T2k[:, :], Ksc, QA[:, :], ALU.mult))
            p.emit("vector", lambda e, Ksc=Ksc: e.tensor_tensor(
                KHk[:, :], Ksc, QKH[:, :], ALU.mult))
            t_kvar = p.tok(s_dve)
            if cut < 2:
                continue
            wv = [(s_dve, p.tok(s_dve))]
            for grp in range(NPT):
                p.emit("tensor", lambda e, grp=grp, Vsc=Vsc: e.transpose(
                    psT_[:, grp * 128:(grp + 1) * 128],
                    Vsc[:, grp * 128:(grp + 1) * 128], idb[:, :]),
                    waits=wv + [(s_dve, t_v)],
                    inc=(s_pe, 1) if grp == NPT - 1 else None)
            t_tr = p.tok(s_pe)
            ve = (vfm2[0:64, :].rearrange("p (g two s) -> p g two s",
                                          two=2, s=128)[:, :, 0, :])
            vo = (vfm2[64:128, :].rearrange("p (g two s) -> p g two s",
                                            two=2, s=128)[:, :, 1, :])
            pe_ = psT_[0:64, :].rearrange("p (g s) -> p g s", s=128)
            po_ = psT_[64:128, :].rearrange("p (g s) -> p g s", s=128)
            p.emit("vector", lambda e, ve=ve, pe_=pe_: e.tensor_copy(ve, pe_),
                   waits=[(s_pe, t_tr)])
            p.emit("vector", lambda e, vo=vo, po_=po_: e.tensor_copy(vo, po_))
            t_vfm = p.tok(s_dve)
            if cut < 3:
                continue
            for gidx in range(2):
                war_ev = [(s_dve, p.tok(s_dve))]
                for hh in range(8):
                    h = gidx * 8 + hh
                    p.emit("tensor", lambda e, h=h, hh=hh, sc=sc: e.matmul(
                        psA_[:, hh * 128:(hh + 1) * 128],
                        vfm2[:, h * 128:(h + 1) * 128],
                        r_fm[:, (h // 2) * T + sc * 128:
                             (h // 2) * T + (sc + 1) * 128],
                        start=True, stop=True),
                        waits=[(s_dve, t_vfm)] + war_ev,
                        inc=(s_pe, 1) if hh == 7 else None)
                t_pe2 = p.tok(s_pe)
                p.emit("vector", lambda e, gidx=gidx: e.tensor_tensor(
                    Amk[:, gidx * D:(gidx + 1) * D], psA_[:, :],
                    maskc[:, gidx * D:(gidx + 1) * D], ALU.mult),
                    waits=[(s_pe, t_pe2)])
                a2d = (Amk2[0:64, gidx * 512:(gidx + 1) * 512]
                       .rearrange("p (g s) -> p g s", s=64))
                a2s = (psA_[0:64, :].rearrange("p (g two s) -> p g two s",
                                               two=2, s=64)[:, :, 1, :])
                p.emit("vector", lambda e, a2d=a2d, a2s=a2s:
                       e.tensor_copy(a2d, a2s))
            t_am = p.tok(s_dve)
            if cut < 4:
                continue
            for h in range(H):
                hc = h * 128
                hi = h * DH
                p.emit("tensor", lambda e, h=h, hi=hi, sc=sc: e.matmul(
                    psA_[0:64, hi:hi + DH],
                    r_fm[:, (h // 2) * T + sc * 128:
                         (h // 2) * T + sc * 128 + 64],
                    S2t[:, hi:hi + DH],
                    start=True, stop=True, skip_group_check=True),
                    waits=[(s_dve, t_am)])
                p.emit("tensor", lambda e, h=h, hi=hi: e.matmul(
                    psA_[64:128, hi:hi + DH],
                    Amk2[:, h * 64:(h + 1) * 64], T2k[:, hi:hi + DH],
                    start=True, stop=True,
                    tile_position=(0, 64), skip_group_check=True))
                p.emit("tensor", lambda e, h=h, hc=hc, hi=hi: e.matmul(
                    psB_[0:64, hi:hi + DH],
                    Amk[:, hc:hc + 64], T1k[:, hi:hi + DH],
                    start=True, stop=True, skip_group_check=True))
                p.emit("tensor", lambda e, h=h, hc=hc, hi=hi: e.matmul(
                    psB_[64:128, hi:hi + DH],
                    Amk[:, hc + 64:hc + 128], T1k[:, hi:hi + DH],
                    start=True, stop=True, tile_position=(0, 64),
                    skip_group_check=True),
                    inc=(s_pe, 1) if h == H - 1 else None)
            t_o1 = p.tok(s_pe)
            dbg_toks = []
            if cut < 5:
                continue
            war = [ao_st[buf]] if ao_st[buf] else []
            p.emit("vector", lambda e, buf=buf: e.tensor_copy(
                attn_t[:, buf * D:(buf + 1) * D], psA_[:, :]),
                waits=[(s_pe, t_o1)] + war)
            p.emit("vector", lambda e, buf=buf: e.tensor_tensor(
                attn_t[:, buf * D:(buf + 1) * D], psB_[:, :],
                attn_t[:, buf * D:(buf + 1) * D], ALU.add))
            p.emit("vector", lambda e, buf=buf: e.tensor_tensor(
                attn_t[:, buf * D:(buf + 1) * D],
                attn_t[:, buf * D:(buf + 1) * D], P2[:, :], ALU.mult))
            t_ao = p.tok(s_dve)
            ao_st[buf] = st_ao.issue("gpsimd", lambda e, sc=sc, buf=buf:
                                     e.dma_start(
                out=attn_o[sc * 128:(sc + 1) * 128, :],
                in_=attn_t[:, buf * D:(buf + 1) * D]),
                waits=[(s_dve, t_ao)], slot=buf)
            if cut < 6:
                continue
            for h in range(H):
                hi = h * DH
                p.emit("tensor", lambda e, h=h, hi=hi, Vsc=Vsc: e.matmul(
                    psC_[0:64, hi:hi + DH], Vsc[:, hi:hi + DH],
                    KHk[:, hi:hi + DH], start=True, stop=True),
                    waits=[(s_dve, t_kvar)],
                    inc=(s_pe, 1) if h == H - 1 else None)
            t_up = p.tok(s_pe)
            if cut < 7:
                continue
            p.emit("vector", lambda e: e.tensor_copy(S2[0:64, :],
                                                     psC_[0:64, :]),
                   waits=[(s_pe, t_up)] + dbg_toks)
            p.emit("vector", lambda e: e.tensor_copy(S2[64:128, :],
                                                     psC_[0:64, :]))
            p.emit("vector", lambda e: e.tensor_tensor(
                S2t[:, :], S2[:, :], W1f[:, :], ALU.mult))
        t_fin = p.tok(s_dve)
        if nsc > 0:
            st_o.issue("gpsimd", lambda e: e.dma_start(
                out=st_o_d, in_=S2[0:64, :]), waits=[(s_dve, t_fin)])

        fin = (st_o.total_tokens() + st_ao.total_tokens()
               + st_go.total_tokens() + st_ro.total_tokens())
        p.emit("sync", lambda e: e.nop(), waits=fin)

        block = p.es.enter_context(nc.Block())
        p.build(block)
    return nc




TH = 17 * 128
NSC5 = 17
THX = TH + 1


def fmh(x_sb, pt, c0=0, n=TH):
    return x_sb[:, pt * TH + c0: pt * TH + c0 + n]


def build_L45():
    nc = bass.Bass("TRN2", target_bir_lowering=False)
    xh_in = dram_in(nc, "xln2h", [D, THX], BF16)
    x1_in = dram_in(nc, "x1", [D, T])
    w_r = dram_in(nc, "w_r", [D, D])
    w_k = dram_in(nc, "w_k", [D, D])
    w_v = dram_in(nc, "w_v", [D, D])
    w_g = dram_in(nc, "w_g", [D, D])
    w_o = dram_in(nc, "w_o", [D, D])
    tmix = dram_in(nc, "tmix", [4 * D, 1])
    tdec = dram_in(nc, "tdec", [1, D])
    lng_in = dram_in(nc, "ln_g", [D, 1])
    lnb_in = dram_in(nc, "ln_b", [D, 1])
    ones1 = dram_in(nc, "ones1", [1, 128])
    onescol = dram_in(nc, "onescol", [128, 1], BF16)
    iden_b = dram_in(nc, "iden_b", [128, 128], BF16)
    iden_f = dram_in(nc, "iden_f", [128, 128])
    maskc_in = dram_in(nc, "maskc", [128, 16 * 128], BF16)
    parity_in = dram_in(nc, "parity", [128, D], BF16)
    iotas = dram_in(nc, "iotas", [1, 6 * 128])
    core0 = dram_in(nc, "core0", [1, 1])

    x2_o = dram_out(nc, "x2", [D, T])
    xln4_o = dram_out(nc, "xln4", [D, T])
    m3_o = dram_out(nc, "m3", [1, 1])

    wscr = nc.dram_tensor("wscr45", [D, D], BF16, kind="Internal").ap()
    wscr2 = nc.dram_tensor("wscr45b", [D, D], BF16, kind="Internal").ap()
    g_int = nc.dram_tensor("g_int", [T, D], BF16, kind="Internal").ap()

    p = Prog(nc)
    with p.es:
        sb = p.es.enter_context
        attn_res = sb(nc.sbuf_tensor("attn_res", [128, NTT * D], BF16))
        tmix_s = sb(nc.sbuf_tensor("tmix_s", [128, 4 * NPT], F32))
        sc_col = sb(nc.sbuf_tensor("sc_col", [128, 4], F32))
        onec = sb(nc.sbuf_tensor("onec", [1, 128], F32))
        onecol_s = sb(nc.sbuf_tensor("onecol_s", [128, 1], BF16))
        idb = sb(nc.sbuf_tensor("idb", [128, 128], BF16))
        lng = sb(nc.sbuf_tensor("lng", [128, NPT], F32))
        lnb = sb(nc.sbuf_tensor("lnb", [128, NPT], F32))
        c0v = sb(nc.sbuf_tensor("c0v", [1, 1], F32))
        c0b = sb(nc.sbuf_tensor("c0b", [128, 1], F32))
        psA_ = sb(nc.psum_tensor("psA_", [128, D], F32))
        psT_ = sb(nc.psum_tensor("psT_", [128, D], BF16))

        p.init_chain_sems()
        s_pe = p.sem("s_pe")
        s_dve = p.chain["vector"]
        s_act = p.chain["scalar"]
        st_c = Stream(p, "stC", 4)
        st_x = Stream(p, "stX", 2)
        st_wc = Stream(p, "stWC", 1)
        st_w = Stream(p, "stW", 4)
        st_go = Stream(p, "stGO", 2)
        st_gl = Stream(p, "stGL", 2)
        st_x1 = Stream(p, "stX1", 2)
        st_o = Stream(p, "stO", 2)
        st_m = Stream(p, "stM", 1)
        s_gps = p.chain["gpsimd"]
        s = dict(pe=s_pe, dve=s_dve, act=s_act)

        const_sb = setup_consts(p, nc, st_c)
        st_c.issue("sync", lambda e: e.dma_start(out=onec[:, :], in_=ones1))
        st_c.issue("sync", lambda e: e.dma_start(out=onecol_s[:, :],
                                                 in_=onescol))
        st_c.issue("sync", lambda e: e.dma_start(out=idb[:, :], in_=iden_b))
        st_c.issue("sync", lambda e: e.dma_start(out=c0v[:, :], in_=core0))
        for i in range(4 * NPT):
            st_c.issue("sync", lambda e, i=i: e.dma_start(
                out=tmix_s[:, i:i + 1], in_=tmix[i * 128:(i + 1) * 128, :]))
        for pt in range(NPT):
            st_c.issue("sync", lambda e, pt=pt: e.dma_start(
                out=lng[:, pt:pt + 1], in_=lng_in[pt * 128:(pt + 1) * 128, :]))
            st_c.issue("sync", lambda e, pt=pt: e.dma_start(
                out=lnb[:, pt:pt + 1], in_=lnb_in[pt * 128:(pt + 1) * 128, :]))
        cgate = st_c.total_tokens()
        s["cgate"] = cgate
        for en in ("tensor", "vector", "scalar", "gpsimd"):
            p.emit(en, lambda e: e.nop(), waits=cgate,
                   dma=(en not in p.CHAINED))

        p.emit("tensor", lambda e: e.matmul(
            psA_[:, 0:1], onec[:, :], c0v[:, :], start=True, stop=True),
            inc=(s_pe, 1))
        t_pe = p.tok(s_pe)
        p.emit("vector", lambda e: e.tensor_copy(c0b[:, :], psA_[:, 0:1]),
               waits=[(s_pe, t_pe)])
        t_c0 = p.tok(s_dve)

        esB = contextlib.ExitStack()
        sbB = esB.enter_context
        r_fm = sbB(nc.sbuf_tensor("r_fm", [128, NPT * T], BF16))
        k_tm = sbB(nc.sbuf_tensor("k_tm", [128, NSC5 * D], BF16))
        mix = sbB(nc.sbuf_tensor("mix", [128, NPT * TH], BF16))
        g_t = sbB(nc.sbuf_tensor("g_t", [128, 2 * D], BF16))
        psB_ = sbB(nc.psum_tensor("psB_", [128, D], F32))
        psC_ = sbB(nc.psum_tensor("psC_", [128, D], F32))

        esB1 = contextlib.ExitStack()
        sb1 = esB1.enter_context
        vmix = sb1(nc.sbuf_tensor("vmix", [128, NPT * TH], BF16))
        xln_t = sb1(nc.sbuf_tensor("xln_t", [128, THX], BF16))
        wT = sb1(nc.sbuf_tensor("wT", [128, 2 * NPT * D], BF16))

        wscrs = [wscr, wscr2]
        widx_ctr = [0]
        proj_done = [0, 0]

        def prep_weight(w_in):
            wb = widx_ctr[0] % 2
            widx_ctr[0] += 1
            tok_cast = st_wc.issue("gpsimd", lambda e, w_in=w_in, wb=wb:
                                   e.dma_start(
                out=wscrs[wb], in_=w_in))
            toks = []
            for m in range(NPT):
                toks.append(st_w.issue("scalar", lambda e, m=m, wb=wb:
                                       e.dma_start_transpose(
                    out=wT[:, wb * NPT * D + m * D: wb * NPT * D + (m + 1) * D],
                    in_=wscrs[wb][:, m * 128:(m + 1) * 128]),
                    waits=[tok_cast, (s_pe, proj_done[wb])]))
            return toks, wb

        t_mix_pt = {}

        def build_mix(widx, t_gate_mix, mb, t_pe_war):
            for pt in range(NPT):
                tok = st_x.issue("sync", lambda e, pt=pt: e.dma_start(
                    out=xln_t[:, :],
                    in_=xh_in[pt * 128:(pt + 1) * 128, :]),
                    waits=[(s_dve, t_gate_mix[0]), (s_gps, t_gm_g[0])],
                    slot=0)
                p.emit("gpsimd", lambda e, pt=pt, mb=mb: e.tensor_tensor(
                    fmh(mb, pt),
                    xln_t[:, 0:TH],
                    xln_t[:, 1:1 + TH],
                    ALU.subtract),
                    waits=[tok, (s_pe, t_pe_war),
                           (s_dve, t_mix_pt.get((id(mb), pt), 0))])
                t_sub = p.tok(s_gps)
                p.emit("vector", lambda e, pt=pt: e.tensor_tensor(
                    sc_col[:, 0:1], xln_t[:, 130:131],
                    xln_t[:, 128:129], ALU.subtract),
                    waits=[(s_dve, t_c0), tok])
                p.emit("vector", lambda e, pt=pt, mb=mb:
                       e.scalar_tensor_tensor(
                    fmh(mb, pt, 128, 1), sc_col[:, 0:1], c0b[:, 0:1],
                    fmh(mb, pt, 128, 1), ALU.mult, ALU.add),
                    waits=[(s_gps, t_sub)])
                p.emit("vector", lambda e, pt=pt, widx=widx, mb=mb:
                       e.scalar_tensor_tensor(
                    fmh(mb, pt), fmh(mb, pt),
                    tmix_s[:, widx * NPT + pt: widx * NPT + pt + 1],
                    xln_t[:, 1:1 + TH],
                    ALU.mult, ALU.add))
                t_gate_mix[0] = p.tok(s_dve)
                t_gm_g[0] = p.tok(s_gps)
                t_mix_pt[(id(mb), pt)] = p.tok(s_dve)
            return p.tok(s_dve)

        t_gm = [0, 0]
        t_gm_g = [0, 0]
        tok_w, wb = prep_weight(w_g)
        t_mix = build_mix(3, t_gm, mix, 0)
        gmix = mix
        g_st = [None, None]
        t_evK = [0, 0]
        for tt in range(1, NSC5):
            buf = tt % 2
            ps = psC_ if buf == 0 else psB_
            for m in range(NPT):
                for half in range(2):
                    p.emit("tensor", lambda e, m=m, half=half, tt=tt, ps=ps,
                           wb=wb: e.matmul(
                        ps[:, half * 512:(half + 1) * 512],
                        fmh(gmix, m)[:, tt * 128:(tt + 1) * 128],
                        wT[:, wb * NPT * D + m * D + half * 512:
                           wb * NPT * D + m * D + (half + 1) * 512],
                        start=(m == 0), stop=(m == NPT - 1)),
                        waits=[tok_w[m], (s_dve, t_mix),
                               (s_act, t_evK[buf])],
                        inc=(s_pe, 1) if (m, half) == (NPT - 1, 1) else None)
            t_mm = p.tok(s_pe)
            war = [g_st[buf]] if g_st[buf] else []
            p.emit("scalar", lambda e, buf=buf, ps=ps: e.activation(
                g_t[:, buf * D:(buf + 1) * D], ps[:, :], AF.Sigmoid),
                waits=[(s_pe, t_mm)] + war)
            t_evK[buf] = p.tok(s_act)
            g_st[buf] = st_go.issue("gpsimd", lambda e, tt=tt, buf=buf:
                                    e.dma_start(
                out=g_int[(tt - 1) * 128:tt * 128, :],
                in_=g_t[:, buf * D:(buf + 1) * D]),
                waits=[(s_act, t_evK[buf])], slot=buf)
        proj_done[wb] = p.tok(s_pe)
        t_projg = p.tok(s_pe)

        tok_w, wb = prep_weight(w_r)
        t_mix = build_mix(0, t_gm, vmix, 0)
        t_evR = 0
        for ct in range(NPT):
            for m in range(NPT):
                for tk in range(4):
                    ps = psA_ if tk < 2 else psB_
                    p.emit("tensor", lambda e, m=m, tk=tk, ct=ct, ps=ps,
                           wb=wb: e.matmul(
                        ps[:, (tk % 2) * 512:((tk % 2) + 1) * 512],
                        wT[:, wb * NPT * D + m * D + ct * 128:
                           wb * NPT * D + m * D + (ct + 1) * 128],
                        fmh(vmix, m)[:, 128 + tk * 512:128 + (tk + 1) * 512],
                        start=(m == 0), stop=(m == NPT - 1)),
                        waits=[tok_w[m], (s_dve, t_mix), (s_dve, t_evR)],
                        inc=(s_pe, 1) if (m, tk) == (NPT - 1, 3)
                        else None)
            t_mm = p.tok(s_pe)
            p.emit("vector", lambda e, ct=ct: e.tensor_copy(
                fm(r_fm, ct)[:, 0:1024], psA_[:, :]), waits=[(s_pe, t_mm)])
            p.emit("vector", lambda e, ct=ct: e.tensor_copy(
                fm(r_fm, ct)[:, 1024:2048], psB_[:, :]))
            t_evR = p.tok(s_dve)
        proj_done[wb] = p.tok(s_pe)

        def proj_tm(w_in, widx, dst, mb, t_pe_war):
            tok_w, wb = prep_weight(w_in)
            t_mix = build_mix(widx, t_gm, mb, t_pe_war)
            t_ev = [0, 0]
            for tt in range(NSC5):
                buf = tt % 2
                ps = psC_ if buf == 0 else psB_
                for m in range(NPT):
                    for half in range(2):
                        p.emit("tensor", lambda e, m=m, half=half, tt=tt,
                               ps=ps, wb=wb, mb=mb: e.matmul(
                            ps[:, half * 512:(half + 1) * 512],
                            fmh(mb, m)[:, tt * 128:(tt + 1) * 128],
                            wT[:, wb * NPT * D + m * D + half * 512:
                               wb * NPT * D + m * D + (half + 1) * 512],
                            start=(m == 0), stop=(m == NPT - 1)),
                            waits=[tok_w[m], (s_dve, t_mix),
                                   (s_dve, t_ev[buf])],
                            inc=(s_pe, 1) if (m, half) == (NPT - 1, 1)
                            else None)
                t_mm = p.tok(s_pe)
                p.emit("vector", lambda e, tt=tt, dst=dst, ps=ps:
                       e.tensor_copy(
                    dst[:, tt * D:(tt + 1) * D], ps[:, :]),
                    waits=[(s_pe, t_mm)])
                t_ev[buf] = p.tok(s_dve)
            proj_done[wb] = p.tok(s_pe)
            return p.tok(s_dve)

        t_projr = p.tok(s_pe)
        t_k = proj_tm(w_k, 1, k_tm, mix, t_projg)
        v_tm = mix
        t_v = proj_tm(w_v, 2, v_tm, vmix, t_projr)

        bar = ([(s_pe, p.tok(s_pe)), (s_dve, p.tok(s_dve)),
                (s_act, p.tok(s_act)), (s_gps, p.tok(s_gps))]
               + st_x.total_tokens()
               + st_w.total_tokens() + st_wc.total_tokens())
        for en in ("tensor", "vector", "scalar", "gpsimd"):
            p.emit(en, lambda e: e.nop(), waits=bar,
                   dma=(en not in p.CHAINED))
        esB1.close()

        esB2 = contextlib.ExitStack()
        sb2 = esB2.enter_context
        maskc = sb2(nc.sbuf_tensor("maskc_sb", [128, 16 * 128], BF16))
        parity = sb2(nc.sbuf_tensor("parity_sb", [128, D], BF16))
        iot = sb2(nc.sbuf_tensor("iot", [1, 6 * 128], F32))
        lnwrow = sb2(nc.sbuf_tensor("lnwrow", [1, D], F32))
        P2 = sb2(nc.sbuf_tensor("P2", [128, D], BF16))
        QT1 = sb2(nc.sbuf_tensor("QT1", [128, D], BF16))
        QA = sb2(nc.sbuf_tensor("QA", [128, D], BF16))
        QKH = sb2(nc.sbuf_tensor("QKH", [128, D], BF16))
        W1f = sb2(nc.sbuf_tensor("W1f", [128, D], BF16))
        S2 = sb2(nc.sbuf_tensor("S2", [128, D], F32))
        S2t = sb2(nc.sbuf_tensor("S2t", [128, D], BF16))
        T1k = sb2(nc.sbuf_tensor("T1k", [128, 2 * D], BF16))
        T2k = sb2(nc.sbuf_tensor("T2k", [128, 2 * D], BF16))
        KHk = sb2(nc.sbuf_tensor("KHk", [128, 2 * D], BF16))
        Amk = sb2(nc.sbuf_tensor("Amk", [128, 4 * D], BF16))
        Amk2 = sb2(nc.sbuf_tensor("Amk2", [128, 2 * D], BF16))
        vfm2 = sb2(nc.sbuf_tensor("vfm2", [128, 2 * D], BF16))

        st_c.issue("sync", lambda e: e.dma_start(out=maskc[:, :],
                                                 in_=maskc_in), waits=bar)
        st_c.issue("sync", lambda e: e.dma_start(out=parity[:, :],
                                                 in_=parity_in), waits=bar)
        st_c.issue("sync", lambda e: e.dma_start(out=iot[:, :], in_=iotas),
                   waits=bar)
        st_c.issue("sync", lambda e: e.dma_start(out=lnwrow[:, :], in_=tdec),
                   waits=bar)
        cgate2 = st_c.total_tokens()
        p.emit("vector", lambda e: e.nop(), waits=cgate2)
        p.emit("scalar", lambda e: e.nop(), waits=cgate2)
        p.emit("scalar", lambda e: e.activation(lnwrow[:, :], lnwrow[:, :],
                                                AF.Exp))
        t_exp = p.tok(s_act)
        p.emit("vector", lambda e: e.tensor_scalar(lnwrow[:, :], lnwrow[:, :],
                                                   -1.0, None, ALU.mult),
               waits=[(s_act, t_exp)])
        t_lnw = p.tok(s_dve)

        pats = [(P2, 0), (QT1, 1), (QA, 2), (QKH, 3), (W1f, 4)]
        for (dst, idx) in pats:
            wv = [(s_act, p.tok(s_act))]
            for half in range(2):
                p.emit("tensor", lambda e, idx=idx, half=half: e.matmul(
                    psA_[:, half * 512:(half + 1) * 512],
                    iot[:, idx * 128:(idx + 1) * 128],
                    lnwrow[:, half * 512:(half + 1) * 512],
                    start=True, stop=True),
                    waits=[(s_dve, t_lnw)] + wv,
                    inc=(s_pe, 1) if half == 1 else None)
            t_pe = p.tok(s_pe)
            p.emit("scalar", lambda e, dst=dst: e.activation(
                dst[:, :], psA_[:, :], AF.Exp), waits=[(s_pe, t_pe)])
        t_pats = p.tok(s_act)
        p.emit("vector", lambda e: e.tensor_tensor(
            W1f[:, :], W1f[:, :], parity[:, :], ALU.mult),
            waits=[(s_act, t_pats)])
        p.emit("vector", lambda e: e.memset(vfm2[:, :], 0.0))
        p.emit("vector", lambda e: e.memset(Amk2[:, :], 0.0))
        p.emit("vector", lambda e: e.memset(S2[:, :], 0.0))
        p.emit("vector", lambda e: e.memset(S2t[:, :], 0.0))
        t_scan_pre = p.tok(s_dve)

        t_am_s = {}
        t_kv_s = {}
        t_s2t_s = {}
        t_s2e_s = {0: 0}
        t_attn_s = {0: 0}

        def prep(sc):
            b = sc % 2
            bD = b * D
            Ksc = k_tm[:, sc * D:(sc + 1) * D]
            Vsc = v_tm[:, sc * D:(sc + 1) * D]
            war_pe = [(s_pe, p.tok(s_pe))]
            p.emit("gpsimd", lambda e, Ksc=Ksc, bD=bD: e.tensor_tensor(
                T1k[:, bD:bD + D], Ksc, QT1[:, :], ALU.mult),
                waits=[(s_dve, t_k), (s_dve, t_v), (s_act, t_pats)] + war_pe)
            p.emit("gpsimd", lambda e, Ksc=Ksc, bD=bD: e.tensor_tensor(
                T2k[:, bD:bD + D], Ksc, QA[:, :], ALU.mult))
            p.emit("gpsimd", lambda e, Ksc=Ksc, bD=bD: e.tensor_tensor(
                KHk[:, bD:bD + D], Ksc, QKH[:, :], ALU.mult))
            t_kv_s[sc] = p.tok(s_gps)
            wv = [(s_dve, p.tok(s_dve))]
            for grp in range(NPT):
                p.emit("tensor", lambda e, grp=grp, Vsc=Vsc: e.transpose(
                    psT_[:, grp * 128:(grp + 1) * 128],
                    Vsc[:, grp * 128:(grp + 1) * 128], idb[:, :]),
                    waits=wv + [(s_dve, t_v)],
                    inc=(s_pe, 1) if grp == NPT - 1 else None)
            t_tr = p.tok(s_pe)
            ve = (vfm2[0:64, :].rearrange("p (g two s) -> p g two s",
                                          two=2, s=128)[:, :, 0, :])
            vo = (vfm2[64:128, :].rearrange("p (g two s) -> p g two s",
                                            two=2, s=128)[:, :, 1, :])
            pe_ = psT_[0:64, :].rearrange("p (g s) -> p g s", s=128)
            po_ = psT_[64:128, :].rearrange("p (g s) -> p g s", s=128)
            p.emit("vector", lambda e, ve=ve, pe_=pe_: e.tensor_copy(ve, pe_),
                   waits=[(s_pe, t_tr)])
            p.emit("vector", lambda e, vo=vo, po_=po_: e.tensor_copy(vo, po_))
            t_vfm = p.tok(s_dve)
            for gidx in range(2):
                war_ev = [(s_dve, p.tok(s_dve))]
                for hh in range(8):
                    h = gidx * 8 + hh
                    p.emit("tensor", lambda e, h=h, hh=hh, sc=sc: e.matmul(
                        psA_[:, hh * 128:(hh + 1) * 128],
                        vfm2[:, h * 128:(h + 1) * 128],
                        r_fm[:, (h // 2) * T + (sc - 1) * 128:
                             (h // 2) * T + sc * 128],
                        start=True, stop=True),
                        waits=[(s_dve, t_vfm)] + war_ev,
                        inc=(s_pe, 1) if hh == 7 else None)
                t_pe2 = p.tok(s_pe)
                p.emit("vector", lambda e, gidx=gidx, b=b: e.tensor_tensor(
                    Amk[:, b * 2 * D + gidx * D: b * 2 * D + (gidx + 1) * D],
                    psA_[:, :], maskc[:, gidx * D:(gidx + 1) * D], ALU.mult),
                    waits=[(s_pe, t_pe2)])
                a2d = (Amk2[0:64, bD + gidx * 512: bD + (gidx + 1) * 512]
                       .rearrange("p (g s) -> p g s", s=64))
                a2s = (psA_[0:64, :].rearrange("p (g two s) -> p g two s",
                                               two=2, s=64)[:, :, 1, :])
                p.emit("vector", lambda e, a2d=a2d, a2s=a2s:
                       e.tensor_copy(a2d, a2s))
            t_am_s[sc] = p.tok(s_dve)

        def out(sc):
            b = sc % 2
            bD = b * D
            Vsc = v_tm[:, sc * D:(sc + 1) * D]
            for h in range(H):
                hc = b * 2 * D + h * 128
                hi = h * DH
                p.emit("tensor", lambda e, h=h, hi=hi, sc=sc: e.matmul(
                    psB_[0:64, hi:hi + DH],
                    r_fm[:, (h // 2) * T + (sc - 1) * 128:
                         (h // 2) * T + (sc - 1) * 128 + 64],
                    S2t[:, hi:hi + DH],
                    start=True, stop=False, skip_group_check=True),
                    waits=[(s_dve, t_am_s[sc]), (s_gps, t_s2t_s[sc - 1]),
                           (s_gps, t_kv_s[sc]),
                           (s_dve, t_attn_s[sc - 1])])
                p.emit("tensor", lambda e, h=h, hi=hi, bD=bD: e.matmul(
                    psB_[64:128, hi:hi + DH],
                    Amk2[:, bD + h * 64: bD + (h + 1) * 64],
                    T2k[:, bD + hi: bD + hi + DH],
                    start=True, stop=False,
                    tile_position=(0, 64), skip_group_check=True))
                p.emit("tensor", lambda e, h=h, hc=hc, hi=hi, bD=bD: e.matmul(
                    psB_[0:64, hi:hi + DH],
                    Amk[:, hc:hc + 64], T1k[:, bD + hi: bD + hi + DH],
                    start=False, stop=True, skip_group_check=True))
                p.emit("tensor", lambda e, h=h, hc=hc, hi=hi, bD=bD: e.matmul(
                    psB_[64:128, hi:hi + DH],
                    Amk[:, hc + 64:hc + 128], T1k[:, bD + hi: bD + hi + DH],
                    start=False, stop=True, tile_position=(0, 64),
                    skip_group_check=True),
                    inc=(s_pe, 1) if h == H - 1 else None)
            t_o1 = p.tok(s_pe)
            dst = attn_res[:, (sc - 1) * D: sc * D]
            p.emit("vector", lambda e, dst=dst: e.tensor_tensor(
                dst, psB_[:, :], P2[:, :], ALU.mult), waits=[(s_pe, t_o1)])
            t_attn_s[sc] = p.tok(s_dve)
            if sc < NSC5 - 1:
                for h in range(H):
                    hi = h * DH
                    p.emit("tensor", lambda e, h=h, hi=hi, Vsc=Vsc, bD=bD:
                           e.matmul(
                        psC_[0:64, hi:hi + DH], Vsc[:, hi:hi + DH],
                        KHk[:, bD + hi: bD + hi + DH],
                        start=True, stop=True),
                        waits=[(s_gps, t_kv_s[sc]),
                               (s_dve, t_s2e_s[sc - 1])],
                        inc=(s_pe, 1) if h == H - 1 else None)
                t_up = p.tok(s_pe)
                p.emit("vector", lambda e: e.tensor_copy(S2[0:64, :],
                                                         psC_[0:64, :]),
                       waits=[(s_pe, t_up)])
                p.emit("vector", lambda e: e.tensor_copy(S2[64:128, :],
                                                         psC_[0:64, :]))
                t_s2e_s[sc] = p.tok(s_dve)
                p.emit("gpsimd", lambda e: e.tensor_tensor(
                    S2t[:, :], S2[:, :], W1f[:, :], ALU.mult),
                    waits=[(s_dve, t_s2e_s[sc])])
                t_s2t_s[sc] = p.tok(s_gps)

        Ksc0 = k_tm[:, 0:D]
        Vsc0 = v_tm[:, 0:D]
        p.emit("gpsimd", lambda e: e.tensor_tensor(
            KHk[:, 0:D], Ksc0, QKH[:, :], ALU.mult),
            waits=[(s_dve, t_k), (s_dve, t_v), (s_act, t_pats)])
        t_kh0 = p.tok(s_gps)
        for h in range(H):
            hi = h * DH
            p.emit("tensor", lambda e, h=h, hi=hi: e.matmul(
                psC_[0:64, hi:hi + DH], Vsc0[:, hi:hi + DH],
                KHk[:, hi:hi + DH], start=True, stop=True),
                waits=[(s_gps, t_kh0)],
                inc=(s_pe, 1) if h == H - 1 else None)
        t_up0 = p.tok(s_pe)
        p.emit("vector", lambda e: e.tensor_copy(S2[0:64, :], psC_[0:64, :]),
               waits=[(s_pe, t_up0)])
        p.emit("vector", lambda e: e.tensor_copy(S2[64:128, :],
                                                 psC_[0:64, :]))
        t_s2e_s[0] = p.tok(s_dve)
        p.emit("gpsimd", lambda e: e.tensor_tensor(
            S2t[:, :], S2[:, :], W1f[:, :], ALU.mult),
            waits=[(s_dve, t_s2e_s[0])])
        t_s2t_s[0] = p.tok(s_gps)

        prep(1)
        for sc in range(1, NSC5):
            if sc < NSC5 - 1:
                prep(sc + 1)
            out(sc)

        bar2 = ([(s_pe, p.tok(s_pe)), (s_dve, p.tok(s_dve)),
                 (s_act, p.tok(s_act)), (s_gps, p.tok(s_gps))]
                + st_go.total_tokens() + st_c.total_tokens())
        for en in ("tensor", "vector", "scalar", "gpsimd"):
            p.emit(en, lambda e: e.nop(), waits=bar2,
                   dma=(en not in p.CHAINED))
        esB2.close()
        esB.close()

        esC = contextlib.ExitStack()
        sbC = esC.enter_context
        y_sb = sbC(nc.sbuf_tensor("y_sb", [128, NPT * T], F32))
        idf = sbC(nc.sbuf_tensor("idf", [128, 128], F32))
        psS = sbC(nc.psum_tensor("psS", [128, T], F32))
        st_c.issue("sync", lambda e: e.dma_start(out=idf[:, :], in_=iden_f),
                   waits=bar2)
        p.emit("tensor", lambda e: e.nop(), waits=st_c.total_tokens(),
               dma=True)
        esC1 = contextlib.ExitStack()
        sbC1 = esC1.enter_context
        agfm = sbC1(nc.sbuf_tensor("agfm", [128, NPT * T], BF16))
        woT = sbC1(nc.sbuf_tensor("woT", [128, NPT * D], BF16))
        g_t2 = sbC1(nc.sbuf_tensor("g_t2", [128, 2 * D], BF16))

        tok_cast = st_wc.issue("gpsimd", lambda e: e.dma_start(
            out=wscr, in_=w_o), waits=bar2)
        tok_w = []
        for m in range(NPT):
            tok_w.append(st_w.issue("scalar", lambda e, m=m:
                                    e.dma_start_transpose(
                out=woT[:, m * D:(m + 1) * D],
                in_=wscr[:, m * 128:(m + 1) * 128]),
                waits=[tok_cast]))

        agv = agfm[:, :].rearrange("p (pt t) -> p pt t", pt=NPT)
        t_evT = 0
        tr_tok = [0, 0]
        for tt in range(NTT):
            buf = tt % 2
            tok_g = st_gl.issue("sync", lambda e, tt=tt, buf=buf: e.dma_start(
                out=g_t2[:, buf * D:(buf + 1) * D],
                in_=g_int[tt * 128:(tt + 1) * 128, :]),
                waits=[(s_dve, p.tok(s_dve))], slot=buf)
            p.emit("gpsimd", lambda e, tt=tt, buf=buf: e.tensor_tensor(
                attn_res[:, tt * D:(tt + 1) * D],
                attn_res[:, tt * D:(tt + 1) * D],
                g_t2[:, buf * D:(buf + 1) * D], ALU.mult),
                waits=[tok_g, (s_pe, tr_tok[buf])])
            t_ag = p.tok(s_gps)
            wv = [(s_dve, p.tok(s_dve))]
            for pt2 in range(NPT):
                p.emit("tensor", lambda e, pt2=pt2, tt=tt: e.transpose(
                    psT_[:, pt2 * 128:(pt2 + 1) * 128],
                    attn_res[:, tt * D + pt2 * 128: tt * D + (pt2 + 1) * 128],
                    idb[:, :]),
                    waits=[(s_gps, t_ag)] + wv,
                    inc=(s_pe, 1) if pt2 == NPT - 1 else None)
            t_tr = p.tok(s_pe)
            tr_tok[buf] = t_tr
            psv = psT_[:, :].rearrange("p (j t) -> p j t", j=NPT)
            dstv = agv[:, :, tt * 128:(tt + 1) * 128]
            p.emit("vector", lambda e, psv=psv, dstv=dstv: e.tensor_copy(
                dstv, psv), waits=[(s_pe, t_tr)])
            t_evT = p.tok(s_dve)

        t_evY = 0
        for ct in range(NPT):
            for m in range(NPT):
                for tk in range(T // 512):
                    ps = psA_ if tk < 2 else psS
                    p.emit("tensor", lambda e, m=m, tk=tk, ct=ct, ps=ps:
                           e.matmul(
                        ps[:, (tk % 2) * 512:((tk % 2) + 1) * 512],
                        woT[:, m * D + ct * 128: m * D + (ct + 1) * 128],
                        fm(agfm, m)[:, tk * 512:(tk + 1) * 512],
                        start=(m == 0), stop=(m == NPT - 1)),
                        waits=tok_w + [(s_dve, t_evT), (s_dve, t_evY)],
                        inc=(s_pe, 1) if (m, tk) == (NPT - 1, 3) else None)
            t_mm = p.tok(s_pe)
            p.emit("vector", lambda e, ct=ct: e.tensor_copy(
                fm(y_sb, ct)[:, 0:1024], psA_[:, :]), waits=[(s_pe, t_mm)])
            p.emit("vector", lambda e, ct=ct: e.tensor_copy(
                fm(y_sb, ct)[:, 1024:2048], psS[:, 0:1024]))
            t_evY = p.tok(s_dve)

        bar3 = ([(s_pe, p.tok(s_pe)), (s_dve, p.tok(s_dve)),
                 (s_act, p.tok(s_act)), (s_gps, p.tok(s_gps))]
                + st_gl.total_tokens()
                + st_w.total_tokens() + st_wc.total_tokens())
        for en in ("tensor", "vector", "scalar", "gpsimd"):
            p.emit(en, lambda e: e.nop(), waits=bar3,
                   dma=(en not in p.CHAINED))
        esC1.close()
        io_t = sbC(nc.sbuf_tensor("io_t", [128, 2 * T], F32))
        xl_t = sbC(nc.sbuf_tensor("xl_t", [128, 2 * T], F32))
        tmp = sbC(nc.sbuf_tensor("tmp", [128, T], F32))
        yb_t = tmp
        scr = sbC(nc.sbuf_tensor("scr", [128, 2 * T], BF16))
        srow = sbC(nc.sbuf_tensor("srow", [1, T], F32))
        sqrow = sbC(nc.sbuf_tensor("sqrow", [1, T], F32))
        mu2row = tmp
        rstdrow = sqrow
        mx = sbC(nc.sbuf_tensor("mx", [128, 1], F32))
        mxt = sbC(nc.sbuf_tensor("mxt", [128, 1], F32))
        mrow = sbC(nc.sbuf_tensor("mrow", [1, 128], F32))
        mval = sbC(nc.sbuf_tensor("mval", [1, 1], F32))

        rows = dict(s=srow, sq=sqrow, rstd=rstdrow, mu2=mu2row[0:1, :])
        t_ln3 = emit_ln(p, y_sb, lambda pt: fm(y_sb, pt), scr, onecol_s,
                        onec, psS, psS, rows, s)
        x2st = [None, None]
        for pt in range(NPT):
            buf = pt % 2
            war_st = [x2st[buf]] if x2st[buf] else []
            tok_x = st_x1.issue("sync", lambda e, pt=pt, buf=buf: e.dma_start(
                out=io_t[:, buf * T:(buf + 1) * T],
                in_=x1_in[pt * 128:(pt + 1) * 128, :]),
                waits=[(s_dve, p.tok(s_dve))] + war_st, slot=buf)
            p.emit("scalar", lambda e, pt=pt: e.activation(
                yb_t[:, :], fm(y_sb, pt), AF.Identity,
                bias=lnb[:, pt:pt + 1], scale=lng[:, pt:pt + 1]),
                waits=[(s_dve, t_ln3), (s_dve, p.tok(s_dve))])
            t_yb = p.tok(s_act)
            war = [x2st[buf]] if x2st[buf] else []
            p.emit("vector", lambda e, buf=buf: e.tensor_tensor(
                io_t[:, buf * T:(buf + 1) * T],
                io_t[:, buf * T:(buf + 1) * T], yb_t[:, :], ALU.add),
                waits=[tok_x, (s_act, t_yb)] + war)
            t_x2 = p.tok(s_dve)
            x2st[buf] = st_o.issue("gpsimd", lambda e, pt=pt, buf=buf:
                                   e.dma_start(
                out=x2_o[pt * 128:(pt + 1) * 128, :],
                in_=io_t[:, buf * T:(buf + 1) * T]),
                waits=[(s_dve, t_x2)], slot=buf)
            p.emit("scalar", lambda e, buf=buf: e.copy(
                scr[:, buf * T:(buf + 1) * T],
                io_t[:, buf * T:(buf + 1) * T]),
                waits=[(s_dve, t_x2), (s_pe, p.tok(s_pe))])
            t_scp = p.tok(s_act)
            for tk in range(T // 512):
                p.emit("tensor", lambda e, buf=buf, tk=tk, pt=pt: e.matmul(
                    psS[0:1, tk * 512:(tk + 1) * 512], onecol_s[:, 0:1],
                    scr[:, buf * T + tk * 512: buf * T + (tk + 1) * 512],
                    start=(pt == 0), stop=(pt == NPT - 1)),
                    waits=[(s_act, t_scp)],
                    inc=(s_pe, 1) if tk == 3 else None)

        rows4 = dict(s=srow, sq=sqrow, rstd=rstdrow, mu2=mu2row[0:1, :])
        emit_ln_streamed_max(p, x2_o, st_x1, io_t, tmp, scr, onecol_s, onec,
                             psS, psA_, psT_, rows4, xl_t, xln4_o, st_o,
                             x2st, mx, mxt, mrow, mval, idf, m3_o, st_m, s,
                             pre_sum=True)

        fin = (st_o.total_tokens() + st_m.total_tokens())
        p.emit("sync", lambda e: e.nop(), waits=fin)
        esC.close()

        block = p.es.enter_context(nc.Block())
        p.build(block)
    return nc


def build_L5():
    nc = bass.Bass("TRN2", target_bir_lowering=False)
    x1_in = dram_in(nc, "x1", [D, T])
    attn_in = dram_in(nc, "attnout", [T, D], BF16)
    g_in = dram_in(nc, "g_out", [T, D], BF16)
    r_in = dram_in(nc, "r_out", [D, T], BF16)
    st_in = dram_in(nc, "st_prev", [DH, D])
    w_o = dram_in(nc, "w_o", [D, D])
    lng_in = dram_in(nc, "ln_g", [D, 1])
    lnb_in = dram_in(nc, "ln_b", [D, 1])
    tdec = dram_in(nc, "tdec", [1, D])
    ones1 = dram_in(nc, "ones1", [1, 128])
    onescol = dram_in(nc, "onescol", [128, 1], BF16)
    iden_b = dram_in(nc, "iden_b", [128, 128], BF16)
    iden_f = dram_in(nc, "iden_f", [128, 128])
    iotas = dram_in(nc, "iotas", [1, 6 * 128])
    parity_in = dram_in(nc, "parity", [128, D], BF16)
    x2_o = dram_out(nc, "x2", [D, T])
    xln4_o = dram_out(nc, "xln4", [D, T])
    m3_o = dram_out(nc, "m3", [1, 1])

    wscr = nc.dram_tensor("wscr5", [D, D], BF16, kind="Internal").ap()

    p = Prog(nc)
    with p.es:
        sb = p.es.enter_context
        y_sb = sb(nc.sbuf_tensor("y_sb", [128, NPT * T], F32))
        agfm = sb(nc.sbuf_tensor("agfm", [128, NPT * T], BF16))
        woT = sb(nc.sbuf_tensor("woT", [128, NPT * D], BF16))
        attn_t = sb(nc.sbuf_tensor("attn_t", [128, 2 * D], BF16))
        g_t = sb(nc.sbuf_tensor("g_t", [128, 2 * D], BF16))
        io_t = sb(nc.sbuf_tensor("io_t", [128, 2 * T], F32))
        xl_t = sb(nc.sbuf_tensor("xl_t", [128, 2 * T], F32))
        tmp = sb(nc.sbuf_tensor("tmp", [128, T], F32))
        yb_t = tmp
        scr = sb(nc.sbuf_tensor("scr", [128, 2 * T], BF16))
        r0_sb = sb(nc.sbuf_tensor("r0_sb", [128, NPT * 128], BF16))
        s0f = sb(nc.sbuf_tensor("s0f", [128, D], F32))
        s0b = sb(nc.sbuf_tensor("s0b", [128, D], BF16))
        par5 = sb(nc.sbuf_tensor("par5", [128, D], BF16))
        Pinit = sb(nc.sbuf_tensor("Pinit", [128, D], BF16))
        lnwrow = sb(nc.sbuf_tensor("lnwrow", [1, D], F32))
        onec = sb(nc.sbuf_tensor("onec", [1, 128], F32))
        onecol_s = sb(nc.sbuf_tensor("onecol_s", [128, 1], BF16))
        idb = sb(nc.sbuf_tensor("idb", [128, 128], BF16))
        idf = sb(nc.sbuf_tensor("idf", [128, 128], F32))
        iot = sb(nc.sbuf_tensor("iot", [1, 6 * 128], F32))
        lng = sb(nc.sbuf_tensor("lng", [128, NPT], F32))
        lnb = sb(nc.sbuf_tensor("lnb", [128, NPT], F32))
        srow = sb(nc.sbuf_tensor("srow", [1, T], F32))
        sqrow = sb(nc.sbuf_tensor("sqrow", [1, T], F32))
        mu2row = tmp
        rstdrow = sqrow
        mx = sb(nc.sbuf_tensor("mx", [128, 1], F32))
        mxt = sb(nc.sbuf_tensor("mxt", [128, 1], F32))
        mrow = sb(nc.sbuf_tensor("mrow", [1, 128], F32))
        mval = sb(nc.sbuf_tensor("mval", [1, 1], F32))
        psA_ = sb(nc.psum_tensor("psA_", [128, D], F32))
        psS = sb(nc.psum_tensor("psS", [128, T], F32))
        psT_ = sb(nc.psum_tensor("psT_", [128, D], BF16))

        p.init_chain_sems()
        s_pe = p.sem("s_pe")
        s_dve = p.chain["vector"]
        s_act = p.chain["scalar"]
        st_c = Stream(p, "stC", 4)
        st_wc = Stream(p, "stWC", 1)
        st_w = Stream(p, "stW", 4)
        st_i = Stream(p, "stI", 2)
        st_x = Stream(p, "stX", 2)
        st_o = Stream(p, "stO", 2)
        st_m = Stream(p, "stM", 1)
        s = dict(pe=s_pe, dve=s_dve, act=s_act)

        setup_consts(p, nc, st_c)
        st_c.issue("sync", lambda e: e.dma_start(out=onec[:, :], in_=ones1))
        st_c.issue("sync", lambda e: e.dma_start(out=onecol_s[:, :],
                                                 in_=onescol))
        st_c.issue("sync", lambda e: e.dma_start(out=idb[:, :], in_=iden_b))
        st_c.issue("sync", lambda e: e.dma_start(out=idf[:, :], in_=iden_f))
        st_c.issue("sync", lambda e: e.dma_start(out=iot[:, :], in_=iotas))
        st_c.issue("sync", lambda e: e.dma_start(out=lnwrow[:, :], in_=tdec))
        st_c.issue("sync", lambda e: e.dma_start(out=s0f[0:64, :],
                                                 in_=st_in))
        st_c.issue("sync", lambda e: e.dma_start(out=s0f[64:128, :],
                                                 in_=st_in))
        st_c.issue("sync", lambda e: e.dma_start(out=par5[:, :],
                                                 in_=parity_in))
        for pt in range(NPT):
            st_c.issue("sync", lambda e, pt=pt: e.dma_start(
                out=lng[:, pt:pt + 1], in_=lng_in[pt * 128:(pt + 1) * 128, :]))
            st_c.issue("sync", lambda e, pt=pt: e.dma_start(
                out=lnb[:, pt:pt + 1], in_=lnb_in[pt * 128:(pt + 1) * 128, :]))
            st_c.issue("sync", lambda e, pt=pt: e.dma_start(
                out=r0_sb[:, pt * 128:(pt + 1) * 128],
                in_=r_in[pt * 128:(pt + 1) * 128, 0:128]))
        cgate = st_c.total_tokens()
        s["cgate"] = cgate
        for en in ("tensor", "vector", "scalar"):
            p.emit(en, lambda e: e.nop(), waits=cgate,
                   dma=(en not in p.CHAINED))

        p.emit("scalar", lambda e: e.activation(lnwrow[:, :], lnwrow[:, :],
                                                AF.Exp))
        t_exp = p.tok(s_act)
        p.emit("vector", lambda e: e.tensor_scalar(
            lnwrow[:, :], lnwrow[:, :], -1.0, None, ALU.mult),
            waits=[(s_act, t_exp)])
        t_lnw = p.tok(s_dve)
        for half in range(2):
            p.emit("tensor", lambda e, half=half: e.matmul(
                psA_[:, half * 512:(half + 1) * 512],
                iot[:, 5 * 128:6 * 128],
                lnwrow[:, half * 512:(half + 1) * 512],
                start=True, stop=True),
                waits=[(s_dve, t_lnw)],
                inc=(s_pe, 1) if half == 1 else None)
        t_pe = p.tok(s_pe)
        p.emit("scalar", lambda e: e.activation(Pinit[:, :], psA_[:, :],
                                                AF.Exp),
               waits=[(s_pe, t_pe)])
        t_pinit = p.tok(s_act)
        p.emit("vector", lambda e: e.tensor_copy(s0b[:, :], s0f[:, :]))
        p.emit("vector", lambda e: e.tensor_tensor(
            s0b[:, :], s0b[:, :], par5[:, :], ALU.mult))
        t_s0 = p.tok(s_dve)

        for h in range(H):
            hi = h * DH
            p.emit("tensor", lambda e, h=h, hi=hi: e.matmul(
                psA_[:, hi:hi + DH],
                r0_sb[:, (h // 2) * 128:(h // 2) * 128 + 128],
                s0b[:, hi:hi + DH],
                start=True, stop=True),
                waits=[(s_dve, t_s0), (s_act, t_pinit)],
                inc=(s_pe, 1) if h == H - 1 else None)
        t_init = p.tok(s_pe)

        tok_cast = st_wc.issue("gpsimd", lambda e: e.dma_start(
            out=wscr, in_=w_o))
        tok_w = []
        for m in range(NPT):
            tok_w.append(st_w.issue("scalar", lambda e, m=m:
                                    e.dma_start_transpose(
                out=woT[:, m * D:(m + 1) * D],
                in_=wscr[:, m * 128:(m + 1) * 128]),
                waits=[tok_cast]))

        agv = agfm[:, :].rearrange("p (pt t) -> p pt t", pt=NPT)
        t_evT = 0
        tr_tok = [0, 0]
        for tt in range(NTT):
            buf = tt % 2
            war_i = [(s_dve, p.tok(s_dve)), (s_pe, tr_tok[buf])]
            tok_a = st_i.issue("sync", lambda e, tt=tt, buf=buf: e.dma_start(
                out=attn_t[:, buf * D:(buf + 1) * D],
                in_=attn_in[tt * 128:(tt + 1) * 128, :]),
                waits=war_i, slot=buf)
            tok_g = st_i.issue("sync", lambda e, tt=tt, buf=buf: e.dma_start(
                out=g_t[:, buf * D:(buf + 1) * D],
                in_=g_in[tt * 128:(tt + 1) * 128, :]),
                waits=war_i, slot=buf)
            if tt == 0:
                p.emit("vector", lambda e: e.tensor_tensor(
                    tmp[:, 0:D], psA_[:, :], Pinit[:, :], ALU.mult),
                    waits=[(s_pe, t_init), tok_a])
                p.emit("vector", lambda e, buf=buf: e.tensor_tensor(
                    attn_t[:, buf * D:(buf + 1) * D],
                    attn_t[:, buf * D:(buf + 1) * D], tmp[:, 0:D], ALU.add))
            p.emit("vector", lambda e, buf=buf: e.tensor_tensor(
                attn_t[:, buf * D:(buf + 1) * D],
                attn_t[:, buf * D:(buf + 1) * D],
                g_t[:, buf * D:(buf + 1) * D], ALU.mult),
                waits=[tok_a, tok_g])
            t_ag = p.tok(s_dve)
            wv = [(s_dve, p.tok(s_dve))]
            for pt2 in range(NPT):
                p.emit("tensor", lambda e, pt2=pt2, buf=buf: e.transpose(
                    psT_[:, pt2 * 128:(pt2 + 1) * 128],
                    attn_t[:, buf * D + pt2 * 128: buf * D + (pt2 + 1) * 128],
                    idb[:, :]),
                    waits=[(s_dve, t_ag)] + wv + [(s_act, t_pinit)],
                    inc=(s_pe, 1) if pt2 == NPT - 1 else None)
            t_tr = p.tok(s_pe)
            tr_tok[buf] = t_tr
            psv = psT_[:, :].rearrange("p (j t) -> p j t", j=NPT)
            dstv = agv[:, :, tt * 128:(tt + 1) * 128]
            p.emit("vector", lambda e, psv=psv, dstv=dstv: e.tensor_copy(
                dstv, psv), waits=[(s_pe, t_tr)])
            t_evT = p.tok(s_dve)

        t_evY = 0
        for ct in range(NPT):
            for m in range(NPT):
                for tk in range(T // 512):
                    ps = psA_ if tk < 2 else psS
                    p.emit("tensor", lambda e, m=m, tk=tk, ct=ct, ps=ps:
                           e.matmul(
                        ps[:, (tk % 2) * 512:((tk % 2) + 1) * 512],
                        woT[:, m * D + ct * 128: m * D + (ct + 1) * 128],
                        fm(agfm, m)[:, tk * 512:(tk + 1) * 512],
                        start=(m == 0), stop=(m == NPT - 1)),
                        waits=tok_w + [(s_dve, t_evT), (s_dve, t_evY)],
                        inc=(s_pe, 1) if (m, tk) == (NPT - 1, 3) else None)
            t_mm = p.tok(s_pe)
            p.emit("vector", lambda e, ct=ct: e.tensor_copy(
                fm(y_sb, ct)[:, 0:1024], psA_[:, :]), waits=[(s_pe, t_mm)])
            p.emit("vector", lambda e, ct=ct: e.tensor_copy(
                fm(y_sb, ct)[:, 1024:2048], psS[:, 0:1024]))
            t_evY = p.tok(s_dve)

        p.emit("tensor", lambda e: e.nop(),
               waits=[(s_dve, p.tok(s_dve))], dma=True)
        p.emit("scalar", lambda e: e.nop(), waits=[(s_dve, p.tok(s_dve))])
        rows = dict(s=srow, sq=sqrow, rstd=rstdrow, mu2=mu2row[0:1, :])
        t_ln3 = emit_ln(p, y_sb, lambda pt: fm(y_sb, pt), scr, onecol_s,
                        onec, psS, psS, rows, s)
        x2st = [None, None]
        for pt in range(NPT):
            buf = pt % 2
            war_st = [x2st[buf]] if x2st[buf] else []
            tok_x = st_x.issue("sync", lambda e, pt=pt, buf=buf: e.dma_start(
                out=io_t[:, buf * T:(buf + 1) * T],
                in_=x1_in[pt * 128:(pt + 1) * 128, :]),
                waits=[(s_dve, p.tok(s_dve))] + war_st, slot=buf)
            p.emit("scalar", lambda e, pt=pt: e.activation(
                yb_t[:, :], fm(y_sb, pt), AF.Identity,
                bias=lnb[:, pt:pt + 1], scale=lng[:, pt:pt + 1]),
                waits=[(s_dve, t_ln3), (s_dve, p.tok(s_dve))])
            t_yb = p.tok(s_act)
            war = [x2st[buf]] if x2st[buf] else []
            p.emit("vector", lambda e, buf=buf: e.tensor_tensor(
                io_t[:, buf * T:(buf + 1) * T],
                io_t[:, buf * T:(buf + 1) * T], yb_t[:, :], ALU.add),
                waits=[tok_x, (s_act, t_yb)] + war)
            t_x2 = p.tok(s_dve)
            x2st[buf] = st_o.issue("gpsimd", lambda e, pt=pt, buf=buf:
                                   e.dma_start(
                out=x2_o[pt * 128:(pt + 1) * 128, :],
                in_=io_t[:, buf * T:(buf + 1) * T]),
                waits=[(s_dve, t_x2)], slot=buf)

        rows4 = dict(s=srow, sq=sqrow, rstd=rstdrow, mu2=mu2row[0:1, :])
        emit_ln_streamed_max(p, x2_o, st_x, io_t, tmp, scr, onecol_s, onec,
                             psS, psA_, psT_, rows4, xl_t, xln4_o, st_o,
                             x2st, mx, mxt, mrow, mval, idf, m3_o, st_m, s)

        fin = (st_o.total_tokens() + st_m.total_tokens())
        p.emit("sync", lambda e: e.nop(), waits=fin)

        block = p.es.enter_context(nc.Block())
        p.build(block)
    return nc


def emit_ln_streamed_max(p, x_o, st_x, io_t, tmp, scr, onecol_s, onec, psS,
                         psB1, psB2, rows, xl_t, xln_o, st_o, x_ready_st,
                         mx, mxt, mrow, mval, idf, m_o, st_m, s,
                         pre_sum=False):
    s_pe, s_dve, s_act = s["pe"], s["dve"], s["act"]
    srow, sqrow, rstdrow = rows["s"], rows["sq"], rows["rstd"]
    x1_ready = [t for t in x_ready_st if t]
    if pre_sum:
        t_pe0 = p.tok(s_pe)
        p.emit("vector", lambda e: e.tensor_copy(srow[:, :], psS[0:1, :]),
               waits=[(s_pe, t_pe0)])
        t_dve = p.tok(s_dve)
        p.emit("tensor", lambda e: e.nop(), waits=[(s_dve, t_dve)])
    for which in range((1 if pre_sum else 0), 2):
        for pt in range(NPT):
            buf = pt % 2
            war = [(s_act, p.tok(s_act))]
            tok = st_x.issue("sync", lambda e, pt=pt, buf=buf: e.dma_start(
                out=io_t[:, buf * T:(buf + 1) * T],
                in_=x_o[pt * 128:(pt + 1) * 128, :]),
                waits=x1_ready + war, slot=buf)
            war2 = [(s_pe, p.tok(s_pe))]
            if which == 0:
                p.emit("scalar", lambda e, buf=buf: e.copy(
                    scr[:, buf * T:(buf + 1) * T],
                    io_t[:, buf * T:(buf + 1) * T]), waits=[tok] + war2)
            else:
                p.emit("scalar", lambda e, buf=buf: e.activation(
                    scr[:, buf * T:(buf + 1) * T],
                    io_t[:, buf * T:(buf + 1) * T], AF.Square),
                    waits=[tok] + war2)
            t_act = p.tok(s_act)
            for tk in range(T // 512):
                p.emit("tensor", lambda e, buf=buf, tk=tk, pt=pt: e.matmul(
                    psS[0:1, tk * 512:(tk + 1) * 512], onecol_s[:, 0:1],
                    scr[:, buf * T + tk * 512: buf * T + (tk + 1) * 512],
                    start=(pt == 0), stop=(pt == NPT - 1)),
                    waits=[(s_act, t_act)],
                    inc=(s_pe, 1) if tk == 3 else None)
        t_pe = p.tok(s_pe)
        dstrow = srow if which == 0 else sqrow
        p.emit("vector", lambda e, dstrow=dstrow: e.tensor_copy(
            dstrow[:, :], psS[0:1, :]), waits=[(s_pe, t_pe)])
        if which == 0:
            t_dve = p.tok(s_dve)
            p.emit("tensor", lambda e: e.nop(), waits=[(s_dve, t_dve)])
    mu2 = rows["mu2"]
    p.emit("vector", lambda e: e.tensor_scalar(
        srow[:, :], srow[:, :], 1.0 / D, None, ALU.mult))
    p.emit("vector", lambda e: e.tensor_tensor(
        mu2, srow[:, :], srow[:, :], ALU.mult))
    p.emit("vector", lambda e: e.scalar_tensor_tensor(
        sqrow[:, :], sqrow[:, :], 1.0 / D, mu2, ALU.mult, ALU.subtract))
    t = p.tok(s_dve)
    p.emit("scalar", lambda e: e.activation(
        sqrow[:, :], sqrow[:, :], AF.Ln, bias=EPS, scale=1.0),
        waits=[(s_dve, t)] + list(s.get("cgate", [])))
    p.emit("scalar", lambda e: e.activation(
        rstdrow[:, :], sqrow[:, :], AF.Exp, bias=0.0, scale=-0.5))
    t_rows = p.tok(s_act)
    wv = [(s_dve, p.tok(s_dve))]
    for tk in range(T // 512):
        p.emit("tensor", lambda e, tk=tk: e.matmul(
            psS[:, tk * 512:(tk + 1) * 512], onec[:, :],
            srow[:, tk * 512:(tk + 1) * 512], start=True, stop=True),
            waits=[(s_act, t_rows)] + wv,
            inc=(s_pe, 1) if tk == 3 else None)
    t_bc = p.tok(s_pe)
    prev_q = 0
    st_tok = [None, None]
    for pt in range(NPT):
        buf = pt % 2
        tok = st_x.issue("sync", lambda e, pt=pt, buf=buf: e.dma_start(
            out=io_t[:, buf * T:(buf + 1) * T],
            in_=x_o[pt * 128:(pt + 1) * 128, :]),
            waits=[(s_dve, prev_q)], slot=buf)
        p.emit("vector", lambda e, buf=buf: e.tensor_tensor(
            tmp[:, :], io_t[:, buf * T:(buf + 1) * T], psS[:, :],
            ALU.subtract), waits=[tok, (s_pe, t_bc)])
        for tk in range(T // 512):
            half = tk % 2
            wv2 = [(s_dve, p.tok(s_dve))]
            p.emit("tensor", lambda e, tk=tk, half=half: e.matmul(
                psB1[:, half * 512:(half + 1) * 512], onec[:, :],
                rstdrow[:, tk * 512:(tk + 1) * 512], start=True, stop=True),
                waits=wv2 + [(s_act, t_rows)],
                inc=(s_pe, 1))
            t_b2 = p.tok(s_pe)
            war = [st_tok[buf]] if st_tok[buf] and tk == 0 else []
            p.emit("vector", lambda e, buf=buf, tk=tk, half=half:
                   e.tensor_tensor(
                xl_t[:, buf * T + tk * 512: buf * T + (tk + 1) * 512],
                tmp[:, tk * 512:(tk + 1) * 512],
                psB1[:, half * 512:(half + 1) * 512], ALU.mult),
                waits=[(s_pe, t_b2)] + war)
        dst = mx if pt == 0 else mxt
        p.emit("vector", lambda e, buf=buf, dst=dst: e.reduce_max(
            dst[:, :], xl_t[:, buf * T:(buf + 1) * T], AX.X,
            apply_absolute_value=True))
        if pt > 0:
            p.emit("vector", lambda e: e.tensor_tensor(
                mx[:, :], mx[:, :], mxt[:, :], ALU.max))
        prev_q = p.tok(s_dve)
        t_dve = p.tok(s_dve)
        st_tok[buf] = st_o.issue("gpsimd", lambda e, pt=pt, buf=buf:
                                 e.dma_start(
            out=xln_o[pt * 128:(pt + 1) * 128, :],
            in_=xl_t[:, buf * T:(buf + 1) * T]),
            waits=[(s_dve, t_dve)], slot=buf)
    t_dve = p.tok(s_dve)
    p.emit("tensor", lambda e: e.transpose(psS[0:1, 0:128], mx[:, :],
                                           idf[:, :]),
           waits=[(s_dve, t_dve)], inc=(s_pe, 1))
    t_pe = p.tok(s_pe)
    p.emit("vector", lambda e: e.tensor_copy(mrow[:, :], psS[0:1, 0:128]),
           waits=[(s_pe, t_pe)])
    p.emit("vector", lambda e: e.reduce_max(mval[:, :], mrow[:, :], AX.X))
    t_dve = p.tok(s_dve)
    st_m.issue("gpsimd", lambda e: e.dma_start(out=m_o, in_=mval[:, :]),
               waits=[(s_dve, t_dve)])


def host_consts():
    c = {}
    c["ones1"] = np.ones((1, 128), np.float32)
    c["onescol"] = np.ones((128, 1), BF)
    c["iden_f"] = np.eye(128, dtype=np.float32)
    cv = np.zeros((128, 16), np.float32)
    cv[:, :len(ACT_CONSTS)] = np.asarray(ACT_CONSTS, np.float32)[None, :]
    c["cvals"] = cv
    c["iden_b"] = np.eye(128, dtype=BF)
    s = np.arange(128)
    blk = ((s[:, None] <= s[None, :])
           & ((s[:, None] < 64) == (s[None, :] < 64))).astype(BF)
    c["maskc"] = np.tile(blk, (1, 16))
    par = np.zeros((128, 1024), np.float32)
    for h in range(16):
        par[(64 * (h % 2)):(64 * (h % 2) + 64), h * 64:(h + 1) * 64] = 1.0
    c["parity"] = par.astype(BF)
    iot = np.zeros((1, 6 * 128), np.float32)
    iot[0, 0:128] = s % 64
    iot[0, 128:256] = -(s % 64)
    iot[0, 256:384] = np.where(s < 64, 64 - s, 200)
    iot[0, 384:512] = 127 - s
    iot[0, 512:640] = 1
    iot[0, 640:768] = s + 1
    c["iotas"] = iot
    return c


_NCS = {}


def _programs():
    if not _NCS:
        _NCS["L1"] = build_L1()
        _NCS["L2"] = build_L2()
        _NCS["L3"] = build_L3(final=False)
        _NCS["L45"] = build_L45()
        _NCS["L7"] = build_L3(final=True)
    return _NCS


def _run(nc, in_maps):
    res = run_bass_kernel_spmd(nc, in_maps, core_ids=list(range(NC)))
    return res.results


def kernel(L0_attn_x, L0_attn_time_decay, L0_attn_time_mix_k,
           L0_attn_time_mix_v, L0_attn_time_mix_r, L0_attn_time_mix_g,
           L0_attn_W_r, L0_attn_W_k, L0_attn_W_v, L0_attn_W_g, L0_attn_W_o,
           L0_attn_ln_gamma, L0_attn_ln_beta, L0_ffn_clip_x,
           L0_ffn_clip_down, L0_ffn_w_gate, L0_ffn_w_up, L0_ffn_w_down):
    f32 = np.float32
    x = np.ascontiguousarray(np.asarray(L0_attn_x, f32).reshape(16384, D))
    wg = np.ascontiguousarray(np.asarray(L0_ffn_w_gate, f32))
    wu = np.ascontiguousarray(np.asarray(L0_ffn_w_up, f32))
    wd = np.ascontiguousarray(np.asarray(L0_ffn_w_down, f32))
    w_r = np.ascontiguousarray(np.asarray(L0_attn_W_r, f32))
    w_k = np.ascontiguousarray(np.asarray(L0_attn_W_k, f32))
    w_v = np.ascontiguousarray(np.asarray(L0_attn_W_v, f32))
    w_g = np.ascontiguousarray(np.asarray(L0_attn_W_g, f32))
    w_o = np.ascontiguousarray(np.asarray(L0_attn_W_o, f32))
    tmix = np.ascontiguousarray(np.concatenate([
        np.asarray(L0_attn_time_mix_r, f32),
        np.asarray(L0_attn_time_mix_k, f32),
        np.asarray(L0_attn_time_mix_v, f32),
        np.asarray(L0_attn_time_mix_g, f32)]).reshape(4 * D, 1))
    tdec = np.ascontiguousarray(np.asarray(L0_attn_time_decay,
                                           f32).reshape(1, D))
    lng = np.ascontiguousarray(np.asarray(L0_attn_ln_gamma, f32).reshape(D, 1))
    lnb = np.ascontiguousarray(np.asarray(L0_attn_ln_beta, f32).reshape(D, 1))
    clip_x = np.asarray(L0_ffn_clip_x, f32).reshape(1, 1)
    clip_d = np.asarray(L0_ffn_clip_down, f32).reshape(1, 1)
    consts = host_consts()
    ncs = _programs()
    zrow = np.zeros((1, D), BF)
    zst = np.zeros((DH, D), f32)
    ones_c0 = np.ones((1, 1), f32)
    zero_c0 = np.zeros((1, 1), f32)

    def cmap(keys):
        return {k: consts[k] for k in keys}

    k1 = cmap(["ones1", "onescol", "iden_f", "cvals"])
    pad = np.ones((3 * 128 * NC - I, D), np.float32)
    wgp = np.concatenate([np.asarray(wg, np.float32), pad], axis=0)
    wup = np.concatenate([np.asarray(wu, np.float32), pad], axis=0)
    r1 = _run(ncs["L1"], [
        dict(k1, x=x[c * T:(c + 1) * T],
             w_gate=wgp[c * 384:(c + 1) * 384],
             w_up=wup[c * 384:(c + 1) * 384],
             w_down=wd[c * 128:(c + 1) * 128])
        for c in range(NC)])
    m1 = np.concatenate([r1[c]["m1"] for c in range(NC)], axis=0)
    qg_f = np.concatenate([r1[c]["qg"] for c in range(NC)], axis=0)[:I]
    qu_f = np.concatenate([r1[c]["qu"] for c in range(NC)], axis=0)[:I]
    sg_f = np.concatenate([r1[c]["sg"] for c in range(NC)], axis=0)[:I]
    su_f = np.concatenate([r1[c]["su"] for c in range(NC)], axis=0)[:I]
    qd_f = np.concatenate([r1[c]["qd"] for c in range(NC)], axis=0)
    sdn_f = np.concatenate([r1[c]["sdn"] for c in range(NC)], axis=0)

    k2 = cmap(["ones1", "iden_f", "cvals"])
    r2 = _run(ncs["L2"], [
        dict(k2, xln=r1[c]["xln"], m_all=m1, clip=clip_x, qg=qg_f,
             qu=qu_f, sg=sg_f, su=su_f)
        for c in range(NC)])
    m2 = np.concatenate([r2[c]["m2"] for c in range(NC)], axis=0)

    k3 = cmap(["ones1", "onescol", "iden_f", "cvals"])
    r3 = _run(ncs["L3"], [
        dict(k3, a=r2[c]["a"], xfm=r1[c]["xfm"], m_all=m2, clip=clip_d,
             qd=qd_f, sdn=sdn_f)
        for c in range(NC)])

    k45 = cmap(["ones1", "onescol", "iden_b", "iden_f", "maskc", "parity",
                "iotas", "cvals"])
    zhalo = np.zeros((D, 129), BF)
    xh = [np.ascontiguousarray(np.concatenate(
        [(np.asarray(r3[c - 1]["xln2"])[:, T - 129:T] if c > 0 else zhalo),
         np.asarray(r3[c]["xln2"])], axis=1)) for c in range(NC)]
    r5 = _run(ncs["L45"], [
        dict(k45, xln2h=xh[c], x1=r3[c]["x1"],
             w_r=w_r, w_k=w_k, w_v=w_v, w_g=w_g, w_o=w_o,
             tmix=tmix, tdec=tdec, ln_g=lng, ln_b=lnb,
             core0=(ones_c0 if c == 0 else zero_c0))
        for c in range(NC)])
    m3 = np.concatenate([r5[c]["m3"] for c in range(NC)], axis=0)

    r6 = _run(ncs["L2"], [
        dict(k2, xln=r5[c]["xln4"], m_all=m3, clip=clip_x, qg=qg_f,
             qu=qu_f, sg=sg_f, su=su_f)
        for c in range(NC)])
    m4 = np.concatenate([r6[c]["m2"] for c in range(NC)], axis=0)

    r7 = _run(ncs["L7"], [
        dict(k3, a=r6[c]["a"], xfm=r5[c]["x2"], m_all=m4, clip=clip_d,
             qd=qd_f, sdn=sdn_f)
        for c in range(NC)])

    out = np.concatenate([r7[c]["out"] for c in range(NC)], axis=0)
    return out.reshape(1, 16384, D)


# version 12 (final)
# speedup vs baseline: 1.0437x; 1.0133x over previous
import contextlib
import numpy as np
import ml_dtypes
import concourse.bass as bass
import concourse.mybir as mybir
from concourse.bass_utils import run_bass_kernel_spmd

F32 = mybir.dt.float32
BF16 = mybir.dt.bfloat16
AF = mybir.ActivationFunctionType
ALU = mybir.AluOpType
AX = mybir.AxisListType

NC = 8
T = 2048
D = 1024
I = 2816
H = 16
DH = 64
SC = 128
NSC = T // SC
NPT = D // 128
NIT = I // 128
NTT = T // 128
MAGIC = float(np.float32(12582912.0))
MAGIC64 = 6755399441055744.0
BIG4 = float(np.float32(4.0 * (MAGIC64 - MAGIC)))
QMAX = 127.0
EPS = 1e-5

BF = ml_dtypes.bfloat16


class Prog:

    ENGINES = ("sync", "vector", "scalar", "tensor", "gpsimd")
    CHAINED = ("vector", "scalar", "gpsimd")

    def __init__(self, nc):
        self.nc = nc
        self.es = contextlib.ExitStack()
        self.ops = []
        self.cnt = {}
        self.chain = {}

    def init_chain_sems(self):
        for en in self.CHAINED:
            self.chain[en] = self.sem(f"chain_{en}")

    def sem(self, name):
        s = self.es.enter_context(self.nc.semaphore(name))
        self.cnt[id(s)] = 0
        return s

    def emit(self, eng, fn, waits=(), inc=None, dma=False):
        incs = []
        waits = [(s, v) for (s, v) in waits if v > 0]
        if eng in self.CHAINED and not dma:
            assert inc is None, "compute ops on chained engines use the chain sem"
            cs = self.chain[eng]
            prior = self.cnt[id(cs)]
            if prior > 0:
                waits = waits + [(cs, prior)]
            self.cnt[id(cs)] += 1
            incs = [(cs, 1)]
        elif inc is not None:
            sem, amt = inc
            self.cnt[id(sem)] += amt
            incs = [(sem, amt)]
        self.ops.append((eng, fn, waits, incs))

    def tok(self, sem):
        return self.cnt[id(sem)]

    def barrier(self, sems):
        toks = [(s, self.tok(s)) for s in sems]
        for en in self.ENGINES:
            self.emit(en, lambda e: e.engine_nop() if hasattr(e, "engine_nop")
                      else e.nop(), waits=toks)

    def build(self, block):
        for en in self.ENGINES:
            ops = [(fn, waits, incs) for (e, fn, waits, incs) in self.ops
                   if e == en]
            if not ops:
                continue

            def body(eng, _ops=ops):
                seen = {}
                for (fn, waits, incs) in _ops:
                    for (sem, v) in waits:
                        if seen.get(id(sem), 0) >= v:
                            continue
                        eng.wait_ge(sem, v)
                        seen[id(sem)] = v
                    inst = fn(eng)
                    for (sem, amt) in incs:
                        inst = inst.then_inc(sem, amt)

            getattr(block, en)(body)


class Stream:

    MAXQ = 1

    def __init__(self, p, name, nslots=1):
        self.p = p
        self.sems = [p.sem(f"{name}{i}") for i in range(nslots)]
        self.counts = [0] * nslots
        self.waited = [0] * nslots
        self.k = 0

    def issue(self, eng, fn, waits=(), slot=None):
        if slot is None:
            slot = self.k % len(self.sems)
            self.k += 1
        s = self.sems[slot]
        waits = list(waits)
        if (self.counts[slot] - self.waited[slot]) // 16 >= self.MAXQ:
            self.waited[slot] = self.counts[slot]
            waits.append((s, self.waited[slot]))
        self.counts[slot] += 16
        self.p.emit(eng, fn, waits=waits, inc=(s, 16), dma=True)
        return (s, self.counts[slot])

    def total_tokens(self):
        return [(s, c) for s, c in zip(self.sems, self.counts) if c > 0]


def dram_in(nc, name, shape, dtype=F32):
    return nc.dram_tensor(name, shape, dtype, kind="ExternalInput").ap()


def dram_out(nc, name, shape, dtype=F32):
    return nc.dram_tensor(name, shape, dtype, kind="ExternalOutput").ap()


ACT_CONSTS = [MAGIC, EPS, -0.5, 128.25, -128.25, 255.5, 0.5, 2.0, 0.25]


def setup_consts(p, nc, st_const):
    cvals = dram_in(nc, "cvals", [128, 16])
    const_sb = p.es.enter_context(nc.sbuf_tensor("const_sb", [128, 16], F32))
    for i, v in enumerate(ACT_CONSTS):
        nc.const_aps.register(F32, float(v), const_sb[:, i:i + 1])
    st_const.issue("sync", lambda e: e.dma_start(out=const_sb[:, :], in_=cvals))
    return const_sb


def emit_adaptive_magic(p, const_sb, s):
    p.emit("vector", lambda e: e.tensor_scalar(
        const_sb[:, 9:10], const_sb[:, 8:9], MAGIC, -MAGIC,
        ALU.add, ALU.add), waits=list(s.get("cgate", [])))
    p.emit("vector", lambda e: e.tensor_scalar(
        const_sb[:, 9:10], const_sb[:, 9:10], BIG4, MAGIC,
        ALU.mult, ALU.add))
    p.emit("vector", lambda e: e.tensor_scalar(
        const_sb[:, 10:11], const_sb[:, 9:10], -1.0, None, ALU.mult))
    return p.tok(s["dve"])


def fm(x_sb, pt, c0=0, n=T):
    return x_sb[:, pt * T + c0: pt * T + c0 + n]


def emit_ln(p, x_sb, xln_dst_fn, scr, onecol_s, onec, psS, psB, rows, s):
    s_pe, s_dve, s_act = s["pe"], s["dve"], s["act"]
    srow, sqrow, rstdrow = rows["s"], rows["sq"], rows["rstd"]
    murow = srow
    for which in range(2):
        for pt in range(NPT):
            buf = pt % 2
            war = [(s_pe, p.tok(s_pe))] if pt >= 2 or which == 1 else []
            if which == 0:
                p.emit("scalar", lambda e, pt=pt, buf=buf: e.copy(
                    scr[:, buf * T:(buf + 1) * T], fm(x_sb, pt)),
                    waits=war)
            else:
                p.emit("scalar", lambda e, pt=pt, buf=buf: e.activation(
                    scr[:, buf * T:(buf + 1) * T], fm(x_sb, pt), AF.Square),
                    waits=war)
            t_act = p.tok(s_act)
            for tk in range(T // 512):
                p.emit("tensor", lambda e, pt=pt, buf=buf, tk=tk: e.matmul(
                    psS[0:1, tk * 512:(tk + 1) * 512], onecol_s[:, 0:1],
                    scr[:, buf * T + tk * 512: buf * T + (tk + 1) * 512],
                    start=(pt == 0), stop=(pt == NPT - 1)),
                    waits=[(s_act, t_act)],
                    inc=(s_pe, 1) if tk == T // 512 - 1 else None)
        t_pe = p.tok(s_pe)
        dstrow = srow if which == 0 else sqrow
        p.emit("vector", lambda e, dstrow=dstrow: e.tensor_copy(
            dstrow[:, :], psS[0:1, :]), waits=[(s_pe, t_pe)])
        if which == 0:
            t_dve = p.tok(s_dve)
            p.emit("tensor", lambda e: e.nop(), waits=[(s_dve, t_dve)])
    mu2 = rows["mu2"]
    p.emit("vector", lambda e: e.tensor_scalar(
        murow[:, :], srow[:, :], 1.0 / D, None, ALU.mult))
    p.emit("vector", lambda e: e.tensor_tensor(
        mu2, murow[:, :], murow[:, :], ALU.mult))
    p.emit("vector", lambda e: e.scalar_tensor_tensor(
        sqrow[:, :], sqrow[:, :], 1.0 / D, mu2,
        ALU.mult, ALU.subtract))
    t = p.tok(s_dve)
    p.emit("scalar", lambda e: e.activation(
        sqrow[:, :], sqrow[:, :], AF.Ln, bias=EPS, scale=1.0),
        waits=[(s_dve, t)] + list(s.get("cgate", [])))
    p.emit("scalar", lambda e: e.activation(
        rstdrow[:, :], sqrow[:, :], AF.Exp, bias=0.0, scale=-0.5))
    t_rows = p.tok(s_act)
    for tk in range(T // 512):
        mu_c = psB[:, (tk % 2) * 512:(tk % 2) * 512 + 512]
        rs_c = psB[:, 1024 + (tk % 2) * 512:1024 + (tk % 2) * 512 + 512]
        war = [(s_dve, p.tok(s_dve))]
        p.emit("tensor", lambda e, tk=tk, mu_c=mu_c: e.matmul(
            mu_c, onec[:, :], murow[:, tk * 512:(tk + 1) * 512],
            start=True, stop=True), waits=[(s_act, t_rows)] + war)
        p.emit("tensor", lambda e, tk=tk, rs_c=rs_c: e.matmul(
            rs_c, onec[:, :], rstdrow[:, tk * 512:(tk + 1) * 512],
            start=True, stop=True), inc=(s_pe, 1))
        t_bc = p.tok(s_pe)
        for pt in range(NPT):
            dst = xln_dst_fn(pt)
            p.emit("vector", lambda e, pt=pt, tk=tk, mu_c=mu_c, dst=dst:
                   e.tensor_tensor(
                tmpcol(dst, tk), fm(x_sb, pt, tk * 512, 512),
                mu_c, ALU.subtract), waits=[(s_pe, t_bc)])
            p.emit("vector", lambda e, pt=pt, tk=tk, rs_c=rs_c, dst=dst:
                   e.tensor_tensor(tmpcol(dst, tk), tmpcol(dst, tk),
                                   rs_c, ALU.mult))
    return p.tok(s_dve)


def tmpcol(dst, tk):
    return dst[:, tk * 512:(tk + 1) * 512]


def emit_absmax_fm(p, src_fn, npt, mx, mxt, idf, psum_m, mrow, mval, s,
                   extra_w=()):
    s_pe, s_dve = s["pe"], s["dve"]
    for pt in range(npt):
        dst = mx if pt == 0 else mxt
        p.emit("vector", lambda e, pt=pt, dst=dst: e.reduce_max(
            dst[:, :], src_fn(pt), AX.X, apply_absolute_value=True),
            waits=list(extra_w))
        if pt > 0:
            p.emit("vector", lambda e: e.tensor_tensor(
                mx[:, :], mx[:, :], mxt[:, :], ALU.max))
    t = p.tok(s_dve)
    p.emit("tensor", lambda e: e.transpose(psum_m[0:1, :], mx[:, :],
                                           idf[:, :]),
           waits=[(s_dve, t)], inc=(s_pe, 1))
    t = p.tok(s_pe)
    p.emit("vector", lambda e: e.tensor_copy(mrow[:, :], psum_m[0:1, :]),
           waits=[(s_pe, t)])
    p.emit("vector", lambda e: e.reduce_max(mval[:, :], mrow[:, :], AX.X),
          )
    return p.tok(s_dve)


def build_L1():
    nc = bass.Bass("TRN2", target_bir_lowering=False)
    x_in = dram_in(nc, "x", [T, D])
    wg_in = dram_in(nc, "w_gate", [3 * 128, D])
    wu_in = dram_in(nc, "w_up", [3 * 128, D])
    wd_in = dram_in(nc, "w_down", [128, I])
    ones1 = dram_in(nc, "ones1", [1, 128])
    onescol = dram_in(nc, "onescol", [128, 1], BF16)
    iden_f = dram_in(nc, "iden_f", [128, 128])

    xfm_o = dram_out(nc, "xfm", [D, T])
    xln_o = dram_out(nc, "xln", [D, T])
    m1_o = dram_out(nc, "m1", [1, 1])
    qg_o = dram_out(nc, "qg", [3 * 128, D], BF16)
    qu_o = dram_out(nc, "qu", [3 * 128, D], BF16)
    qd_o = dram_out(nc, "qd", [128, I], BF16)
    sg_o = dram_out(nc, "sg", [3 * 128, 1])
    su_o = dram_out(nc, "su", [3 * 128, 1])
    sdn_o = dram_out(nc, "sdn", [128, 1])

    p = Prog(nc)
    with p.es:
        sb = p.es.enter_context
        x_sb = sb(nc.sbuf_tensor("x_sb", [128, NPT * T], F32))
        xln_sb = sb(nc.sbuf_tensor("xln_sb", [128, NPT * T], F32))
        xtm = sb(nc.sbuf_tensor("xtm", [128, 2 * D], F32))
        scr = sb(nc.sbuf_tensor("scr", [128, 2 * T], BF16))
        onec = sb(nc.sbuf_tensor("onec", [1, 128], F32))
        onecol_s = sb(nc.sbuf_tensor("onecol_s", [128, 1], BF16))
        idf = sb(nc.sbuf_tensor("idf", [128, 128], F32))
        srow = sb(nc.sbuf_tensor("srow", [1, T], F32))
        sqrow = sb(nc.sbuf_tensor("sqrow", [1, T], F32))
        mu2row = sb(nc.sbuf_tensor("mu2row", [1, T], F32))
        rstdrow = sqrow
        mx = sb(nc.sbuf_tensor("mx", [128, 1], F32))
        mxt = sb(nc.sbuf_tensor("mxt", [128, 1], F32))
        mrow = sb(nc.sbuf_tensor("mrow", [1, 128], F32))
        mval = sb(nc.sbuf_tensor("mval", [1, 1], F32))
        wtile = sb(nc.sbuf_tensor("wtile", [128, 2 * I], F32))
        qtile = sb(nc.sbuf_tensor("qtile", [128, 2 * I], BF16))
        wmax = sb(nc.sbuf_tensor("wmax", [128, 2], F32))
        winv = sb(nc.sbuf_tensor("winv", [128, 2], F32))
        wsc = sb(nc.sbuf_tensor("wsc", [128, 2], F32))
        psA = sb(nc.psum_tensor("psA", [128, 512], F32))
        psB2 = sb(nc.psum_tensor("psB2", [128, 512], F32))
        psS = sb(nc.psum_tensor("psS", [128, T], F32))
        psM = sb(nc.psum_tensor("psM", [128, 128], F32))

        p.init_chain_sems()
        s_pe = p.sem("s_pe")
        s_dve = p.chain["vector"]
        s_act = p.chain["scalar"]
        st_c = Stream(p, "stC", 4)
        st_w = Stream(p, "stW", 2)
        st_x = Stream(p, "stX", 2)
        st_o = Stream(p, "stO", 1)
        st_q = Stream(p, "stQ", 2)
        s = dict(pe=s_pe, dve=s_dve, act=s_act)

        const_sb = setup_consts(p, nc, st_c)
        st_c.issue("sync", lambda e: e.dma_start(out=onec[:, :], in_=ones1))
        st_c.issue("sync", lambda e: e.dma_start(out=onecol_s[:, :],
                                                 in_=onescol))
        st_c.issue("sync", lambda e: e.dma_start(out=idf[:, :],
                                                 in_=iden_f))
        cgate = st_c.total_tokens()
        s["cgate"] = cgate
        for en in ("tensor", "vector", "scalar"):
            p.emit(en, lambda e: e.nop(), waits=cgate,
                   dma=(en not in p.CHAINED))
        emit_adaptive_magic(p, const_sb, s)

        qw_last_dve = [0, 0]
        qw_st = [None, None]

        def quant_weight(w_in, q_out, s_out, ntiles, width):
            for it in range(ntiles):
                buf = it % 2
                tok_ld = st_w.issue("sync", lambda e, it=it, buf=buf:
                                    e.dma_start(
                    out=wtile[:, buf * I:buf * I + width],
                    in_=w_in[it * 128:(it + 1) * 128, :]),
                    waits=[(s_dve, qw_last_dve[buf])], slot=buf)
                war_q = [qw_st[buf]] if qw_st[buf] else []
                p.emit("vector", lambda e, buf=buf, width=width: e.reduce_max(
                    wmax[:, buf:buf + 1], wtile[:, buf * I:buf * I + width],
                    AX.X, apply_absolute_value=True),
                    waits=[tok_ld] + war_q)
                p.emit("vector", lambda e, buf=buf: e.reciprocal(
                    winv[:, buf:buf + 1], wmax[:, buf:buf + 1]))
                p.emit("vector", lambda e, buf=buf: e.tensor_scalar(
                    winv[:, buf:buf + 1], winv[:, buf:buf + 1], QMAX, None,
                    ALU.mult))
                p.emit("vector", lambda e, buf=buf: e.tensor_scalar(
                    wsc[:, buf:buf + 1], wmax[:, buf:buf + 1], 1.0 / QMAX,
                    None, ALU.mult))
                t_dve = p.tok(s_dve)
                p.emit("scalar", lambda e, buf=buf, width=width: e.activation(
                    wtile[:, buf * I:buf * I + width],
                    wtile[:, buf * I:buf * I + width],
                    AF.Identity, scale=winv[:, buf:buf + 1]),
                    waits=[(s_dve, t_dve)] + cgate)
                t_act = p.tok(s_act)
                p.emit("vector", lambda e, buf=buf, width=width:
                       e.tensor_scalar(
                    qtile[:, buf * I:buf * I + width],
                    wtile[:, buf * I:buf * I + width],
                    const_sb[:, 9:10], const_sb[:, 10:11],
                    ALU.add, ALU.add),
                    waits=[(s_act, t_act)])
                qw_last_dve[buf] = p.tok(s_dve)
                t_dve = p.tok(s_dve)
                st_q.issue("gpsimd", lambda e, it=it, buf=buf, width=width,
                           q_out=q_out: e.dma_start(
                    out=q_out[it * 128:(it + 1) * 128, :],
                    in_=qtile[:, buf * I:buf * I + width]),
                    waits=[(s_dve, t_dve)], slot=buf)
                qw_st[buf] = st_q.issue("gpsimd", lambda e, it=it, buf=buf,
                                        s_out=s_out: e.dma_start(
                    out=s_out[it * 128:(it + 1) * 128, :],
                    in_=wsc[:, buf:buf + 1]),
                    waits=[(s_dve, t_dve)], slot=buf)

        quant_weight(wg_in, qg_o, sg_o, 3, D)
        quant_weight(wu_in, qu_o, su_o, 3, D)
        quant_weight(wd_in, qd_o, sdn_o, 1, I)

        xv = x_sb[:, :].rearrange("p (pt t) -> p pt t", pt=NPT)
        evac_tok = {}
        for tt in range(NTT):
            buf = tt % 2
            w = [(s_pe, evac_tok.get(tt - 2, 0))]
            tok_ld = st_x.issue("sync", lambda e, tt=tt, buf=buf: e.dma_start(
                out=xtm[:, buf * D:(buf + 1) * D],
                in_=x_in[tt * 128:(tt + 1) * 128, :]),
                waits=w, slot=buf)
            for g in range(2):
                ps = psA if g == 0 else psB2
                wv = [(s_dve, p.tok(s_dve))]
                for j in range(4):
                    pt = g * 4 + j
                    p.emit("tensor",
                           lambda e, pt=pt, j=j, buf=buf, ps=ps: e.transpose(
                               ps[:, j * 128:(j + 1) * 128],
                               xtm[:, buf * D + pt * 128:
                                   buf * D + (pt + 1) * 128],
                               idf[:, :]),
                           waits=[tok_ld] + cgate + wv,
                           inc=(s_pe, 1) if j == 3 else None)
                t_pe = p.tok(s_pe)
                dstv = xv[:, g * 4:(g + 1) * 4, tt * 128:(tt + 1) * 128]
                psv = ps[:, :].rearrange("p (j t) -> p j t", j=4)
                p.emit("vector", lambda e, dstv=dstv, psv=psv: e.tensor_copy(
                    dstv, psv), waits=[(s_pe, t_pe)])
            evac_tok[tt] = p.tok(s_pe)
        t_transposed = p.tok(s_dve)

        for pt in range(NPT):
            st_o.issue("gpsimd", lambda e, pt=pt: e.dma_start(
                out=xfm_o[pt * 128:(pt + 1) * 128, :], in_=fm(x_sb, pt)),
                waits=[(s_dve, t_transposed)])

        p.emit("scalar", lambda e: e.nop(), waits=[(s_dve, t_transposed)])
        rows = dict(s=srow, sq=sqrow, rstd=rstdrow, mu2=mu2row[:, :])
        t_xln = emit_ln(p, x_sb, lambda pt: fm(xln_sb, pt), scr, onecol_s,
                        onec, psS, psS, rows, s)
        for pt in range(NPT):
            st_o.issue("gpsimd", lambda e, pt=pt: e.dma_start(
                out=xln_o[pt * 128:(pt + 1) * 128, :], in_=fm(xln_sb, pt)),
                waits=[(s_dve, t_xln)])
        t_m = emit_absmax_fm(p, lambda pt: fm(xln_sb, pt), NPT, mx, mxt, idf,
                             psM, mrow, mval, s, extra_w=[(s_dve, t_xln)])
        st_o.issue("gpsimd", lambda e: e.dma_start(out=m1_o, in_=mval[:, :]),
                   waits=[(s_dve, t_m)])

        fin = st_o.total_tokens() + st_q.total_tokens()
        p.emit("sync", lambda e: e.nop(), waits=fin)

        block = p.es.enter_context(nc.Block())
        p.build(block)
    return nc


def emit_global_scale(p, nc, m_in, clip_in, mrow8, clipv, gv, sxv, invv,
                      sx_b, inv_b, onec, ps, s, st_c):
    s_pe, s_dve, s_act = s["pe"], s["dve"], s["act"]
    t1 = st_c.issue("sync", lambda e: e.dma_start(
        out=mrow8[:, :], in_=m_in.rearrange("a b -> b a")))
    t2 = st_c.issue("sync", lambda e: e.dma_start(out=clipv[:, :], in_=clip_in))
    p.emit("vector", lambda e: e.reduce_max(gv[:, :], mrow8[:, :], AX.X),
           waits=[t1])
    p.emit("scalar", lambda e: e.activation(clipv[:, :], clipv[:, :],
                                            AF.Sigmoid), waits=[t2])
    t_act = p.tok(s_act)
    p.emit("vector", lambda e: e.tensor_tensor(gv[:, :], gv[:, :],
                                               clipv[:, :], ALU.mult),
           waits=[(s_act, t_act)])
    p.emit("vector", lambda e: e.tensor_scalar(sxv[:, :], gv[:, :],
                                               1.0 / QMAX, None, ALU.mult))
    p.emit("vector", lambda e: e.reciprocal(invv[:, :], sxv[:, :]))
    t_dve = p.tok(s_dve)
    p.emit("tensor", lambda e: e.matmul(ps[:, 0:1], onec[:, :], sxv[:, :],
                                        start=True, stop=True),
           waits=[(s_dve, t_dve)])
    p.emit("tensor", lambda e: e.matmul(ps[:, 1:2], onec[:, :], invv[:, :],
                                        start=True, stop=True),
           inc=(s_pe, 1))
    t_pe = p.tok(s_pe)
    p.emit("vector", lambda e: e.tensor_copy(sx_b[:, :], ps[:, 0:1]),
           waits=[(s_pe, t_pe)])
    p.emit("vector", lambda e: e.tensor_copy(inv_b[:, :], ps[:, 1:2]))
    return p.tok(s_dve)


def emit_quant_tile(p, dst_bf, src_ap, inv_b, tmp, s, mcol, extra_w=()):
    s_act, s_dve = s["act"], s["dve"]
    p.emit("scalar", lambda e: e.activation(
        tmp, src_ap, AF.Relu, bias=128.25, scale=inv_b[:, 0:1]),
        waits=list(extra_w) + list(s.get("cgate", [])))
    t_act = p.tok(s_act)
    p.emit("vector", lambda e: e.tensor_scalar(
        tmp, tmp, 255.5, -128.25, ALU.min, ALU.add), waits=[(s_act, t_act)])
    p.emit("vector", lambda e: e.tensor_scalar(
        dst_bf, tmp, mcol[:, 9:10], mcol[:, 10:11], ALU.add, ALU.add))
    return p.tok(s_dve)


def build_L2():
    nc = bass.Bass("TRN2", target_bir_lowering=False)
    xln_in = dram_in(nc, "xln", [D, T])
    m_in = dram_in(nc, "m_all", [NC, 1])
    clip_in = dram_in(nc, "clip", [1, 1])
    qg_in = dram_in(nc, "qg", [I, D], BF16)
    qu_in = dram_in(nc, "qu", [I, D], BF16)
    sg_in = dram_in(nc, "sg", [I, 1])
    su_in = dram_in(nc, "su", [I, 1])
    ones1 = dram_in(nc, "ones1", [1, 128])
    iden_f = dram_in(nc, "iden_f", [128, 128])
    a_o = dram_out(nc, "a", [I, T])
    m2_o = dram_out(nc, "m2", [1, 1])

    p = Prog(nc)
    with p.es:
        sb = p.es.enter_context
        xq = sb(nc.sbuf_tensor("xq", [128, NPT * T], BF16))
        qgT = sb(nc.sbuf_tensor("qgT", [128, NPT * I], BF16))
        quT = sb(nc.sbuf_tensor("quT", [128, NPT * I], BF16))
        xln_t = sb(nc.sbuf_tensor("xln_t", [128, 2 * T], F32))
        tmp = sb(nc.sbuf_tensor("tmp", [128, T], F32))
        silu_sb = sb(nc.sbuf_tensor("silu_sb", [128, T], F32))
        a_sb = sb(nc.sbuf_tensor("a_sb", [128, 2 * T], F32))
        onec = sb(nc.sbuf_tensor("onec", [1, 128], F32))
        idf = sb(nc.sbuf_tensor("idf", [128, 128], F32))
        mrow8 = sb(nc.sbuf_tensor("mrow8", [1, NC], F32))
        clipv = sb(nc.sbuf_tensor("clipv", [1, 1], F32))
        gv = sb(nc.sbuf_tensor("gv", [1, 1], F32))
        sxv = sb(nc.sbuf_tensor("sxv", [1, 1], F32))
        invv = sb(nc.sbuf_tensor("invv", [1, 1], F32))
        sx_b = sb(nc.sbuf_tensor("sx_b", [128, 1], F32))
        inv_b = sb(nc.sbuf_tensor("inv_b", [128, 1], F32))
        sg_all = sb(nc.sbuf_tensor("sg_all", [128, NIT], F32))
        su_all = sb(nc.sbuf_tensor("su_all", [128, NIT], F32))
        sxsg = sb(nc.sbuf_tensor("sxsg", [128, NIT], F32))
        sxsu = sb(nc.sbuf_tensor("sxsu", [128, NIT], F32))
        mx = sb(nc.sbuf_tensor("mx", [128, 1], F32))
        mxt = sb(nc.sbuf_tensor("mxt", [128, 1], F32))
        mrow = sb(nc.sbuf_tensor("mrow", [1, 128], F32))
        mval = sb(nc.sbuf_tensor("mval", [1, 1], F32))
        psG = sb(nc.psum_tensor("psG", [128, T], F32))
        psU = sb(nc.psum_tensor("psU", [128, T], F32))

        p.init_chain_sems()
        s_pe = p.sem("s_pe")
        s_dve = p.chain["vector"]
        s_act = p.chain["scalar"]
        st_c = Stream(p, "stC", 4)
        st_xl = Stream(p, "stXL", 2)
        st_w = Stream(p, "stW", 4)
        st_a = Stream(p, "stA", 2)
        st_o = Stream(p, "stO", 1)
        s = dict(pe=s_pe, dve=s_dve, act=s_act)

        const_sb = setup_consts(p, nc, st_c)
        st_c.issue("sync", lambda e: e.dma_start(out=onec[:, :], in_=ones1))
        st_c.issue("sync", lambda e: e.dma_start(out=idf[:, :], in_=iden_f))
        for it in range(NIT):
            st_c.issue("sync", lambda e, it=it: e.dma_start(
                out=sg_all[:, it:it + 1],
                in_=sg_in[it * 128:(it + 1) * 128, :]))
            st_c.issue("sync", lambda e, it=it: e.dma_start(
                out=su_all[:, it:it + 1],
                in_=su_in[it * 128:(it + 1) * 128, :]))
        s["cgate"] = st_c.total_tokens()
        tok_sg = tok_su = s["cgate"][0]
        for en in ("tensor", "vector", "scalar"):
            p.emit(en, lambda e: e.nop(), waits=s["cgate"],
                   dma=(en not in p.CHAINED))

        emit_adaptive_magic(p, const_sb, s)
        t_sc = emit_global_scale(p, nc, m_in, clip_in, mrow8, clipv, gv, sxv,
                                 invv, sx_b, inv_b, onec, psG, s, st_c)
        p.emit("vector", lambda e: e.tensor_scalar(
            sxsg[:, :], sg_all[:, :], sx_b[:, 0:1], None, ALU.mult),
            waits=[(s_dve, t_sc)])
        p.emit("vector", lambda e: e.tensor_scalar(
            sxsu[:, :], su_all[:, :], sx_b[:, 0:1], None, ALU.mult))

        t_xq = [0] * NPT
        q_done = [0, 0]
        for pt in range(NPT):
            buf = pt % 2
            tok = st_xl.issue("sync", lambda e, pt=pt, buf=buf: e.dma_start(
                out=xln_t[:, buf * T:(buf + 1) * T],
                in_=xln_in[pt * 128:(pt + 1) * 128, :]),
                waits=[(s_dve, q_done[buf])], slot=buf)
            emit_quant_tile(p, fm(xq, pt), xln_t[:, buf * T:(buf + 1) * T],
                            inv_b, tmp[:, :], s, const_sb,
                            extra_w=[tok, (s_dve, p.tok(s_dve))])
            q_done[buf] = p.tok(s_dve)
            t_xq[pt] = p.tok(s_dve)

        tokg = [None] * NPT
        toku = [None] * NPT
        for m in range(NPT):
            tokg[m] = st_w.issue("scalar", lambda e, m=m:
                                 e.dma_start_transpose(
                out=qgT[:, m * I:(m + 1) * I],
                in_=qg_in[:, m * 128:(m + 1) * 128]))
            toku[m] = st_w.issue("scalar", lambda e, m=m:
                                 e.dma_start_transpose(
                out=quT[:, m * I:(m + 1) * I],
                in_=qu_in[:, m * 128:(m + 1) * 128]))

        t_silu = 0
        t_astt = 0
        ast_tok = [None, None]
        for ct in range(NIT):
            buf = ct % 2
            for m in range(NPT):
                for tk in range(T // 512):
                    p.emit("tensor", lambda e, m=m, tk=tk, ct=ct: e.matmul(
                        psG[:, tk * 512:(tk + 1) * 512],
                        qgT[:, m * I + ct * 128: m * I + (ct + 1) * 128],
                        xq[:, m * T + tk * 512: m * T + (tk + 1) * 512],
                        start=(m == 0), stop=(m == NPT - 1)),
                        waits=[tokg[m], (s_dve, t_xq[m]),
                               (s_act, t_silu)],
                        inc=(s_pe, 1) if (m, tk) == (NPT - 1, 3) else None)
            t_gate = p.tok(s_pe)
            for m in range(NPT):
                for tk in range(T // 512):
                    p.emit("tensor", lambda e, m=m, tk=tk, ct=ct: e.matmul(
                        psU[:, tk * 512:(tk + 1) * 512],
                        quT[:, m * I + ct * 128: m * I + (ct + 1) * 128],
                        xq[:, m * T + tk * 512: m * T + (tk + 1) * 512],
                        start=(m == 0), stop=(m == NPT - 1)),
                        waits=[toku[m], (s_dve, t_astt)],
                        inc=(s_pe, 1) if (m, tk) == (NPT - 1, 3) else None)
            t_up = p.tok(s_pe)
            p.emit("scalar", lambda e, ct=ct: e.activation(
                silu_sb[:, :], psG[:, :], AF.Silu,
                scale=sxsg[:, ct:ct + 1]),
                waits=[(s_pe, t_gate), (s_dve, t_astt)])
            t_silu = p.tok(s_act)
            war_st = [ast_tok[buf]] if ast_tok[buf] else []
            p.emit("vector", lambda e, ct=ct, buf=buf: e.scalar_tensor_tensor(
                a_sb[:, buf * T:(buf + 1) * T], psU[:, :],
                sxsu[:, ct:ct + 1], silu_sb[:, :], ALU.mult, ALU.mult),
                waits=[(s_pe, t_up), (s_act, t_silu)] + war_st)
            t_astt = p.tok(s_dve)
            if ct == 0:
                p.emit("vector", lambda e, buf=buf: e.reduce_max(
                    mx[:, :], a_sb[:, buf * T:(buf + 1) * T], AX.X,
                    apply_absolute_value=True))
            else:
                p.emit("vector", lambda e, buf=buf: e.reduce_max(
                    mxt[:, :], a_sb[:, buf * T:(buf + 1) * T], AX.X,
                    apply_absolute_value=True))
                p.emit("vector", lambda e: e.tensor_tensor(
                    mx[:, :], mx[:, :], mxt[:, :], ALU.max))
            t_dve = p.tok(s_dve)
            ast_tok[buf] = st_a.issue("gpsimd", lambda e, ct=ct, buf=buf:
                                      e.dma_start(
                out=a_o[ct * 128:(ct + 1) * 128, :],
                in_=a_sb[:, buf * T:(buf + 1) * T]),
                waits=[(s_dve, t_dve)], slot=buf)

        t_dve = p.tok(s_dve)
        p.emit("tensor", lambda e: e.transpose(psG[0:1, 0:128], mx[:, :],
                                               idf[:, :]),
               waits=[(s_dve, t_dve), (s_act, t_silu)], inc=(s_pe, 1))
        t_pe = p.tok(s_pe)
        p.emit("vector", lambda e: e.tensor_copy(mrow[:, :], psG[0:1, 0:128]),
               waits=[(s_pe, t_pe)])
        p.emit("vector", lambda e: e.reduce_max(mval[:, :], mrow[:, :], AX.X))
        t_dve = p.tok(s_dve)
        st_o.issue("gpsimd", lambda e: e.dma_start(out=m2_o, in_=mval[:, :]),
                   waits=[(s_dve, t_dve)])

        fin = st_o.total_tokens() + st_a.total_tokens()
        p.emit("sync", lambda e: e.nop(), waits=fin)

        block = p.es.enter_context(nc.Block())
        p.build(block)
    return nc


def build_L3(final=False):
    nc = bass.Bass("TRN2", target_bir_lowering=False)
    a_in = dram_in(nc, "a", [I, T])
    x_in = dram_in(nc, "xfm", [D, T])
    m_in = dram_in(nc, "m_all", [NC, 1])
    clip_in = dram_in(nc, "clip", [1, 1])
    qd_in = dram_in(nc, "qd", [D, I], BF16)
    sdn_in = dram_in(nc, "sdn", [D, 1])
    ones1 = dram_in(nc, "ones1", [1, 128])
    onescol = dram_in(nc, "onescol", [128, 1], BF16)
    iden_f = dram_in(nc, "iden_f", [128, 128])
    if final:
        out_o = dram_out(nc, "out", [D, T])
    else:
        x1_o = dram_out(nc, "x1", [D, T])
        xln2_o = dram_out(nc, "xln2", [D, T], BF16)
        last_o = dram_out(nc, "lastrow", [1, D], BF16)

    p = Prog(nc)
    with p.es:
        sb = p.es.enter_context
        h_sb = sb(nc.sbuf_tensor("h_sb", [128, NIT * T], BF16))
        qdT = sb(nc.sbuf_tensor("qdT", [128, NIT * D], BF16))
        io_t = sb(nc.sbuf_tensor("io_t", [128, 2 * T], F32))
        x1_t = sb(nc.sbuf_tensor("x1_t", [128, 2 * T], F32))
        tmp = sb(nc.sbuf_tensor("tmp", [128, T], F32))
        onec = sb(nc.sbuf_tensor("onec", [1, 128], F32))
        onecol_s = sb(nc.sbuf_tensor("onecol_s", [128, 1], BF16))
        idf = sb(nc.sbuf_tensor("idf", [128, 128], F32))
        mrow8 = sb(nc.sbuf_tensor("mrow8", [1, NC], F32))
        clipv = sb(nc.sbuf_tensor("clipv", [1, 1], F32))
        gv = sb(nc.sbuf_tensor("gv", [1, 1], F32))
        sdv = sb(nc.sbuf_tensor("sdv", [1, 1], F32))
        invv = sb(nc.sbuf_tensor("invv", [1, 1], F32))
        sd_b = sb(nc.sbuf_tensor("sd_b", [128, 1], F32))
        inv_b = sb(nc.sbuf_tensor("inv_b", [128, 1], F32))
        sdn_all = sb(nc.sbuf_tensor("sdn_all", [128, NPT], F32))
        halfsc = sb(nc.sbuf_tensor("halfsc", [128, NPT], F32))
        if final:
            stage = sb(nc.sbuf_tensor("stage", [128, 1024], F32))
        psD = sb(nc.psum_tensor("psD", [128, T], F32))
        psS = sb(nc.psum_tensor("psS", [128, T], F32))
        if not final:
            scr = sb(nc.sbuf_tensor("scr", [128, 2 * T], BF16))
            srow = sb(nc.sbuf_tensor("srow", [1, T], F32))
            sqrow = sb(nc.sbuf_tensor("sqrow", [1, T], F32))
            rstdrow = sqrow
            xl_t = sb(nc.sbuf_tensor("xl_t", [128, 2 * T], BF16))

        p.init_chain_sems()
        s_pe = p.sem("s_pe")
        s_dve = p.chain["vector"]
        s_act = p.chain["scalar"]
        st_c = Stream(p, "stC", 4)
        st_a = Stream(p, "stA", 2)
        st_x = Stream(p, "stX", 2)
        st_w = Stream(p, "stW", 4)
        st_o = Stream(p, "stO", 1)
        st_x1 = Stream(p, "stX1", 2)
        st_l = Stream(p, "stL", 2)
        s = dict(pe=s_pe, dve=s_dve, act=s_act)

        const_sb = setup_consts(p, nc, st_c)
        st_c.issue("sync", lambda e: e.dma_start(out=onec[:, :], in_=ones1))
        st_c.issue("sync", lambda e: e.dma_start(out=onecol_s[:, :],
                                                 in_=onescol))
        st_c.issue("sync", lambda e: e.dma_start(out=idf[:, :], in_=iden_f))
        for pt in range(NPT):
            st_c.issue("sync", lambda e, pt=pt: e.dma_start(
                out=sdn_all[:, pt:pt + 1],
                in_=sdn_in[pt * 128:(pt + 1) * 128, :]))
        s["cgate"] = st_c.total_tokens()
        for en in ("tensor", "vector", "scalar"):
            p.emit(en, lambda e: e.nop(), waits=s["cgate"],
                   dma=(en not in p.CHAINED))
        emit_adaptive_magic(p, const_sb, s)

        t_sc = emit_global_scale(p, nc, m_in, clip_in, mrow8, clipv, gv, sdv,
                                 invv, sd_b, inv_b, onec, psD, s, st_c)
        p.emit("vector", lambda e: e.tensor_scalar(
            halfsc[:, :], sdn_all[:, :], sd_b[:, 0:1], 0.5,
            ALU.mult, ALU.mult), waits=[(s_dve, t_sc)])

        tok_wd = [None] * NIT
        for it in range(NIT):
            tok_wd[it] = st_w.issue("scalar", lambda e, it=it:
                                    e.dma_start_transpose(
                out=qdT[:, it * D:(it + 1) * D],
                in_=qd_in[:, it * 128:(it + 1) * 128]))

        t_h = [0] * NIT
        for it in range(NIT):
            buf = it % 2
            tok = st_a.issue("sync", lambda e, it=it, buf=buf: e.dma_start(
                out=io_t[:, buf * T:(buf + 1) * T],
                in_=a_in[it * 128:(it + 1) * 128, :]),
                waits=[(s_dve, p.tok(s_dve))], slot=buf)
            emit_quant_tile(p, h_sb[:, it * T:(it + 1) * T],
                            io_t[:, buf * T:(buf + 1) * T],
                            inv_b, tmp[:, :], s, const_sb,
                            extra_w=[tok, (s_dve, p.tok(s_dve))])
            t_h[it] = p.tok(s_dve)
        t_hall = p.tok(s_dve)

        x1_st = [None, None]
        x1t_war = [0, 0]
        io_war = [0, 0]
        t_evac = 0
        for dt in range(NPT):
            buf = dt % 2
            tok_x = st_x.issue("sync", lambda e, dt=dt, buf=buf: e.dma_start(
                out=io_t[:, buf * T:(buf + 1) * T],
                in_=x_in[dt * 128:(dt + 1) * 128, :]),
                waits=[(s_dve, io_war[buf]), (s_act, p.tok(s_act))],
                slot=buf)
            for it in range(NIT):
                for tk in range(T // 512):
                    p.emit("tensor", lambda e, it=it, tk=tk, dt=dt: e.matmul(
                        psD[:, tk * 512:(tk + 1) * 512],
                        qdT[:, it * D + dt * 128: it * D + (dt + 1) * 128],
                        h_sb[:, it * T + tk * 512: it * T + (tk + 1) * 512],
                        start=(it == 0), stop=(it == NIT - 1)),
                        waits=[tok_wd[it], (s_dve, t_h[it]),
                               (s_dve, t_evac)],
                        inc=(s_pe, 1) if (it, tk) == (NIT - 1, 3) else None)
            t_mm = p.tok(s_pe)
            war = [x1_st[buf]] if x1_st[buf] else []
            p.emit("vector", lambda e, dt=dt, buf=buf: e.scalar_tensor_tensor(
                x1_t[:, buf * T:(buf + 1) * T], psD[:, :],
                halfsc[:, dt:dt + 1], io_t[:, buf * T:(buf + 1) * T],
                ALU.mult, ALU.add),
                waits=[(s_pe, t_mm), tok_x] + war)
            t_evac = p.tok(s_dve)
            io_war[buf] = t_evac
            if not final:
                p.emit("scalar", lambda e, buf=buf: e.copy(
                    scr[:, buf * T:(buf + 1) * T],
                    x1_t[:, buf * T:(buf + 1) * T]),
                    waits=[(s_dve, t_evac), (s_pe, p.tok(s_pe))])
                t_scp = p.tok(s_act)
                for tk in range(T // 512):
                    p.emit("tensor", lambda e, buf=buf, tk=tk, dt=dt:
                           e.matmul(
                        psS[0:1, tk * 512:(tk + 1) * 512], onecol_s[:, 0:1],
                        scr[:, buf * T + tk * 512: buf * T + (tk + 1) * 512],
                        start=(dt == 0), stop=(dt == NPT - 1)),
                        waits=[(s_act, t_scp)],
                        inc=(s_pe, 1) if tk == 3 else None)
            if final:
                x1_st[buf] = st_x1.issue(
                    "gpsimd", lambda e, dt=dt, buf=buf: e.dma_start(
                        out=out_o[dt * 128:(dt + 1) * 128, :],
                        in_=x1_t[:, buf * T:(buf + 1) * T]),
                    waits=[(s_dve, t_evac)], slot=buf)
            else:
                x1_st[buf] = st_x1.issue(
                    "gpsimd", lambda e, dt=dt, buf=buf: e.dma_start(
                        out=x1_o[dt * 128:(dt + 1) * 128, :],
                        in_=x1_t[:, buf * T:(buf + 1) * T]),
                    waits=[(s_dve, t_evac)], slot=buf)

        if not final:
            rows = dict(s=srow, sq=sqrow, rstd=rstdrow,
                        mu2=tmp[0:1, :])
            emit_ln_streamed(p, x1_o, st_x, io_t, tmp, scr, onecol_s,
                             onec, psS, psD, rows, xl_t, xln2_o, last_o,
                             st_l, x1_st, s, pre_sum=True)

        fin = (st_o.total_tokens() + st_x1.total_tokens()
               + st_l.total_tokens())
        p.emit("sync", lambda e: e.nop(), waits=fin)

        block = p.es.enter_context(nc.Block())
        p.build(block)
    return nc


def emit_ln_streamed(p, x1_o, st_x, io_t, tmp, scr, onecol_s, onec, psS,
                     psB, rows, xl_t, xln2_o, last_o, st_l, x1_st, s,
                     pre_sum=False):
    s_pe, s_dve, s_act = s["pe"], s["dve"], s["act"]
    srow, sqrow, rstdrow = rows["s"], rows["sq"], rows["rstd"]
    x1_ready = [t for t in x1_st if t]
    if pre_sum:
        t_pe0 = p.tok(s_pe)
        p.emit("vector", lambda e: e.tensor_copy(srow[:, :], psS[0:1, :]),
               waits=[(s_pe, t_pe0)])
        t_dve = p.tok(s_dve)
        p.emit("tensor", lambda e: e.nop(), waits=[(s_dve, t_dve)])
    for which in range((1 if pre_sum else 0), 2):
        for pt in range(NPT):
            buf = pt % 2
            war = [(s_act, p.tok(s_act))]
            tok = st_x.issue("sync", lambda e, pt=pt, buf=buf: e.dma_start(
                out=io_t[:, buf * T:(buf + 1) * T],
                in_=x1_o[pt * 128:(pt + 1) * 128, :]),
                waits=x1_ready + war, slot=buf)
            war2 = [(s_pe, p.tok(s_pe))]
            if which == 0:
                p.emit("scalar", lambda e, buf=buf: e.copy(
                    scr[:, buf * T:(buf + 1) * T],
                    io_t[:, buf * T:(buf + 1) * T]), waits=[tok] + war2)
            else:
                p.emit("scalar", lambda e, buf=buf: e.activation(
                    scr[:, buf * T:(buf + 1) * T],
                    io_t[:, buf * T:(buf + 1) * T], AF.Square),
                    waits=[tok] + war2)
            t_act = p.tok(s_act)
            for tk in range(T // 512):
                p.emit("tensor", lambda e, buf=buf, tk=tk, pt=pt: e.matmul(
                    psS[0:1, tk * 512:(tk + 1) * 512], onecol_s[:, 0:1],
                    scr[:, buf * T + tk * 512: buf * T + (tk + 1) * 512],
                    start=(pt == 0), stop=(pt == NPT - 1)),
                    waits=[(s_act, t_act)],
                    inc=(s_pe, 1) if tk == 3 else None)
        t_pe = p.tok(s_pe)
        dstrow = srow if which == 0 else sqrow
        p.emit("vector", lambda e, dstrow=dstrow: e.tensor_copy(
            dstrow[:, :], psS[0:1, :]), waits=[(s_pe, t_pe)])
        if which == 0:
            t_dve = p.tok(s_dve)
            p.emit("tensor", lambda e: e.nop(), waits=[(s_dve, t_dve)])
    mu2 = rows["mu2"]
    p.emit("vector", lambda e: e.tensor_scalar(
        srow[:, :], srow[:, :], 1.0 / D, None, ALU.mult))
    p.emit("vector", lambda e: e.tensor_tensor(
        mu2, srow[:, :], srow[:, :], ALU.mult))
    p.emit("vector", lambda e: e.scalar_tensor_tensor(
        sqrow[:, :], sqrow[:, :], 1.0 / D, mu2,
        ALU.mult, ALU.subtract))
    t = p.tok(s_dve)
    p.emit("scalar", lambda e: e.activation(
        sqrow[:, :], sqrow[:, :], AF.Ln, bias=EPS, scale=1.0),
        waits=[(s_dve, t)] + list(s.get("cgate", [])))
    p.emit("scalar", lambda e: e.activation(
        rstdrow[:, :], sqrow[:, :], AF.Exp, bias=0.0, scale=-0.5))
    t_rows = p.tok(s_act)
    wv = [(s_dve, p.tok(s_dve))]
    for tk in range(T // 512):
        p.emit("tensor", lambda e, tk=tk: e.matmul(
            psS[:, tk * 512:(tk + 1) * 512], onec[:, :],
            srow[:, tk * 512:(tk + 1) * 512], start=True, stop=True),
            waits=[(s_act, t_rows)] + wv)
        p.emit("tensor", lambda e, tk=tk: e.matmul(
            psB[:, tk * 512:(tk + 1) * 512], onec[:, :],
            rstdrow[:, tk * 512:(tk + 1) * 512], start=True, stop=True),
            inc=(s_pe, 1) if tk == 3 else None)
    t_bc = p.tok(s_pe)
    prev_q = 0
    st_tok = [None, None]
    for pt in range(NPT):
        buf = pt % 2
        tok = st_x.issue("sync", lambda e, pt=pt, buf=buf: e.dma_start(
            out=io_t[:, buf * T:(buf + 1) * T],
            in_=x1_o[pt * 128:(pt + 1) * 128, :]),
            waits=[(s_dve, prev_q)], slot=buf)
        p.emit("vector", lambda e, buf=buf: e.tensor_tensor(
            tmp[:, :], io_t[:, buf * T:(buf + 1) * T], psS[:, :],
            ALU.subtract), waits=[tok, (s_pe, t_bc)])
        war = [st_tok[buf]] if st_tok[buf] else []
        p.emit("vector", lambda e, buf=buf: e.tensor_tensor(
            xl_t[:, buf * T:(buf + 1) * T], tmp[:, :], psB[:, :], ALU.mult),
            waits=war)
        prev_q = p.tok(s_dve)
        t_dve = p.tok(s_dve)
        st_l.issue("gpsimd", lambda e, pt=pt, buf=buf: e.dma_start(
            out=xln2_o[pt * 128:(pt + 1) * 128, :],
            in_=xl_t[:, buf * T:(buf + 1) * T]),
            waits=[(s_dve, t_dve)], slot=buf)
        st_tok[buf] = st_l.issue("gpsimd", lambda e, pt=pt, buf=buf:
                                 e.dma_start(
            out=last_o[0:1, pt * 128:(pt + 1) * 128],
            in_=xl_t[:, buf * T + T - 1: buf * T + T]),
            waits=[(s_dve, t_dve)], slot=buf)


def build_L4(nsc=NSC, cut=99):
    nc = bass.Bass("TRN2", target_bir_lowering=False)
    xln2_in = dram_in(nc, "xln2", [D, T], BF16)
    last_in = dram_in(nc, "lastprev", [1, D], BF16)
    w_r = dram_in(nc, "w_r", [D, D])
    w_k = dram_in(nc, "w_k", [D, D])
    w_v = dram_in(nc, "w_v", [D, D])
    w_g = dram_in(nc, "w_g", [D, D])
    tmix = dram_in(nc, "tmix", [4 * D, 1])
    tdec = dram_in(nc, "tdec", [1, D])
    ones1 = dram_in(nc, "ones1", [1, 128])
    iden_b = dram_in(nc, "iden_b", [128, 128], BF16)
    maskc_in = dram_in(nc, "maskc", [128, 16 * 128], BF16)
    parity_in = dram_in(nc, "parity", [128, D], BF16)
    iotas = dram_in(nc, "iotas", [1, 6 * 128])
    core0 = dram_in(nc, "core0", [1, 1])

    attn_o = dram_out(nc, "attnout", [T, D], BF16)
    g_o = dram_out(nc, "g_out", [T, D], BF16)
    r_o = dram_out(nc, "r_out", [D, T], BF16)
    st_o_d = dram_out(nc, "st_out", [DH, D])

    wscr = nc.dram_tensor("wscr", [D, D], BF16, kind="Internal").ap()

    p = Prog(nc)
    with p.es:
        sb = p.es.enter_context
        mix = sb(nc.sbuf_tensor("mix", [128, NPT * T], BF16))
        xln_t = sb(nc.sbuf_tensor("xln_t", [128, 2 * T], BF16))
        wT = sb(nc.sbuf_tensor("wT", [128, NPT * D], BF16))
        r_fm = sb(nc.sbuf_tensor("r_fm", [128, NPT * T], BF16))
        k_tm = sb(nc.sbuf_tensor("k_tm", [128, NTT * D], BF16))
        v_tm = sb(nc.sbuf_tensor("v_tm", [128, NTT * D], BF16))
        g_t = sb(nc.sbuf_tensor("g_t", [128, 2 * D], BF16))
        sc_col = sb(nc.sbuf_tensor("sc_col", [128, 4], F32))
        attn_t = sb(nc.sbuf_tensor("attn_t", [128, 2 * D], BF16))
        prevcol = sb(nc.sbuf_tensor("prevcol", [128, NPT], BF16))
        tmix_s = sb(nc.sbuf_tensor("tmix_s", [128, 4 * NPT], F32))
        onec = sb(nc.sbuf_tensor("onec", [1, 128], F32))
        idb = sb(nc.sbuf_tensor("idb", [128, 128], BF16))
        maskc = sb(nc.sbuf_tensor("maskc_sb", [128, 16 * 128], BF16))
        iot = sb(nc.sbuf_tensor("iot", [1, 6 * 128], F32))
        lnwrow = sb(nc.sbuf_tensor("lnwrow", [1, D], F32))
        c0v = sb(nc.sbuf_tensor("c0v", [1, 1], F32))
        c0b = sb(nc.sbuf_tensor("c0b", [128, 1], F32))
        P2 = sb(nc.sbuf_tensor("P2", [128, D], BF16))
        QT1 = sb(nc.sbuf_tensor("QT1", [128, D], BF16))
        QA = sb(nc.sbuf_tensor("QA", [128, D], BF16))
        QKH = sb(nc.sbuf_tensor("QKH", [128, D], BF16))
        W1f = sb(nc.sbuf_tensor("W1f", [128, D], BF16))
        S2 = sb(nc.sbuf_tensor("S2", [128, D], F32))
        S2t = sb(nc.sbuf_tensor("S2t", [128, D], BF16))
        T1k = sb(nc.sbuf_tensor("T1k", [128, D], BF16))
        T2k = sb(nc.sbuf_tensor("T2k", [128, D], BF16))
        KHk = sb(nc.sbuf_tensor("KHk", [128, D], BF16))
        Amk = sb(nc.sbuf_tensor("Amk", [128, 2 * D], BF16))
        Amk2 = sb(nc.sbuf_tensor("Amk2", [128, D], BF16))
        vfm2 = sb(nc.sbuf_tensor("vfm2", [128, 2 * D], BF16))
        parity = sb(nc.sbuf_tensor("parity_sb", [128, D], BF16))
        psA_ = sb(nc.psum_tensor("psA_", [128, D], F32))
        psB_ = sb(nc.psum_tensor("psB_", [128, D], F32))
        psC_ = sb(nc.psum_tensor("psC_", [128, D], F32))
        psT_ = sb(nc.psum_tensor("psT_", [128, D], BF16))

        p.init_chain_sems()
        s_pe = p.sem("s_pe")
        s_dve = p.chain["vector"]
        s_act = p.chain["scalar"]
        st_c = Stream(p, "stC", 4)
        st_x = Stream(p, "stX", 2)
        st_wc = Stream(p, "stWC", 1)
        st_w = Stream(p, "stW", 4)
        st_ao = Stream(p, "stAO", 2)
        st_go = Stream(p, "stGO", 2)
        st_ro = Stream(p, "stRO", 2)
        st_o = Stream(p, "stO", 1)
        s = dict(pe=s_pe, dve=s_dve, act=s_act)

        setup_consts(p, nc, st_c)
        st_c.issue("sync", lambda e: e.dma_start(out=onec[:, :], in_=ones1))
        st_c.issue("sync", lambda e: e.dma_start(out=idb[:, :], in_=iden_b))
        st_c.issue("sync", lambda e: e.dma_start(out=maskc[:, :],
                                                 in_=maskc_in))
        st_c.issue("sync", lambda e: e.dma_start(out=parity[:, :],
                                                 in_=parity_in))
        st_c.issue("sync", lambda e: e.dma_start(out=c0v[:, :], in_=core0))
        for i in range(4 * NPT):
            st_c.issue("sync", lambda e, i=i: e.dma_start(
                out=tmix_s[:, i:i + 1], in_=tmix[i * 128:(i + 1) * 128, :]))
        for pt in range(NPT):
            st_c.issue("sync", lambda e, pt=pt: e.dma_start(
                out=prevcol[:, pt:pt + 1],
                in_=last_in[0:1, pt * 128:(pt + 1) * 128]))
        cgate = st_c.total_tokens()
        s["cgate"] = cgate
        for en in ("tensor", "vector", "scalar"):
            p.emit(en, lambda e: e.nop(), waits=cgate,
                   dma=(en not in p.CHAINED))

        p.emit("scalar", lambda e: e.activation(lnwrow[:, :], lnwrow[:, :],
                                                AF.Exp))
        t_exp = p.tok(s_act)
        p.emit("vector", lambda e: e.tensor_scalar(lnwrow[:, :], lnwrow[:, :],
                                                   -1.0, None, ALU.mult),
               waits=[(s_act, t_exp)])
        t_lnw = p.tok(s_dve)
        pats = [(P2, 0), (QT1, 1), (QA, 2), (QKH, 3), (W1f, 4)]
        for (dst, idx) in pats:
            wv = [(s_act, p.tok(s_act))]
            for half in range(2):
                p.emit("tensor", lambda e, idx=idx, half=half: e.matmul(
                    psA_[:, half * 512:(half + 1) * 512],
                    iot[:, idx * 128:(idx + 1) * 128],
                    lnwrow[:, half * 512:(half + 1) * 512],
                    start=True, stop=True),
                    waits=[(s_dve, t_lnw)] + wv,
                    inc=(s_pe, 1) if half == 1 else None)
            t_pe = p.tok(s_pe)
            p.emit("scalar", lambda e, dst=dst: e.activation(
                dst[:, :], psA_[:, :], AF.Exp), waits=[(s_pe, t_pe)])
        t_pats = p.tok(s_act)
        p.emit("vector", lambda e: e.tensor_tensor(
            W1f[:, :], W1f[:, :], parity[:, :], ALU.mult),
            waits=[(s_act, t_pats)])
        p.emit("vector", lambda e: e.memset(vfm2[:, :], 0.0))
        p.emit("vector", lambda e: e.memset(Amk2[:, :], 0.0))
        t_pats_v = p.tok(s_dve)
        p.emit("tensor", lambda e: e.matmul(
            psB_[:, 0:1], onec[:, :], c0v[:, :], start=True, stop=True),
            inc=(s_pe, 1))
        t_pe = p.tok(s_pe)
        p.emit("vector", lambda e: e.tensor_copy(c0b[:, :], psB_[:, 0:1]),
               waits=[(s_pe, t_pe)])
        t_c0 = p.tok(s_dve)

        prev_wtoks = []

        def prep_weight(w_in, widx):
            nonlocal_toks = prev_wtoks[:]
            tok_cast = st_wc.issue("gpsimd", lambda e, w_in=w_in: e.dma_start(
                out=wscr, in_=w_in), waits=nonlocal_toks)
            toks = []
            for m in range(NPT):
                toks.append(st_w.issue("scalar", lambda e, m=m:
                                       e.dma_start_transpose(
                    out=wT[:, m * D:(m + 1) * D],
                    in_=wscr[:, m * 128:(m + 1) * 128]),
                    waits=[tok_cast, (s_pe, p.tok(s_pe))]))
            prev_wtoks[:] = toks
            return toks

        def build_mix(widx, t_gate_mix):
            t_pe_war = p.tok(s_pe)
            for pt in range(NPT):
                buf = pt % 2
                tok = st_x.issue("sync", lambda e, pt=pt, buf=buf:
                                 e.dma_start(
                    out=xln_t[:, buf * T:(buf + 1) * T],
                    in_=xln2_in[pt * 128:(pt + 1) * 128, :]),
                    waits=[(s_dve, t_gate_mix[buf])], slot=buf)
                p.emit("vector", lambda e, pt=pt, buf=buf: e.tensor_tensor(
                    mix[:, pt * T + 1: pt * T + T],
                    xln_t[:, buf * T: buf * T + T - 1],
                    xln_t[:, buf * T + 1: buf * T + T], ALU.subtract),
                    waits=[tok, (s_pe, t_pe_war)])
                p.emit("vector", lambda e, pt=pt, buf=buf: e.tensor_tensor(
                    mix[:, pt * T: pt * T + 1], prevcol[:, pt:pt + 1],
                    xln_t[:, buf * T: buf * T + 1], ALU.subtract))
                p.emit("vector", lambda e, pt=pt, buf=buf: e.tensor_tensor(
                    sc_col[:, 0:1], xln_t[:, buf * T + 1: buf * T + 2],
                    prevcol[:, pt:pt + 1], ALU.subtract),
                    waits=[(s_dve, t_c0)])
                p.emit("vector", lambda e, pt=pt: e.scalar_tensor_tensor(
                    mix[:, pt * T: pt * T + 1], sc_col[:, 0:1], c0b[:, 0:1],
                    mix[:, pt * T: pt * T + 1], ALU.mult, ALU.add))
                p.emit("vector", lambda e, pt=pt, buf=buf, widx=widx:
                       e.scalar_tensor_tensor(
                    fm(mix, pt), fm(mix, pt),
                    tmix_s[:, widx * NPT + pt: widx * NPT + pt + 1],
                    xln_t[:, buf * T:(buf + 1) * T], ALU.mult, ALU.add))
                t_gate_mix[buf] = p.tok(s_dve)
            return p.tok(s_dve)

        t_gm = [0, 0]
        tok_w = prep_weight(w_g, 3)
        t_mix = build_mix(3, t_gm)
        g_st = [None, None]
        t_evK = 0
        for tt in range(NTT):
            buf = tt % 2
            for m in range(NPT):
                for half in range(2):
                    p.emit("tensor", lambda e, m=m, half=half, tt=tt:
                           e.matmul(
                        psC_[:, half * 512:(half + 1) * 512],
                        fm(mix, m)[:, tt * 128:(tt + 1) * 128],
                        wT[:, m * D + half * 512: m * D + (half + 1) * 512],
                        start=(m == 0), stop=(m == NPT - 1)),
                        waits=tok_w + [(s_dve, t_mix), (s_act, t_evK)],
                        inc=(s_pe, 1) if (m, half) == (NPT - 1, 1) else None)
            t_mm = p.tok(s_pe)
            war = [g_st[buf]] if g_st[buf] else []
            p.emit("scalar", lambda e, buf=buf: e.activation(
                g_t[:, buf * D:(buf + 1) * D], psC_[:, :], AF.Sigmoid),
                waits=[(s_pe, t_mm)] + war)
            t_evK = p.tok(s_act)
            g_st[buf] = st_go.issue("gpsimd", lambda e, tt=tt, buf=buf:
                                    e.dma_start(
                out=g_o[tt * 128:(tt + 1) * 128, :],
                in_=g_t[:, buf * D:(buf + 1) * D]),
                waits=[(s_act, t_evK)], slot=buf)

        tok_w = prep_weight(w_r, 0)
        t_mix = build_mix(0, t_gm)
        r_st = [None, None]
        t_evR = 0
        for ct in range(NPT):
            buf = ct % 2
            for m in range(NPT):
                for tk in range(T // 512):
                    ps = psA_ if tk < 2 else psB_
                    p.emit("tensor", lambda e, m=m, tk=tk, ct=ct, ps=ps:
                           e.matmul(
                        ps[:, (tk % 2) * 512:((tk % 2) + 1) * 512],
                        wT[:, m * D + ct * 128: m * D + (ct + 1) * 128],
                        fm(mix, m)[:, tk * 512:(tk + 1) * 512],
                        start=(m == 0), stop=(m == NPT - 1)),
                        waits=tok_w + [(s_dve, t_mix), (s_dve, t_evR)],
                        inc=(s_pe, 1) if (m, tk) == (NPT - 1, 3) else None)
            t_mm = p.tok(s_pe)
            p.emit("vector", lambda e, ct=ct: e.tensor_copy(
                fm(r_fm, ct)[:, 0:1024], psA_[:, :]), waits=[(s_pe, t_mm)])
            p.emit("vector", lambda e, ct=ct: e.tensor_copy(
                fm(r_fm, ct)[:, 1024:2048], psB_[:, :]))
            t_evR = p.tok(s_dve)
            r_st[buf] = st_ro.issue("gpsimd", lambda e, ct=ct: e.dma_start(
                out=r_o[ct * 128:(ct + 1) * 128, :], in_=fm(r_fm, ct)),
                waits=[(s_dve, t_evR)], slot=buf)

        def proj_tm(w_in, widx, dst):
            tok_w = prep_weight(w_in, widx)
            t_mix = build_mix(widx, t_gm)
            t_ev = 0
            for tt in range(NTT):
                for m in range(NPT):
                    for half in range(2):
                        p.emit("tensor", lambda e, m=m, half=half, tt=tt:
                               e.matmul(
                            psC_[:, half * 512:(half + 1) * 512],
                            fm(mix, m)[:, tt * 128:(tt + 1) * 128],
                            wT[:, m * D + half * 512:
                               m * D + (half + 1) * 512],
                            start=(m == 0), stop=(m == NPT - 1)),
                            waits=tok_w + [(s_dve, t_mix), (s_dve, t_ev)],
                            inc=(s_pe, 1) if (m, half) == (NPT - 1, 1)
                            else None)
                t_mm = p.tok(s_pe)
                p.emit("vector", lambda e, tt=tt, dst=dst: e.tensor_copy(
                    dst[:, tt * D:(tt + 1) * D], psC_[:, :]),
                    waits=[(s_pe, t_mm)])
                t_ev = p.tok(s_dve)
            return t_ev

        t_k = proj_tm(w_k, 1, k_tm)
        t_v = proj_tm(w_v, 2, v_tm)

        p.emit("vector", lambda e: e.memset(S2[:, :], 0.0))
        p.emit("vector", lambda e: e.memset(S2t[:, :], 0.0))
        t_scan_pre = p.tok(s_dve)
        ao_st = [None, None]
        dbg_st = []
        for sc in range(nsc):
            buf = sc % 2
            Ksc = k_tm[:, sc * D:(sc + 1) * D]
            Vsc = v_tm[:, sc * D:(sc + 1) * D]
            p.emit("vector", lambda e, Ksc=Ksc: e.tensor_tensor(
                T1k[:, :], Ksc, QT1[:, :], ALU.mult),
                waits=[(s_dve, t_k), (s_dve, t_v), (s_act, t_pats),
                       (s_pe, p.tok(s_pe))] + dbg_st)
            p.emit("vector", lambda e, Ksc=Ksc: e.tensor_tensor(
                T2k[:, :], Ksc, QA[:, :], ALU.mult))
            p.emit("vector", lambda e, Ksc=Ksc: e.tensor_tensor(
                KHk[:, :], Ksc, QKH[:, :], ALU.mult))
            t_kvar = p.tok(s_dve)
            if cut < 2:
                continue
            wv = [(s_dve, p.tok(s_dve))]
            for grp in range(NPT):
                p.emit("tensor", lambda e, grp=grp, Vsc=Vsc: e.transpose(
                    psT_[:, grp * 128:(grp + 1) * 128],
                    Vsc[:, grp * 128:(grp + 1) * 128], idb[:, :]),
                    waits=wv + [(s_dve, t_v)],
                    inc=(s_pe, 1) if grp == NPT - 1 else None)
            t_tr = p.tok(s_pe)
            ve = (vfm2[0:64, :].rearrange("p (g two s) -> p g two s",
                                          two=2, s=128)[:, :, 0, :])
            vo = (vfm2[64:128, :].rearrange("p (g two s) -> p g two s",
                                            two=2, s=128)[:, :, 1, :])
            pe_ = psT_[0:64, :].rearrange("p (g s) -> p g s", s=128)
            po_ = psT_[64:128, :].rearrange("p (g s) -> p g s", s=128)
            p.emit("vector", lambda e, ve=ve, pe_=pe_: e.tensor_copy(ve, pe_),
                   waits=[(s_pe, t_tr)])
            p.emit("vector", lambda e, vo=vo, po_=po_: e.tensor_copy(vo, po_))
            t_vfm = p.tok(s_dve)
            if cut < 3:
                continue
            for gidx in range(2):
                war_ev = [(s_dve, p.tok(s_dve))]
                for hh in range(8):
                    h = gidx * 8 + hh
                    p.emit("tensor", lambda e, h=h, hh=hh, sc=sc: e.matmul(
                        psA_[:, hh * 128:(hh + 1) * 128],
                        vfm2[:, h * 128:(h + 1) * 128],
                        r_fm[:, (h // 2) * T + sc * 128:
                             (h // 2) * T + (sc + 1) * 128],
                        start=True, stop=True),
                        waits=[(s_dve, t_vfm)] + war_ev,
                        inc=(s_pe, 1) if hh == 7 else None)
                t_pe2 = p.tok(s_pe)
                p.emit("vector", lambda e, gidx=gidx: e.tensor_tensor(
                    Amk[:, gidx * D:(gidx + 1) * D], psA_[:, :],
                    maskc[:, gidx * D:(gidx + 1) * D], ALU.mult),
                    waits=[(s_pe, t_pe2)])
                a2d = (Amk2[0:64, gidx * 512:(gidx + 1) * 512]
                       .rearrange("p (g s) -> p g s", s=64))
                a2s = (psA_[0:64, :].rearrange("p (g two s) -> p g two s",
                                               two=2, s=64)[:, :, 1, :])
                p.emit("vector", lambda e, a2d=a2d, a2s=a2s:
                       e.tensor_copy(a2d, a2s))
            t_am = p.tok(s_dve)
            if cut < 4:
                continue
            for h in range(H):
                hc = h * 128
                hi = h * DH
                p.emit("tensor", lambda e, h=h, hi=hi, sc=sc: e.matmul(
                    psA_[0:64, hi:hi + DH],
                    r_fm[:, (h // 2) * T + sc * 128:
                         (h // 2) * T + sc * 128 + 64],
                    S2t[:, hi:hi + DH],
                    start=True, stop=True, skip_group_check=True),
                    waits=[(s_dve, t_am)])
                p.emit("tensor", lambda e, h=h, hi=hi: e.matmul(
                    psA_[64:128, hi:hi + DH],
                    Amk2[:, h * 64:(h + 1) * 64], T2k[:, hi:hi + DH],
                    start=True, stop=True,
                    tile_position=(0, 64), skip_group_check=True))
                p.emit("tensor", lambda e, h=h, hc=hc, hi=hi: e.matmul(
                    psB_[0:64, hi:hi + DH],
                    Amk[:, hc:hc + 64], T1k[:, hi:hi + DH],
                    start=True, stop=True, skip_group_check=True))
                p.emit("tensor", lambda e, h=h, hc=hc, hi=hi: e.matmul(
                    psB_[64:128, hi:hi + DH],
                    Amk[:, hc + 64:hc + 128], T1k[:, hi:hi + DH],
                    start=True, stop=True, tile_position=(0, 64),
                    skip_group_check=True),
                    inc=(s_pe, 1) if h == H - 1 else None)
            t_o1 = p.tok(s_pe)
            dbg_toks = []
            if cut < 5:
                continue
            war = [ao_st[buf]] if ao_st[buf] else []
            p.emit("vector", lambda e, buf=buf: e.tensor_copy(
                attn_t[:, buf * D:(buf + 1) * D], psA_[:, :]),
                waits=[(s_pe, t_o1)] + war)
            p.emit("vector", lambda e, buf=buf: e.tensor_tensor(
                attn_t[:, buf * D:(buf + 1) * D], psB_[:, :],
                attn_t[:, buf * D:(buf + 1) * D], ALU.add))
            p.emit("vector", lambda e, buf=buf: e.tensor_tensor(
                attn_t[:, buf * D:(buf + 1) * D],
                attn_t[:, buf * D:(buf + 1) * D], P2[:, :], ALU.mult))
            t_ao = p.tok(s_dve)
            ao_st[buf] = st_ao.issue("gpsimd", lambda e, sc=sc, buf=buf:
                                     e.dma_start(
                out=attn_o[sc * 128:(sc + 1) * 128, :],
                in_=attn_t[:, buf * D:(buf + 1) * D]),
                waits=[(s_dve, t_ao)], slot=buf)
            if cut < 6:
                continue
            for h in range(H):
                hi = h * DH
                p.emit("tensor", lambda e, h=h, hi=hi, Vsc=Vsc: e.matmul(
                    psC_[0:64, hi:hi + DH], Vsc[:, hi:hi + DH],
                    KHk[:, hi:hi + DH], start=True, stop=True),
                    waits=[(s_dve, t_kvar)],
                    inc=(s_pe, 1) if h == H - 1 else None)
            t_up = p.tok(s_pe)
            if cut < 7:
                continue
            p.emit("vector", lambda e: e.tensor_copy(S2[0:64, :],
                                                     psC_[0:64, :]),
                   waits=[(s_pe, t_up)] + dbg_toks)
            p.emit("vector", lambda e: e.tensor_copy(S2[64:128, :],
                                                     psC_[0:64, :]))
            p.emit("vector", lambda e: e.tensor_tensor(
                S2t[:, :], S2[:, :], W1f[:, :], ALU.mult))
        t_fin = p.tok(s_dve)
        if nsc > 0:
            st_o.issue("gpsimd", lambda e: e.dma_start(
                out=st_o_d, in_=S2[0:64, :]), waits=[(s_dve, t_fin)])

        fin = (st_o.total_tokens() + st_ao.total_tokens()
               + st_go.total_tokens() + st_ro.total_tokens())
        p.emit("sync", lambda e: e.nop(), waits=fin)

        block = p.es.enter_context(nc.Block())
        p.build(block)
    return nc




TH = 17 * 128
NSC5 = 17
THX = TH + 1


def fmh(x_sb, pt, c0=0, n=TH):
    return x_sb[:, pt * TH + c0: pt * TH + c0 + n]


def build_L45():
    nc = bass.Bass("TRN2", target_bir_lowering=False)
    xh_in = dram_in(nc, "xln2h", [D, THX], BF16)
    x1_in = dram_in(nc, "x1", [D, T])
    w_r = dram_in(nc, "w_r", [D, D])
    w_k = dram_in(nc, "w_k", [D, D])
    w_v = dram_in(nc, "w_v", [D, D])
    w_g = dram_in(nc, "w_g", [D, D])
    w_o = dram_in(nc, "w_o", [D, D])
    tmix = dram_in(nc, "tmix", [4 * D, 1])
    tdec = dram_in(nc, "tdec", [1, D])
    lng_in = dram_in(nc, "ln_g", [D, 1])
    lnb_in = dram_in(nc, "ln_b", [D, 1])
    ones1 = dram_in(nc, "ones1", [1, 128])
    onescol = dram_in(nc, "onescol", [128, 1], BF16)
    iden_b = dram_in(nc, "iden_b", [128, 128], BF16)
    iden_f = dram_in(nc, "iden_f", [128, 128])
    maskc_in = dram_in(nc, "maskc", [128, 16 * 128], BF16)
    parity_in = dram_in(nc, "parity", [128, D], BF16)
    iotas = dram_in(nc, "iotas", [1, 6 * 128])
    core0 = dram_in(nc, "core0", [1, 1])

    x2_o = dram_out(nc, "x2", [D, T])
    xln4_o = dram_out(nc, "xln4", [D, T])
    m3_o = dram_out(nc, "m3", [1, 1])

    wscr = nc.dram_tensor("wscr45", [D, D], BF16, kind="Internal").ap()
    wscr2 = nc.dram_tensor("wscr45b", [D, D], BF16, kind="Internal").ap()
    g_int = nc.dram_tensor("g_int", [T, D], BF16, kind="Internal").ap()

    p = Prog(nc)
    with p.es:
        sb = p.es.enter_context
        attn_res = sb(nc.sbuf_tensor("attn_res", [128, NTT * D], BF16))
        tmix_s = sb(nc.sbuf_tensor("tmix_s", [128, 4 * NPT], F32))
        sc_col = sb(nc.sbuf_tensor("sc_col", [128, 4], F32))
        onec = sb(nc.sbuf_tensor("onec", [1, 128], F32))
        onecol_s = sb(nc.sbuf_tensor("onecol_s", [128, 1], BF16))
        idb = sb(nc.sbuf_tensor("idb", [128, 128], BF16))
        lng = sb(nc.sbuf_tensor("lng", [128, NPT], F32))
        lnb = sb(nc.sbuf_tensor("lnb", [128, NPT], F32))
        c0v = sb(nc.sbuf_tensor("c0v", [1, 1], F32))
        c0b = sb(nc.sbuf_tensor("c0b", [128, 1], F32))
        psA_ = sb(nc.psum_tensor("psA_", [128, D], F32))
        psT_ = sb(nc.psum_tensor("psT_", [128, D], BF16))

        p.init_chain_sems()
        s_pe = p.sem("s_pe")
        s_dve = p.chain["vector"]
        s_act = p.chain["scalar"]
        st_c = Stream(p, "stC", 4)
        st_x = Stream(p, "stX", 2)
        st_wc = Stream(p, "stWC", 1)
        st_w = Stream(p, "stW", 4)
        st_go = Stream(p, "stGO", 2)
        st_gl = Stream(p, "stGL", 2)
        st_x1 = Stream(p, "stX1", 2)
        st_o = Stream(p, "stO", 2)
        st_m = Stream(p, "stM", 1)
        s_gps = p.chain["gpsimd"]
        s = dict(pe=s_pe, dve=s_dve, act=s_act)

        const_sb = setup_consts(p, nc, st_c)
        st_c.issue("sync", lambda e: e.dma_start(out=onec[:, :], in_=ones1))
        st_c.issue("sync", lambda e: e.dma_start(out=onecol_s[:, :],
                                                 in_=onescol))
        st_c.issue("sync", lambda e: e.dma_start(out=idb[:, :], in_=iden_b))
        st_c.issue("sync", lambda e: e.dma_start(out=c0v[:, :], in_=core0))
        for i in range(4 * NPT):
            st_c.issue("sync", lambda e, i=i: e.dma_start(
                out=tmix_s[:, i:i + 1], in_=tmix[i * 128:(i + 1) * 128, :]))
        for pt in range(NPT):
            st_c.issue("sync", lambda e, pt=pt: e.dma_start(
                out=lng[:, pt:pt + 1], in_=lng_in[pt * 128:(pt + 1) * 128, :]))
            st_c.issue("sync", lambda e, pt=pt: e.dma_start(
                out=lnb[:, pt:pt + 1], in_=lnb_in[pt * 128:(pt + 1) * 128, :]))
        cgate = st_c.total_tokens()
        s["cgate"] = cgate
        for en in ("tensor", "vector", "scalar", "gpsimd"):
            p.emit(en, lambda e: e.nop(), waits=cgate,
                   dma=(en not in p.CHAINED))

        p.emit("tensor", lambda e: e.matmul(
            psA_[:, 0:1], onec[:, :], c0v[:, :], start=True, stop=True),
            inc=(s_pe, 1))
        t_pe = p.tok(s_pe)
        p.emit("vector", lambda e: e.tensor_copy(c0b[:, :], psA_[:, 0:1]),
               waits=[(s_pe, t_pe)])
        t_c0 = p.tok(s_dve)

        esB = contextlib.ExitStack()
        sbB = esB.enter_context
        r_fm = sbB(nc.sbuf_tensor("r_fm", [128, NPT * T], BF16))
        k_tm = sbB(nc.sbuf_tensor("k_tm", [128, NSC5 * D], BF16))
        mix = sbB(nc.sbuf_tensor("mix", [128, NPT * TH], BF16))
        g_t = sbB(nc.sbuf_tensor("g_t", [128, 2 * D], BF16))
        psB_ = sbB(nc.psum_tensor("psB_", [128, D], F32))
        psC_ = sbB(nc.psum_tensor("psC_", [128, D], F32))

        esB1 = contextlib.ExitStack()
        sb1 = esB1.enter_context
        vmix = sb1(nc.sbuf_tensor("vmix", [128, NPT * TH], BF16))
        xln_t = sb1(nc.sbuf_tensor("xln_t", [128, THX], BF16))
        wT = sb1(nc.sbuf_tensor("wT", [128, 2 * NPT * D], BF16))

        wscrs = [wscr, wscr2]
        widx_ctr = [0]
        proj_done = [0, 0]

        def prep_weight(w_in):
            wb = widx_ctr[0] % 2
            widx_ctr[0] += 1
            tok_cast = st_wc.issue("gpsimd", lambda e, w_in=w_in, wb=wb:
                                   e.dma_start(
                out=wscrs[wb], in_=w_in))
            toks = []
            for m in range(NPT):
                toks.append(st_w.issue("scalar", lambda e, m=m, wb=wb:
                                       e.dma_start_transpose(
                    out=wT[:, wb * NPT * D + m * D: wb * NPT * D + (m + 1) * D],
                    in_=wscrs[wb][:, m * 128:(m + 1) * 128]),
                    waits=[tok_cast, (s_pe, proj_done[wb])]))
            return toks, wb

        t_mix_pt = {}

        def build_mix(widx, t_gate_mix, mb, t_pe_war):
            for pt in range(NPT):
                tok = st_x.issue("sync", lambda e, pt=pt: e.dma_start(
                    out=xln_t[:, :],
                    in_=xh_in[pt * 128:(pt + 1) * 128, :]),
                    waits=[(s_dve, t_gate_mix[0]), (s_gps, t_gm_g[0])],
                    slot=0)
                p.emit("gpsimd", lambda e, pt=pt, mb=mb: e.tensor_tensor(
                    fmh(mb, pt),
                    xln_t[:, 0:TH],
                    xln_t[:, 1:1 + TH],
                    ALU.subtract),
                    waits=[tok, (s_pe, t_pe_war),
                           (s_dve, t_mix_pt.get((id(mb), pt), 0))])
                t_sub = p.tok(s_gps)
                p.emit("vector", lambda e, pt=pt: e.tensor_tensor(
                    sc_col[:, 0:1], xln_t[:, 130:131],
                    xln_t[:, 128:129], ALU.subtract),
                    waits=[(s_dve, t_c0), tok])
                p.emit("vector", lambda e, pt=pt, mb=mb:
                       e.scalar_tensor_tensor(
                    fmh(mb, pt, 128, 1), sc_col[:, 0:1], c0b[:, 0:1],
                    fmh(mb, pt, 128, 1), ALU.mult, ALU.add),
                    waits=[(s_gps, t_sub)])
                p.emit("vector", lambda e, pt=pt, widx=widx, mb=mb:
                       e.scalar_tensor_tensor(
                    fmh(mb, pt), fmh(mb, pt),
                    tmix_s[:, widx * NPT + pt: widx * NPT + pt + 1],
                    xln_t[:, 1:1 + TH],
                    ALU.mult, ALU.add))
                t_gate_mix[0] = p.tok(s_dve)
                t_gm_g[0] = p.tok(s_gps)
                t_mix_pt[(id(mb), pt)] = p.tok(s_dve)
            return p.tok(s_dve)

        t_gm = [0, 0]
        t_gm_g = [0, 0]
        tok_w, wb = prep_weight(w_g)
        t_mix = build_mix(3, t_gm, mix, 0)
        gmix = mix
        g_st = [None, None]
        t_evK = [0, 0]
        for tt in range(1, NSC5):
            buf = tt % 2
            ps = psC_ if buf == 0 else psB_
            for m in range(NPT):
                for half in range(2):
                    p.emit("tensor", lambda e, m=m, half=half, tt=tt, ps=ps,
                           wb=wb: e.matmul(
                        ps[:, half * 512:(half + 1) * 512],
                        fmh(gmix, m)[:, tt * 128:(tt + 1) * 128],
                        wT[:, wb * NPT * D + m * D + half * 512:
                           wb * NPT * D + m * D + (half + 1) * 512],
                        start=(m == 0), stop=(m == NPT - 1)),
                        waits=[tok_w[m], (s_dve, t_mix),
                               (s_act, t_evK[buf])],
                        inc=(s_pe, 1) if (m, half) == (NPT - 1, 1) else None)
            t_mm = p.tok(s_pe)
            war = [g_st[buf]] if g_st[buf] else []
            p.emit("scalar", lambda e, buf=buf, ps=ps: e.activation(
                g_t[:, buf * D:(buf + 1) * D], ps[:, :], AF.Sigmoid),
                waits=[(s_pe, t_mm)] + war)
            t_evK[buf] = p.tok(s_act)
            g_st[buf] = st_go.issue("gpsimd", lambda e, tt=tt, buf=buf:
                                    e.dma_start(
                out=g_int[(tt - 1) * 128:tt * 128, :],
                in_=g_t[:, buf * D:(buf + 1) * D]),
                waits=[(s_act, t_evK[buf])], slot=buf)
        proj_done[wb] = p.tok(s_pe)
        t_projg = p.tok(s_pe)

        tok_w, wb = prep_weight(w_r)
        t_mix = build_mix(0, t_gm, vmix, 0)
        t_evR = 0
        for ct in range(NPT):
            for m in range(NPT):
                for tk in range(4):
                    ps = psA_ if tk < 2 else psB_
                    p.emit("tensor", lambda e, m=m, tk=tk, ct=ct, ps=ps,
                           wb=wb: e.matmul(
                        ps[:, (tk % 2) * 512:((tk % 2) + 1) * 512],
                        wT[:, wb * NPT * D + m * D + ct * 128:
                           wb * NPT * D + m * D + (ct + 1) * 128],
                        fmh(vmix, m)[:, 128 + tk * 512:128 + (tk + 1) * 512],
                        start=(m == 0), stop=(m == NPT - 1)),
                        waits=[tok_w[m], (s_dve, t_mix), (s_dve, t_evR)],
                        inc=(s_pe, 1) if (m, tk) == (NPT - 1, 3)
                        else None)
            t_mm = p.tok(s_pe)
            p.emit("vector", lambda e, ct=ct: e.tensor_copy(
                fm(r_fm, ct)[:, 0:1024], psA_[:, :]), waits=[(s_pe, t_mm)])
            p.emit("vector", lambda e, ct=ct: e.tensor_copy(
                fm(r_fm, ct)[:, 1024:2048], psB_[:, :]))
            t_evR = p.tok(s_dve)
        proj_done[wb] = p.tok(s_pe)

        def proj_tm(w_in, widx, dst, mb, t_pe_war):
            tok_w, wb = prep_weight(w_in)
            t_mix = build_mix(widx, t_gm, mb, t_pe_war)
            t_ev = [0, 0]
            for tt in range(NSC5):
                buf = tt % 2
                ps = psC_ if buf == 0 else psB_
                for m in range(NPT):
                    for half in range(2):
                        p.emit("tensor", lambda e, m=m, half=half, tt=tt,
                               ps=ps, wb=wb, mb=mb: e.matmul(
                            ps[:, half * 512:(half + 1) * 512],
                            fmh(mb, m)[:, tt * 128:(tt + 1) * 128],
                            wT[:, wb * NPT * D + m * D + half * 512:
                               wb * NPT * D + m * D + (half + 1) * 512],
                            start=(m == 0), stop=(m == NPT - 1)),
                            waits=[tok_w[m], (s_dve, t_mix),
                                   (s_dve, t_ev[buf])],
                            inc=(s_pe, 1) if (m, half) == (NPT - 1, 1)
                            else None)
                t_mm = p.tok(s_pe)
                p.emit("vector", lambda e, tt=tt, dst=dst, ps=ps:
                       e.tensor_copy(
                    dst[:, tt * D:(tt + 1) * D], ps[:, :]),
                    waits=[(s_pe, t_mm)])
                t_ev[buf] = p.tok(s_dve)
            proj_done[wb] = p.tok(s_pe)
            return p.tok(s_dve)

        t_projr = p.tok(s_pe)
        t_k = proj_tm(w_k, 1, k_tm, mix, t_projg)
        v_tm = mix
        t_v = proj_tm(w_v, 2, v_tm, vmix, t_projr)

        bar = ([(s_pe, p.tok(s_pe)), (s_dve, p.tok(s_dve)),
                (s_act, p.tok(s_act)), (s_gps, p.tok(s_gps))]
               + st_x.total_tokens()
               + st_w.total_tokens() + st_wc.total_tokens())
        for en in ("tensor", "vector", "scalar", "gpsimd"):
            p.emit(en, lambda e: e.nop(), waits=bar,
                   dma=(en not in p.CHAINED))
        esB1.close()

        esB2 = contextlib.ExitStack()
        sb2 = esB2.enter_context
        maskc = sb2(nc.sbuf_tensor("maskc_sb", [128, 16 * 128], BF16))
        parity = sb2(nc.sbuf_tensor("parity_sb", [128, D], BF16))
        iot = sb2(nc.sbuf_tensor("iot", [1, 6 * 128], F32))
        lnwrow = sb2(nc.sbuf_tensor("lnwrow", [1, D], F32))
        P2 = sb2(nc.sbuf_tensor("P2", [128, D], BF16))
        QT1 = sb2(nc.sbuf_tensor("QT1", [128, D], BF16))
        QA = sb2(nc.sbuf_tensor("QA", [128, D], BF16))
        QKH = sb2(nc.sbuf_tensor("QKH", [128, D], BF16))
        W1f = sb2(nc.sbuf_tensor("W1f", [128, D], BF16))
        S2 = sb2(nc.sbuf_tensor("S2", [128, D], F32))
        S2t = sb2(nc.sbuf_tensor("S2t", [128, D], BF16))
        T1k = sb2(nc.sbuf_tensor("T1k", [128, 2 * D], BF16))
        T2k = sb2(nc.sbuf_tensor("T2k", [128, 2 * D], BF16))
        KHk = sb2(nc.sbuf_tensor("KHk", [128, 2 * D], BF16))
        Amk = sb2(nc.sbuf_tensor("Amk", [128, 4 * D], BF16))
        Amk2 = sb2(nc.sbuf_tensor("Amk2", [128, 2 * D], BF16))
        vfm2 = sb2(nc.sbuf_tensor("vfm2", [128, 2 * D], BF16))

        st_c.issue("sync", lambda e: e.dma_start(out=maskc[:, :],
                                                 in_=maskc_in), waits=bar)
        st_c.issue("sync", lambda e: e.dma_start(out=parity[:, :],
                                                 in_=parity_in), waits=bar)
        st_c.issue("sync", lambda e: e.dma_start(out=iot[:, :], in_=iotas),
                   waits=bar)
        st_c.issue("sync", lambda e: e.dma_start(out=lnwrow[:, :], in_=tdec),
                   waits=bar)
        cgate2 = st_c.total_tokens()
        p.emit("vector", lambda e: e.nop(), waits=cgate2)
        p.emit("scalar", lambda e: e.nop(), waits=cgate2)
        p.emit("scalar", lambda e: e.activation(lnwrow[:, :], lnwrow[:, :],
                                                AF.Exp))
        t_exp = p.tok(s_act)
        p.emit("vector", lambda e: e.tensor_scalar(lnwrow[:, :], lnwrow[:, :],
                                                   -1.0, None, ALU.mult),
               waits=[(s_act, t_exp)])
        t_lnw = p.tok(s_dve)

        pats = [(P2, 0), (QT1, 1), (QA, 2), (QKH, 3), (W1f, 4)]
        for (dst, idx) in pats:
            wv = [(s_act, p.tok(s_act))]
            for half in range(2):
                p.emit("tensor", lambda e, idx=idx, half=half: e.matmul(
                    psA_[:, half * 512:(half + 1) * 512],
                    iot[:, idx * 128:(idx + 1) * 128],
                    lnwrow[:, half * 512:(half + 1) * 512],
                    start=True, stop=True),
                    waits=[(s_dve, t_lnw)] + wv,
                    inc=(s_pe, 1) if half == 1 else None)
            t_pe = p.tok(s_pe)
            p.emit("scalar", lambda e, dst=dst: e.activation(
                dst[:, :], psA_[:, :], AF.Exp), waits=[(s_pe, t_pe)])
        t_pats = p.tok(s_act)
        p.emit("vector", lambda e: e.tensor_tensor(
            W1f[:, :], W1f[:, :], parity[:, :], ALU.mult),
            waits=[(s_act, t_pats)])
        p.emit("vector", lambda e: e.memset(vfm2[:, :], 0.0))
        p.emit("vector", lambda e: e.memset(Amk2[:, :], 0.0))
        p.emit("vector", lambda e: e.memset(S2[:, :], 0.0))
        p.emit("vector", lambda e: e.memset(S2t[:, :], 0.0))
        t_scan_pre = p.tok(s_dve)

        t_am_s = {}
        t_kv_s = {}
        t_s2t_s = {}
        t_s2e_s = {0: 0}
        t_attn_s = {0: 0}

        def prep(sc):
            b = sc % 2
            bD = b * D
            Ksc = k_tm[:, sc * D:(sc + 1) * D]
            Vsc = v_tm[:, sc * D:(sc + 1) * D]
            war_pe = [(s_pe, p.tok(s_pe))]
            p.emit("gpsimd", lambda e, Ksc=Ksc, bD=bD: e.tensor_tensor(
                T1k[:, bD:bD + D], Ksc, QT1[:, :], ALU.mult),
                waits=[(s_dve, t_k), (s_dve, t_v), (s_act, t_pats)] + war_pe)
            p.emit("gpsimd", lambda e, Ksc=Ksc, bD=bD: e.tensor_tensor(
                T2k[:, bD:bD + D], Ksc, QA[:, :], ALU.mult))
            p.emit("gpsimd", lambda e, Ksc=Ksc, bD=bD: e.tensor_tensor(
                KHk[:, bD:bD + D], Ksc, QKH[:, :], ALU.mult))
            t_kv_s[sc] = p.tok(s_gps)
            wv = [(s_dve, p.tok(s_dve))]
            for grp in range(NPT):
                p.emit("tensor", lambda e, grp=grp, Vsc=Vsc: e.transpose(
                    psT_[:, grp * 128:(grp + 1) * 128],
                    Vsc[:, grp * 128:(grp + 1) * 128], idb[:, :]),
                    waits=wv + [(s_dve, t_v)],
                    inc=(s_pe, 1) if grp == NPT - 1 else None)
            t_tr = p.tok(s_pe)
            ve = (vfm2[0:64, :].rearrange("p (g two s) -> p g two s",
                                          two=2, s=128)[:, :, 0, :])
            vo = (vfm2[64:128, :].rearrange("p (g two s) -> p g two s",
                                            two=2, s=128)[:, :, 1, :])
            pe_ = psT_[0:64, :].rearrange("p (g s) -> p g s", s=128)
            po_ = psT_[64:128, :].rearrange("p (g s) -> p g s", s=128)
            p.emit("vector", lambda e, ve=ve, pe_=pe_: e.tensor_copy(ve, pe_),
                   waits=[(s_pe, t_tr)])
            p.emit("vector", lambda e, vo=vo, po_=po_: e.tensor_copy(vo, po_))
            t_vfm = p.tok(s_dve)
            for gidx in range(2):
                war_ev = [(s_dve, p.tok(s_dve))]
                for hh in range(8):
                    h = gidx * 8 + hh
                    p.emit("tensor", lambda e, h=h, hh=hh, sc=sc: e.matmul(
                        psA_[:, hh * 128:(hh + 1) * 128],
                        vfm2[:, h * 128:(h + 1) * 128],
                        r_fm[:, (h // 2) * T + (sc - 1) * 128:
                             (h // 2) * T + sc * 128],
                        start=True, stop=True),
                        waits=[(s_dve, t_vfm)] + war_ev,
                        inc=(s_pe, 1) if hh == 7 else None)
                t_pe2 = p.tok(s_pe)
                p.emit("vector", lambda e, gidx=gidx, b=b: e.tensor_tensor(
                    Amk[:, b * 2 * D + gidx * D: b * 2 * D + (gidx + 1) * D],
                    psA_[:, :], maskc[:, gidx * D:(gidx + 1) * D], ALU.mult),
                    waits=[(s_pe, t_pe2)])
                a2d = (Amk2[0:64, bD + gidx * 512: bD + (gidx + 1) * 512]
                       .rearrange("p (g s) -> p g s", s=64))
                a2s = (psA_[0:64, :].rearrange("p (g two s) -> p g two s",
                                               two=2, s=64)[:, :, 1, :])
                p.emit("vector", lambda e, a2d=a2d, a2s=a2s:
                       e.tensor_copy(a2d, a2s))
            t_am_s[sc] = p.tok(s_dve)

        def out(sc):
            b = sc % 2
            bD = b * D
            Vsc = v_tm[:, sc * D:(sc + 1) * D]
            for h in range(H):
                hc = b * 2 * D + h * 128
                hi = h * DH
                p.emit("tensor", lambda e, h=h, hi=hi, sc=sc: e.matmul(
                    psB_[0:64, hi:hi + DH],
                    r_fm[:, (h // 2) * T + (sc - 1) * 128:
                         (h // 2) * T + (sc - 1) * 128 + 64],
                    S2t[:, hi:hi + DH],
                    start=True, stop=False, skip_group_check=True),
                    waits=[(s_dve, t_am_s[sc]), (s_gps, t_s2t_s[sc - 1]),
                           (s_gps, t_kv_s[sc]),
                           (s_dve, t_attn_s[sc - 1])])
                p.emit("tensor", lambda e, h=h, hi=hi, bD=bD: e.matmul(
                    psB_[64:128, hi:hi + DH],
                    Amk2[:, bD + h * 64: bD + (h + 1) * 64],
                    T2k[:, bD + hi: bD + hi + DH],
                    start=True, stop=False,
                    tile_position=(0, 64), skip_group_check=True))
                p.emit("tensor", lambda e, h=h, hc=hc, hi=hi, bD=bD: e.matmul(
                    psB_[0:64, hi:hi + DH],
                    Amk[:, hc:hc + 64], T1k[:, bD + hi: bD + hi + DH],
                    start=False, stop=True, skip_group_check=True))
                p.emit("tensor", lambda e, h=h, hc=hc, hi=hi, bD=bD: e.matmul(
                    psB_[64:128, hi:hi + DH],
                    Amk[:, hc + 64:hc + 128], T1k[:, bD + hi: bD + hi + DH],
                    start=False, stop=True, tile_position=(0, 64),
                    skip_group_check=True),
                    inc=(s_pe, 1) if h == H - 1 else None)
            t_o1 = p.tok(s_pe)
            dst = attn_res[:, (sc - 1) * D: sc * D]
            p.emit("vector", lambda e, dst=dst: e.tensor_tensor(
                dst, psB_[:, :], P2[:, :], ALU.mult), waits=[(s_pe, t_o1)])
            t_attn_s[sc] = p.tok(s_dve)
            if sc < NSC5 - 1:
                for h in range(H):
                    hi = h * DH
                    p.emit("tensor", lambda e, h=h, hi=hi, Vsc=Vsc, bD=bD:
                           e.matmul(
                        psC_[0:64, hi:hi + DH], Vsc[:, hi:hi + DH],
                        KHk[:, bD + hi: bD + hi + DH],
                        start=True, stop=True),
                        waits=[(s_gps, t_kv_s[sc]),
                               (s_dve, t_s2e_s[sc - 1])],
                        inc=(s_pe, 1) if h == H - 1 else None)
                t_up = p.tok(s_pe)
                p.emit("vector", lambda e: e.tensor_copy(S2[0:64, :],
                                                         psC_[0:64, :]),
                       waits=[(s_pe, t_up)])
                p.emit("vector", lambda e: e.tensor_copy(S2[64:128, :],
                                                         psC_[0:64, :]))
                t_s2e_s[sc] = p.tok(s_dve)
                p.emit("gpsimd", lambda e: e.tensor_tensor(
                    S2t[:, :], S2[:, :], W1f[:, :], ALU.mult),
                    waits=[(s_dve, t_s2e_s[sc])])
                t_s2t_s[sc] = p.tok(s_gps)

        Ksc0 = k_tm[:, 0:D]
        Vsc0 = v_tm[:, 0:D]
        p.emit("gpsimd", lambda e: e.tensor_tensor(
            KHk[:, 0:D], Ksc0, QKH[:, :], ALU.mult),
            waits=[(s_dve, t_k), (s_dve, t_v), (s_act, t_pats)])
        t_kh0 = p.tok(s_gps)
        for h in range(H):
            hi = h * DH
            p.emit("tensor", lambda e, h=h, hi=hi: e.matmul(
                psC_[0:64, hi:hi + DH], Vsc0[:, hi:hi + DH],
                KHk[:, hi:hi + DH], start=True, stop=True),
                waits=[(s_gps, t_kh0)],
                inc=(s_pe, 1) if h == H - 1 else None)
        t_up0 = p.tok(s_pe)
        p.emit("vector", lambda e: e.tensor_copy(S2[0:64, :], psC_[0:64, :]),
               waits=[(s_pe, t_up0)])
        p.emit("vector", lambda e: e.tensor_copy(S2[64:128, :],
                                                 psC_[0:64, :]))
        t_s2e_s[0] = p.tok(s_dve)
        p.emit("gpsimd", lambda e: e.tensor_tensor(
            S2t[:, :], S2[:, :], W1f[:, :], ALU.mult),
            waits=[(s_dve, t_s2e_s[0])])
        t_s2t_s[0] = p.tok(s_gps)

        prep(1)
        for sc in range(1, NSC5):
            if sc < NSC5 - 1:
                prep(sc + 1)
            out(sc)

        bar2 = ([(s_pe, p.tok(s_pe)), (s_dve, p.tok(s_dve)),
                 (s_act, p.tok(s_act)), (s_gps, p.tok(s_gps))]
                + st_go.total_tokens() + st_c.total_tokens())
        for en in ("tensor", "vector", "scalar", "gpsimd"):
            p.emit(en, lambda e: e.nop(), waits=bar2,
                   dma=(en not in p.CHAINED))
        esB2.close()
        esB.close()

        esC = contextlib.ExitStack()
        sbC = esC.enter_context
        y_sb = sbC(nc.sbuf_tensor("y_sb", [128, NPT * T], F32))
        idf = sbC(nc.sbuf_tensor("idf", [128, 128], F32))
        psS = sbC(nc.psum_tensor("psS", [128, T], F32))
        st_c.issue("sync", lambda e: e.dma_start(out=idf[:, :], in_=iden_f),
                   waits=bar2)
        p.emit("tensor", lambda e: e.nop(), waits=st_c.total_tokens(),
               dma=True)
        esC1 = contextlib.ExitStack()
        sbC1 = esC1.enter_context
        agfm = sbC1(nc.sbuf_tensor("agfm", [128, NPT * T], BF16))
        woT = sbC1(nc.sbuf_tensor("woT", [128, NPT * D], BF16))
        g_t2 = sbC1(nc.sbuf_tensor("g_t2", [128, 2 * D], BF16))

        tok_cast = st_wc.issue("gpsimd", lambda e: e.dma_start(
            out=wscr, in_=w_o), waits=bar2)
        tok_w = []
        for m in range(NPT):
            tok_w.append(st_w.issue("scalar", lambda e, m=m:
                                    e.dma_start_transpose(
                out=woT[:, m * D:(m + 1) * D],
                in_=wscr[:, m * 128:(m + 1) * 128]),
                waits=[tok_cast]))

        agv = agfm[:, :].rearrange("p (pt t) -> p pt t", pt=NPT)
        t_evT = 0
        tr_tok = [0, 0]
        for tt in range(NTT):
            buf = tt % 2
            tok_g = st_gl.issue("sync", lambda e, tt=tt, buf=buf: e.dma_start(
                out=g_t2[:, buf * D:(buf + 1) * D],
                in_=g_int[tt * 128:(tt + 1) * 128, :]),
                waits=[(s_dve, p.tok(s_dve))], slot=buf)
            p.emit("gpsimd", lambda e, tt=tt, buf=buf: e.tensor_tensor(
                attn_res[:, tt * D:(tt + 1) * D],
                attn_res[:, tt * D:(tt + 1) * D],
                g_t2[:, buf * D:(buf + 1) * D], ALU.mult),
                waits=[tok_g, (s_pe, tr_tok[buf])])
            t_ag = p.tok(s_gps)
            wv = [(s_dve, p.tok(s_dve))]
            for pt2 in range(NPT):
                p.emit("tensor", lambda e, pt2=pt2, tt=tt: e.transpose(
                    psT_[:, pt2 * 128:(pt2 + 1) * 128],
                    attn_res[:, tt * D + pt2 * 128: tt * D + (pt2 + 1) * 128],
                    idb[:, :]),
                    waits=[(s_gps, t_ag)] + wv,
                    inc=(s_pe, 1) if pt2 == NPT - 1 else None)
            t_tr = p.tok(s_pe)
            tr_tok[buf] = t_tr
            psv = psT_[:, :].rearrange("p (j t) -> p j t", j=NPT)
            dstv = agv[:, :, tt * 128:(tt + 1) * 128]
            p.emit("vector", lambda e, psv=psv, dstv=dstv: e.tensor_copy(
                dstv, psv), waits=[(s_pe, t_tr)])
            t_evT = p.tok(s_dve)

        t_evY = 0
        for ct in range(NPT):
            for m in range(NPT):
                for tk in range(T // 512):
                    ps = psA_ if tk < 2 else psS
                    p.emit("tensor", lambda e, m=m, tk=tk, ct=ct, ps=ps:
                           e.matmul(
                        ps[:, (tk % 2) * 512:((tk % 2) + 1) * 512],
                        woT[:, m * D + ct * 128: m * D + (ct + 1) * 128],
                        fm(agfm, m)[:, tk * 512:(tk + 1) * 512],
                        start=(m == 0), stop=(m == NPT - 1)),
                        waits=tok_w + [(s_dve, t_evT), (s_dve, t_evY)],
                        inc=(s_pe, 1) if (m, tk) == (NPT - 1, 3) else None)
            t_mm = p.tok(s_pe)
            p.emit("vector", lambda e, ct=ct: e.tensor_copy(
                fm(y_sb, ct)[:, 0:1024], psA_[:, :]), waits=[(s_pe, t_mm)])
            p.emit("vector", lambda e, ct=ct: e.tensor_copy(
                fm(y_sb, ct)[:, 1024:2048], psS[:, 0:1024]))
            t_evY = p.tok(s_dve)

        bar3 = ([(s_pe, p.tok(s_pe)), (s_dve, p.tok(s_dve)),
                 (s_act, p.tok(s_act)), (s_gps, p.tok(s_gps))]
                + st_gl.total_tokens()
                + st_w.total_tokens() + st_wc.total_tokens())
        for en in ("tensor", "vector", "scalar", "gpsimd"):
            p.emit(en, lambda e: e.nop(), waits=bar3,
                   dma=(en not in p.CHAINED))
        esC1.close()
        io_t = sbC(nc.sbuf_tensor("io_t", [128, 2 * T], F32))
        xl_t = sbC(nc.sbuf_tensor("xl_t", [128, 2 * T], F32))
        tmp = sbC(nc.sbuf_tensor("tmp", [128, T], F32))
        yb_t = tmp
        scr = sbC(nc.sbuf_tensor("scr", [128, 2 * T], BF16))
        srow = sbC(nc.sbuf_tensor("srow", [1, T], F32))
        sqrow = sbC(nc.sbuf_tensor("sqrow", [1, T], F32))
        mu2row = tmp
        rstdrow = sqrow
        mx = sbC(nc.sbuf_tensor("mx", [128, 1], F32))
        mxt = sbC(nc.sbuf_tensor("mxt", [128, 1], F32))
        mrow = sbC(nc.sbuf_tensor("mrow", [1, 128], F32))
        mval = sbC(nc.sbuf_tensor("mval", [1, 1], F32))

        rows = dict(s=srow, sq=sqrow, rstd=rstdrow, mu2=mu2row[0:1, :])
        t_ln3 = emit_ln(p, y_sb, lambda pt: fm(y_sb, pt), scr, onecol_s,
                        onec, psS, psS, rows, s)
        x2st = [None, None]
        for pt in range(NPT):
            buf = pt % 2
            war_st = [x2st[buf]] if x2st[buf] else []
            tok_x = st_x1.issue("sync", lambda e, pt=pt, buf=buf: e.dma_start(
                out=io_t[:, buf * T:(buf + 1) * T],
                in_=x1_in[pt * 128:(pt + 1) * 128, :]),
                waits=[(s_dve, p.tok(s_dve))] + war_st, slot=buf)
            p.emit("scalar", lambda e, pt=pt: e.activation(
                yb_t[:, :], fm(y_sb, pt), AF.Identity,
                bias=lnb[:, pt:pt + 1], scale=lng[:, pt:pt + 1]),
                waits=[(s_dve, t_ln3), (s_dve, p.tok(s_dve))])
            t_yb = p.tok(s_act)
            war = [x2st[buf]] if x2st[buf] else []
            p.emit("vector", lambda e, buf=buf: e.tensor_tensor(
                io_t[:, buf * T:(buf + 1) * T],
                io_t[:, buf * T:(buf + 1) * T], yb_t[:, :], ALU.add),
                waits=[tok_x, (s_act, t_yb)] + war)
            t_x2 = p.tok(s_dve)
            x2st[buf] = st_o.issue("gpsimd", lambda e, pt=pt, buf=buf:
                                   e.dma_start(
                out=x2_o[pt * 128:(pt + 1) * 128, :],
                in_=io_t[:, buf * T:(buf + 1) * T]),
                waits=[(s_dve, t_x2)], slot=buf)
            p.emit("scalar", lambda e, buf=buf: e.copy(
                scr[:, buf * T:(buf + 1) * T],
                io_t[:, buf * T:(buf + 1) * T]),
                waits=[(s_dve, t_x2), (s_pe, p.tok(s_pe))])
            t_scp = p.tok(s_act)
            for tk in range(T // 512):
                p.emit("tensor", lambda e, buf=buf, tk=tk, pt=pt: e.matmul(
                    psS[0:1, tk * 512:(tk + 1) * 512], onecol_s[:, 0:1],
                    scr[:, buf * T + tk * 512: buf * T + (tk + 1) * 512],
                    start=(pt == 0), stop=(pt == NPT - 1)),
                    waits=[(s_act, t_scp)],
                    inc=(s_pe, 1) if tk == 3 else None)

        rows4 = dict(s=srow, sq=sqrow, rstd=rstdrow, mu2=mu2row[0:1, :])
        emit_ln_streamed_max(p, x2_o, st_x1, io_t, tmp, scr, onecol_s, onec,
                             psS, psA_, psT_, rows4, xl_t, xln4_o, st_o,
                             x2st, mx, mxt, mrow, mval, idf, m3_o, st_m, s,
                             pre_sum=True)

        fin = (st_o.total_tokens() + st_m.total_tokens())
        p.emit("sync", lambda e: e.nop(), waits=fin)
        esC.close()

        block = p.es.enter_context(nc.Block())
        p.build(block)
    return nc


def build_L5():
    nc = bass.Bass("TRN2", target_bir_lowering=False)
    x1_in = dram_in(nc, "x1", [D, T])
    attn_in = dram_in(nc, "attnout", [T, D], BF16)
    g_in = dram_in(nc, "g_out", [T, D], BF16)
    r_in = dram_in(nc, "r_out", [D, T], BF16)
    st_in = dram_in(nc, "st_prev", [DH, D])
    w_o = dram_in(nc, "w_o", [D, D])
    lng_in = dram_in(nc, "ln_g", [D, 1])
    lnb_in = dram_in(nc, "ln_b", [D, 1])
    tdec = dram_in(nc, "tdec", [1, D])
    ones1 = dram_in(nc, "ones1", [1, 128])
    onescol = dram_in(nc, "onescol", [128, 1], BF16)
    iden_b = dram_in(nc, "iden_b", [128, 128], BF16)
    iden_f = dram_in(nc, "iden_f", [128, 128])
    iotas = dram_in(nc, "iotas", [1, 6 * 128])
    parity_in = dram_in(nc, "parity", [128, D], BF16)
    x2_o = dram_out(nc, "x2", [D, T])
    xln4_o = dram_out(nc, "xln4", [D, T])
    m3_o = dram_out(nc, "m3", [1, 1])

    wscr = nc.dram_tensor("wscr5", [D, D], BF16, kind="Internal").ap()

    p = Prog(nc)
    with p.es:
        sb = p.es.enter_context
        y_sb = sb(nc.sbuf_tensor("y_sb", [128, NPT * T], F32))
        agfm = sb(nc.sbuf_tensor("agfm", [128, NPT * T], BF16))
        woT = sb(nc.sbuf_tensor("woT", [128, NPT * D], BF16))
        attn_t = sb(nc.sbuf_tensor("attn_t", [128, 2 * D], BF16))
        g_t = sb(nc.sbuf_tensor("g_t", [128, 2 * D], BF16))
        io_t = sb(nc.sbuf_tensor("io_t", [128, 2 * T], F32))
        xl_t = sb(nc.sbuf_tensor("xl_t", [128, 2 * T], F32))
        tmp = sb(nc.sbuf_tensor("tmp", [128, T], F32))
        yb_t = tmp
        scr = sb(nc.sbuf_tensor("scr", [128, 2 * T], BF16))
        r0_sb = sb(nc.sbuf_tensor("r0_sb", [128, NPT * 128], BF16))
        s0f = sb(nc.sbuf_tensor("s0f", [128, D], F32))
        s0b = sb(nc.sbuf_tensor("s0b", [128, D], BF16))
        par5 = sb(nc.sbuf_tensor("par5", [128, D], BF16))
        Pinit = sb(nc.sbuf_tensor("Pinit", [128, D], BF16))
        lnwrow = sb(nc.sbuf_tensor("lnwrow", [1, D], F32))
        onec = sb(nc.sbuf_tensor("onec", [1, 128], F32))
        onecol_s = sb(nc.sbuf_tensor("onecol_s", [128, 1], BF16))
        idb = sb(nc.sbuf_tensor("idb", [128, 128], BF16))
        idf = sb(nc.sbuf_tensor("idf", [128, 128], F32))
        iot = sb(nc.sbuf_tensor("iot", [1, 6 * 128], F32))
        lng = sb(nc.sbuf_tensor("lng", [128, NPT], F32))
        lnb = sb(nc.sbuf_tensor("lnb", [128, NPT], F32))
        srow = sb(nc.sbuf_tensor("srow", [1, T], F32))
        sqrow = sb(nc.sbuf_tensor("sqrow", [1, T], F32))
        mu2row = tmp
        rstdrow = sqrow
        mx = sb(nc.sbuf_tensor("mx", [128, 1], F32))
        mxt = sb(nc.sbuf_tensor("mxt", [128, 1], F32))
        mrow = sb(nc.sbuf_tensor("mrow", [1, 128], F32))
        mval = sb(nc.sbuf_tensor("mval", [1, 1], F32))
        psA_ = sb(nc.psum_tensor("psA_", [128, D], F32))
        psS = sb(nc.psum_tensor("psS", [128, T], F32))
        psT_ = sb(nc.psum_tensor("psT_", [128, D], BF16))

        p.init_chain_sems()
        s_pe = p.sem("s_pe")
        s_dve = p.chain["vector"]
        s_act = p.chain["scalar"]
        st_c = Stream(p, "stC", 4)
        st_wc = Stream(p, "stWC", 1)
        st_w = Stream(p, "stW", 4)
        st_i = Stream(p, "stI", 2)
        st_x = Stream(p, "stX", 2)
        st_o = Stream(p, "stO", 2)
        st_m = Stream(p, "stM", 1)
        s = dict(pe=s_pe, dve=s_dve, act=s_act)

        setup_consts(p, nc, st_c)
        st_c.issue("sync", lambda e: e.dma_start(out=onec[:, :], in_=ones1))
        st_c.issue("sync", lambda e: e.dma_start(out=onecol_s[:, :],
                                                 in_=onescol))
        st_c.issue("sync", lambda e: e.dma_start(out=idb[:, :], in_=iden_b))
        st_c.issue("sync", lambda e: e.dma_start(out=idf[:, :], in_=iden_f))
        st_c.issue("sync", lambda e: e.dma_start(out=iot[:, :], in_=iotas))
        st_c.issue("sync", lambda e: e.dma_start(out=lnwrow[:, :], in_=tdec))
        st_c.issue("sync", lambda e: e.dma_start(out=s0f[0:64, :],
                                                 in_=st_in))
        st_c.issue("sync", lambda e: e.dma_start(out=s0f[64:128, :],
                                                 in_=st_in))
        st_c.issue("sync", lambda e: e.dma_start(out=par5[:, :],
                                                 in_=parity_in))
        for pt in range(NPT):
            st_c.issue("sync", lambda e, pt=pt: e.dma_start(
                out=lng[:, pt:pt + 1], in_=lng_in[pt * 128:(pt + 1) * 128, :]))
            st_c.issue("sync", lambda e, pt=pt: e.dma_start(
                out=lnb[:, pt:pt + 1], in_=lnb_in[pt * 128:(pt + 1) * 128, :]))
            st_c.issue("sync", lambda e, pt=pt: e.dma_start(
                out=r0_sb[:, pt * 128:(pt + 1) * 128],
                in_=r_in[pt * 128:(pt + 1) * 128, 0:128]))
        cgate = st_c.total_tokens()
        s["cgate"] = cgate
        for en in ("tensor", "vector", "scalar"):
            p.emit(en, lambda e: e.nop(), waits=cgate,
                   dma=(en not in p.CHAINED))

        p.emit("scalar", lambda e: e.activation(lnwrow[:, :], lnwrow[:, :],
                                                AF.Exp))
        t_exp = p.tok(s_act)
        p.emit("vector", lambda e: e.tensor_scalar(
            lnwrow[:, :], lnwrow[:, :], -1.0, None, ALU.mult),
            waits=[(s_act, t_exp)])
        t_lnw = p.tok(s_dve)
        for half in range(2):
            p.emit("tensor", lambda e, half=half: e.matmul(
                psA_[:, half * 512:(half + 1) * 512],
                iot[:, 5 * 128:6 * 128],
                lnwrow[:, half * 512:(half + 1) * 512],
                start=True, stop=True),
                waits=[(s_dve, t_lnw)],
                inc=(s_pe, 1) if half == 1 else None)
        t_pe = p.tok(s_pe)
        p.emit("scalar", lambda e: e.activation(Pinit[:, :], psA_[:, :],
                                                AF.Exp),
               waits=[(s_pe, t_pe)])
        t_pinit = p.tok(s_act)
        p.emit("vector", lambda e: e.tensor_copy(s0b[:, :], s0f[:, :]))
        p.emit("vector", lambda e: e.tensor_tensor(
            s0b[:, :], s0b[:, :], par5[:, :], ALU.mult))
        t_s0 = p.tok(s_dve)

        for h in range(H):
            hi = h * DH
            p.emit("tensor", lambda e, h=h, hi=hi: e.matmul(
                psA_[:, hi:hi + DH],
                r0_sb[:, (h // 2) * 128:(h // 2) * 128 + 128],
                s0b[:, hi:hi + DH],
                start=True, stop=True),
                waits=[(s_dve, t_s0), (s_act, t_pinit)],
                inc=(s_pe, 1) if h == H - 1 else None)
        t_init = p.tok(s_pe)

        tok_cast = st_wc.issue("gpsimd", lambda e: e.dma_start(
            out=wscr, in_=w_o))
        tok_w = []
        for m in range(NPT):
            tok_w.append(st_w.issue("scalar", lambda e, m=m:
                                    e.dma_start_transpose(
                out=woT[:, m * D:(m + 1) * D],
                in_=wscr[:, m * 128:(m + 1) * 128]),
                waits=[tok_cast]))

        agv = agfm[:, :].rearrange("p (pt t) -> p pt t", pt=NPT)
        t_evT = 0
        tr_tok = [0, 0]
        for tt in range(NTT):
            buf = tt % 2
            war_i = [(s_dve, p.tok(s_dve)), (s_pe, tr_tok[buf])]
            tok_a = st_i.issue("sync", lambda e, tt=tt, buf=buf: e.dma_start(
                out=attn_t[:, buf * D:(buf + 1) * D],
                in_=attn_in[tt * 128:(tt + 1) * 128, :]),
                waits=war_i, slot=buf)
            tok_g = st_i.issue("sync", lambda e, tt=tt, buf=buf: e.dma_start(
                out=g_t[:, buf * D:(buf + 1) * D],
                in_=g_in[tt * 128:(tt + 1) * 128, :]),
                waits=war_i, slot=buf)
            if tt == 0:
                p.emit("vector", lambda e: e.tensor_tensor(
                    tmp[:, 0:D], psA_[:, :], Pinit[:, :], ALU.mult),
                    waits=[(s_pe, t_init), tok_a])
                p.emit("vector", lambda e, buf=buf: e.tensor_tensor(
                    attn_t[:, buf * D:(buf + 1) * D],
                    attn_t[:, buf * D:(buf + 1) * D], tmp[:, 0:D], ALU.add))
            p.emit("vector", lambda e, buf=buf: e.tensor_tensor(
                attn_t[:, buf * D:(buf + 1) * D],
                attn_t[:, buf * D:(buf + 1) * D],
                g_t[:, buf * D:(buf + 1) * D], ALU.mult),
                waits=[tok_a, tok_g])
            t_ag = p.tok(s_dve)
            wv = [(s_dve, p.tok(s_dve))]
            for pt2 in range(NPT):
                p.emit("tensor", lambda e, pt2=pt2, buf=buf: e.transpose(
                    psT_[:, pt2 * 128:(pt2 + 1) * 128],
                    attn_t[:, buf * D + pt2 * 128: buf * D + (pt2 + 1) * 128],
                    idb[:, :]),
                    waits=[(s_dve, t_ag)] + wv + [(s_act, t_pinit)],
                    inc=(s_pe, 1) if pt2 == NPT - 1 else None)
            t_tr = p.tok(s_pe)
            tr_tok[buf] = t_tr
            psv = psT_[:, :].rearrange("p (j t) -> p j t", j=NPT)
            dstv = agv[:, :, tt * 128:(tt + 1) * 128]
            p.emit("vector", lambda e, psv=psv, dstv=dstv: e.tensor_copy(
                dstv, psv), waits=[(s_pe, t_tr)])
            t_evT = p.tok(s_dve)

        t_evY = 0
        for ct in range(NPT):
            for m in range(NPT):
                for tk in range(T // 512):
                    ps = psA_ if tk < 2 else psS
                    p.emit("tensor", lambda e, m=m, tk=tk, ct=ct, ps=ps:
                           e.matmul(
                        ps[:, (tk % 2) * 512:((tk % 2) + 1) * 512],
                        woT[:, m * D + ct * 128: m * D + (ct + 1) * 128],
                        fm(agfm, m)[:, tk * 512:(tk + 1) * 512],
                        start=(m == 0), stop=(m == NPT - 1)),
                        waits=tok_w + [(s_dve, t_evT), (s_dve, t_evY)],
                        inc=(s_pe, 1) if (m, tk) == (NPT - 1, 3) else None)
            t_mm = p.tok(s_pe)
            p.emit("vector", lambda e, ct=ct: e.tensor_copy(
                fm(y_sb, ct)[:, 0:1024], psA_[:, :]), waits=[(s_pe, t_mm)])
            p.emit("vector", lambda e, ct=ct: e.tensor_copy(
                fm(y_sb, ct)[:, 1024:2048], psS[:, 0:1024]))
            t_evY = p.tok(s_dve)

        p.emit("tensor", lambda e: e.nop(),
               waits=[(s_dve, p.tok(s_dve))], dma=True)
        p.emit("scalar", lambda e: e.nop(), waits=[(s_dve, p.tok(s_dve))])
        rows = dict(s=srow, sq=sqrow, rstd=rstdrow, mu2=mu2row[0:1, :])
        t_ln3 = emit_ln(p, y_sb, lambda pt: fm(y_sb, pt), scr, onecol_s,
                        onec, psS, psS, rows, s)
        x2st = [None, None]
        for pt in range(NPT):
            buf = pt % 2
            war_st = [x2st[buf]] if x2st[buf] else []
            tok_x = st_x.issue("sync", lambda e, pt=pt, buf=buf: e.dma_start(
                out=io_t[:, buf * T:(buf + 1) * T],
                in_=x1_in[pt * 128:(pt + 1) * 128, :]),
                waits=[(s_dve, p.tok(s_dve))] + war_st, slot=buf)
            p.emit("scalar", lambda e, pt=pt: e.activation(
                yb_t[:, :], fm(y_sb, pt), AF.Identity,
                bias=lnb[:, pt:pt + 1], scale=lng[:, pt:pt + 1]),
                waits=[(s_dve, t_ln3), (s_dve, p.tok(s_dve))])
            t_yb = p.tok(s_act)
            war = [x2st[buf]] if x2st[buf] else []
            p.emit("vector", lambda e, buf=buf: e.tensor_tensor(
                io_t[:, buf * T:(buf + 1) * T],
                io_t[:, buf * T:(buf + 1) * T], yb_t[:, :], ALU.add),
                waits=[tok_x, (s_act, t_yb)] + war)
            t_x2 = p.tok(s_dve)
            x2st[buf] = st_o.issue("gpsimd", lambda e, pt=pt, buf=buf:
                                   e.dma_start(
                out=x2_o[pt * 128:(pt + 1) * 128, :],
                in_=io_t[:, buf * T:(buf + 1) * T]),
                waits=[(s_dve, t_x2)], slot=buf)

        rows4 = dict(s=srow, sq=sqrow, rstd=rstdrow, mu2=mu2row[0:1, :])
        emit_ln_streamed_max(p, x2_o, st_x, io_t, tmp, scr, onecol_s, onec,
                             psS, psA_, psT_, rows4, xl_t, xln4_o, st_o,
                             x2st, mx, mxt, mrow, mval, idf, m3_o, st_m, s)

        fin = (st_o.total_tokens() + st_m.total_tokens())
        p.emit("sync", lambda e: e.nop(), waits=fin)

        block = p.es.enter_context(nc.Block())
        p.build(block)
    return nc


def emit_ln_streamed_max(p, x_o, st_x, io_t, tmp, scr, onecol_s, onec, psS,
                         psB1, psB2, rows, xl_t, xln_o, st_o, x_ready_st,
                         mx, mxt, mrow, mval, idf, m_o, st_m, s,
                         pre_sum=False):
    s_pe, s_dve, s_act = s["pe"], s["dve"], s["act"]
    srow, sqrow, rstdrow = rows["s"], rows["sq"], rows["rstd"]
    x1_ready = [t for t in x_ready_st if t]
    if pre_sum:
        t_pe0 = p.tok(s_pe)
        p.emit("vector", lambda e: e.tensor_copy(srow[:, :], psS[0:1, :]),
               waits=[(s_pe, t_pe0)])
        t_dve = p.tok(s_dve)
        p.emit("tensor", lambda e: e.nop(), waits=[(s_dve, t_dve)])
    for which in range((1 if pre_sum else 0), 2):
        for pt in range(NPT):
            buf = pt % 2
            war = [(s_act, p.tok(s_act))]
            tok = st_x.issue("sync", lambda e, pt=pt, buf=buf: e.dma_start(
                out=io_t[:, buf * T:(buf + 1) * T],
                in_=x_o[pt * 128:(pt + 1) * 128, :]),
                waits=x1_ready + war, slot=buf)
            war2 = [(s_pe, p.tok(s_pe))]
            if which == 0:
                p.emit("scalar", lambda e, buf=buf: e.copy(
                    scr[:, buf * T:(buf + 1) * T],
                    io_t[:, buf * T:(buf + 1) * T]), waits=[tok] + war2)
            else:
                p.emit("scalar", lambda e, buf=buf: e.activation(
                    scr[:, buf * T:(buf + 1) * T],
                    io_t[:, buf * T:(buf + 1) * T], AF.Square),
                    waits=[tok] + war2)
            t_act = p.tok(s_act)
            for tk in range(T // 512):
                p.emit("tensor", lambda e, buf=buf, tk=tk, pt=pt: e.matmul(
                    psS[0:1, tk * 512:(tk + 1) * 512], onecol_s[:, 0:1],
                    scr[:, buf * T + tk * 512: buf * T + (tk + 1) * 512],
                    start=(pt == 0), stop=(pt == NPT - 1)),
                    waits=[(s_act, t_act)],
                    inc=(s_pe, 1) if tk == 3 else None)
        t_pe = p.tok(s_pe)
        dstrow = srow if which == 0 else sqrow
        p.emit("vector", lambda e, dstrow=dstrow: e.tensor_copy(
            dstrow[:, :], psS[0:1, :]), waits=[(s_pe, t_pe)])
        if which == 0:
            t_dve = p.tok(s_dve)
            p.emit("tensor", lambda e: e.nop(), waits=[(s_dve, t_dve)])
    mu2 = rows["mu2"]
    p.emit("vector", lambda e: e.tensor_scalar(
        srow[:, :], srow[:, :], 1.0 / D, None, ALU.mult))
    p.emit("vector", lambda e: e.tensor_tensor(
        mu2, srow[:, :], srow[:, :], ALU.mult))
    p.emit("vector", lambda e: e.scalar_tensor_tensor(
        sqrow[:, :], sqrow[:, :], 1.0 / D, mu2, ALU.mult, ALU.subtract))
    t = p.tok(s_dve)
    p.emit("scalar", lambda e: e.activation(
        sqrow[:, :], sqrow[:, :], AF.Ln, bias=EPS, scale=1.0),
        waits=[(s_dve, t)] + list(s.get("cgate", [])))
    p.emit("scalar", lambda e: e.activation(
        rstdrow[:, :], sqrow[:, :], AF.Exp, bias=0.0, scale=-0.5))
    t_rows = p.tok(s_act)
    wv = [(s_dve, p.tok(s_dve))]
    for tk in range(T // 512):
        p.emit("tensor", lambda e, tk=tk: e.matmul(
            psS[:, tk * 512:(tk + 1) * 512], onec[:, :],
            srow[:, tk * 512:(tk + 1) * 512], start=True, stop=True),
            waits=[(s_act, t_rows)] + wv,
            inc=(s_pe, 1) if tk == 3 else None)
    t_bc = p.tok(s_pe)
    prev_q = 0
    st_tok = [None, None]
    for pt in range(NPT):
        buf = pt % 2
        tok = st_x.issue("sync", lambda e, pt=pt, buf=buf: e.dma_start(
            out=io_t[:, buf * T:(buf + 1) * T],
            in_=x_o[pt * 128:(pt + 1) * 128, :]),
            waits=[(s_dve, prev_q)], slot=buf)
        p.emit("vector", lambda e, buf=buf: e.tensor_tensor(
            tmp[:, :], io_t[:, buf * T:(buf + 1) * T], psS[:, :],
            ALU.subtract), waits=[tok, (s_pe, t_bc)])
        for tk in range(T // 512):
            half = tk % 2
            wv2 = [(s_dve, p.tok(s_dve))]
            p.emit("tensor", lambda e, tk=tk, half=half: e.matmul(
                psB1[:, half * 512:(half + 1) * 512], onec[:, :],
                rstdrow[:, tk * 512:(tk + 1) * 512], start=True, stop=True),
                waits=wv2 + [(s_act, t_rows)],
                inc=(s_pe, 1))
            t_b2 = p.tok(s_pe)
            war = [st_tok[buf]] if st_tok[buf] and tk == 0 else []
            p.emit("vector", lambda e, buf=buf, tk=tk, half=half:
                   e.tensor_tensor(
                xl_t[:, buf * T + tk * 512: buf * T + (tk + 1) * 512],
                tmp[:, tk * 512:(tk + 1) * 512],
                psB1[:, half * 512:(half + 1) * 512], ALU.mult),
                waits=[(s_pe, t_b2)] + war)
        dst = mx if pt == 0 else mxt
        p.emit("vector", lambda e, buf=buf, dst=dst: e.reduce_max(
            dst[:, :], xl_t[:, buf * T:(buf + 1) * T], AX.X,
            apply_absolute_value=True))
        if pt > 0:
            p.emit("vector", lambda e: e.tensor_tensor(
                mx[:, :], mx[:, :], mxt[:, :], ALU.max))
        prev_q = p.tok(s_dve)
        t_dve = p.tok(s_dve)
        st_tok[buf] = st_o.issue("gpsimd", lambda e, pt=pt, buf=buf:
                                 e.dma_start(
            out=xln_o[pt * 128:(pt + 1) * 128, :],
            in_=xl_t[:, buf * T:(buf + 1) * T]),
            waits=[(s_dve, t_dve)], slot=buf)
    t_dve = p.tok(s_dve)
    p.emit("tensor", lambda e: e.transpose(psS[0:1, 0:128], mx[:, :],
                                           idf[:, :]),
           waits=[(s_dve, t_dve)], inc=(s_pe, 1))
    t_pe = p.tok(s_pe)
    p.emit("vector", lambda e: e.tensor_copy(mrow[:, :], psS[0:1, 0:128]),
           waits=[(s_pe, t_pe)])
    p.emit("vector", lambda e: e.reduce_max(mval[:, :], mrow[:, :], AX.X))
    t_dve = p.tok(s_dve)
    st_m.issue("gpsimd", lambda e: e.dma_start(out=m_o, in_=mval[:, :]),
               waits=[(s_dve, t_dve)])


def host_consts():
    c = {}
    c["ones1"] = np.ones((1, 128), np.float32)
    c["onescol"] = np.ones((128, 1), BF)
    c["iden_f"] = np.eye(128, dtype=np.float32)
    cv = np.zeros((128, 16), np.float32)
    cv[:, :len(ACT_CONSTS)] = np.asarray(ACT_CONSTS, np.float32)[None, :]
    c["cvals"] = cv
    c["iden_b"] = np.eye(128, dtype=BF)
    s = np.arange(128)
    blk = ((s[:, None] <= s[None, :])
           & ((s[:, None] < 64) == (s[None, :] < 64))).astype(BF)
    c["maskc"] = np.tile(blk, (1, 16))
    par = np.zeros((128, 1024), np.float32)
    for h in range(16):
        par[(64 * (h % 2)):(64 * (h % 2) + 64), h * 64:(h + 1) * 64] = 1.0
    c["parity"] = par.astype(BF)
    iot = np.zeros((1, 6 * 128), np.float32)
    iot[0, 0:128] = s % 64
    iot[0, 128:256] = -(s % 64)
    iot[0, 256:384] = np.where(s < 64, 64 - s, 200)
    iot[0, 384:512] = 127 - s
    iot[0, 512:640] = 1
    iot[0, 640:768] = s + 1
    c["iotas"] = iot
    return c


_NCS = {}


def _programs():
    if not _NCS:
        _NCS["L1"] = build_L1()
        _NCS["L2"] = build_L2()
        _NCS["L3"] = build_L3(final=False)
        _NCS["L45"] = build_L45()
        _NCS["L7"] = build_L3(final=True)
    return _NCS


def _run(nc, in_maps):
    res = run_bass_kernel_spmd(nc, in_maps, core_ids=list(range(NC)))
    return res.results


def kernel(L0_attn_x, L0_attn_time_decay, L0_attn_time_mix_k,
           L0_attn_time_mix_v, L0_attn_time_mix_r, L0_attn_time_mix_g,
           L0_attn_W_r, L0_attn_W_k, L0_attn_W_v, L0_attn_W_g, L0_attn_W_o,
           L0_attn_ln_gamma, L0_attn_ln_beta, L0_ffn_clip_x,
           L0_ffn_clip_down, L0_ffn_w_gate, L0_ffn_w_up, L0_ffn_w_down):
    f32 = np.float32
    x = np.ascontiguousarray(np.asarray(L0_attn_x, f32).reshape(16384, D))
    wg = np.ascontiguousarray(np.asarray(L0_ffn_w_gate, f32))
    wu = np.ascontiguousarray(np.asarray(L0_ffn_w_up, f32))
    wd = np.ascontiguousarray(np.asarray(L0_ffn_w_down, f32))
    w_r = np.ascontiguousarray(np.asarray(L0_attn_W_r, f32))
    w_k = np.ascontiguousarray(np.asarray(L0_attn_W_k, f32))
    w_v = np.ascontiguousarray(np.asarray(L0_attn_W_v, f32))
    w_g = np.ascontiguousarray(np.asarray(L0_attn_W_g, f32))
    w_o = np.ascontiguousarray(np.asarray(L0_attn_W_o, f32))
    tmix = np.ascontiguousarray(np.concatenate([
        np.asarray(L0_attn_time_mix_r, f32),
        np.asarray(L0_attn_time_mix_k, f32),
        np.asarray(L0_attn_time_mix_v, f32),
        np.asarray(L0_attn_time_mix_g, f32)]).reshape(4 * D, 1))
    tdec = np.ascontiguousarray(np.asarray(L0_attn_time_decay,
                                           f32).reshape(1, D))
    lng = np.ascontiguousarray(np.asarray(L0_attn_ln_gamma, f32).reshape(D, 1))
    lnb = np.ascontiguousarray(np.asarray(L0_attn_ln_beta, f32).reshape(D, 1))
    clip_x = np.asarray(L0_ffn_clip_x, f32).reshape(1, 1)
    clip_d = np.asarray(L0_ffn_clip_down, f32).reshape(1, 1)
    consts = host_consts()
    ncs = _programs()
    zrow = np.zeros((1, D), BF)
    zst = np.zeros((DH, D), f32)
    ones_c0 = np.ones((1, 1), f32)
    zero_c0 = np.zeros((1, 1), f32)

    def cmap(keys):
        return {k: consts[k] for k in keys}

    k1 = cmap(["ones1", "onescol", "iden_f", "cvals"])
    pad = np.ones((3 * 128 * NC - I, D), np.float32)
    wgp = np.concatenate([np.asarray(wg, np.float32), pad], axis=0)
    wup = np.concatenate([np.asarray(wu, np.float32), pad], axis=0)
    r1 = _run(ncs["L1"], [
        dict(k1, x=x[c * T:(c + 1) * T],
             w_gate=wgp[c * 384:(c + 1) * 384],
             w_up=wup[c * 384:(c + 1) * 384],
             w_down=wd[c * 128:(c + 1) * 128])
        for c in range(NC)])
    m1 = np.concatenate([r1[c]["m1"] for c in range(NC)], axis=0)
    qg_f = np.concatenate([r1[c]["qg"] for c in range(NC)], axis=0)[:I]
    qu_f = np.concatenate([r1[c]["qu"] for c in range(NC)], axis=0)[:I]
    sg_f = np.concatenate([r1[c]["sg"] for c in range(NC)], axis=0)[:I]
    su_f = np.concatenate([r1[c]["su"] for c in range(NC)], axis=0)[:I]
    qd_f = np.concatenate([r1[c]["qd"] for c in range(NC)], axis=0)
    sdn_f = np.concatenate([r1[c]["sdn"] for c in range(NC)], axis=0)

    k2 = cmap(["ones1", "iden_f", "cvals"])
    r2 = _run(ncs["L2"], [
        dict(k2, xln=r1[c]["xln"], m_all=m1, clip=clip_x, qg=qg_f,
             qu=qu_f, sg=sg_f, su=su_f)
        for c in range(NC)])
    m2 = np.concatenate([r2[c]["m2"] for c in range(NC)], axis=0)

    k3 = cmap(["ones1", "onescol", "iden_f", "cvals"])
    r3 = _run(ncs["L3"], [
        dict(k3, a=r2[c]["a"], xfm=r1[c]["xfm"], m_all=m2, clip=clip_d,
             qd=qd_f, sdn=sdn_f)
        for c in range(NC)])

    k45 = cmap(["ones1", "onescol", "iden_b", "iden_f", "maskc", "parity",
                "iotas", "cvals"])
    zhalo = np.zeros((D, 129), BF)
    xh = [np.ascontiguousarray(np.concatenate(
        [(np.asarray(r3[c - 1]["xln2"])[:, T - 129:T] if c > 0 else zhalo),
         np.asarray(r3[c]["xln2"])], axis=1)) for c in range(NC)]
    r5 = _run(ncs["L45"], [
        dict(k45, xln2h=xh[c], x1=r3[c]["x1"],
             w_r=w_r, w_k=w_k, w_v=w_v, w_g=w_g, w_o=w_o,
             tmix=tmix, tdec=tdec, ln_g=lng, ln_b=lnb,
             core0=(ones_c0 if c == 0 else zero_c0))
        for c in range(NC)])
    m3 = np.concatenate([r5[c]["m3"] for c in range(NC)], axis=0)

    r6 = _run(ncs["L2"], [
        dict(k2, xln=r5[c]["xln4"], m_all=m3, clip=clip_x, qg=qg_f,
             qu=qu_f, sg=sg_f, su=su_f)
        for c in range(NC)])
    m4 = np.concatenate([r6[c]["m2"] for c in range(NC)], axis=0)

    r7 = _run(ncs["L7"], [
        dict(k3, a=r6[c]["a"], xfm=r5[c]["x2"], m_all=m4, clip=clip_d,
             qd=qd_f, sdn=sdn_f)
        for c in range(NC)])

    out = np.concatenate(
        [np.ascontiguousarray(np.asarray(r7[c]["out"]).T)
         for c in range(NC)], axis=0)
    return out.reshape(1, 16384, D)
